# version 08
# baseline (speedup 1.0000x reference)
import numpy as np
import ml_dtypes
import concourse.bass as bass
import concourse.mybir as mybir
from concourse.bass_utils import run_bass_kernel_spmd

F32 = mybir.dt.float32
BF16 = mybir.dt.bfloat16
AF = mybir.ActivationFunctionType
ALU = mybir.AluOpType
AX = mybir.AxisListType

NCORES = 8
D = 2048
S = 4096
TOK = 1024
NT = TOK // 128
NDC = D // 128
HF = 8192
LN_EPS = 1e-5
L2_EPS = 1e-8
C0_SHIFT = 30.0

bfdt = ml_dtypes.bfloat16


def _hadamard(n):
    h = np.array([[1.0]])
    while h.shape[0] < n:
        h = np.block([[h, h], [h, -h]])
    return h


H128_NP = _hadamard(128)
BD64_NP = np.kron(np.eye(2), _hadamard(64))
BD16_NP = np.kron(np.eye(8), _hadamard(16))



def build_pre():
    nc = bass.Bass("TRN2", target_bir_lowering=False)
    x = nc.dram_tensor("x", [TOK, D], F32, kind="ExternalInput").ap()
    lnT = nc.dram_tensor("lnT", [D, TOK], BF16, kind="ExternalOutput").ap()
    ident_c = nc.inline_tensor(np.eye(128, dtype=bfdt), name="c_ident").ap()

    with (
        nc.sbuf_tensor("xt", [128, NT * D], F32) as xt,
        nc.sbuf_tensor("scr", [128, D], F32) as scr,
        nc.sbuf_tensor("ln", [128, D], BF16) as ln,
        nc.sbuf_tensor("lt", [128, 2 * D], BF16) as lt,
        nc.sbuf_tensor("sm", [128, NT], F32) as sm,
        nc.sbuf_tensor("sq", [128, NT], F32) as sq,
        nc.sbuf_tensor("st", [128, 4 * NT], F32) as st,
        nc.sbuf_tensor("ident", [128, 128], BF16) as ident,
        nc.psum_tensor("pst", [128, 2048], BF16) as pst,
        nc.semaphore("sL") as sL,
        nc.semaphore("sV") as sV,
        nc.semaphore("sA") as sA,
        nc.semaphore("sT") as sT,
        nc.semaphore("sE") as sE,
        nc.semaphore("sO") as sO,
        nc.Block() as block,
    ):
        n = NT
        mean, veps, rstd, nmrs = (st[:, 0:n], st[:, n:2 * n], st[:, 2 * n:3 * n], st[:, 3 * n:4 * n])

        @block.sync
        def _(e):
            e.dma_start(out=ident[:], in_=ident_c[:, :]).then_inc(sL, 16)
            e.dma_start(out=xt[:].rearrange("p (tt d) -> p tt d", d=D),
                        in_=x.rearrange("(tt p) d -> p tt d", p=128)).then_inc(sL, 16)
            for tt in range(NT):
                e.wait_ge(sE, 4 * tt + 4)
                e.dma_start(
                    out=lnT.rearrange("(c p) t -> p c t", p=128)[:, :, tt * 128:(tt + 1) * 128],
                    in_=lt[:, (tt % 2) * D:(tt % 2 + 1) * D].rearrange("p (c t) -> p c t", t=128),
                ).then_inc(sO, 16)
            e.wait_ge(sO, 16 * NT)

        @block.scalar
        def _(s):
            s.wait_ge(sL, 32)
            for tt in range(NT):
                s.activation(scr[:], xt[:, tt * D:(tt + 1) * D], AF.Square,
                             accum_out=sq[:, tt:tt + 1]).then_inc(sA, 1)
            s.wait_ge(sV, 1)
            s.activation(veps, veps, AF.Sqrt).then_inc(sA, 1)
            s.wait_ge(sV, 2)
            for tt in range(NT):
                if tt >= 1:
                    s.wait_ge(sT, 4 * tt)
                s.activation(ln[:], xt[:, tt * D:(tt + 1) * D], AF.Identity,
                             bias=nmrs[:, tt:tt + 1],
                             scale=rstd[:, tt:tt + 1]).then_inc(sA, 1)

        @block.vector
        def _(v):
            v.wait_ge(sL, 32)
            for tt in range(NT):
                v.reduce_sum(sm[:, tt:tt + 1], xt[:, tt * D:(tt + 1) * D], axis=AX.X)
            v.wait_ge(sA, NT)
            v.nop()
            v.tensor_scalar(mean, sm[:], 1.0 / D, None, ALU.mult)
            v.tensor_scalar(veps, sq[:], 1.0 / D, LN_EPS, ALU.mult, ALU.add)
            v.nop()
            v.tensor_tensor(rstd, mean, mean, ALU.mult)
            v.nop()
            v.tensor_sub(veps, veps, rstd).then_inc(sV, 1)
            v.wait_ge(sA, NT + 1)
            v.nop()
            v.reciprocal(rstd, veps)
            v.nop()
            v.tensor_tensor(nmrs, mean, rstd, ALU.mult)
            v.nop()
            v.tensor_scalar(nmrs, nmrs, -1.0, None, ALU.mult).then_inc(sV, 1)
            for tt in range(NT):
                for grp in range(4):
                    gi = tt * 4 + grp
                    v.wait_ge(sT, gi + 1)
                    if tt >= 2:
                        v.wait_ge(sO, 16 * (tt - 1))
                    v.tensor_copy(lt[:, (tt % 2) * D + grp * 512:(tt % 2) * D + (grp + 1) * 512],
                                  pst[:, (gi % 2) * 1024:(gi % 2) * 1024 + 512]).then_inc(sE, 1)

        @block.tensor
        def _(t):
            t.wait_ge(sL, 32)
            for tt in range(NT):
                t.wait_ge(sA, NT + 2 + tt)
                for grp in range(4):
                    gi = tt * 4 + grp
                    if gi >= 2:
                        t.wait_ge(sE, gi - 1)
                    ins = None
                    for j in range(4):
                        fc = grp * 4 + j
                        ins = t.transpose(pst[:, (gi % 2) * 1024 + j * 128:(gi % 2) * 1024 + (j + 1) * 128],
                                          ln[:, fc * 128:(fc + 1) * 128], ident[:])
                    ins.then_inc(sT, 1)

    return nc




def build_att(layer, debug=False, p2=True):
    import os
    DBGV = os.environ.get("ATTDBG", "")
    nc = bass.Bass("TRN2", target_bir_lowering=False)
    lnT = nc.dram_tensor("lnT", [D, S], BF16, kind="ExternalInput").ap()
    w_sl = nc.dram_tensor("w_sl", [D, 1536], F32, kind="ExternalInput").ap()
    if layer == 0:
        bq = nc.dram_tensor("bq", [1536], F32, kind="ExternalInput").ap()
        att_c = nc.dram_tensor("att_c", [512, S], BF16, kind="ExternalOutput").ap()
    else:
        tau_d = nc.dram_tensor("tau", [1], F32, kind="ExternalInput").ap()
        nu_d = nc.dram_tensor("nu", [1], F32, kind="ExternalInput").ap()
        att_c = nc.dram_tensor("att_c", [S, 512], BF16, kind="ExternalOutput").ap()
    if debug:
        qf_d = nc.dram_tensor("qf_d", [128, 4 * S], BF16, kind="ExternalOutput").ap()
        kf_d = nc.dram_tensor("kf_d", [128, 4 * S], BF16, kind="ExternalOutput").ap()
        vt_d = nc.dram_tensor("vt_d", [128, 32 * 528], BF16, kind="ExternalOutput").ap()

    ident_c = nc.inline_tensor(np.eye(128, dtype=bfdt), name="c_ident").ap()
    mask_np = (np.arange(128)[:, None] <= np.arange(128)[None, :]).astype(bfdt)
    mask_c = nc.inline_tensor(mask_np, name="c_mask").ap()
    ones_f = nc.inline_tensor(np.ones((1, 128), dtype=np.float32), name="c_onesf").ap()
    ones_b = nc.inline_tensor(np.ones((1, 128), dtype=bfdt), name="c_onesb").ap()

    NTB = 8
    NQB = 8
    NH = 4

    def kts(qb):
        return 4 * qb + 4

    def hq(h, qb):
        return h * NQB + qb

    def exp_base(h, qb):
        return sum(kts(i % NQB) for i in range(hq(h, qb)))

    smm_ct, pv_ct, rb_ct, pvqt_ct = {}, {}, {}, {}
    ctr = 0
    for i in range(NH * NQB):
        nk_i = kts(i % NQB)
        for kt in range(nk_i):
            ctr += 1
            smm_ct[(i, kt)] = ctr
        if layer == 0:
            if i >= 1:
                ctr += 1
                rb_ct[i - 1] = ctr
            ctr += 1
            pv_ct[i] = ctr
        else:
            for qt in range(4):
                ctr += 1
                pvqt_ct[(i, qt)] = ctr
    if layer == 0:
        ctr += 1
        rb_ct[NH * NQB - 1] = ctr
    qkv_ct, tr_ct = {}, {}
    if layer == 1:
        c2 = 0
        c2 += 1; qkv_ct[0] = c2
        for gt in range(1, 32):
            c2 += 1; qkv_ct[gt] = c2
            c2 += 1; tr_ct[gt - 1] = c2
        c2 += 1; tr_ct[31] = c2

    P1_PE_TB = 12 if layer == 0 else 8
    SCTOT = 80 if layer == 0 else 96
    SWTOT = 32 if layer == 0 else 16

    import contextlib
    es = contextlib.ExitStack()
    with es:
        sb = lambda *a: es.enter_context(nc.sbuf_tensor(*a))
        ps = lambda *a: es.enter_context(nc.psum_tensor(*a))
        sem = lambda n: es.enter_context(nc.semaphore(n))
        QF = sb("QF", [128, NH * S], BF16)
        KF = sb("KF", [128, NH * S], BF16)
        VT = sb("VT", [128, 32 * 528], BF16)
        AR = sb("AR", [128, 41984], BF16)
        PT = AR[:, 0:16384]
        lnb = AR[:, 0:16384]
        ws = AR[:, 16384:40960]
        qtok = AR[:, 40960:41984]
        scrq = sb("scrq", [128, 1536], F32)
        ident = sb("ident", [128, 128], BF16)
        mask01 = sb("mask01", [128, 128], BF16)
        ones128 = sb("ones128", [1, 128], F32)
        bqs = sb("bqs", [128, 12], F32)
        bvrow = sb("bvrow", [1, 512], BF16)
        taunu = sb("taunu", [1, 2], F32)
        tnb = sb("tnb", [128, 4], F32)
        csh = sb("csh", [128, 2], F32)
        ssq = sb("ssq", [128, 36], F32)
        rB = sb("rB", [128, 512], F32)
        rr = sb("rr", [1, 512], F32)
        onesb = sb("onesb", [1, 128], BF16)
        attnb = sb("attnb", [128, 512], BF16)
        psA = ps("psA", [128, 4096], F32)
        sL = sem("sL"); sP0 = sem("sP0"); sPE = sem("sPE"); sA1 = sem("sA1")
        sC = sem("sC"); sB0 = sem("sB0"); sB1 = sem("sB1"); sW = sem("sW"); sD = sem("sD")
        sV1 = sem("sV1"); sT2 = sem("sT2"); sX = sem("sX"); sM = sem("sM")
        sV2 = sem("sV2"); sO = sem("sO")
        block = es.enter_context(nc.Block())
        psQK = psA[:, 0:1024]
        psV = psA[:, 1024:2048]
        psQ1 = psA[:, 0:512]
        psK1 = psA[:, 512:1024]
        psV1_ = psA[:, 1024:1536]
        psT1 = psA[:, 1536:2560].bitcast(BF16)
        psS = psA[:, 0:2048]
        psO = psA[:, 2048:2560]
        psR = psA[:, 2560:3072]
        psSum = psA[0:1, 3072:3584]
        psO3 = psA[:, 2048:4096]
        psB0 = psA[:, 3584:4096]

        @block.sync
        def _(e):
            e.dma_start(out=ident[:], in_=ident_c[:, :]).then_inc(sC, 16)
            e.dma_start(out=mask01[:], in_=mask_c[:, :]).then_inc(sC, 16)
            e.dma_start(out=ones128[:], in_=ones_f[:, :]).then_inc(sC, 16)
            e.dma_start(out=onesb[:], in_=ones_b[:, :]).then_inc(sC, 16)
            if layer == 0:
                e.dma_start(out=bqs[:, 0:12],
                            in_=bq.rearrange("(o p) -> p o", p=128),
                            allow_slow_non_contiguous=True).then_inc(sC, 16)
            else:
                e.dma_start(out=taunu[0:1, 0:1], in_=tau_d.unsqueeze(0)).then_inc(sC, 16)
                e.dma_start(out=taunu[0:1, 1:2], in_=nu_d.unsqueeze(0)).then_inc(sC, 16)
            for tb in range(NTB):
                if tb >= 2:
                    e.wait_ge(sPE, (tb - 1) * P1_PE_TB)
                e.dma_start(
                    out=lnb[:, (tb % 2) * NDC * 512:(tb % 2 + 1) * NDC * 512].rearrange("p (c t) -> p c t", t=512),
                    in_=lnT.rearrange("(c p) t -> p c t", p=128)[:, :, tb * 512:(tb + 1) * 512],
                ).then_inc(sB0 if tb % 2 == 0 else sB1, 16)
            nst = 0
            for h in range(NH):
                if not p2:
                    break
                for qb in range(NQB):
                    i = hq(h, qb)
                    if layer == 0:
                        e.wait_ge(sV2, 2 * i + 2)
                        e.dma_start(out=att_c[h * 128:(h + 1) * 128, qb * 512:(qb + 1) * 512],
                                    in_=attnb[:]).then_inc(sO, 16)
                        nst += 16
                    else:
                        for qt in range(4):
                            e.wait_ge(sV2, 4 * i + qt + 1)
                            e.dma_start(
                                out=att_c[qb * 512 + qt * 128:qb * 512 + (qt + 1) * 128,
                                          h * 128:(h + 1) * 128],
                                in_=attnb[:, qt * 128:(qt + 1) * 128]).then_inc(sO, 16)
                            nst += 16
            if debug:
                e.wait_ge(sPE, NTB * P1_PE_TB)
                if layer == 0:
                    e.wait_ge(sA1, 64)
                    e.wait_ge(sV1, 32)
                else:
                    e.wait_ge(sV1, 128)
                e.dma_start(out=qf_d[:, :], in_=QF[:]).then_inc(sO, 16)
                e.dma_start(out=kf_d[:, :], in_=KF[:]).then_inc(sO, 16)
                e.dma_start(out=vt_d[:, :], in_=VT[:]).then_inc(sO, 16)
                nst += 48
            e.wait_ge(sO, nst)

        @block.gpsimd
        def _(g):
            g.dma_start(out=ws[:].rearrange("p (c o) -> p c o", o=1536),
                        in_=w_sl.rearrange("(c p) o -> p c o", p=128)).then_inc(sW, 16)
            if layer == 0:
                g.dma_start(out=bvrow[0:1, :], in_=bq[1024:1536].unsqueeze(0)).then_inc(sW, 16)

        @block.tensor
        def _(t):
            t.wait_ge(sC, SCTOT)
            t.wait_ge(sW, SWTOT)
            if layer == 1:
                if "notau" in DBGV:
                    t.nop().then_inc(sP0, 1)
                else:
                    t.matmul(psB0[:, 0:2], ones128[0:1, :], taunu[0:1, :],
                             start=True, stop=True).then_inc(sP0, 1)
            for tb in range(NTB):
                t.wait_ge(sB0 if tb % 2 == 0 else sB1, 16 * (tb // 2 + 1))
                if layer == 0:
                    lnv = lnb[:, (tb % 2) * NDC * 512:].rearrange("p (c t) -> p c t", t=512)
                    for of in range(8):
                        if 8 * tb + of >= 2:
                            t.wait_ge(sA1, 8 * tb + of - 1)
                        pso = psQK[:, (of % 2) * 512:(of % 2) * 512 + 512]
                        for dc in range(NDC):
                            ins = t.matmul(pso, ws[:, dc * 1536 + of * 128:dc * 1536 + (of + 1) * 128],
                                           lnv[:, dc, :], start=(dc == 0), stop=(dc == NDC - 1))
                        ins.then_inc(sPE, 1)
                    for tt in range(4):
                        if 4 * tb + tt >= 2:
                            t.wait_ge(sV1, 4 * tb + tt - 1)
                        psv = psV[:, (tt % 2) * 512:(tt % 2) * 512 + 512]
                        for dc in range(NDC):
                            t.matmul(psv, lnb[:, (tb % 2) * NDC * 512 + dc * 512 + tt * 128:
                                               (tb % 2) * NDC * 512 + dc * 512 + (tt + 1) * 128],
                                     ws[:, dc * 1536 + 1024:dc * 1536 + 1536],
                                     start=(dc == 0), stop=False)
                        t.matmul(psv, onesb[0:1, :], bvrow[0:1, :],
                                 start=False, stop=True).then_inc(sPE, 1)
                else:
                    for tt in range(4):
                        gt = tb * 4 + tt
                        if gt == 0:
                            t.wait_ge(sP0, 3)
                        if gt >= 1:
                            t.wait_ge(sV1, 4 * (gt - 1) + 3)
                        for pj, pso in enumerate((psQ1, psK1, psV1_)):
                            for dc in range(NDC):
                                lhs = lnb[:, (tb % 2) * NDC * 512 + dc * 512 + tt * 128:
                                             (tb % 2) * NDC * 512 + dc * 512 + (tt + 1) * 128]
                                ins = t.matmul(pso, lhs,
                                               ws[:, dc * 1536 + pj * 512:dc * 1536 + (pj + 1) * 512],
                                               start=(dc == 0), stop=(dc == NDC - 1))
                        ins.then_inc(sPE, 1)
                        t.wait_ge(sV1, 4 * gt + 3)
                        if "notr" in DBGV:
                            t.nop().then_inc(sPE, 1)
                        else:
                            ins = None
                            for j in range(4):
                                ins = t.transpose(psT1[:, j * 128:(j + 1) * 128],
                                                  qtok[:, j * 128:(j + 1) * 128], ident[:])
                            for j in range(4):
                                ins = t.transpose(psT1[:, 512 + j * 128:512 + (j + 1) * 128],
                                                  qtok[:, 512 + j * 128:512 + (j + 1) * 128], ident[:])
                            ins.then_inc(sPE, 1)
            if not p2:
                return
            t.wait_ge(sA1 if layer == 0 else sV1, 64 if layer == 0 else 128)
            t.wait_ge(sV1 if layer == 0 else sV1, 32 if layer == 0 else 128)
            for h in range(NH):
                for qb in range(NQB):
                    i = hq(h, qb)
                    nk = kts(qb)
                    eb = exp_base(h, qb)
                    pb = pe_base(h, qb)
                    for kt in range(nk):
                        if kt >= 4:
                            t.wait_ge(sX, eb + kt - 3)
                        j = kt - 4 * qb
                        n0 = 128 * max(0, j)
                        t.matmul(psS[:, (kt % 4) * 512 + n0:(kt % 4) * 512 + 512],
                                 KF[:, h * S + kt * 128:h * S + (kt + 1) * 128],
                                 QF[:, h * S + qb * 512 + n0:h * S + (qb + 1) * 512],
                                 start=True, stop=True).then_inc(sT2, 1)
                    t.wait_ge(sX, eb + nk)
                    t.wait_ge(sM, 4 * i + 5)
                    if layer == 0:
                        t.wait_ge(sV2, 2 * i)
                        for kt in range(nk):
                            j = kt - 4 * qb
                            n0 = 128 * max(0, j)
                            vtb = VT[:, kt * 528 + h * 132:]
                            t.matmul(psO[:, n0:512], vtb[:, 0:128],
                                     PT[:, kt * 512 + n0:(kt + 1) * 512],
                                     start=(kt == 0), stop=(kt == nk - 1))
                            ins = t.matmul(psSum[:, n0:512], vtb[:, 128:129],
                                            PT[:, kt * 512 + n0:(kt + 1) * 512],
                                            start=(kt == 0), stop=(kt == nk - 1))
                        ins.then_inc(sT2, 1)
                        t.wait_ge(sV2, 2 * i + 1)
                        t.matmul(psR[:], ones128[0:1, :], rr[0:1, :],
                                 start=True, stop=True).then_inc(sT2, 1)
                    else:
                        t.wait_ge(sV2, 4 * i)
                        for qt in range(4):
                            qtg = 4 * qb + qt
                            for kt in range(qtg + 1):
                                vtb = VT[:, kt * 528 + h * 132:]
                                ins = t.matmul(psO3[:, qt * 512:qt * 512 + 129],
                                               PT[:, kt * 512 + qt * 128:kt * 512 + (qt + 1) * 128],
                                               vtb[:, 0:129],
                                               start=(kt == 0), stop=(kt == qtg))
                            ins.then_inc(sT2, 1)

        @block.scalar
        def _(s):
            if layer == 1:
                s.wait_ge(sP0, 1)
                if "notau" in DBGV:
                    s.nop().then_inc(sP0, 1)
                else:
                    s.activation(tnb[:, 3:4], psB0[:, 0:1], AF.Abs).then_inc(sP0, 1)
            if layer == 0:
                for tb in range(NTB):
                    for of in range(8):
                        s.wait_ge(sPE, tb * 12 + of + 1)
                        dst = (QF if of < 4 else KF)[:, (of % 4) * S + tb * 512:(of % 4) * S + (tb + 1) * 512]
                        s.activation(dst, psQK[:, (of % 2) * 512:(of % 2) * 512 + 512],
                                     AF.Identity, bias=bqs[:, of:of + 1], scale=1.0).then_inc(sA1, 1)
            else:
                for tb in range(NTB):
                    for tt in range(4):
                        gt = tb * 4 + tt
                        s.wait_ge(sPE, 2 * gt + 1)
                        if gt >= 1:
                            s.wait_ge(sV1, 4 * (gt - 1) + 1)
                        s.activation(scrq[:, 0:512], psQ1, AF.Square).then_inc(sA1, 1)
                        s.activation(scrq[:, 512:1024], psK1, AF.Square).then_inc(sA1, 1)
                        s.activation(scrq[:, 1024:1536], psV1_, AF.Square).then_inc(sA1, 1)
                        s.wait_ge(sV1, 4 * gt + 1)
                        s.activation(ssq[:, 12:24], ssq[:, 0:12], AF.Sqrt).then_inc(sA1, 1)
            if not p2:
                return
            s.wait_ge(sM, 1)
            if layer == 1:
                s.wait_ge(sP0, 3)
            for h in range(NH):
                for qb in range(NQB):
                    i = hq(h, qb)
                    nk = kts(qb)
                    eb = exp_base(h, qb)
                    pb = pe_base(h, qb)
                    for kt in range(nk):
                        s.wait_ge(sT2, pb + kt + 1)
                        j = kt - 4 * qb
                        n0 = 128 * max(0, j)
                        if layer == 0:
                            s.activation(PT[:, kt * 512 + n0:(kt + 1) * 512],
                                         psS[:, (kt % 4) * 512 + n0:(kt % 4) * 512 + 512],
                                         AF.Exp, bias=csh[:, 0:1],
                                         scale=csh[:, 1:2]).then_inc(sX, 1)
                        else:
                            s.activation(PT[:, kt * 512 + n0:(kt + 1) * 512],
                                         psS[:, (kt % 4) * 512 + n0:(kt % 4) * 512 + 512],
                                         AF.Exp, bias=tnb[:, 1:2],
                                         scale=tnb[:, 0:1]).then_inc(sX, 1)

        @block.vector
        def _(v):
            nd = [0]

            def dedge(ins):
                nd[0] += 1
                ins.then_inc(sD, 1)
                v.wait_ge(sD, nd[0])

            v.wait_ge(sC, SCTOT)
            v.wait_ge(sW, SWTOT)
            v.memset(VT[:], 1.0)
            v.memset(csh[:, 0:1], -C0_SHIFT)
            v.memset(csh[:, 1:2], 0.08838834764831843).then_inc(sM, 1)
            if layer == 1:
                v.wait_ge(sP0, 2)
                if "notau" in DBGV:
                    v.memset(tnb[:, 0:1], 1.0)
                    v.memset(tnb[:, 1:2], -1.0)
                    v.memset(tnb[:, 2:3], 1.0).then_inc(sP0, 1)
                else:
                    v.tensor_copy(tnb[:, 0:1], psB0[:, 0:1])
                    v.nop()
                    v.tensor_scalar(tnb[:, 1:2], tnb[:, 3:4], -1.0, None, ALU.mult)
                    v.tensor_copy(tnb[:, 2:3], psB0[:, 1:2]).then_inc(sP0, 1)
            if layer == 0:
                for tb in range(NTB):
                    for tt in range(4):
                        v.wait_ge(sM, 1)
                        v.wait_ge(sPE, tb * 12 + 8 + tt + 1)
                        gt = tb * 4 + tt
                        dst = VT[:, gt * 528:(gt + 1) * 528].rearrange("p (h d) -> p h d", d=132)[:, :, 0:128]
                        srcv = psV[:, (tt % 2) * 512:(tt % 2) * 512 + 512].rearrange("p (h d) -> p h d", d=128)
                        v.tensor_copy(dst, srcv).then_inc(sV1, 1)
            else:
                for tb in range(NTB):
                    for tt in range(4):
                        gt = tb * 4 + tt
                        v.wait_ge(sM, 1)
                        v.wait_ge(sA1, 4 * gt + 3)
                        for pj in range(3):
                            ins = v.tensor_reduce(ssq[:, pj * 4:(pj + 1) * 4].unsqueeze(2),
                                                  scrq[:, pj * 512:(pj + 1) * 512].rearrange("p (h d) -> p h d", d=128),
                                                  axis=AX.X, op=ALU.add)
                        ins.then_inc(sV1, 1)
                        v.wait_ge(sA1, 4 * gt + 4)
                        v.nop()
                        dedge(v.tensor_scalar(ssq[:, 24:36], ssq[:, 12:24], L2_EPS, None, ALU.add))
                        v.reciprocal(ssq[:, 12:24], ssq[:, 24:36]).then_inc(sV1, 1)
                        v.wait_ge(sV1, 4 * gt + 2)
                        rq = ssq[:, 12:16].unsqueeze(2).broadcast_to([128, 4, 128])
                        rk = ssq[:, 16:20].unsqueeze(2).broadcast_to([128, 4, 128])
                        rv = ssq[:, 20:24].unsqueeze(2).broadcast_to([128, 4, 128])
                        v.tensor_tensor(qtok[:, 0:512].rearrange("p (h d) -> p h d", d=128),
                                        psQ1.rearrange("p (h d) -> p h d", d=128), rq, ALU.mult)
                        v.tensor_tensor(qtok[:, 512:1024].rearrange("p (h d) -> p h d", d=128),
                                        psK1.rearrange("p (h d) -> p h d", d=128), rk, ALU.mult)
                        dstv = VT[:, gt * 528:(gt + 1) * 528].rearrange("p (h d) -> p h d", d=132)[:, :, 0:128]
                        v.tensor_tensor(dstv, psV1_.rearrange("p (h d) -> p h d", d=128), rv, ALU.mult).then_inc(sV1, 1)
                        v.wait_ge(sPE, 2 * gt + 2)
                        if "notr" in DBGV:
                            v.nop().then_inc(sV1, 1)
                        else:
                            v.tensor_copy(QF[:].rearrange("p (h t) -> p h t", t=S)[:, :, gt * 128:(gt + 1) * 128],
                                          psT1[:, 0:512].rearrange("p (h d) -> p h d", d=128))
                            v.tensor_copy(KF[:].rearrange("p (h t) -> p h t", t=S)[:, :, gt * 128:(gt + 1) * 128],
                                          psT1[:, 512:1024].rearrange("p (h d) -> p h d", d=128)).then_inc(sV1, 1)
            if not p2:
                return
            for h in range(NH):
                for qb in range(NQB):
                    i = hq(h, qb)
                    nk = kts(qb)
                    eb = exp_base(h, qb)
                    pb = pe_base(h, qb)
                    for jj in range(4):
                        kt = 4 * qb + jj
                        v.wait_ge(sX, eb + kt + 1)
                        dg = PT[:, kt * 512 + jj * 128:kt * 512 + (jj + 1) * 128]
                        v.tensor_tensor(dg, dg, mask01[:], ALU.mult).then_inc(sM, 1)
                    if layer == 0:
                        v.wait_ge(sT2, pb + nk + 1)
                        v.nop()
                        v.reciprocal(rr[0:1, :], psSum[:]).then_inc(sV2, 1)
                        v.wait_ge(sT2, pb + nk + 2)
                        if i >= 1:
                            v.wait_ge(sO, 16 * i)
                        dedge(v.tensor_copy(rB[:], psR[:]))
                        v.tensor_tensor(attnb[:], psO[:], rB[:], ALU.mult).then_inc(sV2, 1)
                    else:
                        for qt in range(4):
                            v.wait_ge(sT2, pb + nk + qt + 1)
                            v.wait_ge(sV2, 4 * i + qt)
                            if i >= 1:
                                v.wait_ge(sO, 64 * i)
                            dedge(v.reciprocal(tnb[:, 3:4], psO3[:, qt * 512 + 128:qt * 512 + 129]))
                            dedge(v.tensor_tensor(tnb[:, 3:4], tnb[:, 3:4], tnb[:, 2:3], ALU.mult))
                            v.tensor_scalar(attnb[:, qt * 128:(qt + 1) * 128],
                                            psO3[:, qt * 512:qt * 512 + 128],
                                            tnb[:, 3:4], None, ALU.mult).then_inc(sV2, 1)

    return nc




def build_post(layer):
    nc = bass.Bass("TRN2", target_bir_lowering=False)
    x_in = nc.dram_tensor("x", [TOK, D], F32, kind="ExternalInput").ap()
    if layer == 0:
        attnT = nc.dram_tensor("attnT", [D, TOK], BF16, kind="ExternalInput").ap()
        wd_d = nc.dram_tensor("wd", [D, D], F32, kind="ExternalInput").ap()
        bd_d = nc.dram_tensor("bd", [D], F32, kind="ExternalInput").ap()
    else:
        attn_tok = nc.dram_tensor("attn_tok", [TOK, D], BF16, kind="ExternalInput").ap()
        su_d = nc.dram_tensor("su_ug", [D], F32, kind="ExternalInput").ap()
        wsu_d = nc.dram_tensor("wsu", [HF], F32, kind="ExternalInput").ap()
        wsg_d = nc.dram_tensor("wsg", [HF], F32, kind="ExternalInput").ap()
        sdp_d = nc.dram_tensor("sdp", [HF], F32, kind="ExternalInput").ap()
        wdp_d = nc.dram_tensor("wdp", [D], F32, kind="ExternalInput").ap()
    w1_d = nc.dram_tensor("w1t", [D, HF], F32, kind="ExternalInput").ap()
    w3_d = nc.dram_tensor("w3t", [D, HF], F32, kind="ExternalInput").ap()
    w2_d = nc.dram_tensor("w2t", [HF, D], F32, kind="ExternalInput").ap()
    x2_out = nc.dram_tensor("x2", [TOK, D], F32, kind="ExternalOutput").ap()
    x1_dram = nc.dram_tensor("x1buf", [TOK, D], F32, kind="Internal").ap()
    bounce8 = nc.dram_tensor("bounce8", [HF, TOK], BF16, kind="Internal").ap()
    if layer == 1:
        bounce2 = nc.dram_tensor("bounce2", [D, TOK], BF16, kind="Internal").ap()

    ident_c = nc.inline_tensor(np.eye(128, dtype=bfdt), name="c_ident").ap()
    h128_c = nc.inline_tensor(H128_NP.astype(bfdt), name="c_h128").ap()
    bd64_c = nc.inline_tensor(BD64_NP.astype(bfdt), name="c_bd64").ap()
    bd16_c = nc.inline_tensor(BD16_NP.astype(bfdt), name="c_bd16").ap()

    FWN = float(1.0 / np.sqrt(8192.0))
    FWN2 = float(1.0 / np.sqrt(2048.0))
    NHC = 64
    NSB = 16
    VA_TT = 4 if layer == 0 else 1

    import contextlib
    es = contextlib.ExitStack()
    with es:
        sb = lambda *a: es.enter_context(nc.sbuf_tensor(*a))
        psal = lambda *a: es.enter_context(nc.psum_tensor(*a))
        sem = lambda n: es.enter_context(nc.semaphore(n))

        lnF = sb("lnF", [128, NDC * 1024], BF16)
        AR = sb("AR", [128, 65536], BF16)
        WX = sb("WX", [128, 16384], BF16)
        x1cb = sb("x1cb", [128, 2 * 1024], F32)
        ysb = sb("ysb", [128, 1024], BF16)
        x2b = sb("x2b", [128, 2 * 128], F32)
        consts = sb("consts", [128, 512], BF16)
        pcol = sb("pcol", [128, 32], F32)
        pcol2 = sb("pcol2", [128, 192], F32)
        smq = sb("smq", [128, 2 * NT], F32)
        stt = sb("stt", [128, 4 * NT], F32)
        psA = psal("psA", [128, 4096], F32)

        ident = consts[:, 0:128]
        h128k = consts[:, 128:256]
        bd64k = consts[:, 256:384]
        bd16k = consts[:, 384:512]

        atT = AR[:, 0:16384]
        dnsF = AR[:, 16384:32768]
        wdstr = AR[:, 32768:36864]
        at1 = AR[:, 32768:36864]
        a32 = AR[:, 16384:20480].bitcast(F32)
        xinA = AR[:, 36864:45056].bitcast(F32)
        x1tA = AR[:, 45056:53248].bitcast(F32)
        xtB = AR[:, 0:32768].bitcast(F32)
        lntB = AR[:, 32768:34816]
        scrB = AR[:, 53248:57344].bitcast(F32)
        scrB2 = AR[:, 57344:61440].bitcast(F32)
        wstr = AR[:, 0:32768]
        sgb = AR[:, 32768:33792]
        hck = AR[:, 33792:34816]
        y1b = AR[:, 34816:36864]
        c1b = AR[:, 36864:38912]
        h2 = AR[:, 0:65536]
        rcb = WX[:, 0:2048]
        w2str = WX[:, 0:16384]

        psF = psA[:, 0:2048]
        psTd = psA[:, 2048:3072].bitcast(BF16)
        psU = psA[:, 0:1024]
        psG = psA[:, 1024:2048]
        psH = psA[:, 2048:3072]
        psB2 = psA[:, 3072:4096]
        psY = psA[:, 0:1024]

        sC = sem("sC"); sAT = sem("sAT")
        sATa = sem("sATa"); sATb = sem("sATb")
        sXa = sem("sXa"); sXb = sem("sXb"); sXB = sem("sXB")
        sWDa = sem("sWDa"); sWDb = sem("sWDb")
        sW2a = sem("sW2a"); sW2b = sem("sW2b")
        sXCa = sem("sXCa"); sXCb = sem("sXCb")
        sPA = sem("sPA"); sEA = sem("sEA"); sVA = sem("sVA")
        sPB = sem("sPB"); sAB = sem("sAB"); sVB = sem("sVB")
        sP1 = sem("sP1"); sV1b = sem("sV1b")
        sP2 = sem("sP2"); sA3 = sem("sA3"); sV3 = sem("sV3")
        sP5 = sem("sP5"); sV5 = sem("sV5")
        sP6 = sem("sP6"); sA6 = sem("sA6"); sV6 = sem("sV6")
        sX1Oa = sem("sX1Oa"); sX1Ob = sem("sX1Ob"); sB8wa = sem("sB8wa"); sB8wb = sem("sB8wb")
        sB8ra = sem("sB8ra"); sB8rb = sem("sB8rb")
        sB2wa = sem("sB2wa"); sB2wb = sem("sB2wb"); sB2ra = sem("sB2ra"); sB2rb = sem("sB2rb")
        sOa = sem("sOa"); sOb = sem("sOb"); sD = sem("sD")
        block = es.enter_context(nc.Block())

        SCTOT = 80 if layer == 0 else 144
        W1BASE = 8 * 16 if layer == 0 else 0

        @block.sync
        def _(e):
            e.dma_start(out=consts[:, 0:128], in_=ident_c[:, :]).then_inc(sC, 16)
            e.dma_start(out=consts[:, 128:256], in_=h128_c[:, :]).then_inc(sC, 16)
            e.dma_start(out=consts[:, 256:384], in_=bd64_c[:, :]).then_inc(sC, 16)
            e.dma_start(out=consts[:, 384:512], in_=bd16_c[:, :]).then_inc(sC, 16)
            if layer == 0:
                e.dma_start(out=pcol[:, 0:16], in_=bd_d.rearrange("(c p) -> p c", p=128),
                            allow_slow_non_contiguous=True).then_inc(sC, 16)
                e.dma_start(out=atT[:].rearrange("p (c t) -> p c t", t=1024),
                            in_=attnT.rearrange("(c p) t -> p c t", p=128)).then_inc(sAT, 16)
            else:
                e.dma_start(out=pcol[:, 0:16], in_=su_d.rearrange("(c p) -> p c", p=128),
                            allow_slow_non_contiguous=True).then_inc(sC, 16)
                e.dma_start(out=pcol[:, 16:32], in_=wdp_d.rearrange("(c p) -> p c", p=128),
                            allow_slow_non_contiguous=True).then_inc(sC, 16)
                e.dma_start(out=pcol2[:, 0:64], in_=wsu_d.rearrange("(c p) -> p c", p=128),
                            allow_slow_non_contiguous=True).then_inc(sC, 16)
                e.dma_start(out=pcol2[:, 64:128], in_=wsg_d.rearrange("(c p) -> p c", p=128),
                            allow_slow_non_contiguous=True).then_inc(sC, 16)
                e.dma_start(out=pcol2[:, 128:192], in_=sdp_d.rearrange("(c p) -> p c", p=128),
                            allow_slow_non_contiguous=True).then_inc(sC, 16)
            def ld_x(tt):
                if tt >= 2:
                    e.wait_ge(sVA, VA_TT * (tt - 1))
                e.dma_start(out=xinA[:, (tt % 2) * D:(tt % 2 + 1) * D],
                            in_=x_in[tt * 128:(tt + 1) * 128, :]).then_inc(
                                sXa if tt % 2 == 0 else sXb, 16)
                if layer == 1:
                    e.dma_start(out=at1[:, (tt % 2) * 2048:(tt % 2 + 1) * 2048],
                                in_=attn_tok[tt * 128:(tt + 1) * 128, :]).then_inc(
                                    sATa if tt % 2 == 0 else sATb, 16)

            ld_x(0)
            ld_x(1)
            for tt in range(NT):
                if tt + 2 < NT:
                    ld_x(tt + 2)
                e.wait_ge(sVA, VA_TT * (tt + 1))
                e.dma_start(out=x1_dram[tt * 128:(tt + 1) * 128, :],
                            in_=x1tA[:, (tt % 2) * D:(tt % 2 + 1) * D]).then_inc(
                                sX1Oa if tt % 2 == 0 else sX1Ob, 16)
            e.wait_ge(sX1Oa, 16 * (NT // 2))
            e.wait_ge(sX1Ob, 16 * (NT // 2))
            e.dma_start(out=xtB[:].rearrange("p (tt d) -> p tt d", d=D),
                        in_=x1_dram.rearrange("(tt p) d -> p tt d", p=128)).then_inc(sXB, 16)
            if layer == 1:
                for fc in range(NDC):
                    e.wait_ge(sV1b, fc + 1)
                    e.dma_start(out=bounce2[fc * 128:(fc + 1) * 128, :],
                                in_=c1b[:, (fc % 2) * 1024:(fc % 2 + 1) * 1024]).then_inc(
                                    sB2wa if fc % 2 == 0 else sB2wb, 16)
                b2v = bounce2.rearrange("(p c) t -> p c t", c=16)
                for c in range(NDC):
                    e.wait_ge(sB2wa, 16 * (NDC // 2))
                    e.wait_ge(sB2wb, 16 * (NDC // 2))
                    if c >= 2:
                        e.wait_ge(sP1, 16 + c - 1)
                    e.dma_start(out=rcb[:, (c % 2) * 1024:(c % 2 + 1) * 1024],
                                in_=b2v[:, c, :]).then_inc(sB2ra if c % 2 == 0 else sB2rb, 16)
            for hc in range(NHC):
                e.wait_ge(sV3, 2 * hc + 2)
                e.dma_start(out=bounce8[hc * 128:(hc + 1) * 128, :],
                            in_=y1b[:, (hc % 2) * 1024:(hc % 2 + 1) * 1024]).then_inc(
                                sB8wa if hc % 2 == 0 else sB8wb, 16)
            b8v = bounce8.rearrange("(p c) t -> p c t", c=64)
            for c in range(NHC):
                e.wait_ge(sB8wa, 16 * (NHC // 2))
                e.wait_ge(sB8wb, 16 * (NHC // 2))
                if c >= 2:
                    e.wait_ge(sP5, c - 1)
                e.dma_start(out=rcb[:, (c % 2) * 1024:(c % 2 + 1) * 1024],
                            in_=b8v[:, c, :]).then_inc(sB8ra if c % 2 == 0 else sB8rb, 16)
            def ld_x1cb(doc):
                if doc >= 2:
                    e.wait_ge(sV6, 8 * (doc - 1))
                e.dma_start(out=x1cb[:, (doc % 2) * 1024:(doc % 2 + 1) * 1024].rearrange("p (tt c) -> p tt c", c=128),
                            in_=x1_dram.rearrange("(tt p) d -> p tt d", p=128)[:, :, doc * 128:(doc + 1) * 128],
                            ).then_inc(sXCa if doc % 2 == 0 else sXCb, 16)

            ld_x1cb(0)
            ld_x1cb(1)
            for doc in range(16):
                for tt in range(NT):
                    ti = doc * 8 + tt
                    e.wait_ge(sV6, ti + 1)
                    e.dma_start(out=x2_out[tt * 128:(tt + 1) * 128, doc * 128:(doc + 1) * 128],
                                in_=x2b[:, (ti % 2) * 128:(ti % 2 + 1) * 128]).then_inc(
                                    sOa if ti % 2 == 0 else sOb, 16)
                if doc + 2 < 16:
                    ld_x1cb(doc + 2)
            e.wait_ge(sOa, 16 * 64)
            e.wait_ge(sOb, 16 * 64)

        @block.gpsimd
        def _(g):
            if layer == 0:
                wdv = wd_d.rearrange("(c p) f -> p c f", p=128)
                for fc in range(NDC):
                    if fc >= 2:
                        g.wait_ge(sPA, fc - 1)
                    g.dma_start(out=wdstr[:, (fc % 2) * 2048:(fc % 2 + 1) * 2048].rearrange("p (c f) -> p c f", f=128),
                                in_=wdv[:, :, fc * 128:(fc + 1) * 128]).then_inc(
                                    sWDa if fc % 2 == 0 else sWDb, 16)
            if layer == 0:
                w1v = w1_d.rearrange("(c p) h -> p c h", p=128)
                w3v = w3_d.rearrange("(c p) h -> p c h", p=128)
            else:
                w1v = w1_d.rearrange("(p c) h -> p c h", c=16)
                w3v = w3_d.rearrange("(p c) h -> p c h", c=16)
            for sbk in range(NSB):
                if sbk >= 2:
                    g.wait_ge(sP2, 2 * 4 * (sbk - 1))
                else:
                    g.wait_ge(sAB, 17)
                    g.wait_ge(sVB, 34)
                base = (sbk % 2) * 16384
                g.dma_start(out=wstr[:, base:base + 8192].rearrange("p (c h) -> p c h", h=512),
                            in_=w1v[:, :, sbk * 512:(sbk + 1) * 512]).then_inc(
                                sWDa if sbk % 2 == 0 else sWDb, 16)
                g.dma_start(out=wstr[:, base + 8192:base + 16384].rearrange("p (c h) -> p c h", h=512),
                            in_=w3v[:, :, sbk * 512:(sbk + 1) * 512]).then_inc(
                                sWDa if sbk % 2 == 0 else sWDb, 16)
            w2v = w2_d.rearrange("(p c) dd -> p c dd", c=64)
            for doc in range(16):
                if doc >= 2:
                    g.wait_ge(sP6, 9 * (doc - 1) + 1)
                else:
                    g.wait_ge(sP5, 64)
                g.dma_start(out=w2str[:, (doc % 2) * 8192:(doc % 2 + 1) * 8192].rearrange("p (c dd) -> p c dd", dd=128),
                            in_=w2v[:, :, doc * 128:(doc + 1) * 128]).then_inc(
                                sW2a if doc % 2 == 0 else sW2b, 16)

        @block.tensor
        def _(t):
            t.wait_ge(sC, SCTOT)
            if layer == 0:
                t.wait_ge(sAT, 16)
                atv = atT.rearrange("p (c t) -> p c t", t=1024)
                for fc in range(NDC):
                    t.wait_ge(sWDa if fc % 2 == 0 else sWDb, 16 * (fc // 2 + 1))
                    if fc >= 2:
                        t.wait_ge(sEA, fc - 1)
                    half = (fc % 2) * 1024
                    ins = None
                    for tc in range(2):
                        for dc in range(NDC):
                            ins = t.matmul(psF[:, half + tc * 512:half + (tc + 1) * 512],
                                           wdstr[:, (fc % 2) * 2048 + dc * 128:(fc % 2) * 2048 + (dc + 1) * 128],
                                           atv[:, dc, tc * 512:(tc + 1) * 512],
                                           start=(dc == 0), stop=(dc == NDC - 1))
                    ins.then_inc(sPA, 1)
                t.wait_ge(sEA, 16)
                for tt in range(NT):
                    for grp in range(4):
                        gi = tt * 4 + grp
                        if gi >= 2:
                            t.wait_ge(sVA, 0 if gi < 2 else (gi - 1))
                        ins = None
                        for j in range(4):
                            fc = grp * 4 + j
                            ins = t.transpose(psTd[:, (gi % 2) * 1024 + j * 128:(gi % 2) * 1024 + (j + 1) * 128],
                                              dnsF[:, fc * 1024 + tt * 128:fc * 1024 + (tt + 1) * 128],
                                              ident)
                        ins.then_inc(sPA, 1)
            for tt in range(NT):
                t.wait_ge(sAB, 10 + tt)
                for grp in range(4):
                    gi = tt * 4 + grp
                    if gi >= 2:
                        t.wait_ge(sVB, 2 + gi - 1)
                    ins = None
                    for j in range(4):
                        fc = grp * 4 + j
                        ins = t.transpose(psTd[:, (gi % 2) * 1024 + j * 128:(gi % 2) * 1024 + (j + 1) * 128],
                                          lntB[:, fc * 128:(fc + 1) * 128],
                                          ident)
                    ins.then_inc(sPB, 1)
            if layer == 1:
                t.wait_ge(sVB, 34)
                for fc in range(NDC):
                    if fc >= 1:
                        t.wait_ge(sV1b, fc)
                    ins = None
                    for tc in range(2):
                        ins = t.matmul(psB2[:, tc * 512:(tc + 1) * 512], bd16k,
                                       lnF[:, fc * 1024 + tc * 512:fc * 1024 + (tc + 1) * 512],
                                       start=True, stop=True)
                    ins.then_inc(sP1, 1)
                for c in range(NDC):
                    t.wait_ge(sB2ra if c % 2 == 0 else sB2rb, 16 * (c // 2 + 1))
                    t.wait_ge(sV1b, 16 + c)
                    ins = None
                    for tc in range(2):
                        ins = t.matmul(psB2[:, tc * 512:(tc + 1) * 512], h128k,
                                       rcb[:, (c % 2) * 1024 + tc * 512:(c % 2) * 1024 + (tc + 1) * 512],
                                       start=True, stop=True)
                    ins.then_inc(sP1, 1)
            else:
                t.wait_ge(sVB, 34)
            if layer == 1:
                t.wait_ge(sV1b, 32)
            for sbk in range(NSB):
                t.wait_ge(sWDa if sbk % 2 == 0 else sWDb,
                          W1BASE + 32 * (sbk // 2 + 1))
                for hcl in range(4):
                    hc = sbk * 4 + hcl
                    wb = (sbk % 2) * 16384
                    if hc >= 1:
                        t.wait_ge(sV3, 2 * (hc - 1) + 1)
                        t.wait_ge(sA3, hc)
                    ins = None
                    for cc in range(NDC):
                        for tc in range(2):
                            t.matmul(psU[:, tc * 512:(tc + 1) * 512],
                                     wstr[:, wb + cc * 512 + hcl * 128:wb + cc * 512 + (hcl + 1) * 128],
                                     lnF[:, cc * 1024 + tc * 512:cc * 1024 + (tc + 1) * 512],
                                     start=(cc == 0), stop=(cc == NDC - 1))
                            ins = t.matmul(psG[:, tc * 512:(tc + 1) * 512],
                                           wstr[:, wb + 8192 + cc * 512 + hcl * 128:wb + 8192 + cc * 512 + (hcl + 1) * 128],
                                           lnF[:, cc * 1024 + tc * 512:cc * 1024 + (tc + 1) * 512],
                                           start=(cc == 0), stop=(cc == NDC - 1))
                    ins.then_inc(sP2, 1)
                    t.wait_ge(sV3, 2 * hc + 1)
                    if hc >= 1:
                        t.wait_ge(sV3, 2 * (hc - 1) + 2)
                    ins = None
                    for tc in range(2):
                        ins = t.matmul(psH[:, tc * 512:(tc + 1) * 512], bd64k,
                                       hck[:, tc * 512:(tc + 1) * 512],
                                       start=True, stop=True)
                    ins.then_inc(sP2, 1)
            for c in range(NHC):
                t.wait_ge(sB8ra if c % 2 == 0 else sB8rb, 16 * (c // 2 + 1))
                if c >= 1:
                    t.wait_ge(sV5, c)
                ins = None
                for tc in range(2):
                    ins = t.matmul(psB2[:, tc * 512:(tc + 1) * 512], h128k,
                                   rcb[:, (c % 2) * 1024 + tc * 512:(c % 2) * 1024 + (tc + 1) * 512],
                                   start=True, stop=True)
                ins.then_inc(sP5, 1)
            t.wait_ge(sV5, 64)
            for doc in range(16):
                t.wait_ge(sW2a if doc % 2 == 0 else sW2b, 16 * (doc // 2 + 1))
                if doc >= 1:
                    t.wait_ge(sA6, doc)
                ins = None
                for c in range(NHC):
                    for tc in range(2):
                        ins = t.matmul(psY[:, tc * 512:(tc + 1) * 512],
                                       w2str[:, (doc % 2) * 8192 + c * 128:(doc % 2) * 8192 + (c + 1) * 128],
                                       h2[:, c * 1024 + tc * 512:c * 1024 + (tc + 1) * 512],
                                       start=(c == 0), stop=(c == NHC - 1))
                ins.then_inc(sP6, 1)
                t.wait_ge(sA6, doc + 1)
                for tt in range(NT):
                    ti = doc * 8 + tt
                    if ti >= 2:
                        t.wait_ge(sV6, ti - 1)
                    t.transpose(psTd[:, (ti % 2) * 1024:(ti % 2) * 1024 + 128],
                                ysb[:, tt * 128:(tt + 1) * 128],
                                ident).then_inc(sP6, 1)

        @block.scalar
        def _(s):
            s.wait_ge(sC, SCTOT)
            if layer == 0:
                for fc in range(NDC):
                    s.wait_ge(sPA, fc + 1)
                    s.activation(dnsF[:, fc * 1024:(fc + 1) * 1024],
                                 psF[:, (fc % 2) * 1024:(fc % 2 + 1) * 1024],
                                 AF.Identity, bias=pcol[:, fc:fc + 1],
                                 scale=1.0).then_inc(sEA, 1)
            s.wait_ge(sXB, 16)
            for tt in range(NT):
                if tt >= 1:
                    s.wait_ge(sAB, tt)
                s.activation(scrB[:] if tt % 2 == 0 else scrB2[:],
                             xtB[:, tt * D:(tt + 1) * D], AF.Square,
                             accum_out=smq[:, NT + tt:NT + tt + 1]).then_inc(sAB, 1)
            s.wait_ge(sVB, 1)
            s.activation(stt[:, NT:2 * NT], stt[:, NT:2 * NT], AF.Sqrt).then_inc(sAB, 1)
            s.wait_ge(sVB, 2)
            for tt in range(NT):
                if tt >= 1:
                    s.wait_ge(sPB, 4 * tt)
                s.activation(lntB[:], xtB[:, tt * D:(tt + 1) * D], AF.Identity,
                             bias=stt[:, 3 * NT + tt:3 * NT + tt + 1],
                             scale=stt[:, 2 * NT + tt:2 * NT + tt + 1]).then_inc(sAB, 1)
            for hc in range(NHC):
                s.wait_ge(sP2, 2 * hc + 1)
                if hc >= 1:
                    s.wait_ge(sV3, 2 * (hc - 1) + 1)
                if layer == 0:
                    s.activation(sgb[:], psU[:], AF.Silu).then_inc(sA3, 1)
                else:
                    s.activation(sgb[:], psG[:], AF.Silu,
                                 scale=pcol2[:, 64 + hc:64 + hc + 1]).then_inc(sA3, 1)
            for doc in range(16):
                s.wait_ge(sP6, 9 * doc + 1)
                if layer == 0:
                    s.activation(ysb[:], psY[:], AF.Identity).then_inc(sA6, 1)
                else:
                    s.activation(ysb[:], psY[:], AF.Identity,
                                 scale=pcol[:, 16 + doc:16 + doc + 1]).then_inc(sA6, 1)

        @block.vector
        def _(v):
            nd = [0]

            def dedge(ins):
                nd[0] += 1
                ins.then_inc(sD, 1)
                v.wait_ge(sD, nd[0])

            v.wait_ge(sC, SCTOT)
            if layer == 1:
                dedge(v.tensor_tensor(pcol2[:, 0:64], pcol2[:, 0:64], pcol2[:, 128:192], ALU.mult))
            if layer == 0:
                for tt in range(NT):
                    v.wait_ge(sXa if tt % 2 == 0 else sXb, 16 * (tt // 2 + 1))
                    if tt >= 2:
                        v.wait_ge(sX1Oa if tt % 2 == 0 else sX1Ob, 16 * (tt // 2))
                    for grp in range(4):
                        gi = tt * 4 + grp
                        v.wait_ge(sPA, 16 + gi + 1)
                        v.tensor_tensor(x1tA[:, (tt % 2) * D + grp * 512:(tt % 2) * D + (grp + 1) * 512],
                                        psTd[:, (gi % 2) * 1024:(gi % 2) * 1024 + 512],
                                        xinA[:, (tt % 2) * D + grp * 512:(tt % 2) * D + (grp + 1) * 512],
                                        ALU.add).then_inc(sVA, 1)
            else:
                for tt in range(NT):
                    v.wait_ge(sXa if tt % 2 == 0 else sXb, 16 * (tt // 2 + 1))
                    v.wait_ge(sATa if tt % 2 == 0 else sATb, 16 * (tt // 2 + 1))
                    if tt >= 2:
                        v.wait_ge(sX1Oa if tt % 2 == 0 else sX1Ob, 16 * (tt // 2))
                    if tt >= 1:
                        v.wait_ge(sVA, tt)
                    dedge(v.tensor_copy(a32[:, 0:2048], at1[:, (tt % 2) * 2048:(tt % 2 + 1) * 2048]))
                    v.tensor_tensor(x1tA[:, (tt % 2) * D:(tt % 2 + 1) * D],
                                    a32[:, 0:2048],
                                    xinA[:, (tt % 2) * D:(tt % 2 + 1) * D],
                                    ALU.add).then_inc(sVA, 1)
            n = NT
            mean, veps, rstd, nmrs = (stt[:, 0:n], stt[:, n:2 * n],
                                      stt[:, 2 * n:3 * n], stt[:, 3 * n:4 * n])
            v.wait_ge(sXB, 16)
            for tt in range(NT):
                dedge(v.reduce_sum(smq[:, tt:tt + 1], xtB[:, tt * D:(tt + 1) * D], axis=AX.X))
            v.wait_ge(sAB, NT)
            dedge(v.tensor_scalar(mean, smq[:, 0:NT], 1.0 / D, None, ALU.mult))
            dedge(v.tensor_scalar(veps, smq[:, NT:2 * NT], 1.0 / D, LN_EPS, ALU.mult, ALU.add))
            dedge(v.tensor_tensor(rstd, mean, mean, ALU.mult))
            v.tensor_sub(veps, veps, rstd).then_inc(sVB, 1)
            v.wait_ge(sAB, NT + 1)
            dedge(v.reciprocal(rstd, veps))
            dedge(v.tensor_tensor(nmrs, mean, rstd, ALU.mult))
            v.tensor_scalar(nmrs, nmrs, -1.0, None, ALU.mult).then_inc(sVB, 1)
            for tt in range(NT):
                for grp in range(4):
                    gi = tt * 4 + grp
                    v.wait_ge(sPB, gi + 1)
                    dst3 = lnF[:].rearrange("p (fc t) -> p fc t", t=1024)[
                        :, grp * 4:(grp + 1) * 4, tt * 128:(tt + 1) * 128]
                    src3 = psTd[:, (gi % 2) * 1024:(gi % 2) * 1024 + 512].rearrange(
                        "p (j t) -> p j t", t=128)
                    if layer == 0:
                        v.tensor_copy(dst3, src3).then_inc(sVB, 1)
                    else:
                        sucol = pcol[:, grp * 4:(grp + 1) * 4].unsqueeze(2).broadcast_to([128, 4, 128])
                        v.tensor_tensor(dst3, src3, sucol, ALU.mult).then_inc(sVB, 1)
            if layer == 1:
                for fc in range(NDC):
                    v.wait_ge(sP1, fc + 1)
                    if fc >= 2:
                        v.wait_ge(sB2wa if fc % 2 == 0 else sB2wb, 16 * (fc // 2))
                    v.tensor_copy(c1b[:, (fc % 2) * 1024:(fc % 2 + 1) * 1024],
                                  psB2[:]).then_inc(sV1b, 1)
                for c in range(NDC):
                    v.wait_ge(sP1, 16 + c + 1)
                    v.tensor_scalar(lnF[:, c * 1024:(c + 1) * 1024], psB2[:],
                                    FWN2, None, ALU.mult).then_inc(sV1b, 1)
            for hc in range(NHC):
                v.wait_ge(sA3, hc + 1)
                if hc >= 1:
                    v.wait_ge(sP2, 2 * (hc - 1) + 2)
                if layer == 0:
                    v.tensor_tensor(hck[:], psG[:], sgb[:], ALU.mult).then_inc(sV3, 1)
                else:
                    v.scalar_tensor_tensor(hck[:], psU[:], pcol2[:, hc:hc + 1], sgb[:],
                                           ALU.mult, ALU.mult).then_inc(sV3, 1)
                v.wait_ge(sP2, 2 * hc + 2)
                if hc >= 2:
                    v.wait_ge(sB8wa if hc % 2 == 0 else sB8wb, 16 * (hc // 2))
                v.tensor_copy(y1b[:, (hc % 2) * 1024:(hc % 2 + 1) * 1024],
                              psH[:]).then_inc(sV3, 1)
            for c in range(NHC):
                v.wait_ge(sP5, c + 1)
                v.tensor_scalar(h2[:, c * 1024:(c + 1) * 1024], psB2[:],
                                FWN, None, ALU.mult).then_inc(sV5, 1)
            for doc in range(16):
                v.wait_ge(sXCa if doc % 2 == 0 else sXCb, 16 * (doc // 2 + 1))
                for tt in range(NT):
                    ti = doc * 8 + tt
                    v.wait_ge(sP6, 9 * doc + 1 + tt + 1)
                    if ti >= 2:
                        v.wait_ge(sOa if ti % 2 == 0 else sOb, 16 * (ti // 2))
                    v.tensor_tensor(x2b[:, (ti % 2) * 128:(ti % 2 + 1) * 128],
                                    psTd[:, (ti % 2) * 1024:(ti % 2) * 1024 + 128],
                                    x1cb[:, (doc % 2) * 1024 + tt * 128:(doc % 2) * 1024 + (tt + 1) * 128],
                                    ALU.add).then_inc(sV6, 1)

    return nc



class _NS:
    def __init__(self, **kw):
        self.__dict__.update(kw)


def emit_pre_f(C):
    sL, sL2, sV, sA, sT, sE, sO = (C.sems[k] for k in "sL sL2 sV sA sT sE sO".split())
    B, io = C.B, C.io
    n = NT

    def f_sync(e):
        xv = io.x.rearrange("(tt p) d -> p tt d", p=128)
        xo = B.xt[:].rearrange("p (tt d) -> p tt d", d=D)
        e.dma_start(out=xo[:, 0:4, :], in_=xv[:, 0:4, :]).then_inc(sL, 16)
        e.dma_start(out=xo[:, 4:8, :], in_=xv[:, 4:8, :]).then_inc(sL2, 16)
        for tt in range(NT):
            e.wait_ge(sE, 4 * tt + 4)
            e.dma_start(
                out=io.lnT[tt // 2].rearrange("(c p) t -> p c t", p=128)[:, :, (tt % 2) * 128:(tt % 2 + 1) * 128],
                in_=B.lt[:, (tt % 2) * D:(tt % 2 + 1) * D].rearrange("p (c t) -> p c t", t=128),
            ).then_inc(sO, 16)
        e.wait_ge(sO, 16 * NT)

    def f_scalar(s):
        s.wait_ge(sL, 16)
        for tt in range(NT):
            if tt == 4:
                s.wait_ge(sL2, 16)
            if tt >= 1:
                s.wait_ge(sA, tt)
            s.activation(B.scr[:], B.xt[:, tt * D:(tt + 1) * D], AF.Square,
                         accum_out=B.sq[:, tt:tt + 1]).then_inc(sA, 1)
        s.wait_ge(sV, 1)
        s.activation(B.st[:, n:2 * n], B.st[:, n:2 * n], AF.Sqrt).then_inc(sA, 1)
        s.wait_ge(sV, 2)
        for tt in range(NT):
            if tt >= 2:
                s.wait_ge(sT, 4 * (tt - 1))
            s.activation(B.ln[:, (tt % 2) * D:(tt % 2 + 1) * D],
                         B.xt[:, tt * D:(tt + 1) * D], AF.Identity,
                         bias=B.st[:, 3 * n + tt:3 * n + tt + 1],
                         scale=B.st[:, 2 * n + tt:2 * n + tt + 1]).then_inc(sA, 1)

    def f_vector(v):
        nd = [0]
        sD = C.sems["sD"]

        def dedge(ins):
            nd[0] += 1
            ins.then_inc(sD, 1)
            v.wait_ge(sD, nd[0])

        mean, veps, rstd, nmrs = (B.st[:, 0:n], B.st[:, n:2 * n],
                                  B.st[:, 2 * n:3 * n], B.st[:, 3 * n:4 * n])
        v.wait_ge(sL, 16)
        for tt in range(NT):
            if tt == 4:
                v.wait_ge(sL2, 16)
            dedge(v.reduce_sum(B.sm[:, tt:tt + 1], B.xt[:, tt * D:(tt + 1) * D], axis=AX.X))
        v.wait_ge(sA, NT)
        dedge(v.tensor_scalar(mean, B.sm[:], 1.0 / D, None, ALU.mult))
        dedge(v.tensor_scalar(veps, B.sq[:], 1.0 / D, LN_EPS, ALU.mult, ALU.add))
        dedge(v.tensor_tensor(rstd, mean, mean, ALU.mult))
        v.tensor_sub(veps, veps, rstd).then_inc(sV, 1)
        v.wait_ge(sA, NT + 1)
        dedge(v.reciprocal(rstd, veps))
        dedge(v.tensor_tensor(nmrs, mean, rstd, ALU.mult))
        v.tensor_scalar(nmrs, nmrs, -1.0, None, ALU.mult).then_inc(sV, 1)
        for tt in range(NT):
            for grp in range(4):
                gi = tt * 4 + grp
                v.wait_ge(sT, gi + 1)
                if tt >= 2:
                    v.wait_ge(sO, 16 * tt)
                v.tensor_copy(B.lt[:, (tt % 2) * D + grp * 512:(tt % 2) * D + (grp + 1) * 512],
                              B.psT[:, (gi % 2) * 1024:(gi % 2) * 1024 + 512]).then_inc(sE, 1)

    def f_tensor(t):
        t.wait_ge(sL, 16)
        t.wait_ge(sL2, 16)
        for tt in range(NT):
            t.wait_ge(sA, NT + 2 + tt)
            for grp in range(4):
                gi = tt * 4 + grp
                if gi >= 2:
                    t.wait_ge(sE, gi - 1)
                ins = None
                for j in range(4):
                    fc = grp * 4 + j
                    ins = t.transpose(B.psT[:, (gi % 2) * 1024 + j * 128:(gi % 2) * 1024 + (j + 1) * 128],
                                      B.ln[:, (tt % 2) * D + fc * 128:(tt % 2) * D + (fc + 1) * 128],
                                      C.cst.ident)
                ins.then_inc(sT, 1)

    def f_gpsimd(g):
        pf = getattr(C, "prefetch", None)
        if pf is not None:
            wsb, wsl, wsem = pf
            wv = wsl.rearrange("(c p) o -> p c o", p=128)
            for q in range(4):
                g.dma_start(out=wsb[:, q * 6144:(q + 1) * 6144].rearrange("p (c o) -> p c o", o=1536),
                            in_=wv[:, q * 4:(q + 1) * 4, :]).then_inc(wsem, 16)

    C.bodies = {"sync": f_sync, "scalar": f_scalar, "vector": f_vector,
                "tensor": f_tensor, "gpsimd": f_gpsimd}


def emit_att_f(C, layer):
    sm = C.sems
    sL, sW = sm["sL"], sm["sW"]
    sP0, sPE, sA1, sV1 = sm["sP0"], sm["sPE"], sm["sA1"], sm["sV1"]
    sT2, sX, sM, sV2, sO = sm["sT2"], sm["sX"], sm["sM"], sm["sV2"], sm["sO"]
    sB0, sB1 = sm["sB0"], sm["sB1"]
    B, io, cst = C.B, C.io, C.cst
    NTB, NQB, NH = 8, 8, 4

    def kts(qb):
        return 4 * qb + 4

    def hq(h, qb):
        return h * NQB + qb

    def exp_base(h, qb):
        return sum(kts(i % NQB) for i in range(hq(h, qb)))

    smm_ct, pv_ct, rb_ct, pvqt_ct = {}, {}, {}, {}
    ctr = 0
    for i in range(NH * NQB):
        nk_i = kts(i % NQB)
        for kt in range(nk_i):
            ctr += 1
            smm_ct[(i, kt)] = ctr
        if layer == 0:
            if i >= 1:
                ctr += 1
                rb_ct[i - 1] = ctr
            ctr += 1
            pv_ct[i] = ctr
        else:
            for qt in range(4):
                ctr += 1
                pvqt_ct[(i, qt)] = ctr
    if layer == 0:
        ctr += 1
        rb_ct[NH * NQB - 1] = ctr
    qkv_ct, tr_ct = {}, {}
    if layer == 1:
        c2 = 0
        c2 += 1; qkv_ct[0] = c2
        for gt in range(1, 32):
            c2 += 1; qkv_ct[gt] = c2
            c2 += 1; tr_ct[gt - 1] = c2
        c2 += 1; tr_ct[31] = c2

    P1_PE_TB = 12 if layer == 0 else 8
    SCTOT = 16 if layer == 0 else 32
    SWTOT = 80 if layer == 0 else 64


    def f_sync(e):
        if layer == 0:
            e.dma_start(out=B.bqs[:, 0:12],
                        in_=io.bq.rearrange("(o p) -> p o", p=128),
                        allow_slow_non_contiguous=True).then_inc(sL, 16)
        else:
            e.dma_start(out=B.taunu[0:1, 0:1], in_=io.tau.unsqueeze(0)).then_inc(sL, 16)
            e.dma_start(out=B.taunu[0:1, 1:2], in_=io.nu.unsqueeze(0)).then_inc(sL, 16)
        for tb in range(NTB):
            if tb >= 2:
                if layer == 0:
                    e.wait_ge(sPE, (tb - 1) * P1_PE_TB)
                else:
                    e.wait_ge(sPE, qkv_ct[4 * (tb - 1)])
            for j in range(2):
                q = 2 * (tb % 2) + j
                e.wait_ge(C.ccsem, q + 1)
                e.dma_start(
                    out=B.lnb[:, (tb % 2) * NDC * 512:(tb % 2 + 1) * NDC * 512].rearrange("p (c t) -> p c t", t=512)[:, :, j * 256:(j + 1) * 256],
                    in_=io.lnTg[q].rearrange("(r cp) t -> r cp t", r=4)[tb // 2].rearrange("(c p) t -> p c t", p=128),
                ).then_inc(sB0 if tb % 2 == 0 else sB1, 16)
        nst = 0
        for h in range(NH):
            for qb in range(NQB):
                i = hq(h, qb)
                if layer == 0:
                    e.wait_ge(sV2, 2 * i + 2)
                    e.dma_start(out=io.att_c[h * 128:(h + 1) * 128, qb * 512:(qb + 1) * 512],
                                in_=B.attnb[:]).then_inc(sO, 16)
                    nst += 16
                else:
                    for qt in range(4):
                        e.wait_ge(sV2, 4 * i + qt + 1)
                        e.dma_start(
                            out=io.att_c[h, qb * 512 + qt * 128:qb * 512 + (qt + 1) * 128, :],
                            in_=B.attnb[:, qt * 128:(qt + 1) * 128]).then_inc(sO, 16)
                        nst += 16
        e.wait_ge(sO, nst)

    def f_gpsimd(g):
        if layer == 0:
            g.dma_start(out=B.bvrow[0:1, :], in_=io.bq[1024:1536].unsqueeze(0)).then_inc(sW, 16)

    def f_tensor(t):
        t.wait_ge(sL, SCTOT)
        t.wait_ge(sW, SWTOT)
        if layer == 1:
            t.matmul(B.psB0[:, 0:2], C.cst.ones128[0:1, :], B.taunu[0:1, :],
                     start=True, stop=True).then_inc(sP0, 1)
        for tb in range(NTB if layer == 0 else 0):
            t.wait_ge(sB0 if tb % 2 == 0 else sB1, 32 * (tb // 2 + 1))
            if layer == 0:
                lnv = B.lnb[:, (tb % 2) * NDC * 512:].rearrange("p (c t) -> p c t", t=512)
                for of in range(8):
                    if 8 * tb + of >= 2:
                        t.wait_ge(sA1, 8 * tb + of - 1)
                    pso = B.psQK[:, (of % 2) * 512:(of % 2) * 512 + 512]
                    ins = None
                    for dc in range(NDC):
                        ins = t.matmul(pso, B.ws[:, dc * 1536 + of * 128:dc * 1536 + (of + 1) * 128],
                                       lnv[:, dc, :], start=(dc == 0), stop=(dc == NDC - 1))
                    ins.then_inc(sPE, 1)
                for tt in range(4):
                    if 4 * tb + tt >= 2:
                        t.wait_ge(sV1, 4 * tb + tt - 1)
                    psv = B.psV[:, (tt % 2) * 512:(tt % 2) * 512 + 512]
                    for dc in range(NDC):
                        t.matmul(psv, B.lnb[:, (tb % 2) * NDC * 512 + dc * 512 + tt * 128:
                                            (tb % 2) * NDC * 512 + dc * 512 + (tt + 1) * 128],
                                 B.ws[:, dc * 1536 + 1024:dc * 1536 + 1536],
                                 start=(dc == 0), stop=False)
                    t.matmul(psv, C.cst.onesb[0:1, :], B.bvrow[0:1, :],
                             start=False, stop=True).then_inc(sPE, 1)
            else:
                pass
        if layer == 1:
            def qkv1(gt):
                tb, tt = gt // 4, gt % 4
                if tt == 0:
                    t.wait_ge(sB0 if tb % 2 == 0 else sB1, 32 * (tb // 2 + 1))
                p = gt % 2
                if gt == 0:
                    t.wait_ge(sP0, 3)
                if gt >= 2:
                    t.wait_ge(sV1, 4 * (gt - 2) + 3)
                ins = None
                for pj, pso in enumerate((B.psQ1, B.psK1, B.psV1_)):
                    for dc in range(NDC):
                        lhs = B.lnb[:, (tb % 2) * NDC * 512 + dc * 512 + tt * 128:
                                    (tb % 2) * NDC * 512 + dc * 512 + (tt + 1) * 128]
                        ins = t.matmul(pso[:, p * 512:(p + 1) * 512], lhs,
                                       B.ws[:, dc * 1536 + pj * 512:dc * 1536 + (pj + 1) * 512],
                                       start=(dc == 0), stop=(dc == NDC - 1))
                ins.then_inc(sPE, 1)

            def tr1(gt):
                t.wait_ge(sV1, 4 * gt + 3)
                ins = None
                for j in range(4):
                    ins = t.transpose(B.psT1[:, j * 128:(j + 1) * 128],
                                      B.qtok[:, j * 128:(j + 1) * 128], C.cst.ident)
                for j in range(4):
                    ins = t.transpose(B.psT1[:, 512 + j * 128:512 + (j + 1) * 128],
                                      B.qtok[:, 512 + j * 128:512 + (j + 1) * 128], C.cst.ident)
                ins.then_inc(sPE, 1)

            qkv1(0)
            for gt in range(1, 32):
                qkv1(gt)
                tr1(gt - 1)
            tr1(31)
        t.wait_ge(sA1 if layer == 0 else sV1, 64 if layer == 0 else 128)
        t.wait_ge(sV1, 32 if layer == 0 else 128)
        def rbcast(i):
            t.wait_ge(sV2, 2 * i + 1)
            t.matmul(B.psR[:], C.cst.ones128[0:1, :], B.rr[0:1, :],
                     start=True, stop=True).then_inc(sT2, 1)

        for h in range(NH):
            for qb in range(NQB):
                i = hq(h, qb)
                nk = kts(qb)
                eb = exp_base(h, qb)
                for kt in range(nk):
                    if kt >= 4:
                        t.wait_ge(sX, eb + kt - 3)
                    j = kt - 4 * qb
                    n0 = 128 * max(0, j)
                    t.matmul(B.psS[:, (kt % 4) * 512 + n0:(kt % 4) * 512 + 512],
                             B.KF[:, h * S + kt * 128:h * S + (kt + 1) * 128],
                             B.QF[:, h * S + qb * 512 + n0:h * S + (qb + 1) * 512],
                             start=True, stop=True).then_inc(sT2, 1)
                if layer == 0 and i >= 1:
                    rbcast(i - 1)
                if layer == 0:
                    t.wait_ge(sV2, 2 * i)
                    ins = None
                    for kt in range(nk):
                        j = kt - 4 * qb
                        n0 = 128 * max(0, j)
                        t.wait_ge(sX, eb + kt + 1)
                        if j >= 0:
                            t.wait_ge(sM, 4 * i + j + 2)
                        vtb = B.VT[:, kt * 528 + h * 132:]
                        t.matmul(B.psO[:, n0:512], vtb[:, 0:128],
                                 B.PT[:, kt * 512 + n0:(kt + 1) * 512],
                                 start=(kt == 0), stop=(kt == nk - 1))
                        ins = t.matmul(B.psSum[:, n0:512], vtb[:, 128:129],
                                       B.PT[:, kt * 512 + n0:(kt + 1) * 512],
                                       start=(kt == 0), stop=(kt == nk - 1))
                    ins.then_inc(sT2, 1)
                    if i == NH * NQB - 1:
                        rbcast(i)
                else:
                    t.wait_ge(sV2, 4 * i)
                    for qt in range(4):
                        qtg = 4 * qb + qt
                        ins = None
                        for kt in range(qtg + 1):
                            t.wait_ge(sX, eb + kt + 1)
                            j = kt - 4 * qb
                            if j >= 0:
                                t.wait_ge(sM, 4 * i + j + 2)
                            vtb = B.VT[:, kt * 528 + h * 132:]
                            ins = t.matmul(B.psO3[:, qt * 512:qt * 512 + 129],
                                           B.PT[:, kt * 512 + qt * 128:kt * 512 + (qt + 1) * 128],
                                           vtb[:, 0:129],
                                           start=(kt == 0), stop=(kt == qtg))
                        ins.then_inc(sT2, 1)

    def f_scalar(s):
        if layer == 1:
            s.wait_ge(sP0, 1)
            s.activation(B.tnb[:, 3:4], B.psB0[:, 0:1], AF.Abs).then_inc(sP0, 1)
        if layer == 0:
            for tb in range(8):
                for of in range(8):
                    s.wait_ge(sPE, tb * 12 + of + 1)
                    dst = (B.QF if of < 4 else B.KF)[:, (of % 4) * S + tb * 512:(of % 4) * S + (tb + 1) * 512]
                    s.activation(dst, B.psQK[:, (of % 2) * 512:(of % 2) * 512 + 512],
                                 AF.Identity, bias=B.bqs[:, of:of + 1], scale=1.0).then_inc(sA1, 1)
        else:
            for gt in range(32):
                p = gt % 2
                s.wait_ge(sPE, qkv_ct[gt])
                if gt >= 1:
                    s.wait_ge(sV1, 4 * (gt - 1) + 1)
                s.activation(B.scrq[:, 0:512], B.psQ1[:, p * 512:(p + 1) * 512], AF.Square).then_inc(sA1, 1)
                s.activation(B.scrq[:, 512:1024], B.psK1[:, p * 512:(p + 1) * 512], AF.Square).then_inc(sA1, 1)
                s.activation(B.scrq[:, 1024:1536], B.psV1_[:, p * 512:(p + 1) * 512], AF.Square).then_inc(sA1, 1)
                s.wait_ge(sV1, 4 * gt + 1)
                s.activation(B.ssq[:, 12:24], B.ssq[:, 0:12], AF.Sqrt).then_inc(sA1, 1)
        s.wait_ge(sM, 1)
        if layer == 1:
            s.wait_ge(sP0, 3)
        for h in range(NH):
            for qb in range(NQB):
                i = hq(h, qb)
                nk = kts(qb)
                eb = exp_base(h, qb)
                for kt in range(nk):
                    s.wait_ge(sT2, smm_ct[(i, kt)])
                    j = kt - 4 * qb
                    n0 = 128 * max(0, j)
                    if layer == 0:
                        s.activation(B.PT[:, kt * 512 + n0:(kt + 1) * 512],
                                     B.psS[:, (kt % 4) * 512 + n0:(kt % 4) * 512 + 512],
                                     AF.Exp, bias=B.csh[:, 0:1],
                                     scale=B.csh[:, 1:2]).then_inc(sX, 1)
                    else:
                        s.activation(B.PT[:, kt * 512 + n0:(kt + 1) * 512],
                                     B.psS[:, (kt % 4) * 512 + n0:(kt % 4) * 512 + 512],
                                     AF.Exp, bias=B.tnb[:, 1:2],
                                     scale=B.tnb[:, 0:1]).then_inc(sX, 1)

    def f_vector(v):
        nd = [0]
        sD = sm["sD"]

        def dedge(ins):
            nd[0] += 1
            ins.then_inc(sD, 1)
            v.wait_ge(sD, nd[0])

        v.wait_ge(sL, SCTOT)
        v.wait_ge(sW, SWTOT)
        v.memset(B.VT[:], 1.0)
        v.memset(B.csh[:, 0:1], -C0_SHIFT)
        v.memset(B.csh[:, 1:2], 0.08838834764831843).then_inc(sM, 1)
        if layer == 1:
            v.wait_ge(sP0, 2)
            v.tensor_copy(B.tnb[:, 0:1], B.psB0[:, 0:1])
            v.nop()
            v.tensor_scalar(B.tnb[:, 1:2], B.tnb[:, 3:4], -1.0, None, ALU.mult)
            v.tensor_copy(B.tnb[:, 2:3], B.psB0[:, 1:2]).then_inc(sP0, 1)
        if layer == 0:
            for tb in range(8):
                for tt in range(4):
                    v.wait_ge(sM, 1)
                    v.wait_ge(sPE, tb * 12 + 8 + tt + 1)
                    gt = tb * 4 + tt
                    dst = B.VT[:, gt * 528:(gt + 1) * 528].rearrange("p (h d) -> p h d", d=132)[:, :, 0:128]
                    srcv = B.psV[:, (tt % 2) * 512:(tt % 2) * 512 + 512].rearrange("p (h d) -> p h d", d=128)
                    v.tensor_copy(dst, srcv).then_inc(sV1, 1)
        else:
            for gt in range(32):
                p = gt % 2
                v.wait_ge(sM, 1)
                v.wait_ge(sA1, 4 * gt + 3)
                ins = None
                for pj in range(3):
                    ins = v.tensor_reduce(B.ssq[:, pj * 4:(pj + 1) * 4].unsqueeze(2),
                                          B.scrq[:, pj * 512:(pj + 1) * 512].rearrange("p (h d) -> p h d", d=128),
                                          axis=AX.X, op=ALU.add)
                ins.then_inc(sV1, 1)
                v.wait_ge(sA1, 4 * gt + 4)
                v.nop()
                dedge(v.tensor_scalar(B.ssq[:, 24:36], B.ssq[:, 12:24], L2_EPS, None, ALU.add))
                v.reciprocal(B.ssq[:, 12:24], B.ssq[:, 24:36]).then_inc(sV1, 1)
                v.wait_ge(sV1, 4 * gt + 2)
                if gt >= 1:
                    v.wait_ge(sPE, tr_ct[gt - 1])
                rq = B.ssq[:, 12:16].unsqueeze(2).broadcast_to([128, 4, 128])
                rk = B.ssq[:, 16:20].unsqueeze(2).broadcast_to([128, 4, 128])
                rv = B.ssq[:, 20:24].unsqueeze(2).broadcast_to([128, 4, 128])
                v.tensor_tensor(B.qtok[:, 0:512].rearrange("p (h d) -> p h d", d=128),
                                B.psQ1[:, p * 512:(p + 1) * 512].rearrange("p (h d) -> p h d", d=128), rq, ALU.mult)
                v.tensor_tensor(B.qtok[:, 512:1024].rearrange("p (h d) -> p h d", d=128),
                                B.psK1[:, p * 512:(p + 1) * 512].rearrange("p (h d) -> p h d", d=128), rk, ALU.mult)
                dstv = B.VT[:, gt * 528:(gt + 1) * 528].rearrange("p (h d) -> p h d", d=132)[:, :, 0:128]
                v.tensor_tensor(dstv, B.psV1_[:, p * 512:(p + 1) * 512].rearrange("p (h d) -> p h d", d=128), rv, ALU.mult).then_inc(sV1, 1)
                v.wait_ge(sPE, tr_ct[gt])
                v.tensor_copy(B.QF[:].rearrange("p (h t) -> p h t", t=S)[:, :, gt * 128:(gt + 1) * 128],
                              B.psT1[:, 0:512].rearrange("p (h d) -> p h d", d=128))
                v.tensor_copy(B.KF[:].rearrange("p (h t) -> p h t", t=S)[:, :, gt * 128:(gt + 1) * 128],
                              B.psT1[:, 512:1024].rearrange("p (h d) -> p h d", d=128)).then_inc(sV1, 1)
        for h in range(NH):
            for qb in range(NQB):
                i = hq(h, qb)
                nk = kts(qb)
                eb = exp_base(h, qb)
                for jj in range(4):
                    kt = 4 * qb + jj
                    v.wait_ge(sX, eb + kt + 1)
                    dg = B.PT[:, kt * 512 + jj * 128:kt * 512 + (jj + 1) * 128]
                    v.tensor_tensor(dg, dg, C.cst.mask01, ALU.mult).then_inc(sM, 1)
                if layer == 0:
                    v.wait_ge(sT2, pv_ct[i])
                    v.reciprocal(B.rr[0:1, :], B.psSum[:]).then_inc(sV2, 1)
                    v.wait_ge(sT2, rb_ct[i])
                    if i >= 1:
                        v.wait_ge(sO, 16 * i)
                    dedge(v.tensor_copy(B.rB[:], B.psR[:]))
                    v.tensor_tensor(B.attnb[:], B.psO[:], B.rB[:], ALU.mult).then_inc(sV2, 1)
                else:
                    for qt in range(4):
                        v.wait_ge(sT2, pvqt_ct[(i, qt)])
                        v.wait_ge(sV2, 4 * i + qt)
                        if i >= 1:
                            v.wait_ge(sO, 64 * i)
                        dedge(v.reciprocal(B.tnb[:, 3:4], B.psO3[:, qt * 512 + 128:qt * 512 + 129]))
                        dedge(v.tensor_tensor(B.tnb[:, 3:4], B.tnb[:, 3:4], B.tnb[:, 2:3], ALU.mult))
                        v.tensor_scalar(B.attnb[:, qt * 128:(qt + 1) * 128],
                                        B.psO3[:, qt * 512:qt * 512 + 128],
                                        B.tnb[:, 3:4], None, ALU.mult).then_inc(sV2, 1)

    C.bodies = {"sync": f_sync, "scalar": f_scalar, "vector": f_vector,
                "tensor": f_tensor, "gpsimd": f_gpsimd}


def emit_post_f(C, layer):
    sm = C.sems
    B, io, cst = C.B, C.io, C.cst
    sC2, sAT = sm["sC2"], sm["sAT"]
    sATa, sATb = sm["sATa"], sm["sATb"]
    sXa, sXb, sXB, sXBb = sm["sXa"], sm["sXb"], sm["sXB"], sm["sXBb"]
    sWDa, sWDb = sm["sWDa"], sm["sWDb"]
    sW2a, sW2b = sm["sW2a"], sm["sW2b"]
    sXCa, sXCb = sm["sXCa"], sm["sXCb"]
    sPA, sEA, sVA = sm["sPA"], sm["sEA"], sm["sVA"]
    sPB, sAB, sVB = sm["sPB"], sm["sAB"], sm["sVB"]
    sP1, sV1b = sm["sP1"], sm["sV1b"]
    sP2, sP2a, sA3, sV3 = sm["sP2"], sm["sP2a"], sm["sA3"], sm["sV3"]
    sP5, sV5 = sm["sP5"], sm["sV5"]
    sP6, sP6a, sA6, sV6 = sm["sP6"], sm["sP6a"], sm["sA6"], sm["sV6"]
    sX1Oa, sX1Ob = sm["sX1Oa"], sm["sX1Ob"]
    sB8wa, sB8wb = sm["sB8wa"], sm["sB8wb"]
    sB8ra, sB8rb, sB8rc = sm["sB8ra"], sm["sB8rb"], sm["sB8rc"]
    sB2wa, sB2wb = sm["sB2wa"], sm["sB2wb"]
    sB2ra, sB2rb, sB2rc = sm["sB2ra"], sm["sB2rb"], sm["sB2rc"]
    sOa, sOb = sm["sOa"], sm["sOb"]

    NHC = 64
    NSB = 16
    VA_TT = 4 if layer == 0 else 1
    FWN = float(1.0 / np.sqrt(8192.0))
    FWN2 = float(1.0 / np.sqrt(2048.0))
    SCTOT = 16 if layer == 0 else 80
    W1BASE = 8 * 16 if layer == 0 else 0
    n = NT

    def f_sync(e):
        def ld_xx(tt):
            if tt >= 2:
                e.wait_ge(sVA, VA_TT * (tt - 1))
            e.dma_start(out=B.xinA[:, (tt % 2) * D:(tt % 2 + 1) * D],
                        in_=io.x[tt * 128:(tt + 1) * 128, :]).then_inc(
                            sXa if tt % 2 == 0 else sXb, 16)

        def ld_at(tt):
            attg4 = io.attg.rearrange("h (r t) d -> t (h r) d", r=4)
            e.dma_start(
                out=B.at1[:, (tt % 2) * 2048:(tt % 2 + 1) * 2048],
                in_=attg4[bass.ds(C.loc + tt * 128, 128), :, :],
            ).then_inc(sATa if tt % 2 == 0 else sATb, 16)

        if layer == 0:
            e.dma_start(out=B.pcol[:, 0:16], in_=io.bd.rearrange("(c p) -> p c", p=128),
                        allow_slow_non_contiguous=True).then_inc(sC2, 16)
            ld_xx(0)
            ld_xx(1)
            e.wait_ge(C.ccsem, C.ccval)
            e.dma_start(out=B.atT[:].rearrange("p (c t) -> p c t", t=1024),
                        in_=io.attg.rearrange("h (r p) t -> p (h r) t", p=128)[:, :, bass.ds(C.loc, 1024)],
                        ).then_inc(sAT, 16)
        else:
            e.dma_start(out=B.pcol[:, 0:16], in_=io.su_ug.rearrange("(c p) -> p c", p=128),
                        allow_slow_non_contiguous=True).then_inc(sC2, 16)
            e.dma_start(out=B.pcol[:, 16:32], in_=io.wdp.rearrange("(c p) -> p c", p=128),
                        allow_slow_non_contiguous=True).then_inc(sC2, 16)
            e.dma_start(out=B.pcol2[:, 0:64], in_=io.wsu.rearrange("(c p) -> p c", p=128),
                        allow_slow_non_contiguous=True).then_inc(sC2, 16)
            e.dma_start(out=B.pcol2[:, 64:128], in_=io.wsg.rearrange("(c p) -> p c", p=128),
                        allow_slow_non_contiguous=True).then_inc(sC2, 16)
            e.dma_start(out=B.pcol2[:, 128:192], in_=io.sdp.rearrange("(c p) -> p c", p=128),
                        allow_slow_non_contiguous=True).then_inc(sC2, 16)
            ld_xx(0)
            ld_xx(1)
            e.wait_ge(C.ccsem, C.ccval)
            ld_at(0)
            ld_at(1)

        for tt in range(NT):
            if tt + 2 < NT:
                ld_xx(tt + 2)
                if layer == 1:
                    ld_at(tt + 2)
            e.wait_ge(sVA, VA_TT * (tt + 1))
            e.dma_start(out=io.x1buf[tt * 128:(tt + 1) * 128, :],
                        in_=B.x1tA[:, (tt % 2) * D:(tt % 2 + 1) * D]).then_inc(
                            sX1Oa if tt % 2 == 0 else sX1Ob, 16)
        e.wait_ge(sX1Oa, 16 * (NT // 2))
        e.wait_ge(sX1Ob, 16 * (NT // 2))
        for tt in range(NT):
            if tt >= 2:
                e.wait_ge(sAB, 8 + tt)
            e.dma_start(out=B.xtB[:, (tt % 2) * D:(tt % 2 + 1) * D],
                        in_=io.x1buf[tt * 128:(tt + 1) * 128, :]).then_inc(
                            sXB if tt % 2 == 0 else sXBb, 16)
        if layer == 1:
            for fc in range(NDC):
                e.wait_ge(sV1b, fc + 1)
                e.dma_start(out=io.bounce2[fc * 128:(fc + 1) * 128, :],
                            in_=B.c1b[:, (fc % 2) * 1024:(fc % 2 + 1) * 1024]).then_inc(
                                sB2wa if fc % 2 == 0 else sB2wb, 16)
            b2v = io.bounce2.rearrange("(p c) t -> p c t", c=16)
            e.wait_ge(sB2wa, 16 * (NDC // 2))
            e.wait_ge(sB2wb, 16 * (NDC // 2))
            for g in range(NDC // 2):
                if g >= 3:
                    e.wait_ge(sP1, 16 + 2 * (g - 2))
                e.dma_start(out=B.rcb[:, (g % 3) * 2048:(g % 3) * 2048 + 2048].rearrange("p (c t) -> p c t", t=1024),
                            in_=b2v[:, 2 * g:2 * g + 2, :]).then_inc(
                                [sB2ra, sB2rb, sB2rc][g % 3], 16)
        for hc in range(NHC):
            e.wait_ge(sV3, 2 * hc + 2)
            e.dma_start(out=io.bounce8[hc * 128:(hc + 1) * 128, :],
                        in_=B.y1b[:, (hc % 2) * 1024:(hc % 2 + 1) * 1024]).then_inc(
                            sB8wa if hc % 2 == 0 else sB8wb, 16)
        b8v = io.bounce8.rearrange("(p c) t -> p c t", c=64)
        e.wait_ge(sB8wa, 16 * (NHC // 2))
        e.wait_ge(sB8wb, 16 * (NHC // 2))
        for g in range(NHC // 2):
            if g >= 3:
                e.wait_ge(sP5, 2 * (g - 2))
            e.dma_start(out=B.rcb[:, (g % 3) * 2048:(g % 3) * 2048 + 2048].rearrange("p (c t) -> p c t", t=1024),
                        in_=b8v[:, 2 * g:2 * g + 2, :]).then_inc(
                            [sB8ra, sB8rb, sB8rc][g % 3], 16)

        def ld_x1cb(doc):
            if doc >= 2:
                e.wait_ge(sV6, 8 * (doc - 1))
            e.dma_start(out=B.x1cb[:, (doc % 2) * 1024:(doc % 2 + 1) * 1024].rearrange("p (tt c) -> p tt c", c=128),
                        in_=io.x1buf.rearrange("(tt p) d -> p tt d", p=128)[:, :, doc * 128:(doc + 1) * 128],
                        ).then_inc(sXCa if doc % 2 == 0 else sXCb, 16)

        ld_x1cb(0)
        ld_x1cb(1)
        for doc in range(16):
            for tt in range(NT):
                ti = doc * 8 + tt
                e.wait_ge(sV6, ti + 1)
                e.dma_start(out=io.xout[tt * 128:(tt + 1) * 128, doc * 128:(doc + 1) * 128],
                            in_=B.x2b[:, (ti % 2) * 128:(ti % 2 + 1) * 128]).then_inc(
                                sOa if ti % 2 == 0 else sOb, 16)
            if doc + 2 < 16:
                ld_x1cb(doc + 2)
        e.wait_ge(sOa, 16 * 64)
        e.wait_ge(sOb, 16 * 64)

    def f_gpsimd(g):
        if layer == 0:
            wdv = io.wd.rearrange("(c p) f -> p c f", p=128)
            for fc in range(NDC):
                if fc >= 2:
                    g.wait_ge(sPA, fc - 1)
                g.dma_start(out=B.wdstr[:, (fc % 2) * 2048:(fc % 2 + 1) * 2048].rearrange("p (c f) -> p c f", f=128),
                            in_=wdv[:, :, fc * 128:(fc + 1) * 128]).then_inc(
                                sWDa if fc % 2 == 0 else sWDb, 16)
        if layer == 0:
            w1v = io.w1t.rearrange("(c p) h -> p c h", p=128)
            w3v = io.w3t.rearrange("(c p) h -> p c h", p=128)
        else:
            w1v = io.w1t.rearrange("(p c) h -> p c h", c=16)
            w3v = io.w3t.rearrange("(p c) h -> p c h", c=16)
        for sbk in range(NSB):
            if sbk >= 2:
                g.wait_ge(sP2, 4 * (sbk - 1))
            else:
                g.wait_ge(sAB, 17)
                g.wait_ge(sVB, 34)
            base = (sbk % 2) * 16384
            g.dma_start(out=B.wstr[:, base:base + 8192].rearrange("p (c h) -> p c h", h=512),
                        in_=w1v[:, :, sbk * 512:(sbk + 1) * 512]).then_inc(
                            sWDa if sbk % 2 == 0 else sWDb, 16)
            g.dma_start(out=B.wstr[:, base + 8192:base + 16384].rearrange("p (c h) -> p c h", h=512),
                        in_=w3v[:, :, sbk * 512:(sbk + 1) * 512]).then_inc(
                            sWDa if sbk % 2 == 0 else sWDb, 16)
        w2v = io.w2t.rearrange("(p c) dd -> p c dd", c=64)
        for doc in range(16):
            if doc >= 2:
                g.wait_ge(sP6, doc - 1)
            else:
                g.wait_ge(sP2, 64)
            g.dma_start(out=B.w2str[:, (doc % 2) * 8192:(doc % 2 + 1) * 8192].rearrange("p (c dd) -> p c dd", dd=128),
                        in_=w2v[:, :, doc * 128:(doc + 1) * 128]).then_inc(
                            sW2a if doc % 2 == 0 else sW2b, 16)

    def f_tensor(t):
        t.wait_ge(sC2, SCTOT)
        if layer == 0:
            t.wait_ge(sAT, 16)
            atv = B.atT.rearrange("p (c t) -> p c t", t=1024)
            for fc in range(NDC):
                t.wait_ge(sWDa if fc % 2 == 0 else sWDb, 16 * (fc // 2 + 1))
                if fc >= 2:
                    t.wait_ge(sEA, fc - 1)
                half = (fc % 2) * 1024
                ins = None
                for tc in range(2):
                    for dc in range(NDC):
                        ins = t.matmul(B.psF[:, half + tc * 512:half + (tc + 1) * 512],
                                       B.wdstr[:, (fc % 2) * 2048 + dc * 128:(fc % 2) * 2048 + (dc + 1) * 128],
                                       atv[:, dc, tc * 512:(tc + 1) * 512],
                                       start=(dc == 0), stop=(dc == NDC - 1))
                ins.then_inc(sPA, 1)
            t.wait_ge(sEA, 16)
            for tt in range(NT):
                for grp in range(4):
                    gi = tt * 4 + grp
                    if gi >= 2:
                        t.wait_ge(sVA, gi - 1)
                    ins = None
                    for j in range(4):
                        fc = grp * 4 + j
                        ins = t.transpose(B.psTd[:, (gi % 2) * 1024 + j * 128:(gi % 2) * 1024 + (j + 1) * 128],
                                          B.dnsF[:, fc * 1024 + tt * 128:fc * 1024 + (tt + 1) * 128],
                                          cst.ident)
                    ins.then_inc(sPA, 1)
        for tt in range(NT):
            t.wait_ge(sAB, 10 + tt)
            for grp in range(4):
                gi = tt * 4 + grp
                if gi >= 2:
                    t.wait_ge(sVB, 2 + gi - 1)
                ins = None
                for j in range(4):
                    fc = grp * 4 + j
                    ins = t.transpose(B.psTd[:, (gi % 2) * 1024 + j * 128:(gi % 2) * 1024 + (j + 1) * 128],
                                      B.lntB[:, (tt % 2) * D + fc * 128:(tt % 2) * D + (fc + 1) * 128],
                                      cst.ident)
                ins.then_inc(sPB, 1)
        if layer == 1:
            t.wait_ge(sVB, 34)
            for fc in range(NDC):
                if fc >= 1:
                    t.wait_ge(sV1b, fc)
                ins = None
                for tc in range(2):
                    ins = t.matmul(B.psB2[:, tc * 512:(tc + 1) * 512], cst.bd16,
                                   B.lnF[:, fc * 1024 + tc * 512:fc * 1024 + (tc + 1) * 512],
                                   start=True, stop=True)
                ins.then_inc(sP1, 1)
            for c in range(NDC):
                g = c // 2
                rb = (g % 3) * 2048 + (c % 2) * 1024
                t.wait_ge([sB2ra, sB2rb, sB2rc][g % 3], 16 * (g // 3 + 1))
                t.wait_ge(sV1b, 16 + c)
                ins = None
                for tc in range(2):
                    ins = t.matmul(B.psB2[:, tc * 512:(tc + 1) * 512], cst.h128,
                                   B.rcb[:, rb + tc * 512:rb + (tc + 1) * 512],
                                   start=True, stop=True)
                ins.then_inc(sP1, 1)
        else:
            t.wait_ge(sVB, 34)
        if layer == 1:
            t.wait_ge(sV1b, 32)

        def upgate(hc):
            sbk, hcl = hc // 4, hc % 4
            if hcl == 0:
                t.wait_ge(sWDa if sbk % 2 == 0 else sWDb, W1BASE + 32 * (sbk // 2 + 1))
            p = hc % 2
            if hc >= 2:
                t.wait_ge(sV3, 2 * (hc - 2) + 2)
                t.wait_ge(sA3, hc - 1)
            wb = (sbk % 2) * 16384
            ins = None
            for cc in range(NDC):
                for tc in range(2):
                    t.matmul(B.psU[:, p * 1024 + tc * 512:p * 1024 + (tc + 1) * 512],
                             B.wstr[:, wb + cc * 512 + hcl * 128:wb + cc * 512 + (hcl + 1) * 128],
                             B.lnF[:, cc * 1024 + tc * 512:cc * 1024 + (tc + 1) * 512],
                             start=(cc == 0), stop=(cc == NDC - 1))
                    ins = t.matmul(B.psG[:, p * 1024 + tc * 512:p * 1024 + (tc + 1) * 512],
                                   B.wstr[:, wb + 8192 + cc * 512 + hcl * 128:wb + 8192 + cc * 512 + (hcl + 1) * 128],
                                   B.lnF[:, cc * 1024 + tc * 512:cc * 1024 + (tc + 1) * 512],
                                   start=(cc == 0), stop=(cc == NDC - 1))
            ins.then_inc(sP2, 1)

        def stageA(hc):
            p = hc % 2
            t.wait_ge(sV3, 2 * hc + 1)
            t.wait_ge(sA3, hc + 1)
            ins = None
            for tc in range(2):
                ins = t.matmul(B.psU[:, p * 1024 + tc * 512:p * 1024 + (tc + 1) * 512],
                               cst.bd64,
                               B.hck[:, p * 1024 + tc * 512:p * 1024 + (tc + 1) * 512],
                               start=True, stop=True)
            ins.then_inc(sP2a, 1)

        upgate(0)
        for hc in range(1, NHC):
            upgate(hc)
            stageA(hc - 1)
        stageA(NHC - 1)
        for c in range(NHC):
            g = c // 2
            rb = (g % 3) * 2048 + (c % 2) * 1024
            ps5 = B.psB2 if c % 2 == 0 else B.psB2b
            t.wait_ge([sB8ra, sB8rb, sB8rc][g % 3], 16 * (g // 3 + 1))
            if c >= 2:
                t.wait_ge(sV5, c - 1)
            ins = None
            for tc in range(2):
                ins = t.matmul(ps5[:, tc * 512:(tc + 1) * 512], cst.h128,
                               B.rcb[:, rb + tc * 512:rb + (tc + 1) * 512],
                               start=True, stop=True)
            ins.then_inc(sP5, 1)
        t.wait_ge(sV5, 64)

        def psy(doc):
            p = doc % 2
            t.wait_ge(sW2a if p == 0 else sW2b, 16 * (doc // 2 + 1))
            if doc >= 2:
                t.wait_ge(sA6, doc - 1)
            ins = None
            for c in range(NHC):
                for tc in range(2):
                    ins = t.matmul(B.psY[:, p * 1024 + tc * 512:p * 1024 + (tc + 1) * 512],
                                   B.w2str[:, p * 8192 + c * 128:p * 8192 + (c + 1) * 128],
                                   B.h2[:, c * 1024 + tc * 512:c * 1024 + (tc + 1) * 512],
                                   start=(c == 0), stop=(c == NHC - 1))
            ins.then_inc(sP6, 1)

        def trans6(doc):
            t.wait_ge(sA6, doc + 1)
            for tt in range(NT):
                ti = doc * 8 + tt
                if ti >= 2:
                    t.wait_ge(sV6, ti - 1)
                t.transpose(B.psTd[:, (ti % 2) * 1024:(ti % 2) * 1024 + 128],
                            B.ysb[:, (doc % 2) * 1024 + tt * 128:(doc % 2) * 1024 + (tt + 1) * 128],
                            cst.ident).then_inc(sP6a, 1)

        psy(0)
        for doc in range(1, 16):
            psy(doc)
            trans6(doc - 1)
        trans6(15)

    def f_scalar(s):
        s.wait_ge(sC2, SCTOT)
        if layer == 0:
            for fc in range(NDC):
                s.wait_ge(sPA, fc + 1)
                s.activation(B.dnsF[:, fc * 1024:(fc + 1) * 1024],
                             B.psF[:, (fc % 2) * 1024:(fc % 2 + 1) * 1024],
                             AF.Identity, bias=B.pcol[:, fc:fc + 1],
                             scale=1.0).then_inc(sEA, 1)
        for tt in range(NT):
            s.wait_ge(sVA, VA_TT * (tt + 1))
            if tt >= 1:
                s.wait_ge(sAB, tt)
            s.activation(B.scrB[:],
                         B.x1tA[:, (tt % 2) * D:(tt % 2 + 1) * D], AF.Square,
                         accum_out=B.smq[:, NT + tt:NT + tt + 1]).then_inc(sAB, 1)
        s.wait_ge(sVB, 1)
        s.activation(B.stt[:, NT:2 * NT], B.stt[:, NT:2 * NT], AF.Sqrt).then_inc(sAB, 1)
        s.wait_ge(sVB, 2)
        for tt in range(NT):
            s.wait_ge(sXB if tt % 2 == 0 else sXBb, 16 * (tt // 2 + 1))
            if tt >= 2:
                s.wait_ge(sPB, 4 * (tt - 1))
            s.activation(B.lntB[:, (tt % 2) * D:(tt % 2 + 1) * D],
                         B.xtB[:, (tt % 2) * D:(tt % 2 + 1) * D], AF.Identity,
                         bias=B.stt[:, 3 * NT + tt:3 * NT + tt + 1],
                         scale=B.stt[:, 2 * NT + tt:2 * NT + tt + 1]).then_inc(sAB, 1)
        for hc in range(NHC):
            p = hc % 2
            s.wait_ge(sP2, hc + 1)
            if hc >= 2:
                s.wait_ge(sV3, 2 * (hc - 2) + 1)
            if layer == 0:
                s.activation(B.sgb[:, p * 1024:(p + 1) * 1024],
                             B.psU[:, p * 1024:(p + 1) * 1024], AF.Silu).then_inc(sA3, 1)
            else:
                s.activation(B.sgb[:, p * 1024:(p + 1) * 1024],
                             B.psG[:, p * 1024:(p + 1) * 1024], AF.Silu,
                             scale=B.pcol2[:, 64 + hc:64 + hc + 1]).then_inc(sA3, 1)
        for doc in range(16):
            p = doc % 2
            s.wait_ge(sP6, doc + 1)
            if doc >= 2:
                s.wait_ge(sP6a, 8 * (doc - 1))
            if layer == 0:
                s.activation(B.ysb[:, p * 1024:(p + 1) * 1024],
                             B.psY[:, p * 1024:(p + 1) * 1024], AF.Identity).then_inc(sA6, 1)
            else:
                s.activation(B.ysb[:, p * 1024:(p + 1) * 1024],
                             B.psY[:, p * 1024:(p + 1) * 1024], AF.Identity,
                             scale=B.pcol[:, 16 + doc:16 + doc + 1]).then_inc(sA6, 1)

    def f_vector(v):
        nd = [0]
        sD = sm["sD"]

        def dedge(ins):
            nd[0] += 1
            ins.then_inc(sD, 1)
            v.wait_ge(sD, nd[0])

        v.wait_ge(sC2, SCTOT)
        if layer == 1:
            dedge(v.tensor_tensor(B.pcol2[:, 0:64], B.pcol2[:, 0:64], B.pcol2[:, 128:192], ALU.mult))
        if layer == 0:
            for tt in range(NT):
                v.wait_ge(sXa if tt % 2 == 0 else sXb, 16 * (tt // 2 + 1))
                if tt >= 2:
                    v.wait_ge(sX1Oa if tt % 2 == 0 else sX1Ob, 16 * (tt // 2))
                    v.wait_ge(sAB, tt - 1)
                for grp in range(4):
                    gi = tt * 4 + grp
                    v.wait_ge(sPA, 16 + gi + 1)
                    v.tensor_tensor(B.x1tA[:, (tt % 2) * D + grp * 512:(tt % 2) * D + (grp + 1) * 512],
                                    B.psTd[:, (gi % 2) * 1024:(gi % 2) * 1024 + 512],
                                    B.xinA[:, (tt % 2) * D + grp * 512:(tt % 2) * D + (grp + 1) * 512],
                                    ALU.add).then_inc(sVA, 1)
                v.wait_ge(sVA, 4 * (tt + 1))
                dedge(v.reduce_sum(B.smq[:, tt:tt + 1],
                                   B.x1tA[:, (tt % 2) * D:(tt % 2 + 1) * D], axis=AX.X))
        else:
            for tt in range(NT):
                v.wait_ge(sXa if tt % 2 == 0 else sXb, 16 * (tt // 2 + 1))
                v.wait_ge(sATa if tt % 2 == 0 else sATb, 16 * (tt // 2 + 1))
                if tt >= 2:
                    v.wait_ge(sX1Oa if tt % 2 == 0 else sX1Ob, 16 * (tt // 2))
                    v.wait_ge(sAB, tt - 1)
                if tt >= 1:
                    v.wait_ge(sVA, tt)
                a32v = B.a32[:, 0:2048].rearrange("p (r h d) -> p r h d", h=4, d=128)
                for h in range(4):
                    dedge(v.tensor_copy(a32v[:, :, h, :],
                                        B.at1[:, (tt % 2) * 2048 + h * 512:(tt % 2) * 2048 + (h + 1) * 512].rearrange("p (r d) -> p r d", d=128)))
                v.tensor_tensor(B.x1tA[:, (tt % 2) * D:(tt % 2 + 1) * D],
                                B.a32[:, 0:2048],
                                B.xinA[:, (tt % 2) * D:(tt % 2 + 1) * D],
                                ALU.add).then_inc(sVA, 1)
                v.wait_ge(sVA, tt + 1)
                dedge(v.reduce_sum(B.smq[:, tt:tt + 1],
                                   B.x1tA[:, (tt % 2) * D:(tt % 2 + 1) * D], axis=AX.X))
        mean, veps, rstd, nmrs = (B.stt[:, 0:n], B.stt[:, n:2 * n],
                                  B.stt[:, 2 * n:3 * n], B.stt[:, 3 * n:4 * n])

        v.wait_ge(sAB, NT)
        dedge(v.tensor_scalar(mean, B.smq[:, 0:NT], 1.0 / D, None, ALU.mult))
        dedge(v.tensor_scalar(veps, B.smq[:, NT:2 * NT], 1.0 / D, LN_EPS, ALU.mult, ALU.add))
        dedge(v.tensor_tensor(rstd, mean, mean, ALU.mult))
        v.tensor_sub(veps, veps, rstd).then_inc(sVB, 1)
        v.wait_ge(sAB, NT + 1)
        dedge(v.reciprocal(rstd, veps))
        dedge(v.tensor_tensor(nmrs, mean, rstd, ALU.mult))
        v.tensor_scalar(nmrs, nmrs, -1.0, None, ALU.mult).then_inc(sVB, 1)
        for tt in range(NT):
            for grp in range(4):
                gi = tt * 4 + grp
                v.wait_ge(sPB, gi + 1)
                dst3 = B.lnF[:].rearrange("p (fc t) -> p fc t", t=1024)[
                    :, grp * 4:(grp + 1) * 4, tt * 128:(tt + 1) * 128]
                src3 = B.psTd[:, (gi % 2) * 1024:(gi % 2) * 1024 + 512].rearrange(
                    "p (j t) -> p j t", t=128)
                if layer == 0:
                    v.tensor_copy(dst3, src3).then_inc(sVB, 1)
                else:
                    sucol = B.pcol[:, grp * 4:(grp + 1) * 4].unsqueeze(2).broadcast_to([128, 4, 128])
                    v.tensor_tensor(dst3, src3, sucol, ALU.mult).then_inc(sVB, 1)
        if layer == 1:
            for fc in range(NDC):
                v.wait_ge(sP1, fc + 1)
                if fc >= 2:
                    v.wait_ge(sB2wa if fc % 2 == 0 else sB2wb, 16 * (fc // 2))
                v.tensor_copy(B.c1b[:, (fc % 2) * 1024:(fc % 2 + 1) * 1024],
                              B.psB2[:]).then_inc(sV1b, 1)
            for c in range(NDC):
                v.wait_ge(sP1, 16 + c + 1)
                v.tensor_scalar(B.lnF[:, c * 1024:(c + 1) * 1024], B.psB2[:],
                                FWN2, None, ALU.mult).then_inc(sV1b, 1)
        for hc in range(NHC):
            p = hc % 2
            v.wait_ge(sA3, hc + 1)
            if hc >= 2:
                v.wait_ge(sP2a, hc - 1)
            if layer == 0:
                v.tensor_tensor(B.hck[:, p * 1024:(p + 1) * 1024],
                                B.psG[:, p * 1024:(p + 1) * 1024],
                                B.sgb[:, p * 1024:(p + 1) * 1024], ALU.mult).then_inc(sV3, 1)
            else:
                v.scalar_tensor_tensor(B.hck[:, p * 1024:(p + 1) * 1024],
                                       B.psU[:, p * 1024:(p + 1) * 1024],
                                       B.pcol2[:, hc:hc + 1],
                                       B.sgb[:, p * 1024:(p + 1) * 1024],
                                       ALU.mult, ALU.mult).then_inc(sV3, 1)
            v.wait_ge(sP2a, hc + 1)
            if hc >= 2:
                v.wait_ge(sB8wa if hc % 2 == 0 else sB8wb, 16 * (hc // 2))
            v.tensor_copy(B.y1b[:, p * 1024:(p + 1) * 1024],
                          B.psU[:, p * 1024:(p + 1) * 1024]).then_inc(sV3, 1)
        for c in range(NHC):
            v.wait_ge(sP5, c + 1)
            v.tensor_scalar(B.h2[:, c * 1024:(c + 1) * 1024],
                            (B.psB2 if c % 2 == 0 else B.psB2b)[:],
                            FWN, None, ALU.mult).then_inc(sV5, 1)
        for doc in range(16):
            v.wait_ge(sXCa if doc % 2 == 0 else sXCb, 16 * (doc // 2 + 1))
            for tt in range(NT):
                ti = doc * 8 + tt
                v.wait_ge(sP6a, ti + 1)
                if ti >= 2:
                    v.wait_ge(sOa if ti % 2 == 0 else sOb, 16 * (ti // 2))
                v.tensor_tensor(B.x2b[:, (ti % 2) * 128:(ti % 2 + 1) * 128],
                                B.psTd[:, (ti % 2) * 1024:(ti % 2) * 1024 + 128],
                                B.x1cb[:, (doc % 2) * 1024 + tt * 128:(doc % 2) * 1024 + (tt + 1) * 128],
                                ALU.add).then_inc(sV6, 1)

    C.bodies = {"sync": f_sync, "scalar": f_scalar, "vector": f_vector,
                "tensor": f_tensor, "gpsimd": f_gpsimd}


def build_fused():
    nc = bass.Bass("TRN2", target_bir_lowering=False)
    I32 = mybir.dt.int32
    dt_ = nc.dram_tensor
    io_x = dt_("x", [TOK, D], F32, kind="ExternalInput").ap()
    io_wqkv = dt_("wqkv_sl", [D, 1536], F32, kind="ExternalInput").ap()
    io_bqkv = dt_("bqkv_sl", [1536], F32, kind="ExternalInput").ap()
    io_wd = dt_("wd", [D, D], F32, kind="ExternalInput").ap()
    io_bd = dt_("bd", [D], F32, kind="ExternalInput").ap()
    io_w1t0 = dt_("w1t0", [D, HF], F32, kind="ExternalInput").ap()
    io_w3t0 = dt_("w3t0", [D, HF], F32, kind="ExternalInput").ap()
    io_w2t0 = dt_("w2t0", [HF, D], F32, kind="ExternalInput").ap()
    io_wsl1 = dt_("wsl1", [D, 1536], F32, kind="ExternalInput").ap()
    io_tau = dt_("tau", [1], F32, kind="ExternalInput").ap()
    io_nu = dt_("nu", [1], F32, kind="ExternalInput").ap()
    io_su = dt_("su_ug", [D], F32, kind="ExternalInput").ap()
    io_wsu = dt_("wsu", [HF], F32, kind="ExternalInput").ap()
    io_wsg = dt_("wsg", [HF], F32, kind="ExternalInput").ap()
    io_sdp = dt_("sdp", [HF], F32, kind="ExternalInput").ap()
    io_wdp = dt_("wdp", [D], F32, kind="ExternalInput").ap()
    io_w1t1 = dt_("w1t1", [D, HF], F32, kind="ExternalInput").ap()
    io_w3t1 = dt_("w3t1", [D, HF], F32, kind="ExternalInput").ap()
    io_w2t1 = dt_("w2t1", [HF, D], F32, kind="ExternalInput").ap()
    io_loc = dt_("myloc", [1], I32, kind="ExternalInput").ap()
    io_out = dt_("out", [TOK, D], F32, kind="ExternalOutput").ap()

    cc_ln0_in = dt_("cc_ln0_in", [4, D, 256], BF16, kind="Internal").ap()
    cc_ln0_out = dt_("cc_ln0_out", [4, 4 * D, 256], BF16, kind="Internal").ap()
    cc_at0_in = dt_("cc_at0_in", [512, S], BF16, kind="Internal").ap()
    cc_at0_out = dt_("cc_at0_out", [4, 512, S], BF16, kind="Internal").ap()
    cc_ln1_in = dt_("cc_ln1_in", [4, D, 256], BF16, kind="Internal").ap()
    cc_ln1_out = dt_("cc_ln1_out", [4, 4 * D, 256], BF16, kind="Internal").ap()
    cc_at1_in = dt_("cc_at1_in", [4, S, 128], BF16, kind="Internal").ap()
    cc_at1_out = dt_("cc_at1_out", [4, 4 * S, 128], BF16, kind="Internal").ap()
    x1mid = dt_("x1mid", [TOK, D], F32, kind="Internal").ap()
    x1buf0 = dt_("x1buf0", [TOK, D], F32, kind="Internal").ap()
    x1buf1 = dt_("x1buf1", [TOK, D], F32, kind="Internal").ap()
    bounce8a = dt_("bounce8a", [HF, TOK], BF16, kind="Internal").ap()
    bounce8b = dt_("bounce8b", [HF, TOK], BF16, kind="Internal").ap()
    bounce2 = dt_("bounce2", [D, TOK], BF16, kind="Internal").ap()

    ident_c = nc.inline_tensor(np.eye(128, dtype=bfdt), name="c_ident").ap()
    mask_np = (np.arange(128)[:, None] <= np.arange(128)[None, :]).astype(bfdt)
    mask_c = nc.inline_tensor(mask_np, name="c_mask").ap()
    h128_c = nc.inline_tensor(H128_NP.astype(bfdt), name="c_h128").ap()
    bd64_c = nc.inline_tensor(BD64_NP.astype(bfdt), name="c_bd64").ap()
    bd16_c = nc.inline_tensor(BD16_NP.astype(bfdt), name="c_bd16").ap()
    ones_f = nc.inline_tensor(np.ones((1, 128), dtype=np.float32), name="c_onesf").ap()
    ones_b = nc.inline_tensor(np.ones((1, 128), dtype=bfdt), name="c_onesb").ap()

    import contextlib
    es = contextlib.ExitStack()
    with es:
        sb = lambda *a: es.enter_context(nc.sbuf_tensor(*a))
        sem = lambda nm: es.enter_context(nc.semaphore(nm))

        ARN = sb("ARN", [128, 91648], BF16)
        cst = _NS(
            ident=sb("c_ident_s", [128, 128], BF16)[:],
            mask01=sb("c_mask_s", [128, 128], BF16)[:],
            h128=sb("c_h128_s", [128, 128], BF16)[:],
            bd64=sb("c_bd64_s", [128, 128], BF16)[:],
            bd16=sb("c_bd16_s", [128, 128], BF16)[:],
            ones128=sb("c_onesf_s", [1, 128], F32)[:],
            onesb=sb("c_onesb_s", [1, 128], BF16)[:],
        )
        locs_t = sb("locs", [1, 1], I32)
        att_small = dict(
            bqs=sb("a_bqs", [128, 12], F32)[:],
            bvrow=sb("a_bvrow", [1, 512], BF16)[:],
            taunu=sb("a_taunu", [1, 2], F32)[:],
            tnb=sb("a_tnb", [128, 4], F32)[:],
            ssq=sb("a_ssq", [128, 36], F32)[:],
            csh=sb("a_csh", [128, 2], F32)[:],
            rr=sb("a_rr", [1, 512], F32)[:],
            rB=sb("a_rB", [128, 512], F32)[:],
            attnb=sb("a_attnb", [128, 512], BF16)[:],
            scrq=sb("a_scrq", [128, 1536], F32)[:],
        )
        post_small = dict(
            pcol=sb("p_pcol", [128, 32], F32)[:],
            pcol2=sb("p_pcol2", [128, 192], F32)[:],
            smq=sb("p_smq", [128, 2 * NT], F32)[:],
            stt=sb("p_stt", [128, 4 * NT], F32)[:],
            x1cb=sb("p_x1cb", [128, 2048], F32)[:],
            ysb=sb("p_ysb", [128, 2048], BF16)[:],
            x2b=sb("p_x2b", [128, 256], F32)[:],
        )
        psA = es.enter_context(nc.psum_tensor("psA", [128, 4096], F32))
        reg = es.enter_context(nc.sync.register("locreg"))

        sCst = sem("sCst")
        sBAR = sem("sBAR")
        sCC = [sem(f"sCC{i}") for i in range(4)]
        PRE_KEYS = ["sL", "sL2", "sV", "sA", "sT", "sE", "sO", "sD"]
        ATT_KEYS = ["sL", "sW", "sP0", "sPE", "sA1", "sV1", "sT2", "sX",
                    "sM", "sV2", "sO", "sB0", "sB1", "sD"]
        POST_KEYS = ["sC2", "sAT", "sATa", "sATb", "sXa", "sXb", "sXB",
                     "sWDa", "sWDb", "sW2a", "sW2b", "sXCa", "sXCb",
                     "sPA", "sEA", "sVA", "sPB", "sAB", "sVB", "sP1", "sV1b",
                     "sP2", "sP2a", "sA3", "sV3", "sP5", "sV5", "sP6", "sP6a", "sA6", "sV6",
                     "sX1Oa", "sX1Ob", "sB8wa", "sB8wb", "sB8ra", "sB8rb", "sB8rc",
                     "sB2wa", "sB2wb", "sB2ra", "sB2rb", "sB2rc", "sOa", "sOb",
                     "sXBb", "sD"]
        PRE_SEMS = {k: sem("pre_" + k) for k in PRE_KEYS}
        ATT_SEMS = {k: sem("att_" + k) for k in ATT_KEYS}
        POST_SEMS = {k: sem("post_" + k) for k in POST_KEYS}

        def sem_range(d):
            nums = sorted(h.num for h in d.values())
            assert nums == list(range(nums[0], nums[-1] + 1)), nums
            return range(nums[0], nums[-1] + 1)

        PRE_RANGE = sem_range(PRE_SEMS)
        ATT_RANGE = sem_range(ATT_SEMS)
        POST_RANGE = sem_range(POST_SEMS)

        A = ARN

        pre_sm = sb("pre_sm", [128, NT], F32)[:]
        pre_sq = sb("pre_sq", [128, NT], F32)[:]
        pre_st = sb("pre_st", [128, 4 * NT], F32)[:]

        def mk_pre(io_xa, io_lnTa, _n=[0]):
            C = _NS()
            C.cst = cst
            C.sems = PRE_SEMS
            C.B = _NS(
                xt=A[:, 0:32768].bitcast(F32),
                scr=A[:, 32768:36864].bitcast(F32),
                ln=A[:, 36864:40960],
                lt=A[:, 40960:49152],
                sm=pre_sm, sq=pre_sq, st=pre_st,
                psT=psA[:, 2048:3072].bitcast(BF16),
            )
            C.io = _NS(x=io_xa, lnT=io_lnTa)
            emit_pre_f(C)
            return C

        def mk_att(layer, io_lnTg, io_attc, ccsem):
            C = _NS()
            C.cst = cst
            C.sems = ATT_SEMS
            ab = 49664
            C.B = _NS(
                QF=A[:, 0:16384], KF=A[:, 16384:32768], VT=A[:, 32768:49664],
                PT=A[:, ab:ab + 16384], lnb=A[:, ab:ab + 16384],
                ws=A[:, ab + 16384:ab + 40960], qtok=A[:, ab + 40960:ab + 41984],
                psQK=psA[:, 0:1024], psV=psA[:, 1024:2048],
                psQ1=psA[:, 0:1024], psK1=psA[:, 1024:2048], psV1_=psA[:, 2048:3072],
                psT1=psA[:, 3072:4096].bitcast(BF16),
                psS=psA[:, 0:2048], psO=psA[:, 2048:2560], psR=psA[:, 2560:3072],
                psSum=psA[0:1, 3072:3584], psO3=psA[:, 2048:4096],
                psB0=psA[:, 3584:4096],
                **att_small,
            )
            if layer == 0:
                C.io = _NS(lnTg=io_lnTg, att_c=io_attc, w_sl=io_wqkv, bq=io_bqkv)
            else:
                C.io = _NS(lnTg=io_lnTg, att_c=io_attc, w_sl=io_wsl1, tau=io_tau, nu=io_nu)
            C.ccsem = ccsem
            C.ccval = 1
            emit_att_f(C, layer)
            return C

        def mk_post(layer, io_xa, io_attg, io_xouta, io_x1bufa, io_b8, ccsem, loc_ref):
            C = _NS()
            C.cst = cst
            C.sems = POST_SEMS
            C.B = _NS(
                atT=A[:, 0:16384], dnsF=A[:, 16384:32768],
                wdstr=A[:, 32768:36864], at1=A[:, 32768:36864],
                a32=A[:, 16384:20480].bitcast(F32),
                xinA=A[:, 36864:45056].bitcast(F32),
                x1tA=A[:, 45056:53248].bitcast(F32),
                xtB=A[:, 53248:61440].bitcast(F32),
                lntB=A[:, 32768:36864],
                scrB=A[:, 61440:65536].bitcast(F32),
                wstr=A[:, 0:32768],
                sgb=A[:, 32768:34816], hck=A[:, 34816:38912],
                y1b=A[:, 38912:40960], c1b=A[:, 36864 + 4096:40960 + 0] if False else A[:, 61440:63488],
                h2=A[:, 0:65536],
                lnF=A[:, 65536:81920],
                rcb=A[:, 83968:90112], w2str=A[:, 67584:83968],
                psF=psA[:, 0:2048], psTd=psA[:, 2048:3072].bitcast(BF16),
                psU=psA[:, 0:2048], psG=psA[:, 2048:4096],
                psB2=psA[:, 3072:4096], psB2b=psA[:, 0:1024], psY=psA[:, 0:2048],
                **post_small,
            )
            if layer == 0:
                C.io = _NS(x=io_xa, attg=io_attg, xout=io_xouta, x1buf=io_x1bufa,
                           bounce8=io_b8, wd=io_wd, bd=io_bd,
                           w1t=io_w1t0, w3t=io_w3t0, w2t=io_w2t0)
            else:
                C.io = _NS(x=io_xa, attg=io_attg, xout=io_xouta, x1buf=io_x1bufa,
                           bounce8=io_b8, bounce2=bounce2,
                           su_ug=io_su, wsu=io_wsu, wsg=io_wsg, sdp=io_sdp,
                           wdp=io_wdp, w1t=io_w1t1, w3t=io_w3t1, w2t=io_w2t1)
            C.ccsem = ccsem
            C.ccval = 4
            C.loc = loc_ref
            emit_post_f(C, layer)
            return C

        loc_holder = _NS(v=None)

        class _LazyLoc:
            def __add__(self, other):
                return loc_holder.v + other

            def __radd__(self, other):
                return other + loc_holder.v

        phases = []
        pre0 = mk_pre(io_x, cc_ln0_in)
        att0 = mk_att(0, cc_ln0_out, cc_at0_in, sCC[0])
        post0 = mk_post(0, io_x, cc_at0_out, x1mid, x1buf0, bounce8a, sCC[1], None)
        pre1 = mk_pre(x1mid, cc_ln1_in)
        pre1.prefetch = (att0.B.ws, io_wsl1, ATT_SEMS["sW"])
        att1 = mk_att(1, cc_ln1_out, cc_at1_in, sCC[2])
        post1 = mk_post(1, x1mid, cc_at1_out, io_out, x1buf1, bounce8b, sCC[3], None)
        phases = [pre0, att0, post0, pre1, att1, post1]

        rg4 = [[0, 1, 2, 3], [4, 5, 6, 7]]

        def ln_pieces(ph, cin, cout, ccs):
            return [(ph, "sO", 16 * 2 * (q + 1), cin[q], cout[q], ccs)
                    for q in range(4)]

        def at0_pieces(ph, cin, cout, ccs):
            return [(ph, "sO", 16 * 8 * (h + 1), cin[h * 128:(h + 1) * 128, :],
                     cout[h], ccs) for h in range(4)]

        def at1_pieces(ph, cin, cout, ccs):
            return [(ph, "sO", 16 * 32 * (h + 1), cin[h], cout[h], ccs)
                    for h in range(4)]

        ag_spec = {
            0: ln_pieces(pre0, cc_ln0_in, cc_ln0_out, sCC[0]),
            1: at0_pieces(att0, cc_at0_in, cc_at0_out, sCC[1]),
            3: ln_pieces(pre1, cc_ln1_in, cc_ln1_out, sCC[2]),
            4: at1_pieces(att1, cc_at1_in, cc_at1_out, sCC[3]),
        }

        block = es.enter_context(nc.Block())
        NPH = len(phases)

        def run_bodies(eng_name):
            def body(eng):
                if eng_name == "sync":
                    eng.dma_start(out=cst.ident, in_=ident_c[:, :]).then_inc(sCst, 16)
                    eng.dma_start(out=cst.mask01, in_=mask_c[:, :]).then_inc(sCst, 16)
                    eng.dma_start(out=cst.h128, in_=h128_c[:, :]).then_inc(sCst, 16)
                    eng.dma_start(out=cst.bd64, in_=bd64_c[:, :]).then_inc(sCst, 16)
                    eng.dma_start(out=cst.bd16, in_=bd16_c[:, :]).then_inc(sCst, 16)
                    eng.dma_start(out=cst.ones128, in_=ones_f[:, :]).then_inc(sCst, 16)
                    eng.dma_start(out=cst.onesb, in_=ones_b[:, :]).then_inc(sCst, 16)
                    eng.dma_start(out=locs_t[0:1, :], in_=io_loc.unsqueeze(0)).then_inc(sCst, 16)
                    eng.wait_ge(sCst, 128)
                    eng.reg_load(reg, locs_t[0:1, 0:1])
                    loc_holder.v = eng.snap(reg)
                    post0.loc = loc_holder.v
                    post1.loc = loc_holder.v
                else:
                    if eng_name == "gpsimd":
                        wv0 = io_wqkv.rearrange("(c p) o -> p c o", p=128)
                        for q in range(4):
                            eng.dma_start(
                                out=att0.B.ws[:, q * 6144:(q + 1) * 6144].rearrange("p (c o) -> p c o", o=1536),
                                in_=wv0[:, q * 4:(q + 1) * 4, :]).then_inc(ATT_SEMS["sW"], 16)
                    eng.wait_ge(sCst, 128)
                clear_spec = {0: PRE_RANGE, 1: ATT_RANGE, 2: POST_RANGE,
                              3: PRE_RANGE, 4: ATT_RANGE}
                for k, C in enumerate(phases):
                    C.bodies[eng_name](eng)
                    if eng_name == "gpsimd" and k in ag_spec:
                        for (ph, semname, cnt, cin, cout, ccs) in ag_spec[k]:
                            eng.wait_ge(ph.sems[semname], cnt)
                            eng.collective_compute(
                                "AllGather", ALU.bypass, replica_groups=rg4,
                                ins=[cin.opt()], outs=[cout.opt()],
                            ).then_inc(ccs)
                    eng.sem_inc(sBAR, 1)
                    eng.wait_ge(sBAR, 6 * k + 5)
                    if eng_name == "gpsimd":
                        if k in clear_spec:
                            eng.dma_reset(clear_spec[k])
                            eng.sem_clear(clear_spec[k])
                        eng.sem_inc(sBAR, 1)
                    eng.wait_ge(sBAR, 6 * k + 6)
            return body

        block.sync(run_bodies("sync"))
        block.gpsimd(run_bodies("gpsimd"))
        block.tensor(run_bodies("tensor"))
        block.scalar(run_bodies("scalar"))
        block.vector(run_bodies("vector"))

    return nc


def kernel_fused(inputs):
    cN = np.ascontiguousarray
    x = cN(inputs["L0_attn_hidden_states"].reshape(2 * S, D)).astype(np.float32)
    Wqkv = inputs["L0_attn_W_qkv"]
    bqkv = inputs["L0_attn_b_qkv"]
    wq, wk, wv = inputs["L1_attn_wq"], inputs["L1_attn_wk"], inputs["L1_attn_wv"]
    wsc = inputs["L1_ffn_Wscale_ug"]
    maps = []
    shared = {
        "wd": cN(inputs["L0_attn_W_dense"].reshape(4, 4, 128, D).transpose(1, 0, 2, 3).reshape(D, D)),
        "bd": cN(inputs["L0_attn_b_dense"]),
        "w1t0": cN(inputs["L0_ffn_W1"].T),
        "w3t0": cN(inputs["L0_ffn_W3"].T),
        "w2t0": cN(inputs["L0_ffn_W2"].T),
        "tau": cN(inputs["L1_attn_tau"]),
        "nu": cN(inputs["L1_attn_nu"]),
        "su_ug": cN(inputs["L1_ffn_SU_ug"]),
        "wsu": cN(wsc[:HF]), "wsg": cN(wsc[HF:]),
        "sdp": cN(inputs["L1_ffn_SU_dp"]),
        "wdp": cN(inputs["L1_ffn_Wscale_dp"]),
        "w1t1": cN(inputs["L1_ffn_W_up"].T),
        "w3t1": cN(inputs["L1_ffn_W_gate"].T),
        "w2t1": cN(inputs["L1_ffn_W_down"].T),
    }
    for c in range(NCORES):
        g = c % 4
        cols = np.concatenate([
            Wqkv[:, 512 * g:512 * (g + 1)],
            Wqkv[:, 2048 + 512 * g:2048 + 512 * (g + 1)],
            Wqkv[:, 4096 + 512 * g:4096 + 512 * (g + 1)]], axis=1)
        bsl = np.concatenate([
            bqkv[512 * g:512 * (g + 1)],
            bqkv[2048 + 512 * g:2048 + 512 * (g + 1)],
            bqkv[4096 + 512 * g:4096 + 512 * (g + 1)]])
        wsl1 = np.concatenate([
            cN(wq[512 * g:512 * (g + 1), :].T),
            cN(wk[512 * g:512 * (g + 1), :].T),
            cN(wv[512 * g:512 * (g + 1), :].T)], axis=1)
        m = dict(shared)
        m["x"] = cN(x[c * TOK:(c + 1) * TOK])
        m["wqkv_sl"] = cN(cols)
        m["bqkv_sl"] = cN(bsl)
        m["wsl1"] = cN(wsl1)
        m["myloc"] = np.array([g * TOK], dtype=np.int32)
        maps.append(m)
    nc = _get_nc("fused", build_fused)
    res = run_bass_kernel_spmd(nc, maps, core_ids=list(range(NCORES)))
    out = np.concatenate([res.results[c]["out"] for c in range(NCORES)], axis=0)
    return out.reshape(2, S, D).astype(np.float32)


_NC_CACHE = {}


def _get_nc(name, builder):
    if name not in _NC_CACHE:
        _NC_CACHE[name] = builder()
    return _NC_CACHE[name]


def _run(nc, in_maps):
    return run_bass_kernel_spmd(nc, in_maps, core_ids=list(range(NCORES)))


def kernel(**inputs):
    return kernel_fused(inputs)


def kernel_unfused(**inputs):
    cN = np.ascontiguousarray
    x = cN(inputs["L0_attn_hidden_states"].reshape(2 * S, D)).astype(np.float32)
    xs = [cN(x[c * TOK:(c + 1) * TOK]) for c in range(NCORES)]

    Wqkv = inputs["L0_attn_W_qkv"]
    bqkv = inputs["L0_attn_b_qkv"]
    Wd = cN(inputs["L0_attn_W_dense"])
    bd = cN(inputs["L0_attn_b_dense"])
    W1T = cN(inputs["L0_ffn_W1"].T)
    W2T = cN(inputs["L0_ffn_W2"].T)
    W3T = cN(inputs["L0_ffn_W3"].T)
    wq, wk, wv = inputs["L1_attn_wq"], inputs["L1_attn_wk"], inputs["L1_attn_wv"]
    tau = cN(inputs["L1_attn_tau"])
    nu = cN(inputs["L1_attn_nu"])
    su_ug = cN(inputs["L1_ffn_SU_ug"])
    sdp = cN(inputs["L1_ffn_SU_dp"])
    wsc = inputs["L1_ffn_Wscale_ug"]
    wsu = cN(wsc[:HF]); wsg = cN(wsc[HF:])
    wdp = cN(inputs["L1_ffn_Wscale_dp"])
    WupT = cN(inputs["L1_ffn_W_up"].T)
    WgateT = cN(inputs["L1_ffn_W_gate"].T)
    WdownT = cN(inputs["L1_ffn_W_down"].T)

    nc_pre = _get_nc("pre", build_pre)
    nc_att0 = _get_nc("att0", lambda: build_att(0))
    nc_att1 = _get_nc("att1", lambda: build_att(1))
    nc_post0 = _get_nc("post0", lambda: build_post(0))
    nc_post1 = _get_nc("post1", lambda: build_post(1))

    def gather_lnT(res):
        out = []
        for b in range(2):
            out.append(np.concatenate(
                [res.results[b * 4 + j]["lnT"] for j in range(4)], axis=1))
        return out

    r = _run(nc_pre, [{"x": xs[c]} for c in range(NCORES)])
    lnT_b = gather_lnT(r)

    att0_maps = []
    for c in range(NCORES):
        g = c % 4
        cols = np.concatenate([
            Wqkv[:, 512 * g:512 * (g + 1)],
            Wqkv[:, 2048 + 512 * g:2048 + 512 * (g + 1)],
            Wqkv[:, 4096 + 512 * g:4096 + 512 * (g + 1)]], axis=1)
        bsl = np.concatenate([
            bqkv[512 * g:512 * (g + 1)],
            bqkv[2048 + 512 * g:2048 + 512 * (g + 1)],
            bqkv[4096 + 512 * g:4096 + 512 * (g + 1)]])
        att0_maps.append({"lnT": cN(lnT_b[c // 4]), "w_sl": cN(cols), "bq": cN(bsl)})
    r = _run(nc_att0, att0_maps)
    attF_b = [np.concatenate([r.results[b * 4 + j]["att_c"] for j in range(4)], axis=0)
              for b in range(2)]

    post0_maps = []
    for c in range(NCORES):
        b, loc = c // 4, (c % 4) * TOK
        post0_maps.append({
            "x": xs[c],
            "attnT": cN(attF_b[b][:, loc:loc + TOK]),
            "wd": Wd, "bd": bd, "w1t": W1T, "w3t": W3T, "w2t": W2T})
    r = _run(nc_post0, post0_maps)
    x1s = [r.results[c]["x2"] for c in range(NCORES)]

    r = _run(nc_pre, [{"x": cN(x1s[c])} for c in range(NCORES)])
    lnT_b = gather_lnT(r)

    att1_maps = []
    for c in range(NCORES):
        g = c % 4
        wsl = np.concatenate([
            cN(wq[512 * g:512 * (g + 1), :].T),
            cN(wk[512 * g:512 * (g + 1), :].T),
            cN(wv[512 * g:512 * (g + 1), :].T)], axis=1)
        att1_maps.append({"lnT": cN(lnT_b[c // 4]), "w_sl": cN(wsl),
                          "tau": tau, "nu": nu})
    r = _run(nc_att1, att1_maps)
    attT_b = [np.concatenate([r.results[b * 4 + j]["att_c"] for j in range(4)], axis=1)
              for b in range(2)]

    post1_maps = []
    for c in range(NCORES):
        b, loc = c // 4, (c % 4) * TOK
        post1_maps.append({
            "x": cN(x1s[c]),
            "attn_tok": cN(attT_b[b][loc:loc + TOK, :]),
            "su_ug": su_ug, "wsu": wsu, "wsg": wsg, "sdp": sdp, "wdp": wdp,
            "w1t": WupT, "w3t": WgateT, "w2t": WdownT})
    r = _run(nc_post1, post1_maps)
    out = np.concatenate([r.results[c]["x2"] for c in range(NCORES)], axis=0)
    return out.reshape(2, S, D).astype(np.float32)


# version 09
# speedup vs baseline: 1.0014x; 1.0014x over previous
import numpy as np
import ml_dtypes
import concourse.bass as bass
import concourse.mybir as mybir
from concourse.bass_utils import run_bass_kernel_spmd

F32 = mybir.dt.float32
BF16 = mybir.dt.bfloat16
AF = mybir.ActivationFunctionType
ALU = mybir.AluOpType
AX = mybir.AxisListType

NCORES = 8
D = 2048
S = 4096
TOK = 1024
NT = TOK // 128
NDC = D // 128
HF = 8192
LN_EPS = 1e-5
L2_EPS = 1e-8
C0_SHIFT = 30.0

bfdt = ml_dtypes.bfloat16


def _hadamard(n):
    h = np.array([[1.0]])
    while h.shape[0] < n:
        h = np.block([[h, h], [h, -h]])
    return h


H128_NP = _hadamard(128)
BD64_NP = np.kron(np.eye(2), _hadamard(64))
BD16_NP = np.kron(np.eye(8), _hadamard(16))



def build_pre():
    nc = bass.Bass("TRN2", target_bir_lowering=False)
    x = nc.dram_tensor("x", [TOK, D], F32, kind="ExternalInput").ap()
    lnT = nc.dram_tensor("lnT", [D, TOK], BF16, kind="ExternalOutput").ap()
    ident_c = nc.inline_tensor(np.eye(128, dtype=bfdt), name="c_ident").ap()

    with (
        nc.sbuf_tensor("xt", [128, NT * D], F32) as xt,
        nc.sbuf_tensor("scr", [128, D], F32) as scr,
        nc.sbuf_tensor("ln", [128, D], BF16) as ln,
        nc.sbuf_tensor("lt", [128, 2 * D], BF16) as lt,
        nc.sbuf_tensor("sm", [128, NT], F32) as sm,
        nc.sbuf_tensor("sq", [128, NT], F32) as sq,
        nc.sbuf_tensor("st", [128, 4 * NT], F32) as st,
        nc.sbuf_tensor("ident", [128, 128], BF16) as ident,
        nc.psum_tensor("pst", [128, 2048], BF16) as pst,
        nc.semaphore("sL") as sL,
        nc.semaphore("sV") as sV,
        nc.semaphore("sA") as sA,
        nc.semaphore("sT") as sT,
        nc.semaphore("sE") as sE,
        nc.semaphore("sO") as sO,
        nc.Block() as block,
    ):
        n = NT
        mean, veps, rstd, nmrs = (st[:, 0:n], st[:, n:2 * n], st[:, 2 * n:3 * n], st[:, 3 * n:4 * n])

        @block.sync
        def _(e):
            e.dma_start(out=ident[:], in_=ident_c[:, :]).then_inc(sL, 16)
            e.dma_start(out=xt[:].rearrange("p (tt d) -> p tt d", d=D),
                        in_=x.rearrange("(tt p) d -> p tt d", p=128)).then_inc(sL, 16)
            for tt in range(NT):
                e.wait_ge(sE, 4 * tt + 4)
                e.dma_start(
                    out=lnT.rearrange("(c p) t -> p c t", p=128)[:, :, tt * 128:(tt + 1) * 128],
                    in_=lt[:, (tt % 2) * D:(tt % 2 + 1) * D].rearrange("p (c t) -> p c t", t=128),
                ).then_inc(sO, 16)
            e.wait_ge(sO, 16 * NT)

        @block.scalar
        def _(s):
            s.wait_ge(sL, 32)
            for tt in range(NT):
                s.activation(scr[:], xt[:, tt * D:(tt + 1) * D], AF.Square,
                             accum_out=sq[:, tt:tt + 1]).then_inc(sA, 1)
            s.wait_ge(sV, 1)
            s.activation(veps, veps, AF.Sqrt).then_inc(sA, 1)
            s.wait_ge(sV, 2)
            for tt in range(NT):
                if tt >= 1:
                    s.wait_ge(sT, 4 * tt)
                s.activation(ln[:], xt[:, tt * D:(tt + 1) * D], AF.Identity,
                             bias=nmrs[:, tt:tt + 1],
                             scale=rstd[:, tt:tt + 1]).then_inc(sA, 1)

        @block.vector
        def _(v):
            v.wait_ge(sL, 32)
            for tt in range(NT):
                v.reduce_sum(sm[:, tt:tt + 1], xt[:, tt * D:(tt + 1) * D], axis=AX.X)
            v.wait_ge(sA, NT)
            v.nop()
            v.tensor_scalar(mean, sm[:], 1.0 / D, None, ALU.mult)
            v.tensor_scalar(veps, sq[:], 1.0 / D, LN_EPS, ALU.mult, ALU.add)
            v.nop()
            v.tensor_tensor(rstd, mean, mean, ALU.mult)
            v.nop()
            v.tensor_sub(veps, veps, rstd).then_inc(sV, 1)
            v.wait_ge(sA, NT + 1)
            v.nop()
            v.reciprocal(rstd, veps)
            v.nop()
            v.tensor_tensor(nmrs, mean, rstd, ALU.mult)
            v.nop()
            v.tensor_scalar(nmrs, nmrs, -1.0, None, ALU.mult).then_inc(sV, 1)
            for tt in range(NT):
                for grp in range(4):
                    gi = tt * 4 + grp
                    v.wait_ge(sT, gi + 1)
                    if tt >= 2:
                        v.wait_ge(sO, 16 * (tt - 1))
                    v.tensor_copy(lt[:, (tt % 2) * D + grp * 512:(tt % 2) * D + (grp + 1) * 512],
                                  pst[:, (gi % 2) * 1024:(gi % 2) * 1024 + 512]).then_inc(sE, 1)

        @block.tensor
        def _(t):
            t.wait_ge(sL, 32)
            for tt in range(NT):
                t.wait_ge(sA, NT + 2 + tt)
                for grp in range(4):
                    gi = tt * 4 + grp
                    if gi >= 2:
                        t.wait_ge(sE, gi - 1)
                    ins = None
                    for j in range(4):
                        fc = grp * 4 + j
                        ins = t.transpose(pst[:, (gi % 2) * 1024 + j * 128:(gi % 2) * 1024 + (j + 1) * 128],
                                          ln[:, fc * 128:(fc + 1) * 128], ident[:])
                    ins.then_inc(sT, 1)

    return nc




def build_att(layer, debug=False, p2=True):
    import os
    DBGV = os.environ.get("ATTDBG", "")
    nc = bass.Bass("TRN2", target_bir_lowering=False)
    lnT = nc.dram_tensor("lnT", [D, S], BF16, kind="ExternalInput").ap()
    w_sl = nc.dram_tensor("w_sl", [D, 1536], F32, kind="ExternalInput").ap()
    if layer == 0:
        bq = nc.dram_tensor("bq", [1536], F32, kind="ExternalInput").ap()
        att_c = nc.dram_tensor("att_c", [512, S], BF16, kind="ExternalOutput").ap()
    else:
        tau_d = nc.dram_tensor("tau", [1], F32, kind="ExternalInput").ap()
        nu_d = nc.dram_tensor("nu", [1], F32, kind="ExternalInput").ap()
        att_c = nc.dram_tensor("att_c", [S, 512], BF16, kind="ExternalOutput").ap()
    if debug:
        qf_d = nc.dram_tensor("qf_d", [128, 4 * S], BF16, kind="ExternalOutput").ap()
        kf_d = nc.dram_tensor("kf_d", [128, 4 * S], BF16, kind="ExternalOutput").ap()
        vt_d = nc.dram_tensor("vt_d", [128, 32 * 528], BF16, kind="ExternalOutput").ap()

    ident_c = nc.inline_tensor(np.eye(128, dtype=bfdt), name="c_ident").ap()
    mask_np = (np.arange(128)[:, None] <= np.arange(128)[None, :]).astype(bfdt)
    mask_c = nc.inline_tensor(mask_np, name="c_mask").ap()
    ones_f = nc.inline_tensor(np.ones((1, 128), dtype=np.float32), name="c_onesf").ap()
    ones_b = nc.inline_tensor(np.ones((1, 128), dtype=bfdt), name="c_onesb").ap()

    NTB = 8
    NQB = 8
    NH = 4

    def kts(qb):
        return 4 * qb + 4

    def hq(h, qb):
        return h * NQB + qb

    def exp_base(h, qb):
        return sum(kts(i % NQB) for i in range(hq(h, qb)))

    smm_ct, pv_ct, rb_ct, pvqt_ct = {}, {}, {}, {}
    ctr = 0
    for i in range(NH * NQB):
        nk_i = kts(i % NQB)
        for kt in range(nk_i):
            ctr += 1
            smm_ct[(i, kt)] = ctr
        if layer == 0:
            if i >= 1:
                ctr += 1
                rb_ct[i - 1] = ctr
            ctr += 1
            pv_ct[i] = ctr
        else:
            for qt in range(4):
                ctr += 1
                pvqt_ct[(i, qt)] = ctr
    if layer == 0:
        ctr += 1
        rb_ct[NH * NQB - 1] = ctr
    qkv_ct, tr_ct = {}, {}
    if layer == 1:
        c2 = 0
        c2 += 1; qkv_ct[0] = c2
        for gt in range(1, 32):
            c2 += 1; qkv_ct[gt] = c2
            c2 += 1; tr_ct[gt - 1] = c2
        c2 += 1; tr_ct[31] = c2

    P1_PE_TB = 12 if layer == 0 else 8
    SCTOT = 80 if layer == 0 else 96
    SWTOT = 32 if layer == 0 else 16

    import contextlib
    es = contextlib.ExitStack()
    with es:
        sb = lambda *a: es.enter_context(nc.sbuf_tensor(*a))
        ps = lambda *a: es.enter_context(nc.psum_tensor(*a))
        sem = lambda n: es.enter_context(nc.semaphore(n))
        QF = sb("QF", [128, NH * S], BF16)
        KF = sb("KF", [128, NH * S], BF16)
        VT = sb("VT", [128, 32 * 528], BF16)
        AR = sb("AR", [128, 41984], BF16)
        PT = AR[:, 0:16384]
        lnb = AR[:, 0:16384]
        ws = AR[:, 16384:40960]
        qtok = AR[:, 40960:41984]
        scrq = sb("scrq", [128, 1536], F32)
        ident = sb("ident", [128, 128], BF16)
        mask01 = sb("mask01", [128, 128], BF16)
        ones128 = sb("ones128", [1, 128], F32)
        bqs = sb("bqs", [128, 12], F32)
        bvrow = sb("bvrow", [1, 512], BF16)
        taunu = sb("taunu", [1, 2], F32)
        tnb = sb("tnb", [128, 4], F32)
        csh = sb("csh", [128, 2], F32)
        ssq = sb("ssq", [128, 36], F32)
        rB = sb("rB", [128, 512], F32)
        rr = sb("rr", [1, 512], F32)
        onesb = sb("onesb", [1, 128], BF16)
        attnb = sb("attnb", [128, 512], BF16)
        psA = ps("psA", [128, 4096], F32)
        sL = sem("sL"); sP0 = sem("sP0"); sPE = sem("sPE"); sA1 = sem("sA1")
        sC = sem("sC"); sB0 = sem("sB0"); sB1 = sem("sB1"); sW = sem("sW"); sD = sem("sD")
        sV1 = sem("sV1"); sT2 = sem("sT2"); sX = sem("sX"); sM = sem("sM")
        sV2 = sem("sV2"); sO = sem("sO")
        block = es.enter_context(nc.Block())
        psQK = psA[:, 0:1024]
        psV = psA[:, 1024:2048]
        psQ1 = psA[:, 0:512]
        psK1 = psA[:, 512:1024]
        psV1_ = psA[:, 1024:1536]
        psT1 = psA[:, 1536:2560].bitcast(BF16)
        psS = psA[:, 0:2048]
        psO = psA[:, 2048:2560]
        psR = psA[:, 2560:3072]
        psSum = psA[0:1, 3072:3584]
        psO3 = psA[:, 2048:4096]
        psB0 = psA[:, 3584:4096]

        @block.sync
        def _(e):
            e.dma_start(out=ident[:], in_=ident_c[:, :]).then_inc(sC, 16)
            e.dma_start(out=mask01[:], in_=mask_c[:, :]).then_inc(sC, 16)
            e.dma_start(out=ones128[:], in_=ones_f[:, :]).then_inc(sC, 16)
            e.dma_start(out=onesb[:], in_=ones_b[:, :]).then_inc(sC, 16)
            if layer == 0:
                e.dma_start(out=bqs[:, 0:12],
                            in_=bq.rearrange("(o p) -> p o", p=128),
                            allow_slow_non_contiguous=True).then_inc(sC, 16)
            else:
                e.dma_start(out=taunu[0:1, 0:1], in_=tau_d.unsqueeze(0)).then_inc(sC, 16)
                e.dma_start(out=taunu[0:1, 1:2], in_=nu_d.unsqueeze(0)).then_inc(sC, 16)
            for tb in range(NTB):
                if tb >= 2:
                    e.wait_ge(sPE, (tb - 1) * P1_PE_TB)
                e.dma_start(
                    out=lnb[:, (tb % 2) * NDC * 512:(tb % 2 + 1) * NDC * 512].rearrange("p (c t) -> p c t", t=512),
                    in_=lnT.rearrange("(c p) t -> p c t", p=128)[:, :, tb * 512:(tb + 1) * 512],
                ).then_inc(sB0 if tb % 2 == 0 else sB1, 16)
            nst = 0
            for h in range(NH):
                if not p2:
                    break
                for qb in range(NQB):
                    i = hq(h, qb)
                    if layer == 0:
                        e.wait_ge(sV2, 2 * i + 2)
                        e.dma_start(out=att_c[h * 128:(h + 1) * 128, qb * 512:(qb + 1) * 512],
                                    in_=attnb[:]).then_inc(sO, 16)
                        nst += 16
                    else:
                        for qt in range(4):
                            e.wait_ge(sV2, 4 * i + qt + 1)
                            e.dma_start(
                                out=att_c[qb * 512 + qt * 128:qb * 512 + (qt + 1) * 128,
                                          h * 128:(h + 1) * 128],
                                in_=attnb[:, qt * 128:(qt + 1) * 128]).then_inc(sO, 16)
                            nst += 16
            if debug:
                e.wait_ge(sPE, NTB * P1_PE_TB)
                if layer == 0:
                    e.wait_ge(sA1, 64)
                    e.wait_ge(sV1, 32)
                else:
                    e.wait_ge(sV1, 128)
                e.dma_start(out=qf_d[:, :], in_=QF[:]).then_inc(sO, 16)
                e.dma_start(out=kf_d[:, :], in_=KF[:]).then_inc(sO, 16)
                e.dma_start(out=vt_d[:, :], in_=VT[:]).then_inc(sO, 16)
                nst += 48
            e.wait_ge(sO, nst)

        @block.gpsimd
        def _(g):
            g.dma_start(out=ws[:].rearrange("p (c o) -> p c o", o=1536),
                        in_=w_sl.rearrange("(c p) o -> p c o", p=128)).then_inc(sW, 16)
            if layer == 0:
                g.dma_start(out=bvrow[0:1, :], in_=bq[1024:1536].unsqueeze(0)).then_inc(sW, 16)

        @block.tensor
        def _(t):
            t.wait_ge(sC, SCTOT)
            t.wait_ge(sW, SWTOT)
            if layer == 1:
                if "notau" in DBGV:
                    t.nop().then_inc(sP0, 1)
                else:
                    t.matmul(psB0[:, 0:2], ones128[0:1, :], taunu[0:1, :],
                             start=True, stop=True).then_inc(sP0, 1)
            for tb in range(NTB):
                t.wait_ge(sB0 if tb % 2 == 0 else sB1, 16 * (tb // 2 + 1))
                if layer == 0:
                    lnv = lnb[:, (tb % 2) * NDC * 512:].rearrange("p (c t) -> p c t", t=512)
                    for of in range(8):
                        if 8 * tb + of >= 2:
                            t.wait_ge(sA1, 8 * tb + of - 1)
                        pso = psQK[:, (of % 2) * 512:(of % 2) * 512 + 512]
                        for dc in range(NDC):
                            ins = t.matmul(pso, ws[:, dc * 1536 + of * 128:dc * 1536 + (of + 1) * 128],
                                           lnv[:, dc, :], start=(dc == 0), stop=(dc == NDC - 1))
                        ins.then_inc(sPE, 1)
                    for tt in range(4):
                        if 4 * tb + tt >= 2:
                            t.wait_ge(sV1, 4 * tb + tt - 1)
                        psv = psV[:, (tt % 2) * 512:(tt % 2) * 512 + 512]
                        for dc in range(NDC):
                            t.matmul(psv, lnb[:, (tb % 2) * NDC * 512 + dc * 512 + tt * 128:
                                               (tb % 2) * NDC * 512 + dc * 512 + (tt + 1) * 128],
                                     ws[:, dc * 1536 + 1024:dc * 1536 + 1536],
                                     start=(dc == 0), stop=False)
                        t.matmul(psv, onesb[0:1, :], bvrow[0:1, :],
                                 start=False, stop=True).then_inc(sPE, 1)
                else:
                    for tt in range(4):
                        gt = tb * 4 + tt
                        if gt == 0:
                            t.wait_ge(sP0, 3)
                        if gt >= 1:
                            t.wait_ge(sV1, 4 * (gt - 1) + 3)
                        for pj, pso in enumerate((psQ1, psK1, psV1_)):
                            for dc in range(NDC):
                                lhs = lnb[:, (tb % 2) * NDC * 512 + dc * 512 + tt * 128:
                                             (tb % 2) * NDC * 512 + dc * 512 + (tt + 1) * 128]
                                ins = t.matmul(pso, lhs,
                                               ws[:, dc * 1536 + pj * 512:dc * 1536 + (pj + 1) * 512],
                                               start=(dc == 0), stop=(dc == NDC - 1))
                        ins.then_inc(sPE, 1)
                        t.wait_ge(sV1, 4 * gt + 3)
                        if "notr" in DBGV:
                            t.nop().then_inc(sPE, 1)
                        else:
                            ins = None
                            for j in range(4):
                                ins = t.transpose(psT1[:, j * 128:(j + 1) * 128],
                                                  qtok[:, j * 128:(j + 1) * 128], ident[:])
                            for j in range(4):
                                ins = t.transpose(psT1[:, 512 + j * 128:512 + (j + 1) * 128],
                                                  qtok[:, 512 + j * 128:512 + (j + 1) * 128], ident[:])
                            ins.then_inc(sPE, 1)
            if not p2:
                return
            t.wait_ge(sA1 if layer == 0 else sV1, 64 if layer == 0 else 128)
            t.wait_ge(sV1 if layer == 0 else sV1, 32 if layer == 0 else 128)
            for h in range(NH):
                for qb in range(NQB):
                    i = hq(h, qb)
                    nk = kts(qb)
                    eb = exp_base(h, qb)
                    pb = pe_base(h, qb)
                    for kt in range(nk):
                        if kt >= 4:
                            t.wait_ge(sX, eb + kt - 3)
                        j = kt - 4 * qb
                        n0 = 128 * max(0, j)
                        t.matmul(psS[:, (kt % 4) * 512 + n0:(kt % 4) * 512 + 512],
                                 KF[:, h * S + kt * 128:h * S + (kt + 1) * 128],
                                 QF[:, h * S + qb * 512 + n0:h * S + (qb + 1) * 512],
                                 start=True, stop=True).then_inc(sT2, 1)
                    t.wait_ge(sX, eb + nk)
                    t.wait_ge(sM, 4 * i + 5)
                    if layer == 0:
                        t.wait_ge(sV2, 2 * i)
                        for kt in range(nk):
                            j = kt - 4 * qb
                            n0 = 128 * max(0, j)
                            vtb = VT[:, kt * 528 + h * 132:]
                            t.matmul(psO[:, n0:512], vtb[:, 0:128],
                                     PT[:, kt * 512 + n0:(kt + 1) * 512],
                                     start=(kt == 0), stop=(kt == nk - 1))
                            ins = t.matmul(psSum[:, n0:512], vtb[:, 128:129],
                                            PT[:, kt * 512 + n0:(kt + 1) * 512],
                                            start=(kt == 0), stop=(kt == nk - 1))
                        ins.then_inc(sT2, 1)
                        t.wait_ge(sV2, 2 * i + 1)
                        t.matmul(psR[:], ones128[0:1, :], rr[0:1, :],
                                 start=True, stop=True).then_inc(sT2, 1)
                    else:
                        t.wait_ge(sV2, 4 * i)
                        for qt in range(4):
                            qtg = 4 * qb + qt
                            for kt in range(qtg + 1):
                                vtb = VT[:, kt * 528 + h * 132:]
                                ins = t.matmul(psO3[:, qt * 512:qt * 512 + 129],
                                               PT[:, kt * 512 + qt * 128:kt * 512 + (qt + 1) * 128],
                                               vtb[:, 0:129],
                                               start=(kt == 0), stop=(kt == qtg))
                            ins.then_inc(sT2, 1)

        @block.scalar
        def _(s):
            if layer == 1:
                s.wait_ge(sP0, 1)
                if "notau" in DBGV:
                    s.nop().then_inc(sP0, 1)
                else:
                    s.activation(tnb[:, 3:4], psB0[:, 0:1], AF.Abs).then_inc(sP0, 1)
            if layer == 0:
                for tb in range(NTB):
                    for of in range(8):
                        s.wait_ge(sPE, tb * 12 + of + 1)
                        dst = (QF if of < 4 else KF)[:, (of % 4) * S + tb * 512:(of % 4) * S + (tb + 1) * 512]
                        s.activation(dst, psQK[:, (of % 2) * 512:(of % 2) * 512 + 512],
                                     AF.Identity, bias=bqs[:, of:of + 1], scale=1.0).then_inc(sA1, 1)
            else:
                for tb in range(NTB):
                    for tt in range(4):
                        gt = tb * 4 + tt
                        s.wait_ge(sPE, 2 * gt + 1)
                        if gt >= 1:
                            s.wait_ge(sV1, 4 * (gt - 1) + 1)
                        s.activation(scrq[:, 0:512], psQ1, AF.Square).then_inc(sA1, 1)
                        s.activation(scrq[:, 512:1024], psK1, AF.Square).then_inc(sA1, 1)
                        s.activation(scrq[:, 1024:1536], psV1_, AF.Square).then_inc(sA1, 1)
                        s.wait_ge(sV1, 4 * gt + 1)
                        s.activation(ssq[:, 12:24], ssq[:, 0:12], AF.Sqrt).then_inc(sA1, 1)
            if not p2:
                return
            s.wait_ge(sM, 1)
            if layer == 1:
                s.wait_ge(sP0, 3)
            for h in range(NH):
                for qb in range(NQB):
                    i = hq(h, qb)
                    nk = kts(qb)
                    eb = exp_base(h, qb)
                    pb = pe_base(h, qb)
                    for kt in range(nk):
                        s.wait_ge(sT2, pb + kt + 1)
                        j = kt - 4 * qb
                        n0 = 128 * max(0, j)
                        if layer == 0:
                            s.activation(PT[:, kt * 512 + n0:(kt + 1) * 512],
                                         psS[:, (kt % 4) * 512 + n0:(kt % 4) * 512 + 512],
                                         AF.Exp, bias=csh[:, 0:1],
                                         scale=csh[:, 1:2]).then_inc(sX, 1)
                        else:
                            s.activation(PT[:, kt * 512 + n0:(kt + 1) * 512],
                                         psS[:, (kt % 4) * 512 + n0:(kt % 4) * 512 + 512],
                                         AF.Exp, bias=tnb[:, 1:2],
                                         scale=tnb[:, 0:1]).then_inc(sX, 1)

        @block.vector
        def _(v):
            nd = [0]

            def dedge(ins):
                nd[0] += 1
                ins.then_inc(sD, 1)
                v.wait_ge(sD, nd[0])

            v.wait_ge(sC, SCTOT)
            v.wait_ge(sW, SWTOT)
            v.memset(VT[:], 1.0)
            v.memset(csh[:, 0:1], -C0_SHIFT)
            v.memset(csh[:, 1:2], 0.08838834764831843).then_inc(sM, 1)
            if layer == 1:
                v.wait_ge(sP0, 2)
                if "notau" in DBGV:
                    v.memset(tnb[:, 0:1], 1.0)
                    v.memset(tnb[:, 1:2], -1.0)
                    v.memset(tnb[:, 2:3], 1.0).then_inc(sP0, 1)
                else:
                    v.tensor_copy(tnb[:, 0:1], psB0[:, 0:1])
                    v.nop()
                    v.tensor_scalar(tnb[:, 1:2], tnb[:, 3:4], -1.0, None, ALU.mult)
                    v.tensor_copy(tnb[:, 2:3], psB0[:, 1:2]).then_inc(sP0, 1)
            if layer == 0:
                for tb in range(NTB):
                    for tt in range(4):
                        v.wait_ge(sM, 1)
                        v.wait_ge(sPE, tb * 12 + 8 + tt + 1)
                        gt = tb * 4 + tt
                        dst = VT[:, gt * 528:(gt + 1) * 528].rearrange("p (h d) -> p h d", d=132)[:, :, 0:128]
                        srcv = psV[:, (tt % 2) * 512:(tt % 2) * 512 + 512].rearrange("p (h d) -> p h d", d=128)
                        v.tensor_copy(dst, srcv).then_inc(sV1, 1)
            else:
                for tb in range(NTB):
                    for tt in range(4):
                        gt = tb * 4 + tt
                        v.wait_ge(sM, 1)
                        v.wait_ge(sA1, 4 * gt + 3)
                        for pj in range(3):
                            ins = v.tensor_reduce(ssq[:, pj * 4:(pj + 1) * 4].unsqueeze(2),
                                                  scrq[:, pj * 512:(pj + 1) * 512].rearrange("p (h d) -> p h d", d=128),
                                                  axis=AX.X, op=ALU.add)
                        ins.then_inc(sV1, 1)
                        v.wait_ge(sA1, 4 * gt + 4)
                        v.nop()
                        dedge(v.tensor_scalar(ssq[:, 24:36], ssq[:, 12:24], L2_EPS, None, ALU.add))
                        v.reciprocal(ssq[:, 12:24], ssq[:, 24:36]).then_inc(sV1, 1)
                        v.wait_ge(sV1, 4 * gt + 2)
                        rq = ssq[:, 12:16].unsqueeze(2).broadcast_to([128, 4, 128])
                        rk = ssq[:, 16:20].unsqueeze(2).broadcast_to([128, 4, 128])
                        rv = ssq[:, 20:24].unsqueeze(2).broadcast_to([128, 4, 128])
                        v.tensor_tensor(qtok[:, 0:512].rearrange("p (h d) -> p h d", d=128),
                                        psQ1.rearrange("p (h d) -> p h d", d=128), rq, ALU.mult)
                        v.tensor_tensor(qtok[:, 512:1024].rearrange("p (h d) -> p h d", d=128),
                                        psK1.rearrange("p (h d) -> p h d", d=128), rk, ALU.mult)
                        dstv = VT[:, gt * 528:(gt + 1) * 528].rearrange("p (h d) -> p h d", d=132)[:, :, 0:128]
                        v.tensor_tensor(dstv, psV1_.rearrange("p (h d) -> p h d", d=128), rv, ALU.mult).then_inc(sV1, 1)
                        v.wait_ge(sPE, 2 * gt + 2)
                        if "notr" in DBGV:
                            v.nop().then_inc(sV1, 1)
                        else:
                            v.tensor_copy(QF[:].rearrange("p (h t) -> p h t", t=S)[:, :, gt * 128:(gt + 1) * 128],
                                          psT1[:, 0:512].rearrange("p (h d) -> p h d", d=128))
                            v.tensor_copy(KF[:].rearrange("p (h t) -> p h t", t=S)[:, :, gt * 128:(gt + 1) * 128],
                                          psT1[:, 512:1024].rearrange("p (h d) -> p h d", d=128)).then_inc(sV1, 1)
            if not p2:
                return
            for h in range(NH):
                for qb in range(NQB):
                    i = hq(h, qb)
                    nk = kts(qb)
                    eb = exp_base(h, qb)
                    pb = pe_base(h, qb)
                    for jj in range(4):
                        kt = 4 * qb + jj
                        v.wait_ge(sX, eb + kt + 1)
                        dg = PT[:, kt * 512 + jj * 128:kt * 512 + (jj + 1) * 128]
                        v.tensor_tensor(dg, dg, mask01[:], ALU.mult).then_inc(sM, 1)
                    if layer == 0:
                        v.wait_ge(sT2, pb + nk + 1)
                        v.nop()
                        v.reciprocal(rr[0:1, :], psSum[:]).then_inc(sV2, 1)
                        v.wait_ge(sT2, pb + nk + 2)
                        if i >= 1:
                            v.wait_ge(sO, 16 * i)
                        dedge(v.tensor_copy(rB[:], psR[:]))
                        v.tensor_tensor(attnb[:], psO[:], rB[:], ALU.mult).then_inc(sV2, 1)
                    else:
                        for qt in range(4):
                            v.wait_ge(sT2, pb + nk + qt + 1)
                            v.wait_ge(sV2, 4 * i + qt)
                            if i >= 1:
                                v.wait_ge(sO, 64 * i)
                            dedge(v.reciprocal(tnb[:, 3:4], psO3[:, qt * 512 + 128:qt * 512 + 129]))
                            dedge(v.tensor_tensor(tnb[:, 3:4], tnb[:, 3:4], tnb[:, 2:3], ALU.mult))
                            v.tensor_scalar(attnb[:, qt * 128:(qt + 1) * 128],
                                            psO3[:, qt * 512:qt * 512 + 128],
                                            tnb[:, 3:4], None, ALU.mult).then_inc(sV2, 1)

    return nc




def build_post(layer):
    nc = bass.Bass("TRN2", target_bir_lowering=False)
    x_in = nc.dram_tensor("x", [TOK, D], F32, kind="ExternalInput").ap()
    if layer == 0:
        attnT = nc.dram_tensor("attnT", [D, TOK], BF16, kind="ExternalInput").ap()
        wd_d = nc.dram_tensor("wd", [D, D], F32, kind="ExternalInput").ap()
        bd_d = nc.dram_tensor("bd", [D], F32, kind="ExternalInput").ap()
    else:
        attn_tok = nc.dram_tensor("attn_tok", [TOK, D], BF16, kind="ExternalInput").ap()
        su_d = nc.dram_tensor("su_ug", [D], F32, kind="ExternalInput").ap()
        wsu_d = nc.dram_tensor("wsu", [HF], F32, kind="ExternalInput").ap()
        wsg_d = nc.dram_tensor("wsg", [HF], F32, kind="ExternalInput").ap()
        sdp_d = nc.dram_tensor("sdp", [HF], F32, kind="ExternalInput").ap()
        wdp_d = nc.dram_tensor("wdp", [D], F32, kind="ExternalInput").ap()
    w1_d = nc.dram_tensor("w1t", [D, HF], F32, kind="ExternalInput").ap()
    w3_d = nc.dram_tensor("w3t", [D, HF], F32, kind="ExternalInput").ap()
    w2_d = nc.dram_tensor("w2t", [HF, D], F32, kind="ExternalInput").ap()
    x2_out = nc.dram_tensor("x2", [TOK, D], F32, kind="ExternalOutput").ap()
    x1_dram = nc.dram_tensor("x1buf", [TOK, D], F32, kind="Internal").ap()
    bounce8 = nc.dram_tensor("bounce8", [HF, TOK], BF16, kind="Internal").ap()
    if layer == 1:
        bounce2 = nc.dram_tensor("bounce2", [D, TOK], BF16, kind="Internal").ap()

    ident_c = nc.inline_tensor(np.eye(128, dtype=bfdt), name="c_ident").ap()
    h128_c = nc.inline_tensor(H128_NP.astype(bfdt), name="c_h128").ap()
    bd64_c = nc.inline_tensor(BD64_NP.astype(bfdt), name="c_bd64").ap()
    bd16_c = nc.inline_tensor(BD16_NP.astype(bfdt), name="c_bd16").ap()

    FWN = float(1.0 / np.sqrt(8192.0))
    FWN2 = float(1.0 / np.sqrt(2048.0))
    NHC = 64
    NSB = 16
    VA_TT = 4 if layer == 0 else 1

    import contextlib
    es = contextlib.ExitStack()
    with es:
        sb = lambda *a: es.enter_context(nc.sbuf_tensor(*a))
        psal = lambda *a: es.enter_context(nc.psum_tensor(*a))
        sem = lambda n: es.enter_context(nc.semaphore(n))

        lnF = sb("lnF", [128, NDC * 1024], BF16)
        AR = sb("AR", [128, 65536], BF16)
        WX = sb("WX", [128, 16384], BF16)
        x1cb = sb("x1cb", [128, 2 * 1024], F32)
        ysb = sb("ysb", [128, 1024], BF16)
        x2b = sb("x2b", [128, 2 * 128], F32)
        consts = sb("consts", [128, 512], BF16)
        pcol = sb("pcol", [128, 32], F32)
        pcol2 = sb("pcol2", [128, 192], F32)
        smq = sb("smq", [128, 2 * NT], F32)
        stt = sb("stt", [128, 4 * NT], F32)
        psA = psal("psA", [128, 4096], F32)

        ident = consts[:, 0:128]
        h128k = consts[:, 128:256]
        bd64k = consts[:, 256:384]
        bd16k = consts[:, 384:512]

        atT = AR[:, 0:16384]
        dnsF = AR[:, 16384:32768]
        wdstr = AR[:, 32768:36864]
        at1 = AR[:, 32768:36864]
        a32 = AR[:, 16384:20480].bitcast(F32)
        xinA = AR[:, 36864:45056].bitcast(F32)
        x1tA = AR[:, 45056:53248].bitcast(F32)
        xtB = AR[:, 0:32768].bitcast(F32)
        lntB = AR[:, 32768:34816]
        scrB = AR[:, 53248:57344].bitcast(F32)
        scrB2 = AR[:, 57344:61440].bitcast(F32)
        wstr = AR[:, 0:32768]
        sgb = AR[:, 32768:33792]
        hck = AR[:, 33792:34816]
        y1b = AR[:, 34816:36864]
        c1b = AR[:, 36864:38912]
        h2 = AR[:, 0:65536]
        rcb = WX[:, 0:2048]
        w2str = WX[:, 0:16384]

        psF = psA[:, 0:2048]
        psTd = psA[:, 2048:3072].bitcast(BF16)
        psU = psA[:, 0:1024]
        psG = psA[:, 1024:2048]
        psH = psA[:, 2048:3072]
        psB2 = psA[:, 3072:4096]
        psY = psA[:, 0:1024]

        sC = sem("sC"); sAT = sem("sAT")
        sATa = sem("sATa"); sATb = sem("sATb")
        sXa = sem("sXa"); sXb = sem("sXb"); sXB = sem("sXB")
        sWDa = sem("sWDa"); sWDb = sem("sWDb")
        sW2a = sem("sW2a"); sW2b = sem("sW2b")
        sXCa = sem("sXCa"); sXCb = sem("sXCb")
        sPA = sem("sPA"); sEA = sem("sEA"); sVA = sem("sVA")
        sPB = sem("sPB"); sAB = sem("sAB"); sVB = sem("sVB")
        sP1 = sem("sP1"); sV1b = sem("sV1b")
        sP2 = sem("sP2"); sA3 = sem("sA3"); sV3 = sem("sV3")
        sP5 = sem("sP5"); sV5 = sem("sV5")
        sP6 = sem("sP6"); sA6 = sem("sA6"); sV6 = sem("sV6")
        sX1Oa = sem("sX1Oa"); sX1Ob = sem("sX1Ob"); sB8wa = sem("sB8wa"); sB8wb = sem("sB8wb")
        sB8ra = sem("sB8ra"); sB8rb = sem("sB8rb")
        sB2wa = sem("sB2wa"); sB2wb = sem("sB2wb"); sB2ra = sem("sB2ra"); sB2rb = sem("sB2rb")
        sOa = sem("sOa"); sOb = sem("sOb"); sD = sem("sD")
        block = es.enter_context(nc.Block())

        SCTOT = 80 if layer == 0 else 144
        W1BASE = 8 * 16 if layer == 0 else 0

        @block.sync
        def _(e):
            e.dma_start(out=consts[:, 0:128], in_=ident_c[:, :]).then_inc(sC, 16)
            e.dma_start(out=consts[:, 128:256], in_=h128_c[:, :]).then_inc(sC, 16)
            e.dma_start(out=consts[:, 256:384], in_=bd64_c[:, :]).then_inc(sC, 16)
            e.dma_start(out=consts[:, 384:512], in_=bd16_c[:, :]).then_inc(sC, 16)
            if layer == 0:
                e.dma_start(out=pcol[:, 0:16], in_=bd_d.rearrange("(c p) -> p c", p=128),
                            allow_slow_non_contiguous=True).then_inc(sC, 16)
                e.dma_start(out=atT[:].rearrange("p (c t) -> p c t", t=1024),
                            in_=attnT.rearrange("(c p) t -> p c t", p=128)).then_inc(sAT, 16)
            else:
                e.dma_start(out=pcol[:, 0:16], in_=su_d.rearrange("(c p) -> p c", p=128),
                            allow_slow_non_contiguous=True).then_inc(sC, 16)
                e.dma_start(out=pcol[:, 16:32], in_=wdp_d.rearrange("(c p) -> p c", p=128),
                            allow_slow_non_contiguous=True).then_inc(sC, 16)
                e.dma_start(out=pcol2[:, 0:64], in_=wsu_d.rearrange("(c p) -> p c", p=128),
                            allow_slow_non_contiguous=True).then_inc(sC, 16)
                e.dma_start(out=pcol2[:, 64:128], in_=wsg_d.rearrange("(c p) -> p c", p=128),
                            allow_slow_non_contiguous=True).then_inc(sC, 16)
                e.dma_start(out=pcol2[:, 128:192], in_=sdp_d.rearrange("(c p) -> p c", p=128),
                            allow_slow_non_contiguous=True).then_inc(sC, 16)
            def ld_x(tt):
                if tt >= 2:
                    e.wait_ge(sVA, VA_TT * (tt - 1))
                e.dma_start(out=xinA[:, (tt % 2) * D:(tt % 2 + 1) * D],
                            in_=x_in[tt * 128:(tt + 1) * 128, :]).then_inc(
                                sXa if tt % 2 == 0 else sXb, 16)
                if layer == 1:
                    e.dma_start(out=at1[:, (tt % 2) * 2048:(tt % 2 + 1) * 2048],
                                in_=attn_tok[tt * 128:(tt + 1) * 128, :]).then_inc(
                                    sATa if tt % 2 == 0 else sATb, 16)

            ld_x(0)
            ld_x(1)
            for tt in range(NT):
                if tt + 2 < NT:
                    ld_x(tt + 2)
                e.wait_ge(sVA, VA_TT * (tt + 1))
                e.dma_start(out=x1_dram[tt * 128:(tt + 1) * 128, :],
                            in_=x1tA[:, (tt % 2) * D:(tt % 2 + 1) * D]).then_inc(
                                sX1Oa if tt % 2 == 0 else sX1Ob, 16)
            e.wait_ge(sX1Oa, 16 * (NT // 2))
            e.wait_ge(sX1Ob, 16 * (NT // 2))
            e.dma_start(out=xtB[:].rearrange("p (tt d) -> p tt d", d=D),
                        in_=x1_dram.rearrange("(tt p) d -> p tt d", p=128)).then_inc(sXB, 16)
            if layer == 1:
                for fc in range(NDC):
                    e.wait_ge(sV1b, fc + 1)
                    e.dma_start(out=bounce2[fc * 128:(fc + 1) * 128, :],
                                in_=c1b[:, (fc % 2) * 1024:(fc % 2 + 1) * 1024]).then_inc(
                                    sB2wa if fc % 2 == 0 else sB2wb, 16)
                b2v = bounce2.rearrange("(p c) t -> p c t", c=16)
                for c in range(NDC):
                    e.wait_ge(sB2wa, 16 * (NDC // 2))
                    e.wait_ge(sB2wb, 16 * (NDC // 2))
                    if c >= 2:
                        e.wait_ge(sP1, 16 + c - 1)
                    e.dma_start(out=rcb[:, (c % 2) * 1024:(c % 2 + 1) * 1024],
                                in_=b2v[:, c, :]).then_inc(sB2ra if c % 2 == 0 else sB2rb, 16)
            for hc in range(NHC):
                e.wait_ge(sV3, 2 * hc + 2)
                e.dma_start(out=bounce8[hc * 128:(hc + 1) * 128, :],
                            in_=y1b[:, (hc % 2) * 1024:(hc % 2 + 1) * 1024]).then_inc(
                                sB8wa if hc % 2 == 0 else sB8wb, 16)
            b8v = bounce8.rearrange("(p c) t -> p c t", c=64)
            for c in range(NHC):
                e.wait_ge(sB8wa, 16 * (NHC // 2))
                e.wait_ge(sB8wb, 16 * (NHC // 2))
                if c >= 2:
                    e.wait_ge(sP5, c - 1)
                e.dma_start(out=rcb[:, (c % 2) * 1024:(c % 2 + 1) * 1024],
                            in_=b8v[:, c, :]).then_inc(sB8ra if c % 2 == 0 else sB8rb, 16)
            def ld_x1cb(doc):
                if doc >= 2:
                    e.wait_ge(sV6, 8 * (doc - 1))
                e.dma_start(out=x1cb[:, (doc % 2) * 1024:(doc % 2 + 1) * 1024].rearrange("p (tt c) -> p tt c", c=128),
                            in_=x1_dram.rearrange("(tt p) d -> p tt d", p=128)[:, :, doc * 128:(doc + 1) * 128],
                            ).then_inc(sXCa if doc % 2 == 0 else sXCb, 16)

            ld_x1cb(0)
            ld_x1cb(1)
            for doc in range(16):
                for tt in range(NT):
                    ti = doc * 8 + tt
                    e.wait_ge(sV6, ti + 1)
                    e.dma_start(out=x2_out[tt * 128:(tt + 1) * 128, doc * 128:(doc + 1) * 128],
                                in_=x2b[:, (ti % 2) * 128:(ti % 2 + 1) * 128]).then_inc(
                                    sOa if ti % 2 == 0 else sOb, 16)
                if doc + 2 < 16:
                    ld_x1cb(doc + 2)
            e.wait_ge(sOa, 16 * 64)
            e.wait_ge(sOb, 16 * 64)

        @block.gpsimd
        def _(g):
            if layer == 0:
                wdv = wd_d.rearrange("(c p) f -> p c f", p=128)
                for fc in range(NDC):
                    if fc >= 2:
                        g.wait_ge(sPA, fc - 1)
                    g.dma_start(out=wdstr[:, (fc % 2) * 2048:(fc % 2 + 1) * 2048].rearrange("p (c f) -> p c f", f=128),
                                in_=wdv[:, :, fc * 128:(fc + 1) * 128]).then_inc(
                                    sWDa if fc % 2 == 0 else sWDb, 16)
            if layer == 0:
                w1v = w1_d.rearrange("(c p) h -> p c h", p=128)
                w3v = w3_d.rearrange("(c p) h -> p c h", p=128)
            else:
                w1v = w1_d.rearrange("(p c) h -> p c h", c=16)
                w3v = w3_d.rearrange("(p c) h -> p c h", c=16)
            for sbk in range(NSB):
                if sbk >= 2:
                    g.wait_ge(sP2, 2 * 4 * (sbk - 1))
                else:
                    g.wait_ge(sAB, 17)
                    g.wait_ge(sVB, 34)
                base = (sbk % 2) * 16384
                g.dma_start(out=wstr[:, base:base + 8192].rearrange("p (c h) -> p c h", h=512),
                            in_=w1v[:, :, sbk * 512:(sbk + 1) * 512]).then_inc(
                                sWDa if sbk % 2 == 0 else sWDb, 16)
                g.dma_start(out=wstr[:, base + 8192:base + 16384].rearrange("p (c h) -> p c h", h=512),
                            in_=w3v[:, :, sbk * 512:(sbk + 1) * 512]).then_inc(
                                sWDa if sbk % 2 == 0 else sWDb, 16)
            w2v = w2_d.rearrange("(p c) dd -> p c dd", c=64)
            for doc in range(16):
                if doc >= 2:
                    g.wait_ge(sP6, 9 * (doc - 1) + 1)
                else:
                    g.wait_ge(sP5, 64)
                g.dma_start(out=w2str[:, (doc % 2) * 8192:(doc % 2 + 1) * 8192].rearrange("p (c dd) -> p c dd", dd=128),
                            in_=w2v[:, :, doc * 128:(doc + 1) * 128]).then_inc(
                                sW2a if doc % 2 == 0 else sW2b, 16)

        @block.tensor
        def _(t):
            t.wait_ge(sC, SCTOT)
            if layer == 0:
                t.wait_ge(sAT, 16)
                atv = atT.rearrange("p (c t) -> p c t", t=1024)
                for fc in range(NDC):
                    t.wait_ge(sWDa if fc % 2 == 0 else sWDb, 16 * (fc // 2 + 1))
                    if fc >= 2:
                        t.wait_ge(sEA, fc - 1)
                    half = (fc % 2) * 1024
                    ins = None
                    for tc in range(2):
                        for dc in range(NDC):
                            ins = t.matmul(psF[:, half + tc * 512:half + (tc + 1) * 512],
                                           wdstr[:, (fc % 2) * 2048 + dc * 128:(fc % 2) * 2048 + (dc + 1) * 128],
                                           atv[:, dc, tc * 512:(tc + 1) * 512],
                                           start=(dc == 0), stop=(dc == NDC - 1))
                    ins.then_inc(sPA, 1)
                t.wait_ge(sEA, 16)
                for tt in range(NT):
                    for grp in range(4):
                        gi = tt * 4 + grp
                        if gi >= 2:
                            t.wait_ge(sVA, 0 if gi < 2 else (gi - 1))
                        ins = None
                        for j in range(4):
                            fc = grp * 4 + j
                            ins = t.transpose(psTd[:, (gi % 2) * 1024 + j * 128:(gi % 2) * 1024 + (j + 1) * 128],
                                              dnsF[:, fc * 1024 + tt * 128:fc * 1024 + (tt + 1) * 128],
                                              ident)
                        ins.then_inc(sPA, 1)
            for tt in range(NT):
                t.wait_ge(sAB, 10 + tt)
                for grp in range(4):
                    gi = tt * 4 + grp
                    if gi >= 2:
                        t.wait_ge(sVB, 2 + gi - 1)
                    ins = None
                    for j in range(4):
                        fc = grp * 4 + j
                        ins = t.transpose(psTd[:, (gi % 2) * 1024 + j * 128:(gi % 2) * 1024 + (j + 1) * 128],
                                          lntB[:, fc * 128:(fc + 1) * 128],
                                          ident)
                    ins.then_inc(sPB, 1)
            if layer == 1:
                t.wait_ge(sVB, 34)
                for fc in range(NDC):
                    if fc >= 1:
                        t.wait_ge(sV1b, fc)
                    ins = None
                    for tc in range(2):
                        ins = t.matmul(psB2[:, tc * 512:(tc + 1) * 512], bd16k,
                                       lnF[:, fc * 1024 + tc * 512:fc * 1024 + (tc + 1) * 512],
                                       start=True, stop=True)
                    ins.then_inc(sP1, 1)
                for c in range(NDC):
                    t.wait_ge(sB2ra if c % 2 == 0 else sB2rb, 16 * (c // 2 + 1))
                    t.wait_ge(sV1b, 16 + c)
                    ins = None
                    for tc in range(2):
                        ins = t.matmul(psB2[:, tc * 512:(tc + 1) * 512], h128k,
                                       rcb[:, (c % 2) * 1024 + tc * 512:(c % 2) * 1024 + (tc + 1) * 512],
                                       start=True, stop=True)
                    ins.then_inc(sP1, 1)
            else:
                t.wait_ge(sVB, 34)
            if layer == 1:
                t.wait_ge(sV1b, 32)
            for sbk in range(NSB):
                t.wait_ge(sWDa if sbk % 2 == 0 else sWDb,
                          W1BASE + 32 * (sbk // 2 + 1))
                for hcl in range(4):
                    hc = sbk * 4 + hcl
                    wb = (sbk % 2) * 16384
                    if hc >= 1:
                        t.wait_ge(sV3, 2 * (hc - 1) + 1)
                        t.wait_ge(sA3, hc)
                    ins = None
                    for cc in range(NDC):
                        for tc in range(2):
                            t.matmul(psU[:, tc * 512:(tc + 1) * 512],
                                     wstr[:, wb + cc * 512 + hcl * 128:wb + cc * 512 + (hcl + 1) * 128],
                                     lnF[:, cc * 1024 + tc * 512:cc * 1024 + (tc + 1) * 512],
                                     start=(cc == 0), stop=(cc == NDC - 1))
                            ins = t.matmul(psG[:, tc * 512:(tc + 1) * 512],
                                           wstr[:, wb + 8192 + cc * 512 + hcl * 128:wb + 8192 + cc * 512 + (hcl + 1) * 128],
                                           lnF[:, cc * 1024 + tc * 512:cc * 1024 + (tc + 1) * 512],
                                           start=(cc == 0), stop=(cc == NDC - 1))
                    ins.then_inc(sP2, 1)
                    t.wait_ge(sV3, 2 * hc + 1)
                    if hc >= 1:
                        t.wait_ge(sV3, 2 * (hc - 1) + 2)
                    ins = None
                    for tc in range(2):
                        ins = t.matmul(psH[:, tc * 512:(tc + 1) * 512], bd64k,
                                       hck[:, tc * 512:(tc + 1) * 512],
                                       start=True, stop=True)
                    ins.then_inc(sP2, 1)
            for c in range(NHC):
                t.wait_ge(sB8ra if c % 2 == 0 else sB8rb, 16 * (c // 2 + 1))
                if c >= 1:
                    t.wait_ge(sV5, c)
                ins = None
                for tc in range(2):
                    ins = t.matmul(psB2[:, tc * 512:(tc + 1) * 512], h128k,
                                   rcb[:, (c % 2) * 1024 + tc * 512:(c % 2) * 1024 + (tc + 1) * 512],
                                   start=True, stop=True)
                ins.then_inc(sP5, 1)
            t.wait_ge(sV5, 64)
            for doc in range(16):
                t.wait_ge(sW2a if doc % 2 == 0 else sW2b, 16 * (doc // 2 + 1))
                if doc >= 1:
                    t.wait_ge(sA6, doc)
                ins = None
                for c in range(NHC):
                    for tc in range(2):
                        ins = t.matmul(psY[:, tc * 512:(tc + 1) * 512],
                                       w2str[:, (doc % 2) * 8192 + c * 128:(doc % 2) * 8192 + (c + 1) * 128],
                                       h2[:, c * 1024 + tc * 512:c * 1024 + (tc + 1) * 512],
                                       start=(c == 0), stop=(c == NHC - 1))
                ins.then_inc(sP6, 1)
                t.wait_ge(sA6, doc + 1)
                for tt in range(NT):
                    ti = doc * 8 + tt
                    if ti >= 2:
                        t.wait_ge(sV6, ti - 1)
                    t.transpose(psTd[:, (ti % 2) * 1024:(ti % 2) * 1024 + 128],
                                ysb[:, tt * 128:(tt + 1) * 128],
                                ident).then_inc(sP6, 1)

        @block.scalar
        def _(s):
            s.wait_ge(sC, SCTOT)
            if layer == 0:
                for fc in range(NDC):
                    s.wait_ge(sPA, fc + 1)
                    s.activation(dnsF[:, fc * 1024:(fc + 1) * 1024],
                                 psF[:, (fc % 2) * 1024:(fc % 2 + 1) * 1024],
                                 AF.Identity, bias=pcol[:, fc:fc + 1],
                                 scale=1.0).then_inc(sEA, 1)
            s.wait_ge(sXB, 16)
            for tt in range(NT):
                if tt >= 1:
                    s.wait_ge(sAB, tt)
                s.activation(scrB[:] if tt % 2 == 0 else scrB2[:],
                             xtB[:, tt * D:(tt + 1) * D], AF.Square,
                             accum_out=smq[:, NT + tt:NT + tt + 1]).then_inc(sAB, 1)
            s.wait_ge(sVB, 1)
            s.activation(stt[:, NT:2 * NT], stt[:, NT:2 * NT], AF.Sqrt).then_inc(sAB, 1)
            s.wait_ge(sVB, 2)
            for tt in range(NT):
                if tt >= 1:
                    s.wait_ge(sPB, 4 * tt)
                s.activation(lntB[:], xtB[:, tt * D:(tt + 1) * D], AF.Identity,
                             bias=stt[:, 3 * NT + tt:3 * NT + tt + 1],
                             scale=stt[:, 2 * NT + tt:2 * NT + tt + 1]).then_inc(sAB, 1)
            for hc in range(NHC):
                s.wait_ge(sP2, 2 * hc + 1)
                if hc >= 1:
                    s.wait_ge(sV3, 2 * (hc - 1) + 1)
                if layer == 0:
                    s.activation(sgb[:], psU[:], AF.Silu).then_inc(sA3, 1)
                else:
                    s.activation(sgb[:], psG[:], AF.Silu,
                                 scale=pcol2[:, 64 + hc:64 + hc + 1]).then_inc(sA3, 1)
            for doc in range(16):
                s.wait_ge(sP6, 9 * doc + 1)
                if layer == 0:
                    s.activation(ysb[:], psY[:], AF.Identity).then_inc(sA6, 1)
                else:
                    s.activation(ysb[:], psY[:], AF.Identity,
                                 scale=pcol[:, 16 + doc:16 + doc + 1]).then_inc(sA6, 1)

        @block.vector
        def _(v):
            nd = [0]

            def dedge(ins):
                nd[0] += 1
                ins.then_inc(sD, 1)
                v.wait_ge(sD, nd[0])

            v.wait_ge(sC, SCTOT)
            if layer == 1:
                dedge(v.tensor_tensor(pcol2[:, 0:64], pcol2[:, 0:64], pcol2[:, 128:192], ALU.mult))
            if layer == 0:
                for tt in range(NT):
                    v.wait_ge(sXa if tt % 2 == 0 else sXb, 16 * (tt // 2 + 1))
                    if tt >= 2:
                        v.wait_ge(sX1Oa if tt % 2 == 0 else sX1Ob, 16 * (tt // 2))
                    for grp in range(4):
                        gi = tt * 4 + grp
                        v.wait_ge(sPA, 16 + gi + 1)
                        v.tensor_tensor(x1tA[:, (tt % 2) * D + grp * 512:(tt % 2) * D + (grp + 1) * 512],
                                        psTd[:, (gi % 2) * 1024:(gi % 2) * 1024 + 512],
                                        xinA[:, (tt % 2) * D + grp * 512:(tt % 2) * D + (grp + 1) * 512],
                                        ALU.add).then_inc(sVA, 1)
            else:
                for tt in range(NT):
                    v.wait_ge(sXa if tt % 2 == 0 else sXb, 16 * (tt // 2 + 1))
                    v.wait_ge(sATa if tt % 2 == 0 else sATb, 16 * (tt // 2 + 1))
                    if tt >= 2:
                        v.wait_ge(sX1Oa if tt % 2 == 0 else sX1Ob, 16 * (tt // 2))
                    if tt >= 1:
                        v.wait_ge(sVA, tt)
                    dedge(v.tensor_copy(a32[:, 0:2048], at1[:, (tt % 2) * 2048:(tt % 2 + 1) * 2048]))
                    v.tensor_tensor(x1tA[:, (tt % 2) * D:(tt % 2 + 1) * D],
                                    a32[:, 0:2048],
                                    xinA[:, (tt % 2) * D:(tt % 2 + 1) * D],
                                    ALU.add).then_inc(sVA, 1)
            n = NT
            mean, veps, rstd, nmrs = (stt[:, 0:n], stt[:, n:2 * n],
                                      stt[:, 2 * n:3 * n], stt[:, 3 * n:4 * n])
            v.wait_ge(sXB, 16)
            for tt in range(NT):
                dedge(v.reduce_sum(smq[:, tt:tt + 1], xtB[:, tt * D:(tt + 1) * D], axis=AX.X))
            v.wait_ge(sAB, NT)
            dedge(v.tensor_scalar(mean, smq[:, 0:NT], 1.0 / D, None, ALU.mult))
            dedge(v.tensor_scalar(veps, smq[:, NT:2 * NT], 1.0 / D, LN_EPS, ALU.mult, ALU.add))
            dedge(v.tensor_tensor(rstd, mean, mean, ALU.mult))
            v.tensor_sub(veps, veps, rstd).then_inc(sVB, 1)
            v.wait_ge(sAB, NT + 1)
            dedge(v.reciprocal(rstd, veps))
            dedge(v.tensor_tensor(nmrs, mean, rstd, ALU.mult))
            v.tensor_scalar(nmrs, nmrs, -1.0, None, ALU.mult).then_inc(sVB, 1)
            for tt in range(NT):
                for grp in range(4):
                    gi = tt * 4 + grp
                    v.wait_ge(sPB, gi + 1)
                    dst3 = lnF[:].rearrange("p (fc t) -> p fc t", t=1024)[
                        :, grp * 4:(grp + 1) * 4, tt * 128:(tt + 1) * 128]
                    src3 = psTd[:, (gi % 2) * 1024:(gi % 2) * 1024 + 512].rearrange(
                        "p (j t) -> p j t", t=128)
                    if layer == 0:
                        v.tensor_copy(dst3, src3).then_inc(sVB, 1)
                    else:
                        sucol = pcol[:, grp * 4:(grp + 1) * 4].unsqueeze(2).broadcast_to([128, 4, 128])
                        v.tensor_tensor(dst3, src3, sucol, ALU.mult).then_inc(sVB, 1)
            if layer == 1:
                for fc in range(NDC):
                    v.wait_ge(sP1, fc + 1)
                    if fc >= 2:
                        v.wait_ge(sB2wa if fc % 2 == 0 else sB2wb, 16 * (fc // 2))
                    v.tensor_copy(c1b[:, (fc % 2) * 1024:(fc % 2 + 1) * 1024],
                                  psB2[:]).then_inc(sV1b, 1)
                for c in range(NDC):
                    v.wait_ge(sP1, 16 + c + 1)
                    v.tensor_scalar(lnF[:, c * 1024:(c + 1) * 1024], psB2[:],
                                    FWN2, None, ALU.mult).then_inc(sV1b, 1)
            for hc in range(NHC):
                v.wait_ge(sA3, hc + 1)
                if hc >= 1:
                    v.wait_ge(sP2, 2 * (hc - 1) + 2)
                if layer == 0:
                    v.tensor_tensor(hck[:], psG[:], sgb[:], ALU.mult).then_inc(sV3, 1)
                else:
                    v.scalar_tensor_tensor(hck[:], psU[:], pcol2[:, hc:hc + 1], sgb[:],
                                           ALU.mult, ALU.mult).then_inc(sV3, 1)
                v.wait_ge(sP2, 2 * hc + 2)
                if hc >= 2:
                    v.wait_ge(sB8wa if hc % 2 == 0 else sB8wb, 16 * (hc // 2))
                v.tensor_copy(y1b[:, (hc % 2) * 1024:(hc % 2 + 1) * 1024],
                              psH[:]).then_inc(sV3, 1)
            for c in range(NHC):
                v.wait_ge(sP5, c + 1)
                v.tensor_scalar(h2[:, c * 1024:(c + 1) * 1024], psB2[:],
                                FWN, None, ALU.mult).then_inc(sV5, 1)
            for doc in range(16):
                v.wait_ge(sXCa if doc % 2 == 0 else sXCb, 16 * (doc // 2 + 1))
                for tt in range(NT):
                    ti = doc * 8 + tt
                    v.wait_ge(sP6, 9 * doc + 1 + tt + 1)
                    if ti >= 2:
                        v.wait_ge(sOa if ti % 2 == 0 else sOb, 16 * (ti // 2))
                    v.tensor_tensor(x2b[:, (ti % 2) * 128:(ti % 2 + 1) * 128],
                                    psTd[:, (ti % 2) * 1024:(ti % 2) * 1024 + 128],
                                    x1cb[:, (doc % 2) * 1024 + tt * 128:(doc % 2) * 1024 + (tt + 1) * 128],
                                    ALU.add).then_inc(sV6, 1)

    return nc



class _NS:
    def __init__(self, **kw):
        self.__dict__.update(kw)


def emit_pre_f(C):
    sL, sL2, sV, sA, sT, sE, sO = (C.sems[k] for k in "sL sL2 sV sA sT sE sO".split())
    B, io = C.B, C.io
    n = NT

    def f_sync(e):
        xv = io.x.rearrange("(tt p) d -> p tt d", p=128)
        xo = B.xt[:].rearrange("p (tt d) -> p tt d", d=D)
        e.dma_start(out=xo[:, 0:4, :], in_=xv[:, 0:4, :]).then_inc(sL, 16)
        e.dma_start(out=xo[:, 4:8, :], in_=xv[:, 4:8, :]).then_inc(sL2, 16)
        for tt in range(NT):
            e.wait_ge(sE, 4 * tt + 4)
            e.dma_start(
                out=io.lnT[tt // 2].rearrange("(c p) t -> p c t", p=128)[:, :, (tt % 2) * 128:(tt % 2 + 1) * 128],
                in_=B.lt[:, (tt % 2) * D:(tt % 2 + 1) * D].rearrange("p (c t) -> p c t", t=128),
            ).then_inc(sO, 16)
        e.wait_ge(sO, 16 * NT)

    def f_scalar(s):
        s.wait_ge(sL, 16)
        for tt in range(NT):
            if tt == 4:
                s.wait_ge(sL2, 16)
            if tt >= 1:
                s.wait_ge(sA, tt)
            s.activation(B.scr[:], B.xt[:, tt * D:(tt + 1) * D], AF.Square,
                         accum_out=B.sq[:, tt:tt + 1]).then_inc(sA, 1)
        s.wait_ge(sV, 1)
        s.activation(B.st[:, n:2 * n], B.st[:, n:2 * n], AF.Sqrt).then_inc(sA, 1)
        s.wait_ge(sV, 2)
        for tt in range(NT):
            if tt >= 2:
                s.wait_ge(sT, 4 * (tt - 1))
            s.activation(B.ln[:, (tt % 2) * D:(tt % 2 + 1) * D],
                         B.xt[:, tt * D:(tt + 1) * D], AF.Identity,
                         bias=B.st[:, 3 * n + tt:3 * n + tt + 1],
                         scale=B.st[:, 2 * n + tt:2 * n + tt + 1]).then_inc(sA, 1)

    def f_vector(v):
        nd = [0]
        sD = C.sems["sD"]

        def dedge(ins):
            nd[0] += 1
            ins.then_inc(sD, 1)
            v.wait_ge(sD, nd[0])

        mean, veps, rstd, nmrs = (B.st[:, 0:n], B.st[:, n:2 * n],
                                  B.st[:, 2 * n:3 * n], B.st[:, 3 * n:4 * n])
        v.wait_ge(sL, 16)
        for tt in range(NT):
            if tt == 4:
                v.wait_ge(sL2, 16)
            dedge(v.reduce_sum(B.sm[:, tt:tt + 1], B.xt[:, tt * D:(tt + 1) * D], axis=AX.X))
        v.wait_ge(sA, NT)
        dedge(v.tensor_scalar(mean, B.sm[:], 1.0 / D, None, ALU.mult))
        dedge(v.tensor_scalar(veps, B.sq[:], 1.0 / D, LN_EPS, ALU.mult, ALU.add))
        dedge(v.tensor_tensor(rstd, mean, mean, ALU.mult))
        v.tensor_sub(veps, veps, rstd).then_inc(sV, 1)
        v.wait_ge(sA, NT + 1)
        dedge(v.reciprocal(rstd, veps))
        dedge(v.tensor_tensor(nmrs, mean, rstd, ALU.mult))
        v.tensor_scalar(nmrs, nmrs, -1.0, None, ALU.mult).then_inc(sV, 1)
        for tt in range(NT):
            for grp in range(4):
                gi = tt * 4 + grp
                v.wait_ge(sT, gi + 1)
                if tt >= 2:
                    v.wait_ge(sO, 16 * tt)
                v.tensor_copy(B.lt[:, (tt % 2) * D + grp * 512:(tt % 2) * D + (grp + 1) * 512],
                              B.psT[:, (gi % 2) * 1024:(gi % 2) * 1024 + 512]).then_inc(sE, 1)

    def f_tensor(t):
        t.wait_ge(sL, 16)
        t.wait_ge(sL2, 16)
        for tt in range(NT):
            t.wait_ge(sA, NT + 2 + tt)
            for grp in range(4):
                gi = tt * 4 + grp
                if gi >= 2:
                    t.wait_ge(sE, gi - 1)
                ins = None
                for j in range(4):
                    fc = grp * 4 + j
                    ins = t.transpose(B.psT[:, (gi % 2) * 1024 + j * 128:(gi % 2) * 1024 + (j + 1) * 128],
                                      B.ln[:, (tt % 2) * D + fc * 128:(tt % 2) * D + (fc + 1) * 128],
                                      C.cst.ident)
                ins.then_inc(sT, 1)

    def f_gpsimd(g):
        pf = getattr(C, "prefetch", None)
        if pf is not None:
            wsb, wsl, wsem = pf
            wv = wsl.rearrange("(c p) o -> p c o", p=128)
            for q in range(4):
                g.dma_start(out=wsb[:, q * 6144:(q + 1) * 6144].rearrange("p (c o) -> p c o", o=1536),
                            in_=wv[:, q * 4:(q + 1) * 4, :]).then_inc(wsem, 16)

    C.bodies = {"sync": f_sync, "scalar": f_scalar, "vector": f_vector,
                "tensor": f_tensor, "gpsimd": f_gpsimd}


def emit_att_f(C, layer):
    sm = C.sems
    sL, sW = sm["sL"], sm["sW"]
    sP0, sPE, sA1, sV1 = sm["sP0"], sm["sPE"], sm["sA1"], sm["sV1"]
    sT2, sX, sM, sV2, sO = sm["sT2"], sm["sX"], sm["sM"], sm["sV2"], sm["sO"]
    sB0, sB1 = sm["sB0"], sm["sB1"]
    B, io, cst = C.B, C.io, C.cst
    NTB, NQB, NH = 8, 8, 4

    def kts(qb):
        return 4 * qb + 4

    def hq(h, qb):
        return h * NQB + qb

    def exp_base(h, qb):
        return sum(kts(i % NQB) for i in range(hq(h, qb)))

    smm_ct, pv_ct, rb_ct, pvqt_ct = {}, {}, {}, {}
    ctr = 0
    for i in range(NH * NQB):
        nk_i = kts(i % NQB)
        for kt in range(nk_i):
            ctr += 1
            smm_ct[(i, kt)] = ctr
        if layer == 0:
            if i >= 1:
                ctr += 1
                rb_ct[i - 1] = ctr
            ctr += 1
            pv_ct[i] = ctr
        else:
            for qt in range(4):
                ctr += 1
                pvqt_ct[(i, qt)] = ctr
    if layer == 0:
        ctr += 1
        rb_ct[NH * NQB - 1] = ctr
    qkv_ct, tr_ct = {}, {}
    if layer == 1:
        c2 = 0
        c2 += 1; qkv_ct[0] = c2
        for gt in range(1, 32):
            c2 += 1; qkv_ct[gt] = c2
            c2 += 1; tr_ct[gt - 1] = c2
        c2 += 1; tr_ct[31] = c2

    P1_PE_TB = 12 if layer == 0 else 8
    SCTOT = 16 if layer == 0 else 32
    SWTOT = 80 if layer == 0 else 64


    def f_sync(e):
        if layer == 0:
            e.dma_start(out=B.bqs[:, 0:12],
                        in_=io.bq.rearrange("(o p) -> p o", p=128),
                        allow_slow_non_contiguous=True).then_inc(sL, 16)
        else:
            e.dma_start(out=B.taunu[0:1, 0:1], in_=io.tau.unsqueeze(0)).then_inc(sL, 16)
            e.dma_start(out=B.taunu[0:1, 1:2], in_=io.nu.unsqueeze(0)).then_inc(sL, 16)
        for tb in range(NTB):
            if tb >= 2:
                if layer == 0:
                    e.wait_ge(sPE, (tb - 1) * P1_PE_TB)
                else:
                    e.wait_ge(sPE, qkv_ct[4 * (tb - 1)])
            for j in range(2):
                q = 2 * (tb % 2) + j
                e.wait_ge(C.ccsem, q + 1)
                e.dma_start(
                    out=B.lnb[:, (tb % 2) * NDC * 512:(tb % 2 + 1) * NDC * 512].rearrange("p (c t) -> p c t", t=512)[:, :, j * 256:(j + 1) * 256],
                    in_=io.lnTg[q].rearrange("(r cp) t -> r cp t", r=4)[tb // 2].rearrange("(c p) t -> p c t", p=128),
                ).then_inc(sB0 if tb % 2 == 0 else sB1, 16)
        nst = 0
        for h in range(NH):
            for qb in range(NQB):
                i = hq(h, qb)
                if layer == 0:
                    e.wait_ge(sV2, 2 * i + 2)
                    e.dma_start(out=io.att_c[h * 128:(h + 1) * 128, qb * 512:(qb + 1) * 512],
                                in_=B.attnb[:]).then_inc(sO, 16)
                    nst += 16
                else:
                    for qt in range(4):
                        e.wait_ge(sV2, 4 * i + qt + 1)
                        e.dma_start(
                            out=io.att_c[h, qb * 512 + qt * 128:qb * 512 + (qt + 1) * 128, :],
                            in_=B.attnb[:, qt * 128:(qt + 1) * 128]).then_inc(sO, 16)
                        nst += 16
        e.wait_ge(sO, nst)

    def f_gpsimd(g):
        if layer == 0:
            g.dma_start(out=B.bvrow[0:1, :], in_=io.bq[1024:1536].unsqueeze(0)).then_inc(sW, 16)

    def f_tensor(t):
        t.wait_ge(sL, SCTOT)
        t.wait_ge(sW, SWTOT)
        if layer == 1:
            t.matmul(B.psB0[:, 0:2], C.cst.ones128[0:1, :], B.taunu[0:1, :],
                     start=True, stop=True).then_inc(sP0, 1)
        for tb in range(NTB if layer == 0 else 0):
            t.wait_ge(sB0 if tb % 2 == 0 else sB1, 32 * (tb // 2 + 1))
            if layer == 0:
                lnv = B.lnb[:, (tb % 2) * NDC * 512:].rearrange("p (c t) -> p c t", t=512)
                for of in range(8):
                    if 8 * tb + of >= 2:
                        t.wait_ge(sA1, 8 * tb + of - 1)
                    pso = B.psQK[:, (of % 2) * 512:(of % 2) * 512 + 512]
                    ins = None
                    for dc in range(NDC):
                        ins = t.matmul(pso, B.ws[:, dc * 1536 + of * 128:dc * 1536 + (of + 1) * 128],
                                       lnv[:, dc, :], start=(dc == 0), stop=(dc == NDC - 1))
                    ins.then_inc(sPE, 1)
                for tt in range(4):
                    if 4 * tb + tt >= 2:
                        t.wait_ge(sV1, 4 * tb + tt - 1)
                    psv = B.psV[:, (tt % 2) * 512:(tt % 2) * 512 + 512]
                    for dc in range(NDC):
                        t.matmul(psv, B.lnb[:, (tb % 2) * NDC * 512 + dc * 512 + tt * 128:
                                            (tb % 2) * NDC * 512 + dc * 512 + (tt + 1) * 128],
                                 B.ws[:, dc * 1536 + 1024:dc * 1536 + 1536],
                                 start=(dc == 0), stop=False)
                    t.matmul(psv, C.cst.onesb[0:1, :], B.bvrow[0:1, :],
                             start=False, stop=True).then_inc(sPE, 1)
            else:
                pass
        if layer == 1:
            def qkv1(gt):
                tb, tt = gt // 4, gt % 4
                if tt == 0:
                    t.wait_ge(sB0 if tb % 2 == 0 else sB1, 32 * (tb // 2 + 1))
                p = gt % 2
                if gt == 0:
                    t.wait_ge(sP0, 3)
                if gt >= 2:
                    t.wait_ge(sV1, 4 * (gt - 2) + 3)
                ins = None
                for pj, pso in enumerate((B.psQ1, B.psK1, B.psV1_)):
                    for dc in range(NDC):
                        lhs = B.lnb[:, (tb % 2) * NDC * 512 + dc * 512 + tt * 128:
                                    (tb % 2) * NDC * 512 + dc * 512 + (tt + 1) * 128]
                        ins = t.matmul(pso[:, p * 512:(p + 1) * 512], lhs,
                                       B.ws[:, dc * 1536 + pj * 512:dc * 1536 + (pj + 1) * 512],
                                       start=(dc == 0), stop=(dc == NDC - 1))
                ins.then_inc(sPE, 1)

            def tr1(gt):
                t.wait_ge(sV1, 4 * gt + 3)
                ins = None
                for j in range(4):
                    ins = t.transpose(B.psT1[:, j * 128:(j + 1) * 128],
                                      B.qtok[:, j * 128:(j + 1) * 128], C.cst.ident)
                for j in range(4):
                    ins = t.transpose(B.psT1[:, 512 + j * 128:512 + (j + 1) * 128],
                                      B.qtok[:, 512 + j * 128:512 + (j + 1) * 128], C.cst.ident)
                ins.then_inc(sPE, 1)

            qkv1(0)
            for gt in range(1, 32):
                qkv1(gt)
                tr1(gt - 1)
            tr1(31)
        t.wait_ge(sA1 if layer == 0 else sV1, 64 if layer == 0 else 128)
        t.wait_ge(sV1, 32 if layer == 0 else 128)
        def rbcast(i):
            t.wait_ge(sV2, 2 * i + 1)
            t.matmul(B.psR[:], C.cst.ones128[0:1, :], B.rr[0:1, :],
                     start=True, stop=True).then_inc(sT2, 1)

        for h in range(NH):
            for qb in range(NQB):
                i = hq(h, qb)
                nk = kts(qb)
                eb = exp_base(h, qb)
                for kt in range(nk):
                    if kt >= 4:
                        t.wait_ge(sX, eb + kt - 3)
                    j = kt - 4 * qb
                    n0 = 128 * max(0, j)
                    t.matmul(B.psS[:, (kt % 4) * 512 + n0:(kt % 4) * 512 + 512],
                             B.KF[:, h * S + kt * 128:h * S + (kt + 1) * 128],
                             B.QF[:, h * S + qb * 512 + n0:h * S + (qb + 1) * 512],
                             start=True, stop=True).then_inc(sT2, 1)
                if layer == 0 and i >= 1:
                    rbcast(i - 1)
                if layer == 0:
                    t.wait_ge(sV2, 2 * i)
                    ins = None
                    for kt in range(nk):
                        j = kt - 4 * qb
                        n0 = 128 * max(0, j)
                        t.wait_ge(sX, eb + kt + 1)
                        if j >= 0:
                            t.wait_ge(sM, 4 * i + j + 2)
                        vtb = B.VT[:, kt * 528 + h * 132:]
                        t.matmul(B.psO[:, n0:512], vtb[:, 0:128],
                                 B.PT[:, kt * 512 + n0:(kt + 1) * 512],
                                 start=(kt == 0), stop=(kt == nk - 1))
                        ins = t.matmul(B.psSum[:, n0:512], vtb[:, 128:129],
                                       B.PT[:, kt * 512 + n0:(kt + 1) * 512],
                                       start=(kt == 0), stop=(kt == nk - 1))
                    ins.then_inc(sT2, 1)
                    if i == NH * NQB - 1:
                        rbcast(i)
                else:
                    t.wait_ge(sV2, 4 * i)
                    for qt in range(4):
                        qtg = 4 * qb + qt
                        ins = None
                        for kt in range(qtg + 1):
                            t.wait_ge(sX, eb + kt + 1)
                            j = kt - 4 * qb
                            if j >= 0:
                                t.wait_ge(sM, 4 * i + j + 2)
                            vtb = B.VT[:, kt * 528 + h * 132:]
                            ins = t.matmul(B.psO3[:, qt * 512:qt * 512 + 129],
                                           B.PT[:, kt * 512 + qt * 128:kt * 512 + (qt + 1) * 128],
                                           vtb[:, 0:129],
                                           start=(kt == 0), stop=(kt == qtg))
                        ins.then_inc(sT2, 1)

    def f_scalar(s):
        if layer == 1:
            s.wait_ge(sP0, 1)
            s.activation(B.tnb[:, 3:4], B.psB0[:, 0:1], AF.Abs).then_inc(sP0, 1)
        if layer == 0:
            for tb in range(8):
                for of in range(8):
                    s.wait_ge(sPE, tb * 12 + of + 1)
                    dst = (B.QF if of < 4 else B.KF)[:, (of % 4) * S + tb * 512:(of % 4) * S + (tb + 1) * 512]
                    s.activation(dst, B.psQK[:, (of % 2) * 512:(of % 2) * 512 + 512],
                                 AF.Identity, bias=B.bqs[:, of:of + 1], scale=1.0).then_inc(sA1, 1)
        else:
            for gt in range(32):
                p = gt % 2
                s.wait_ge(sPE, qkv_ct[gt])
                if gt >= 1:
                    s.wait_ge(sV1, 4 * (gt - 1) + 1)
                s.activation(B.scrq[:, 0:512], B.psQ1[:, p * 512:(p + 1) * 512], AF.Square).then_inc(sA1, 1)
                s.activation(B.scrq[:, 512:1024], B.psK1[:, p * 512:(p + 1) * 512], AF.Square).then_inc(sA1, 1)
                s.activation(B.scrq[:, 1024:1536], B.psV1_[:, p * 512:(p + 1) * 512], AF.Square).then_inc(sA1, 1)
                s.wait_ge(sV1, 4 * gt + 1)
                s.activation(B.ssq[:, 12:24], B.ssq[:, 0:12], AF.Sqrt).then_inc(sA1, 1)
        s.wait_ge(sM, 1)
        if layer == 1:
            s.wait_ge(sP0, 3)
        for h in range(NH):
            for qb in range(NQB):
                i = hq(h, qb)
                nk = kts(qb)
                eb = exp_base(h, qb)
                for kt in range(nk):
                    s.wait_ge(sT2, smm_ct[(i, kt)])
                    j = kt - 4 * qb
                    n0 = 128 * max(0, j)
                    if layer == 0:
                        s.activation(B.PT[:, kt * 512 + n0:(kt + 1) * 512],
                                     B.psS[:, (kt % 4) * 512 + n0:(kt % 4) * 512 + 512],
                                     AF.Exp, bias=B.csh[:, 0:1],
                                     scale=B.csh[:, 1:2]).then_inc(sX, 1)
                    else:
                        s.activation(B.PT[:, kt * 512 + n0:(kt + 1) * 512],
                                     B.psS[:, (kt % 4) * 512 + n0:(kt % 4) * 512 + 512],
                                     AF.Exp, bias=B.tnb[:, 1:2],
                                     scale=B.tnb[:, 0:1]).then_inc(sX, 1)

    def f_vector(v):
        nd = [0]
        sD = sm["sD"]

        def dedge(ins):
            nd[0] += 1
            ins.then_inc(sD, 1)
            v.wait_ge(sD, nd[0])

        v.wait_ge(sL, SCTOT)
        v.wait_ge(sW, SWTOT)
        v.memset(B.VT[:], 1.0)
        v.memset(B.csh[:, 0:1], -C0_SHIFT)
        v.memset(B.csh[:, 1:2], 0.08838834764831843).then_inc(sM, 1)
        if layer == 1:
            v.wait_ge(sP0, 2)
            v.tensor_copy(B.tnb[:, 0:1], B.psB0[:, 0:1])
            v.nop()
            v.tensor_scalar(B.tnb[:, 1:2], B.tnb[:, 3:4], -1.0, None, ALU.mult)
            v.tensor_copy(B.tnb[:, 2:3], B.psB0[:, 1:2]).then_inc(sP0, 1)
        if layer == 0:
            for tb in range(8):
                for tt in range(4):
                    v.wait_ge(sM, 1)
                    v.wait_ge(sPE, tb * 12 + 8 + tt + 1)
                    gt = tb * 4 + tt
                    dst = B.VT[:, gt * 528:(gt + 1) * 528].rearrange("p (h d) -> p h d", d=132)[:, :, 0:128]
                    srcv = B.psV[:, (tt % 2) * 512:(tt % 2) * 512 + 512].rearrange("p (h d) -> p h d", d=128)
                    v.tensor_copy(dst, srcv).then_inc(sV1, 1)
        else:
            for gt in range(32):
                p = gt % 2
                v.wait_ge(sM, 1)
                v.wait_ge(sA1, 4 * gt + 3)
                ins = None
                for pj in range(3):
                    ins = v.tensor_reduce(B.ssq[:, pj * 4:(pj + 1) * 4].unsqueeze(2),
                                          B.scrq[:, pj * 512:(pj + 1) * 512].rearrange("p (h d) -> p h d", d=128),
                                          axis=AX.X, op=ALU.add)
                ins.then_inc(sV1, 1)
                v.wait_ge(sA1, 4 * gt + 4)
                v.nop()
                dedge(v.tensor_scalar(B.ssq[:, 24:36], B.ssq[:, 12:24], L2_EPS, None, ALU.add))
                v.reciprocal(B.ssq[:, 12:24], B.ssq[:, 24:36]).then_inc(sV1, 1)
                v.wait_ge(sV1, 4 * gt + 2)
                if gt >= 1:
                    v.wait_ge(sPE, tr_ct[gt - 1])
                rq = B.ssq[:, 12:16].unsqueeze(2).broadcast_to([128, 4, 128])
                rk = B.ssq[:, 16:20].unsqueeze(2).broadcast_to([128, 4, 128])
                rv = B.ssq[:, 20:24].unsqueeze(2).broadcast_to([128, 4, 128])
                v.tensor_tensor(B.qtok[:, 0:512].rearrange("p (h d) -> p h d", d=128),
                                B.psQ1[:, p * 512:(p + 1) * 512].rearrange("p (h d) -> p h d", d=128), rq, ALU.mult)
                v.tensor_tensor(B.qtok[:, 512:1024].rearrange("p (h d) -> p h d", d=128),
                                B.psK1[:, p * 512:(p + 1) * 512].rearrange("p (h d) -> p h d", d=128), rk, ALU.mult)
                dstv = B.VT[:, gt * 528:(gt + 1) * 528].rearrange("p (h d) -> p h d", d=132)[:, :, 0:128]
                v.tensor_tensor(dstv, B.psV1_[:, p * 512:(p + 1) * 512].rearrange("p (h d) -> p h d", d=128), rv, ALU.mult).then_inc(sV1, 1)
                v.wait_ge(sPE, tr_ct[gt])
                v.tensor_copy(B.QF[:].rearrange("p (h t) -> p h t", t=S)[:, :, gt * 128:(gt + 1) * 128],
                              B.psT1[:, 0:512].rearrange("p (h d) -> p h d", d=128))
                v.tensor_copy(B.KF[:].rearrange("p (h t) -> p h t", t=S)[:, :, gt * 128:(gt + 1) * 128],
                              B.psT1[:, 512:1024].rearrange("p (h d) -> p h d", d=128)).then_inc(sV1, 1)
        for h in range(NH):
            for qb in range(NQB):
                i = hq(h, qb)
                nk = kts(qb)
                eb = exp_base(h, qb)
                for jj in range(4):
                    kt = 4 * qb + jj
                    v.wait_ge(sX, eb + kt + 1)
                    dg = B.PT[:, kt * 512 + jj * 128:kt * 512 + (jj + 1) * 128]
                    v.tensor_tensor(dg, dg, C.cst.mask01, ALU.mult).then_inc(sM, 1)
                if layer == 0:
                    v.wait_ge(sT2, pv_ct[i])
                    v.reciprocal(B.rr[0:1, :], B.psSum[:]).then_inc(sV2, 1)
                    v.wait_ge(sT2, rb_ct[i])
                    if i >= 1:
                        v.wait_ge(sO, 16 * i)
                    dedge(v.tensor_copy(B.rB[:], B.psR[:]))
                    v.tensor_tensor(B.attnb[:], B.psO[:], B.rB[:], ALU.mult).then_inc(sV2, 1)
                else:
                    for qt in range(4):
                        v.wait_ge(sT2, pvqt_ct[(i, qt)])
                        v.wait_ge(sV2, 4 * i + qt)
                        if i >= 1:
                            v.wait_ge(sO, 64 * i)
                        dedge(v.reciprocal(B.tnb[:, 3:4], B.psO3[:, qt * 512 + 128:qt * 512 + 129]))
                        dedge(v.tensor_tensor(B.tnb[:, 3:4], B.tnb[:, 3:4], B.tnb[:, 2:3], ALU.mult))
                        v.tensor_scalar(B.attnb[:, qt * 128:(qt + 1) * 128],
                                        B.psO3[:, qt * 512:qt * 512 + 128],
                                        B.tnb[:, 3:4], None, ALU.mult).then_inc(sV2, 1)

    C.bodies = {"sync": f_sync, "scalar": f_scalar, "vector": f_vector,
                "tensor": f_tensor, "gpsimd": f_gpsimd}


def emit_post_f(C, layer):
    sm = C.sems
    B, io, cst = C.B, C.io, C.cst
    sC2, sAT = sm["sC2"], sm["sAT"]
    sATa, sATb, sATc, sATd = sm["sATa"], sm["sATb"], sm["sATc"], sm["sATd"]
    sXc, sXd, sX1Oc, sX1Od = sm["sXc"], sm["sXd"], sm["sX1Oc"], sm["sX1Od"]
    SX4 = None
    sXa, sXb, sXB, sXBb = sm["sXa"], sm["sXb"], sm["sXB"], sm["sXBb"]
    SX4 = [sXa, sXb, sXc, sXd]
    SAT4 = [sATa, sATb, sATc, sATd]
    SXO4 = [sm["sX1Oa"], sm["sX1Ob"], sX1Oc, sX1Od]
    sWDa, sWDb = sm["sWDa"], sm["sWDb"]
    sW2a, sW2b = sm["sW2a"], sm["sW2b"]
    sXCa, sXCb = sm["sXCa"], sm["sXCb"]
    sPA, sEA, sVA = sm["sPA"], sm["sEA"], sm["sVA"]
    sPB, sAB, sVB = sm["sPB"], sm["sAB"], sm["sVB"]
    sP1, sV1b = sm["sP1"], sm["sV1b"]
    sP2, sP2a, sA3, sV3 = sm["sP2"], sm["sP2a"], sm["sA3"], sm["sV3"]
    sP5, sV5 = sm["sP5"], sm["sV5"]
    sP6, sP6a, sA6, sV6 = sm["sP6"], sm["sP6a"], sm["sA6"], sm["sV6"]
    sX1Oa, sX1Ob = sm["sX1Oa"], sm["sX1Ob"]
    sB8wa, sB8wb = sm["sB8wa"], sm["sB8wb"]
    sB8ra, sB8rb, sB8rc = sm["sB8ra"], sm["sB8rb"], sm["sB8rc"]
    sB2wa, sB2wb = sm["sB2wa"], sm["sB2wb"]
    sB2ra, sB2rb, sB2rc = sm["sB2ra"], sm["sB2rb"], sm["sB2rc"]
    sOa, sOb = sm["sOa"], sm["sOb"]

    NHC = 64
    NSB = 16
    VA_TT = 4 if layer == 0 else 1
    FWN = float(1.0 / np.sqrt(8192.0))
    FWN2 = float(1.0 / np.sqrt(2048.0))
    SCTOT = 16 if layer == 0 else 80
    W1BASE = 8 * 16 if layer == 0 else 0
    n = NT

    def f_sync(e):
        def ld_xx(tt):
            if layer == 0:
                if tt >= 2:
                    e.wait_ge(sVA, VA_TT * (tt - 1))
                e.dma_start(out=B.xinA[:, (tt % 2) * D:(tt % 2 + 1) * D],
                            in_=io.x[tt * 128:(tt + 1) * 128, :]).then_inc(
                                sXa if tt % 2 == 0 else sXb, 16)
            else:
                if tt >= 4:
                    e.wait_ge(sVA, tt - 3)
                e.dma_start(out=B.xinA[:, (tt % 4) * D:(tt % 4 + 1) * D],
                            in_=io.x[tt * 128:(tt + 1) * 128, :]).then_inc(SX4[tt % 4], 16)

        def ld_at(tt):
            attg4 = io.attg.rearrange("h (r t) d -> t (h r) d", r=4)
            e.dma_start(
                out=B.at1[:, (tt % 4) * 2048:(tt % 4 + 1) * 2048],
                in_=attg4[bass.ds(C.loc + tt * 128, 128), :, :],
            ).then_inc(SAT4[tt % 4], 16)

        if layer == 0:
            e.dma_start(out=B.pcol[:, 0:16], in_=io.bd.rearrange("(c p) -> p c", p=128),
                        allow_slow_non_contiguous=True).then_inc(sC2, 16)
            ld_xx(0)
            ld_xx(1)
            e.wait_ge(C.ccsem, C.ccval)
            e.dma_start(out=B.atT[:].rearrange("p (c t) -> p c t", t=1024),
                        in_=io.attg.rearrange("h (r p) t -> p (h r) t", p=128)[:, :, bass.ds(C.loc, 1024)],
                        ).then_inc(sAT, 16)
        else:
            e.dma_start(out=B.pcol[:, 0:16], in_=io.su_ug.rearrange("(c p) -> p c", p=128),
                        allow_slow_non_contiguous=True).then_inc(sC2, 16)
            e.dma_start(out=B.pcol[:, 16:32], in_=io.wdp.rearrange("(c p) -> p c", p=128),
                        allow_slow_non_contiguous=True).then_inc(sC2, 16)
            e.dma_start(out=B.pcol2[:, 0:64], in_=io.wsu.rearrange("(c p) -> p c", p=128),
                        allow_slow_non_contiguous=True).then_inc(sC2, 16)
            e.dma_start(out=B.pcol2[:, 64:128], in_=io.wsg.rearrange("(c p) -> p c", p=128),
                        allow_slow_non_contiguous=True).then_inc(sC2, 16)
            e.dma_start(out=B.pcol2[:, 128:192], in_=io.sdp.rearrange("(c p) -> p c", p=128),
                        allow_slow_non_contiguous=True).then_inc(sC2, 16)
            for q in range(4):
                ld_xx(q)
            e.wait_ge(C.ccsem, C.ccval)
            for q in range(4):
                ld_at(q)

        NPF = 2 if layer == 0 else 4
        for tt in range(NT):
            if tt + NPF < NT:
                ld_xx(tt + NPF)
                if layer == 1:
                    ld_at(tt + NPF)
            e.wait_ge(sVA, VA_TT * (tt + 1))
            if layer == 0:
                e.dma_start(out=io.x1buf[tt * 128:(tt + 1) * 128, :],
                            in_=B.x1tA[:, (tt % 2) * D:(tt % 2 + 1) * D]).then_inc(
                                sX1Oa if tt % 2 == 0 else sX1Ob, 16)
            else:
                e.dma_start(out=io.x1buf[tt * 128:(tt + 1) * 128, :],
                            in_=B.x1tA[:, (tt % 4) * D:(tt % 4 + 1) * D]).then_inc(SXO4[tt % 4], 16)
        if layer == 0:
            e.wait_ge(sX1Oa, 16 * (NT // 2))
            e.wait_ge(sX1Ob, 16 * (NT // 2))
        else:
            for q in range(4):
                e.wait_ge(SXO4[q], 32)
        for tt in range(NT if layer == 0 else 4):
            if tt >= 2:
                e.wait_ge(sAB, 8 + tt)
            e.dma_start(out=B.xtB[:, (tt % 2) * D:(tt % 2 + 1) * D],
                        in_=io.x1buf[tt * 128:(tt + 1) * 128, :]).then_inc(
                            sXB if tt % 2 == 0 else sXBb, 16)
        if layer == 1:
            for fc in range(NDC):
                e.wait_ge(sV1b, fc + 1)
                e.dma_start(out=io.bounce2[fc * 128:(fc + 1) * 128, :],
                            in_=B.c1b[:, (fc % 2) * 1024:(fc % 2 + 1) * 1024]).then_inc(
                                sB2wa if fc % 2 == 0 else sB2wb, 16)
            b2v = io.bounce2.rearrange("(p c) t -> p c t", c=16)
            e.wait_ge(sB2wa, 16 * (NDC // 2))
            e.wait_ge(sB2wb, 16 * (NDC // 2))
            for g in range(NDC // 2):
                if g >= 3:
                    e.wait_ge(sP1, 16 + 2 * (g - 2))
                e.dma_start(out=B.rcb[:, (g % 3) * 2048:(g % 3) * 2048 + 2048].rearrange("p (c t) -> p c t", t=1024),
                            in_=b2v[:, 2 * g:2 * g + 2, :]).then_inc(
                                [sB2ra, sB2rb, sB2rc][g % 3], 16)
        for hc in range(NHC):
            e.wait_ge(sV3, 2 * hc + 2)
            e.dma_start(out=io.bounce8[hc * 128:(hc + 1) * 128, :],
                        in_=B.y1b[:, (hc % 2) * 1024:(hc % 2 + 1) * 1024]).then_inc(
                            sB8wa if hc % 2 == 0 else sB8wb, 16)
        b8v = io.bounce8.rearrange("(p c) t -> p c t", c=64)
        e.wait_ge(sB8wa, 16 * (NHC // 2))
        e.wait_ge(sB8wb, 16 * (NHC // 2))
        for g in range(NHC // 2):
            if g >= 3:
                e.wait_ge(sP5, 2 * (g - 2))
            e.dma_start(out=B.rcb[:, (g % 3) * 2048:(g % 3) * 2048 + 2048].rearrange("p (c t) -> p c t", t=1024),
                        in_=b8v[:, 2 * g:2 * g + 2, :]).then_inc(
                            [sB8ra, sB8rb, sB8rc][g % 3], 16)

        def ld_x1cb(doc):
            if doc >= 2:
                e.wait_ge(sV6, 8 * (doc - 1))
            e.dma_start(out=B.x1cb[:, (doc % 2) * 1024:(doc % 2 + 1) * 1024].rearrange("p (tt c) -> p tt c", c=128),
                        in_=io.x1buf.rearrange("(tt p) d -> p tt d", p=128)[:, :, doc * 128:(doc + 1) * 128],
                        ).then_inc(sXCa if doc % 2 == 0 else sXCb, 16)

        ld_x1cb(0)
        ld_x1cb(1)
        for doc in range(16):
            for tt in range(NT):
                ti = doc * 8 + tt
                e.wait_ge(sV6, ti + 1)
                e.dma_start(out=io.xout[tt * 128:(tt + 1) * 128, doc * 128:(doc + 1) * 128],
                            in_=B.x2b[:, (ti % 2) * 128:(ti % 2 + 1) * 128]).then_inc(
                                sOa if ti % 2 == 0 else sOb, 16)
            if doc + 2 < 16:
                ld_x1cb(doc + 2)
        e.wait_ge(sOa, 16 * 64)
        e.wait_ge(sOb, 16 * 64)

    def f_gpsimd(g):
        if layer == 0:
            wdv = io.wd.rearrange("(c p) f -> p c f", p=128)
            for fc in range(NDC):
                if fc >= 2:
                    g.wait_ge(sPA, fc - 1)
                g.dma_start(out=B.wdstr[:, (fc % 2) * 2048:(fc % 2 + 1) * 2048].rearrange("p (c f) -> p c f", f=128),
                            in_=wdv[:, :, fc * 128:(fc + 1) * 128]).then_inc(
                                sWDa if fc % 2 == 0 else sWDb, 16)
        if layer == 0:
            w1v = io.w1t.rearrange("(c p) h -> p c h", p=128)
            w3v = io.w3t.rearrange("(c p) h -> p c h", p=128)
        else:
            w1v = io.w1t.rearrange("(p c) h -> p c h", c=16)
            w3v = io.w3t.rearrange("(p c) h -> p c h", c=16)
        for sbk in range(NSB):
            if sbk >= 2:
                g.wait_ge(sP2, 4 * (sbk - 1))
            else:
                g.wait_ge(sAB, 17)
                g.wait_ge(sVB, 34)
            base = (sbk % 2) * 16384
            g.dma_start(out=B.wstr[:, base:base + 8192].rearrange("p (c h) -> p c h", h=512),
                        in_=w1v[:, :, sbk * 512:(sbk + 1) * 512]).then_inc(
                            sWDa if sbk % 2 == 0 else sWDb, 16)
            g.dma_start(out=B.wstr[:, base + 8192:base + 16384].rearrange("p (c h) -> p c h", h=512),
                        in_=w3v[:, :, sbk * 512:(sbk + 1) * 512]).then_inc(
                            sWDa if sbk % 2 == 0 else sWDb, 16)
        w2v = io.w2t.rearrange("(p c) dd -> p c dd", c=64)
        for doc in range(16):
            if doc >= 2:
                g.wait_ge(sP6, doc - 1)
            else:
                g.wait_ge(sP2, 64)
            g.dma_start(out=B.w2str[:, (doc % 2) * 8192:(doc % 2 + 1) * 8192].rearrange("p (c dd) -> p c dd", dd=128),
                        in_=w2v[:, :, doc * 128:(doc + 1) * 128]).then_inc(
                            sW2a if doc % 2 == 0 else sW2b, 16)

    def f_tensor(t):
        t.wait_ge(sC2, SCTOT)
        if layer == 0:
            t.wait_ge(sAT, 16)
            atv = B.atT.rearrange("p (c t) -> p c t", t=1024)
            for fc in range(NDC):
                t.wait_ge(sWDa if fc % 2 == 0 else sWDb, 16 * (fc // 2 + 1))
                if fc >= 2:
                    t.wait_ge(sEA, fc - 1)
                half = (fc % 2) * 1024
                ins = None
                for tc in range(2):
                    for dc in range(NDC):
                        ins = t.matmul(B.psF[:, half + tc * 512:half + (tc + 1) * 512],
                                       B.wdstr[:, (fc % 2) * 2048 + dc * 128:(fc % 2) * 2048 + (dc + 1) * 128],
                                       atv[:, dc, tc * 512:(tc + 1) * 512],
                                       start=(dc == 0), stop=(dc == NDC - 1))
                ins.then_inc(sPA, 1)
            t.wait_ge(sEA, 16)
            for tt in range(NT):
                for grp in range(4):
                    gi = tt * 4 + grp
                    if gi >= 2:
                        t.wait_ge(sVA, gi - 1)
                    ins = None
                    for j in range(4):
                        fc = grp * 4 + j
                        ins = t.transpose(B.psTd[:, (gi % 2) * 1024 + j * 128:(gi % 2) * 1024 + (j + 1) * 128],
                                          B.dnsF[:, fc * 1024 + tt * 128:fc * 1024 + (tt + 1) * 128],
                                          cst.ident)
                    ins.then_inc(sPA, 1)
        for tt in range(NT):
            t.wait_ge(sAB, 10 + tt)
            for grp in range(4):
                gi = tt * 4 + grp
                if gi >= 2:
                    t.wait_ge(sVB, 2 + gi - 1)
                ins = None
                for j in range(4):
                    fc = grp * 4 + j
                    ins = t.transpose(B.psTd[:, (gi % 2) * 1024 + j * 128:(gi % 2) * 1024 + (j + 1) * 128],
                                      B.lntB[:, (tt % 2) * D + fc * 128:(tt % 2) * D + (fc + 1) * 128],
                                      cst.ident)
                ins.then_inc(sPB, 1)
        if layer == 1:
            t.wait_ge(sVB, 34)
            for fc in range(NDC):
                if fc >= 1:
                    t.wait_ge(sV1b, fc)
                ins = None
                for tc in range(2):
                    ins = t.matmul(B.psB2[:, tc * 512:(tc + 1) * 512], cst.bd16,
                                   B.lnF[:, fc * 1024 + tc * 512:fc * 1024 + (tc + 1) * 512],
                                   start=True, stop=True)
                ins.then_inc(sP1, 1)
            for c in range(NDC):
                g = c // 2
                rb = (g % 3) * 2048 + (c % 2) * 1024
                t.wait_ge([sB2ra, sB2rb, sB2rc][g % 3], 16 * (g // 3 + 1))
                t.wait_ge(sV1b, 16 + c)
                ins = None
                for tc in range(2):
                    ins = t.matmul(B.psB2[:, tc * 512:(tc + 1) * 512], cst.h128,
                                   B.rcb[:, rb + tc * 512:rb + (tc + 1) * 512],
                                   start=True, stop=True)
                ins.then_inc(sP1, 1)
        else:
            t.wait_ge(sVB, 34)
        if layer == 1:
            t.wait_ge(sV1b, 32)

        def upgate(hc):
            sbk, hcl = hc // 4, hc % 4
            if hcl == 0:
                t.wait_ge(sWDa if sbk % 2 == 0 else sWDb, W1BASE + 32 * (sbk // 2 + 1))
            p = hc % 2
            if hc >= 2:
                t.wait_ge(sV3, 2 * (hc - 2) + 2)
                t.wait_ge(sA3, hc - 1)
            wb = (sbk % 2) * 16384
            ins = None
            for cc in range(NDC):
                for tc in range(2):
                    t.matmul(B.psU[:, p * 1024 + tc * 512:p * 1024 + (tc + 1) * 512],
                             B.wstr[:, wb + cc * 512 + hcl * 128:wb + cc * 512 + (hcl + 1) * 128],
                             B.lnF[:, cc * 1024 + tc * 512:cc * 1024 + (tc + 1) * 512],
                             start=(cc == 0), stop=(cc == NDC - 1))
                    ins = t.matmul(B.psG[:, p * 1024 + tc * 512:p * 1024 + (tc + 1) * 512],
                                   B.wstr[:, wb + 8192 + cc * 512 + hcl * 128:wb + 8192 + cc * 512 + (hcl + 1) * 128],
                                   B.lnF[:, cc * 1024 + tc * 512:cc * 1024 + (tc + 1) * 512],
                                   start=(cc == 0), stop=(cc == NDC - 1))
            ins.then_inc(sP2, 1)

        def stageA(hc):
            p = hc % 2
            t.wait_ge(sV3, 2 * hc + 1)
            t.wait_ge(sA3, hc + 1)
            ins = None
            for tc in range(2):
                ins = t.matmul(B.psU[:, p * 1024 + tc * 512:p * 1024 + (tc + 1) * 512],
                               cst.bd64,
                               B.hck[:, p * 1024 + tc * 512:p * 1024 + (tc + 1) * 512],
                               start=True, stop=True)
            ins.then_inc(sP2a, 1)

        upgate(0)
        for hc in range(1, NHC):
            upgate(hc)
            stageA(hc - 1)
        stageA(NHC - 1)
        for c in range(NHC):
            g = c // 2
            rb = (g % 3) * 2048 + (c % 2) * 1024
            ps5 = B.psB2 if c % 2 == 0 else B.psB2b
            t.wait_ge([sB8ra, sB8rb, sB8rc][g % 3], 16 * (g // 3 + 1))
            if c >= 2:
                t.wait_ge(sV5, c - 1)
            ins = None
            for tc in range(2):
                ins = t.matmul(ps5[:, tc * 512:(tc + 1) * 512], cst.h128,
                               B.rcb[:, rb + tc * 512:rb + (tc + 1) * 512],
                               start=True, stop=True)
            ins.then_inc(sP5, 1)
        t.wait_ge(sV5, 64)

        def psy(doc):
            p = doc % 2
            t.wait_ge(sW2a if p == 0 else sW2b, 16 * (doc // 2 + 1))
            if doc >= 2:
                t.wait_ge(sA6, doc - 1)
            ins = None
            for c in range(NHC):
                for tc in range(2):
                    ins = t.matmul(B.psY[:, p * 1024 + tc * 512:p * 1024 + (tc + 1) * 512],
                                   B.w2str[:, p * 8192 + c * 128:p * 8192 + (c + 1) * 128],
                                   B.h2[:, c * 1024 + tc * 512:c * 1024 + (tc + 1) * 512],
                                   start=(c == 0), stop=(c == NHC - 1))
            ins.then_inc(sP6, 1)

        def trans6(doc):
            t.wait_ge(sA6, doc + 1)
            for tt in range(NT):
                ti = doc * 8 + tt
                if ti >= 2:
                    t.wait_ge(sV6, ti - 1)
                t.transpose(B.psTd[:, (ti % 2) * 1024:(ti % 2) * 1024 + 128],
                            B.ysb[:, (doc % 2) * 1024 + tt * 128:(doc % 2) * 1024 + (tt + 1) * 128],
                            cst.ident).then_inc(sP6a, 1)

        psy(0)
        for doc in range(1, 16):
            psy(doc)
            trans6(doc - 1)
        trans6(15)

    def f_scalar(s):
        s.wait_ge(sC2, SCTOT)
        if layer == 0:
            for fc in range(NDC):
                s.wait_ge(sPA, fc + 1)
                s.activation(B.dnsF[:, fc * 1024:(fc + 1) * 1024],
                             B.psF[:, (fc % 2) * 1024:(fc % 2 + 1) * 1024],
                             AF.Identity, bias=B.pcol[:, fc:fc + 1],
                             scale=1.0).then_inc(sEA, 1)
        NSL = 2 if layer == 0 else 4
        for tt in range(NT):
            s.wait_ge(sVA, VA_TT * (tt + 1))
            if tt >= 1:
                s.wait_ge(sAB, tt)
            s.activation(B.scrB[:],
                         B.x1tA[:, (tt % NSL) * D:(tt % NSL + 1) * D], AF.Square,
                         accum_out=B.smq[:, NT + tt:NT + tt + 1]).then_inc(sAB, 1)
        s.wait_ge(sVB, 1)
        s.activation(B.stt[:, NT:2 * NT], B.stt[:, NT:2 * NT], AF.Sqrt).then_inc(sAB, 1)
        s.wait_ge(sVB, 2)
        for tt in range(NT):
            if layer == 0 or tt < 4:
                s.wait_ge(sXB if tt % 2 == 0 else sXBb, 16 * (tt // 2 + 1))
                asrc = B.xtB[:, (tt % 2) * D:(tt % 2 + 1) * D]
            else:
                asrc = B.x1tA[:, (tt % 4) * D:(tt % 4 + 1) * D]
            if tt >= 2:
                s.wait_ge(sPB, 4 * (tt - 1))
            s.activation(B.lntB[:, (tt % 2) * D:(tt % 2 + 1) * D],
                         asrc, AF.Identity,
                         bias=B.stt[:, 3 * NT + tt:3 * NT + tt + 1],
                         scale=B.stt[:, 2 * NT + tt:2 * NT + tt + 1]).then_inc(sAB, 1)
        for hc in range(NHC):
            p = hc % 2
            s.wait_ge(sP2, hc + 1)
            if hc >= 2:
                s.wait_ge(sV3, 2 * (hc - 2) + 1)
            if layer == 0:
                s.activation(B.sgb[:, p * 1024:(p + 1) * 1024],
                             B.psU[:, p * 1024:(p + 1) * 1024], AF.Silu).then_inc(sA3, 1)
            else:
                s.activation(B.sgb[:, p * 1024:(p + 1) * 1024],
                             B.psG[:, p * 1024:(p + 1) * 1024], AF.Silu,
                             scale=B.pcol2[:, 64 + hc:64 + hc + 1]).then_inc(sA3, 1)
        for doc in range(16):
            p = doc % 2
            s.wait_ge(sP6, doc + 1)
            if doc >= 2:
                s.wait_ge(sP6a, 8 * (doc - 1))
            if layer == 0:
                s.activation(B.ysb[:, p * 1024:(p + 1) * 1024],
                             B.psY[:, p * 1024:(p + 1) * 1024], AF.Identity).then_inc(sA6, 1)
            else:
                s.activation(B.ysb[:, p * 1024:(p + 1) * 1024],
                             B.psY[:, p * 1024:(p + 1) * 1024], AF.Identity,
                             scale=B.pcol[:, 16 + doc:16 + doc + 1]).then_inc(sA6, 1)

    def f_vector(v):
        nd = [0]
        sD = sm["sD"]

        def dedge(ins):
            nd[0] += 1
            ins.then_inc(sD, 1)
            v.wait_ge(sD, nd[0])

        v.wait_ge(sC2, SCTOT)
        if layer == 1:
            dedge(v.tensor_tensor(B.pcol2[:, 0:64], B.pcol2[:, 0:64], B.pcol2[:, 128:192], ALU.mult))
        if layer == 0:
            for tt in range(NT):
                v.wait_ge(sXa if tt % 2 == 0 else sXb, 16 * (tt // 2 + 1))
                if tt >= 2:
                    v.wait_ge(sX1Oa if tt % 2 == 0 else sX1Ob, 16 * (tt // 2))
                    v.wait_ge(sAB, tt - 1)
                for grp in range(4):
                    gi = tt * 4 + grp
                    v.wait_ge(sPA, 16 + gi + 1)
                    v.tensor_tensor(B.x1tA[:, (tt % 2) * D + grp * 512:(tt % 2) * D + (grp + 1) * 512],
                                    B.psTd[:, (gi % 2) * 1024:(gi % 2) * 1024 + 512],
                                    B.xinA[:, (tt % 2) * D + grp * 512:(tt % 2) * D + (grp + 1) * 512],
                                    ALU.add).then_inc(sVA, 1)
                v.wait_ge(sVA, 4 * (tt + 1))
                dedge(v.reduce_sum(B.smq[:, tt:tt + 1],
                                   B.x1tA[:, (tt % 2) * D:(tt % 2 + 1) * D], axis=AX.X))
        else:
            for tt in range(NT):
                v.wait_ge(SX4[tt % 4], 16 * (tt // 4 + 1))
                v.wait_ge(SAT4[tt % 4], 16 * (tt // 4 + 1))
                if tt >= 4:
                    v.wait_ge(SXO4[tt % 4], 16 * (tt // 4))
                    v.wait_ge(sAB, tt - 3)
                if tt >= 1:
                    v.wait_ge(sVA, tt)
                a32v = B.a32[:, 0:2048].rearrange("p (r h d) -> p r h d", h=4, d=128)
                for h in range(4):
                    dedge(v.tensor_copy(a32v[:, :, h, :],
                                        B.at1[:, (tt % 4) * 2048 + h * 512:(tt % 4) * 2048 + (h + 1) * 512].rearrange("p (r d) -> p r d", d=128)))
                v.tensor_tensor(B.x1tA[:, (tt % 4) * D:(tt % 4 + 1) * D],
                                B.a32[:, 0:2048],
                                B.xinA[:, (tt % 4) * D:(tt % 4 + 1) * D],
                                ALU.add).then_inc(sVA, 1)
                v.wait_ge(sVA, tt + 1)
                dedge(v.reduce_sum(B.smq[:, tt:tt + 1],
                                   B.x1tA[:, (tt % 4) * D:(tt % 4 + 1) * D], axis=AX.X))
        mean, veps, rstd, nmrs = (B.stt[:, 0:n], B.stt[:, n:2 * n],
                                  B.stt[:, 2 * n:3 * n], B.stt[:, 3 * n:4 * n])

        v.wait_ge(sAB, NT)
        dedge(v.tensor_scalar(mean, B.smq[:, 0:NT], 1.0 / D, None, ALU.mult))
        dedge(v.tensor_scalar(veps, B.smq[:, NT:2 * NT], 1.0 / D, LN_EPS, ALU.mult, ALU.add))
        dedge(v.tensor_tensor(rstd, mean, mean, ALU.mult))
        v.tensor_sub(veps, veps, rstd).then_inc(sVB, 1)
        v.wait_ge(sAB, NT + 1)
        dedge(v.reciprocal(rstd, veps))
        dedge(v.tensor_tensor(nmrs, mean, rstd, ALU.mult))
        v.tensor_scalar(nmrs, nmrs, -1.0, None, ALU.mult).then_inc(sVB, 1)
        for tt in range(NT):
            for grp in range(4):
                gi = tt * 4 + grp
                v.wait_ge(sPB, gi + 1)
                dst3 = B.lnF[:].rearrange("p (fc t) -> p fc t", t=1024)[
                    :, grp * 4:(grp + 1) * 4, tt * 128:(tt + 1) * 128]
                src3 = B.psTd[:, (gi % 2) * 1024:(gi % 2) * 1024 + 512].rearrange(
                    "p (j t) -> p j t", t=128)
                if layer == 0:
                    v.tensor_copy(dst3, src3).then_inc(sVB, 1)
                else:
                    sucol = B.pcol[:, grp * 4:(grp + 1) * 4].unsqueeze(2).broadcast_to([128, 4, 128])
                    v.tensor_tensor(dst3, src3, sucol, ALU.mult).then_inc(sVB, 1)
        if layer == 1:
            for fc in range(NDC):
                v.wait_ge(sP1, fc + 1)
                if fc >= 2:
                    v.wait_ge(sB2wa if fc % 2 == 0 else sB2wb, 16 * (fc // 2))
                v.tensor_copy(B.c1b[:, (fc % 2) * 1024:(fc % 2 + 1) * 1024],
                              B.psB2[:]).then_inc(sV1b, 1)
            for c in range(NDC):
                v.wait_ge(sP1, 16 + c + 1)
                v.tensor_scalar(B.lnF[:, c * 1024:(c + 1) * 1024], B.psB2[:],
                                FWN2, None, ALU.mult).then_inc(sV1b, 1)
        for hc in range(NHC):
            p = hc % 2
            v.wait_ge(sA3, hc + 1)
            if hc >= 2:
                v.wait_ge(sP2a, hc - 1)
            if layer == 0:
                v.tensor_tensor(B.hck[:, p * 1024:(p + 1) * 1024],
                                B.psG[:, p * 1024:(p + 1) * 1024],
                                B.sgb[:, p * 1024:(p + 1) * 1024], ALU.mult).then_inc(sV3, 1)
            else:
                v.scalar_tensor_tensor(B.hck[:, p * 1024:(p + 1) * 1024],
                                       B.psU[:, p * 1024:(p + 1) * 1024],
                                       B.pcol2[:, hc:hc + 1],
                                       B.sgb[:, p * 1024:(p + 1) * 1024],
                                       ALU.mult, ALU.mult).then_inc(sV3, 1)
            v.wait_ge(sP2a, hc + 1)
            if hc >= 2:
                v.wait_ge(sB8wa if hc % 2 == 0 else sB8wb, 16 * (hc // 2))
            v.tensor_copy(B.y1b[:, p * 1024:(p + 1) * 1024],
                          B.psU[:, p * 1024:(p + 1) * 1024]).then_inc(sV3, 1)
        for c in range(NHC):
            v.wait_ge(sP5, c + 1)
            v.tensor_scalar(B.h2[:, c * 1024:(c + 1) * 1024],
                            (B.psB2 if c % 2 == 0 else B.psB2b)[:],
                            FWN, None, ALU.mult).then_inc(sV5, 1)
        for doc in range(16):
            v.wait_ge(sXCa if doc % 2 == 0 else sXCb, 16 * (doc // 2 + 1))
            for tt in range(NT):
                ti = doc * 8 + tt
                v.wait_ge(sP6a, ti + 1)
                if ti >= 2:
                    v.wait_ge(sOa if ti % 2 == 0 else sOb, 16 * (ti // 2))
                v.tensor_tensor(B.x2b[:, (ti % 2) * 128:(ti % 2 + 1) * 128],
                                B.psTd[:, (ti % 2) * 1024:(ti % 2) * 1024 + 128],
                                B.x1cb[:, (doc % 2) * 1024 + tt * 128:(doc % 2) * 1024 + (tt + 1) * 128],
                                ALU.add).then_inc(sV6, 1)

    C.bodies = {"sync": f_sync, "scalar": f_scalar, "vector": f_vector,
                "tensor": f_tensor, "gpsimd": f_gpsimd}


def build_fused():
    nc = bass.Bass("TRN2", target_bir_lowering=False)
    I32 = mybir.dt.int32
    dt_ = nc.dram_tensor
    io_x = dt_("x", [TOK, D], F32, kind="ExternalInput").ap()
    io_wqkv = dt_("wqkv_sl", [D, 1536], F32, kind="ExternalInput").ap()
    io_bqkv = dt_("bqkv_sl", [1536], F32, kind="ExternalInput").ap()
    io_wd = dt_("wd", [D, D], F32, kind="ExternalInput").ap()
    io_bd = dt_("bd", [D], F32, kind="ExternalInput").ap()
    io_w1t0 = dt_("w1t0", [D, HF], F32, kind="ExternalInput").ap()
    io_w3t0 = dt_("w3t0", [D, HF], F32, kind="ExternalInput").ap()
    io_w2t0 = dt_("w2t0", [HF, D], F32, kind="ExternalInput").ap()
    io_wsl1 = dt_("wsl1", [D, 1536], F32, kind="ExternalInput").ap()
    io_tau = dt_("tau", [1], F32, kind="ExternalInput").ap()
    io_nu = dt_("nu", [1], F32, kind="ExternalInput").ap()
    io_su = dt_("su_ug", [D], F32, kind="ExternalInput").ap()
    io_wsu = dt_("wsu", [HF], F32, kind="ExternalInput").ap()
    io_wsg = dt_("wsg", [HF], F32, kind="ExternalInput").ap()
    io_sdp = dt_("sdp", [HF], F32, kind="ExternalInput").ap()
    io_wdp = dt_("wdp", [D], F32, kind="ExternalInput").ap()
    io_w1t1 = dt_("w1t1", [D, HF], F32, kind="ExternalInput").ap()
    io_w3t1 = dt_("w3t1", [D, HF], F32, kind="ExternalInput").ap()
    io_w2t1 = dt_("w2t1", [HF, D], F32, kind="ExternalInput").ap()
    io_loc = dt_("myloc", [1], I32, kind="ExternalInput").ap()
    io_out = dt_("out", [TOK, D], F32, kind="ExternalOutput").ap()

    cc_ln0_in = dt_("cc_ln0_in", [4, D, 256], BF16, kind="Internal").ap()
    cc_ln0_out = dt_("cc_ln0_out", [4, 4 * D, 256], BF16, kind="Internal").ap()
    cc_at0_in = dt_("cc_at0_in", [512, S], BF16, kind="Internal").ap()
    cc_at0_out = dt_("cc_at0_out", [4, 512, S], BF16, kind="Internal").ap()
    cc_ln1_in = dt_("cc_ln1_in", [4, D, 256], BF16, kind="Internal").ap()
    cc_ln1_out = dt_("cc_ln1_out", [4, 4 * D, 256], BF16, kind="Internal").ap()
    cc_at1_in = dt_("cc_at1_in", [4, S, 128], BF16, kind="Internal").ap()
    cc_at1_out = dt_("cc_at1_out", [4, 4 * S, 128], BF16, kind="Internal").ap()
    x1mid = dt_("x1mid", [TOK, D], F32, kind="Internal").ap()
    x1buf0 = dt_("x1buf0", [TOK, D], F32, kind="Internal").ap()
    x1buf1 = dt_("x1buf1", [TOK, D], F32, kind="Internal").ap()
    bounce8a = dt_("bounce8a", [HF, TOK], BF16, kind="Internal").ap()
    bounce8b = dt_("bounce8b", [HF, TOK], BF16, kind="Internal").ap()
    bounce2 = dt_("bounce2", [D, TOK], BF16, kind="Internal").ap()

    ident_c = nc.inline_tensor(np.eye(128, dtype=bfdt), name="c_ident").ap()
    mask_np = (np.arange(128)[:, None] <= np.arange(128)[None, :]).astype(bfdt)
    mask_c = nc.inline_tensor(mask_np, name="c_mask").ap()
    h128_c = nc.inline_tensor(H128_NP.astype(bfdt), name="c_h128").ap()
    bd64_c = nc.inline_tensor(BD64_NP.astype(bfdt), name="c_bd64").ap()
    bd16_c = nc.inline_tensor(BD16_NP.astype(bfdt), name="c_bd16").ap()
    ones_f = nc.inline_tensor(np.ones((1, 128), dtype=np.float32), name="c_onesf").ap()
    ones_b = nc.inline_tensor(np.ones((1, 128), dtype=bfdt), name="c_onesb").ap()

    import contextlib
    es = contextlib.ExitStack()
    with es:
        sb = lambda *a: es.enter_context(nc.sbuf_tensor(*a))
        sem = lambda nm: es.enter_context(nc.semaphore(nm))

        ARN = sb("ARN", [128, 91648], BF16)
        cst = _NS(
            ident=sb("c_ident_s", [128, 128], BF16)[:],
            mask01=sb("c_mask_s", [128, 128], BF16)[:],
            h128=sb("c_h128_s", [128, 128], BF16)[:],
            bd64=sb("c_bd64_s", [128, 128], BF16)[:],
            bd16=sb("c_bd16_s", [128, 128], BF16)[:],
            ones128=sb("c_onesf_s", [1, 128], F32)[:],
            onesb=sb("c_onesb_s", [1, 128], BF16)[:],
        )
        locs_t = sb("locs", [1, 1], I32)
        att_small = dict(
            bqs=sb("a_bqs", [128, 12], F32)[:],
            bvrow=sb("a_bvrow", [1, 512], BF16)[:],
            taunu=sb("a_taunu", [1, 2], F32)[:],
            tnb=sb("a_tnb", [128, 4], F32)[:],
            ssq=sb("a_ssq", [128, 36], F32)[:],
            csh=sb("a_csh", [128, 2], F32)[:],
            rr=sb("a_rr", [1, 512], F32)[:],
            rB=sb("a_rB", [128, 512], F32)[:],
            attnb=sb("a_attnb", [128, 512], BF16)[:],
            scrq=sb("a_scrq", [128, 1536], F32)[:],
        )
        post_small = dict(
            pcol=sb("p_pcol", [128, 32], F32)[:],
            pcol2=sb("p_pcol2", [128, 192], F32)[:],
            smq=sb("p_smq", [128, 2 * NT], F32)[:],
            stt=sb("p_stt", [128, 4 * NT], F32)[:],
            x1cb=sb("p_x1cb", [128, 2048], F32)[:],
            ysb=sb("p_ysb", [128, 2048], BF16)[:],
            x2b=sb("p_x2b", [128, 256], F32)[:],
        )
        psA = es.enter_context(nc.psum_tensor("psA", [128, 4096], F32))
        reg = es.enter_context(nc.sync.register("locreg"))

        sCst = sem("sCst")
        sBAR = sem("sBAR")
        sCC = [sem(f"sCC{i}") for i in range(4)]
        PRE_KEYS = ["sL", "sL2", "sV", "sA", "sT", "sE", "sO", "sD"]
        ATT_KEYS = ["sL", "sW", "sP0", "sPE", "sA1", "sV1", "sT2", "sX",
                    "sM", "sV2", "sO", "sB0", "sB1", "sD"]
        POST_KEYS = ["sC2", "sAT", "sATa", "sATb", "sATc", "sATd",
                     "sXa", "sXb", "sXc", "sXd", "sX1Oc", "sX1Od", "sXB",
                     "sWDa", "sWDb", "sW2a", "sW2b", "sXCa", "sXCb",
                     "sPA", "sEA", "sVA", "sPB", "sAB", "sVB", "sP1", "sV1b",
                     "sP2", "sP2a", "sA3", "sV3", "sP5", "sV5", "sP6", "sP6a", "sA6", "sV6",
                     "sX1Oa", "sX1Ob", "sB8wa", "sB8wb", "sB8ra", "sB8rb", "sB8rc",
                     "sB2wa", "sB2wb", "sB2ra", "sB2rb", "sB2rc", "sOa", "sOb",
                     "sXBb", "sD"]
        PRE_SEMS = {k: sem("pre_" + k) for k in PRE_KEYS}
        ATT_SEMS = {k: sem("att_" + k) for k in ATT_KEYS}
        POST_SEMS = {k: sem("post_" + k) for k in POST_KEYS}

        def sem_range(d):
            nums = sorted(h.num for h in d.values())
            assert nums == list(range(nums[0], nums[-1] + 1)), nums
            return range(nums[0], nums[-1] + 1)

        PRE_RANGE = sem_range(PRE_SEMS)
        ATT_RANGE = sem_range(ATT_SEMS)
        POST_RANGE = sem_range(POST_SEMS)

        A = ARN

        pre_sm = sb("pre_sm", [128, NT], F32)[:]
        pre_sq = sb("pre_sq", [128, NT], F32)[:]
        pre_st = sb("pre_st", [128, 4 * NT], F32)[:]

        def mk_pre(io_xa, io_lnTa, _n=[0]):
            C = _NS()
            C.cst = cst
            C.sems = PRE_SEMS
            C.B = _NS(
                xt=A[:, 0:32768].bitcast(F32),
                scr=A[:, 32768:36864].bitcast(F32),
                ln=A[:, 36864:40960],
                lt=A[:, 40960:49152],
                sm=pre_sm, sq=pre_sq, st=pre_st,
                psT=psA[:, 2048:3072].bitcast(BF16),
            )
            C.io = _NS(x=io_xa, lnT=io_lnTa)
            emit_pre_f(C)
            return C

        def mk_att(layer, io_lnTg, io_attc, ccsem):
            C = _NS()
            C.cst = cst
            C.sems = ATT_SEMS
            ab = 49664
            C.B = _NS(
                QF=A[:, 0:16384], KF=A[:, 16384:32768], VT=A[:, 32768:49664],
                PT=A[:, ab:ab + 16384], lnb=A[:, ab:ab + 16384],
                ws=A[:, ab + 16384:ab + 40960], qtok=A[:, ab + 40960:ab + 41984],
                psQK=psA[:, 0:1024], psV=psA[:, 1024:2048],
                psQ1=psA[:, 0:1024], psK1=psA[:, 1024:2048], psV1_=psA[:, 2048:3072],
                psT1=psA[:, 3072:4096].bitcast(BF16),
                psS=psA[:, 0:2048], psO=psA[:, 2048:2560], psR=psA[:, 2560:3072],
                psSum=psA[0:1, 3072:3584], psO3=psA[:, 2048:4096],
                psB0=psA[:, 3584:4096],
                **att_small,
            )
            if layer == 0:
                C.io = _NS(lnTg=io_lnTg, att_c=io_attc, w_sl=io_wqkv, bq=io_bqkv)
            else:
                C.io = _NS(lnTg=io_lnTg, att_c=io_attc, w_sl=io_wsl1, tau=io_tau, nu=io_nu)
            C.ccsem = ccsem
            C.ccval = 1
            emit_att_f(C, layer)
            return C

        def mk_post(layer, io_xa, io_attg, io_xouta, io_x1bufa, io_b8, ccsem, loc_ref):
            C = _NS()
            C.cst = cst
            C.sems = POST_SEMS
            if layer == 0:
                lay = dict(
                    at1=A[:, 32768:36864],
                    a32=A[:, 16384:20480].bitcast(F32),
                    xinA=A[:, 36864:45056].bitcast(F32),
                    x1tA=A[:, 45056:53248].bitcast(F32),
                    lntB=A[:, 32768:36864],
                )
            else:
                lay = dict(
                    at1=A[:, 0:8192],
                    a32=A[:, 24576:28672].bitcast(F32),
                    xinA=A[:, 8192:24576].bitcast(F32),
                    x1tA=A[:, 28672:45056].bitcast(F32),
                    lntB=A[:, 45056:49152],
                )
            C.B = _NS(
                atT=A[:, 0:16384], dnsF=A[:, 16384:32768],
                wdstr=A[:, 32768:36864],
                xtB=A[:, 53248:61440].bitcast(F32),
                scrB=A[:, 61440:65536].bitcast(F32),
                **lay,
                wstr=A[:, 0:32768],
                sgb=A[:, 32768:34816], hck=A[:, 34816:38912],
                y1b=A[:, 38912:40960], c1b=A[:, 36864 + 4096:40960 + 0] if False else A[:, 61440:63488],
                h2=A[:, 0:65536],
                lnF=A[:, 65536:81920],
                rcb=A[:, 83968:90112], w2str=A[:, 67584:83968],
                psF=psA[:, 0:2048], psTd=psA[:, 2048:3072].bitcast(BF16),
                psU=psA[:, 0:2048], psG=psA[:, 2048:4096],
                psB2=psA[:, 3072:4096], psB2b=psA[:, 0:1024], psY=psA[:, 0:2048],
                **post_small,
            )
            if layer == 0:
                C.io = _NS(x=io_xa, attg=io_attg, xout=io_xouta, x1buf=io_x1bufa,
                           bounce8=io_b8, wd=io_wd, bd=io_bd,
                           w1t=io_w1t0, w3t=io_w3t0, w2t=io_w2t0)
            else:
                C.io = _NS(x=io_xa, attg=io_attg, xout=io_xouta, x1buf=io_x1bufa,
                           bounce8=io_b8, bounce2=bounce2,
                           su_ug=io_su, wsu=io_wsu, wsg=io_wsg, sdp=io_sdp,
                           wdp=io_wdp, w1t=io_w1t1, w3t=io_w3t1, w2t=io_w2t1)
            C.ccsem = ccsem
            C.ccval = 4
            C.loc = loc_ref
            emit_post_f(C, layer)
            return C

        loc_holder = _NS(v=None)

        class _LazyLoc:
            def __add__(self, other):
                return loc_holder.v + other

            def __radd__(self, other):
                return other + loc_holder.v

        phases = []
        pre0 = mk_pre(io_x, cc_ln0_in)
        att0 = mk_att(0, cc_ln0_out, cc_at0_in, sCC[0])
        post0 = mk_post(0, io_x, cc_at0_out, x1mid, x1buf0, bounce8a, sCC[1], None)
        pre1 = mk_pre(x1mid, cc_ln1_in)
        pre1.prefetch = (att0.B.ws, io_wsl1, ATT_SEMS["sW"])
        att1 = mk_att(1, cc_ln1_out, cc_at1_in, sCC[2])
        post1 = mk_post(1, x1mid, cc_at1_out, io_out, x1buf1, bounce8b, sCC[3], None)
        phases = [pre0, att0, post0, pre1, att1, post1]

        rg4 = [[0, 1, 2, 3], [4, 5, 6, 7]]

        def ln_pieces(ph, cin, cout, ccs):
            return [(ph, "sO", 16 * 2 * (q + 1), cin[q], cout[q], ccs)
                    for q in range(4)]

        def at0_pieces(ph, cin, cout, ccs):
            return [(ph, "sO", 16 * 8 * (h + 1), cin[h * 128:(h + 1) * 128, :],
                     cout[h], ccs) for h in range(4)]

        def at1_pieces(ph, cin, cout, ccs):
            return [(ph, "sO", 16 * 32 * (h + 1), cin[h], cout[h], ccs)
                    for h in range(4)]

        ag_spec = {
            0: ln_pieces(pre0, cc_ln0_in, cc_ln0_out, sCC[0]),
            1: at0_pieces(att0, cc_at0_in, cc_at0_out, sCC[1]),
            3: ln_pieces(pre1, cc_ln1_in, cc_ln1_out, sCC[2]),
            4: at1_pieces(att1, cc_at1_in, cc_at1_out, sCC[3]),
        }

        block = es.enter_context(nc.Block())
        NPH = len(phases)

        def run_bodies(eng_name):
            def body(eng):
                if eng_name == "sync":
                    eng.dma_start(out=cst.ident, in_=ident_c[:, :]).then_inc(sCst, 16)
                    eng.dma_start(out=cst.mask01, in_=mask_c[:, :]).then_inc(sCst, 16)
                    eng.dma_start(out=cst.h128, in_=h128_c[:, :]).then_inc(sCst, 16)
                    eng.dma_start(out=cst.bd64, in_=bd64_c[:, :]).then_inc(sCst, 16)
                    eng.dma_start(out=cst.bd16, in_=bd16_c[:, :]).then_inc(sCst, 16)
                    eng.dma_start(out=cst.ones128, in_=ones_f[:, :]).then_inc(sCst, 16)
                    eng.dma_start(out=cst.onesb, in_=ones_b[:, :]).then_inc(sCst, 16)
                    eng.dma_start(out=locs_t[0:1, :], in_=io_loc.unsqueeze(0)).then_inc(sCst, 16)
                    eng.wait_ge(sCst, 128)
                    eng.reg_load(reg, locs_t[0:1, 0:1])
                    loc_holder.v = eng.snap(reg)
                    post0.loc = loc_holder.v
                    post1.loc = loc_holder.v
                else:
                    if eng_name == "gpsimd":
                        wv0 = io_wqkv.rearrange("(c p) o -> p c o", p=128)
                        for q in range(4):
                            eng.dma_start(
                                out=att0.B.ws[:, q * 6144:(q + 1) * 6144].rearrange("p (c o) -> p c o", o=1536),
                                in_=wv0[:, q * 4:(q + 1) * 4, :]).then_inc(ATT_SEMS["sW"], 16)
                    eng.wait_ge(sCst, 128)
                clear_spec = {0: PRE_RANGE, 1: ATT_RANGE, 2: POST_RANGE,
                              3: PRE_RANGE, 4: ATT_RANGE}
                for k, C in enumerate(phases):
                    C.bodies[eng_name](eng)
                    if eng_name == "gpsimd" and k in ag_spec:
                        for (ph, semname, cnt, cin, cout, ccs) in ag_spec[k]:
                            eng.wait_ge(ph.sems[semname], cnt)
                            eng.collective_compute(
                                "AllGather", ALU.bypass, replica_groups=rg4,
                                ins=[cin.opt()], outs=[cout.opt()],
                            ).then_inc(ccs)
                    eng.sem_inc(sBAR, 1)
                    eng.wait_ge(sBAR, 6 * k + 5)
                    if eng_name == "gpsimd":
                        if k in clear_spec:
                            eng.dma_reset(clear_spec[k])
                            eng.sem_clear(clear_spec[k])
                        eng.sem_inc(sBAR, 1)
                    eng.wait_ge(sBAR, 6 * k + 6)
            return body

        block.sync(run_bodies("sync"))
        block.gpsimd(run_bodies("gpsimd"))
        block.tensor(run_bodies("tensor"))
        block.scalar(run_bodies("scalar"))
        block.vector(run_bodies("vector"))

    return nc


def kernel_fused(inputs):
    cN = np.ascontiguousarray
    x = cN(inputs["L0_attn_hidden_states"].reshape(2 * S, D)).astype(np.float32)
    Wqkv = inputs["L0_attn_W_qkv"]
    bqkv = inputs["L0_attn_b_qkv"]
    wq, wk, wv = inputs["L1_attn_wq"], inputs["L1_attn_wk"], inputs["L1_attn_wv"]
    wsc = inputs["L1_ffn_Wscale_ug"]
    maps = []
    shared = {
        "wd": cN(inputs["L0_attn_W_dense"].reshape(4, 4, 128, D).transpose(1, 0, 2, 3).reshape(D, D)),
        "bd": cN(inputs["L0_attn_b_dense"]),
        "w1t0": cN(inputs["L0_ffn_W1"].T),
        "w3t0": cN(inputs["L0_ffn_W3"].T),
        "w2t0": cN(inputs["L0_ffn_W2"].T),
        "tau": cN(inputs["L1_attn_tau"]),
        "nu": cN(inputs["L1_attn_nu"]),
        "su_ug": cN(inputs["L1_ffn_SU_ug"]),
        "wsu": cN(wsc[:HF]), "wsg": cN(wsc[HF:]),
        "sdp": cN(inputs["L1_ffn_SU_dp"]),
        "wdp": cN(inputs["L1_ffn_Wscale_dp"]),
        "w1t1": cN(inputs["L1_ffn_W_up"].T),
        "w3t1": cN(inputs["L1_ffn_W_gate"].T),
        "w2t1": cN(inputs["L1_ffn_W_down"].T),
    }
    for c in range(NCORES):
        g = c % 4
        cols = np.concatenate([
            Wqkv[:, 512 * g:512 * (g + 1)],
            Wqkv[:, 2048 + 512 * g:2048 + 512 * (g + 1)],
            Wqkv[:, 4096 + 512 * g:4096 + 512 * (g + 1)]], axis=1)
        bsl = np.concatenate([
            bqkv[512 * g:512 * (g + 1)],
            bqkv[2048 + 512 * g:2048 + 512 * (g + 1)],
            bqkv[4096 + 512 * g:4096 + 512 * (g + 1)]])
        wsl1 = np.concatenate([
            cN(wq[512 * g:512 * (g + 1), :].T),
            cN(wk[512 * g:512 * (g + 1), :].T),
            cN(wv[512 * g:512 * (g + 1), :].T)], axis=1)
        m = dict(shared)
        m["x"] = cN(x[c * TOK:(c + 1) * TOK])
        m["wqkv_sl"] = cN(cols)
        m["bqkv_sl"] = cN(bsl)
        m["wsl1"] = cN(wsl1)
        m["myloc"] = np.array([g * TOK], dtype=np.int32)
        maps.append(m)
    nc = _get_nc("fused", build_fused)
    res = run_bass_kernel_spmd(nc, maps, core_ids=list(range(NCORES)))
    out = np.concatenate([res.results[c]["out"] for c in range(NCORES)], axis=0)
    return out.reshape(2, S, D).astype(np.float32)


_NC_CACHE = {}


def _get_nc(name, builder):
    if name not in _NC_CACHE:
        _NC_CACHE[name] = builder()
    return _NC_CACHE[name]


def _run(nc, in_maps):
    return run_bass_kernel_spmd(nc, in_maps, core_ids=list(range(NCORES)))


def kernel(**inputs):
    return kernel_fused(inputs)


def kernel_unfused(**inputs):
    cN = np.ascontiguousarray
    x = cN(inputs["L0_attn_hidden_states"].reshape(2 * S, D)).astype(np.float32)
    xs = [cN(x[c * TOK:(c + 1) * TOK]) for c in range(NCORES)]

    Wqkv = inputs["L0_attn_W_qkv"]
    bqkv = inputs["L0_attn_b_qkv"]
    Wd = cN(inputs["L0_attn_W_dense"])
    bd = cN(inputs["L0_attn_b_dense"])
    W1T = cN(inputs["L0_ffn_W1"].T)
    W2T = cN(inputs["L0_ffn_W2"].T)
    W3T = cN(inputs["L0_ffn_W3"].T)
    wq, wk, wv = inputs["L1_attn_wq"], inputs["L1_attn_wk"], inputs["L1_attn_wv"]
    tau = cN(inputs["L1_attn_tau"])
    nu = cN(inputs["L1_attn_nu"])
    su_ug = cN(inputs["L1_ffn_SU_ug"])
    sdp = cN(inputs["L1_ffn_SU_dp"])
    wsc = inputs["L1_ffn_Wscale_ug"]
    wsu = cN(wsc[:HF]); wsg = cN(wsc[HF:])
    wdp = cN(inputs["L1_ffn_Wscale_dp"])
    WupT = cN(inputs["L1_ffn_W_up"].T)
    WgateT = cN(inputs["L1_ffn_W_gate"].T)
    WdownT = cN(inputs["L1_ffn_W_down"].T)

    nc_pre = _get_nc("pre", build_pre)
    nc_att0 = _get_nc("att0", lambda: build_att(0))
    nc_att1 = _get_nc("att1", lambda: build_att(1))
    nc_post0 = _get_nc("post0", lambda: build_post(0))
    nc_post1 = _get_nc("post1", lambda: build_post(1))

    def gather_lnT(res):
        out = []
        for b in range(2):
            out.append(np.concatenate(
                [res.results[b * 4 + j]["lnT"] for j in range(4)], axis=1))
        return out

    r = _run(nc_pre, [{"x": xs[c]} for c in range(NCORES)])
    lnT_b = gather_lnT(r)

    att0_maps = []
    for c in range(NCORES):
        g = c % 4
        cols = np.concatenate([
            Wqkv[:, 512 * g:512 * (g + 1)],
            Wqkv[:, 2048 + 512 * g:2048 + 512 * (g + 1)],
            Wqkv[:, 4096 + 512 * g:4096 + 512 * (g + 1)]], axis=1)
        bsl = np.concatenate([
            bqkv[512 * g:512 * (g + 1)],
            bqkv[2048 + 512 * g:2048 + 512 * (g + 1)],
            bqkv[4096 + 512 * g:4096 + 512 * (g + 1)]])
        att0_maps.append({"lnT": cN(lnT_b[c // 4]), "w_sl": cN(cols), "bq": cN(bsl)})
    r = _run(nc_att0, att0_maps)
    attF_b = [np.concatenate([r.results[b * 4 + j]["att_c"] for j in range(4)], axis=0)
              for b in range(2)]

    post0_maps = []
    for c in range(NCORES):
        b, loc = c // 4, (c % 4) * TOK
        post0_maps.append({
            "x": xs[c],
            "attnT": cN(attF_b[b][:, loc:loc + TOK]),
            "wd": Wd, "bd": bd, "w1t": W1T, "w3t": W3T, "w2t": W2T})
    r = _run(nc_post0, post0_maps)
    x1s = [r.results[c]["x2"] for c in range(NCORES)]

    r = _run(nc_pre, [{"x": cN(x1s[c])} for c in range(NCORES)])
    lnT_b = gather_lnT(r)

    att1_maps = []
    for c in range(NCORES):
        g = c % 4
        wsl = np.concatenate([
            cN(wq[512 * g:512 * (g + 1), :].T),
            cN(wk[512 * g:512 * (g + 1), :].T),
            cN(wv[512 * g:512 * (g + 1), :].T)], axis=1)
        att1_maps.append({"lnT": cN(lnT_b[c // 4]), "w_sl": cN(wsl),
                          "tau": tau, "nu": nu})
    r = _run(nc_att1, att1_maps)
    attT_b = [np.concatenate([r.results[b * 4 + j]["att_c"] for j in range(4)], axis=1)
              for b in range(2)]

    post1_maps = []
    for c in range(NCORES):
        b, loc = c // 4, (c % 4) * TOK
        post1_maps.append({
            "x": cN(x1s[c]),
            "attn_tok": cN(attT_b[b][loc:loc + TOK, :]),
            "su_ug": su_ug, "wsu": wsu, "wsg": wsg, "sdp": sdp, "wdp": wdp,
            "w1t": WupT, "w3t": WgateT, "w2t": WdownT})
    r = _run(nc_post1, post1_maps)
    out = np.concatenate([r.results[c]["x2"] for c in range(NCORES)], axis=0)
    return out.reshape(2, S, D).astype(np.float32)


# version 10
# speedup vs baseline: 1.0128x; 1.0114x over previous
import numpy as np
import ml_dtypes
import concourse.bass as bass
import concourse.mybir as mybir
from concourse.bass_utils import run_bass_kernel_spmd

F32 = mybir.dt.float32
BF16 = mybir.dt.bfloat16
AF = mybir.ActivationFunctionType
ALU = mybir.AluOpType
AX = mybir.AxisListType

NCORES = 8
D = 2048
S = 4096
TOK = 1024
NT = TOK // 128
NDC = D // 128
HF = 8192
LN_EPS = 1e-5
L2_EPS = 1e-8
C0_SHIFT = 30.0

bfdt = ml_dtypes.bfloat16


def _hadamard(n):
    h = np.array([[1.0]])
    while h.shape[0] < n:
        h = np.block([[h, h], [h, -h]])
    return h


H128_NP = _hadamard(128)
BD64_NP = np.kron(np.eye(2), _hadamard(64))
BD16_NP = np.kron(np.eye(8), _hadamard(16))



def build_pre():
    nc = bass.Bass("TRN2", target_bir_lowering=False)
    x = nc.dram_tensor("x", [TOK, D], F32, kind="ExternalInput").ap()
    lnT = nc.dram_tensor("lnT", [D, TOK], BF16, kind="ExternalOutput").ap()
    ident_c = nc.inline_tensor(np.eye(128, dtype=bfdt), name="c_ident").ap()

    with (
        nc.sbuf_tensor("xt", [128, NT * D], F32) as xt,
        nc.sbuf_tensor("scr", [128, D], F32) as scr,
        nc.sbuf_tensor("ln", [128, D], BF16) as ln,
        nc.sbuf_tensor("lt", [128, 2 * D], BF16) as lt,
        nc.sbuf_tensor("sm", [128, NT], F32) as sm,
        nc.sbuf_tensor("sq", [128, NT], F32) as sq,
        nc.sbuf_tensor("st", [128, 4 * NT], F32) as st,
        nc.sbuf_tensor("ident", [128, 128], BF16) as ident,
        nc.psum_tensor("pst", [128, 2048], BF16) as pst,
        nc.semaphore("sL") as sL,
        nc.semaphore("sV") as sV,
        nc.semaphore("sA") as sA,
        nc.semaphore("sT") as sT,
        nc.semaphore("sE") as sE,
        nc.semaphore("sO") as sO,
        nc.Block() as block,
    ):
        n = NT
        mean, veps, rstd, nmrs = (st[:, 0:n], st[:, n:2 * n], st[:, 2 * n:3 * n], st[:, 3 * n:4 * n])

        @block.sync
        def _(e):
            e.dma_start(out=ident[:], in_=ident_c[:, :]).then_inc(sL, 16)
            e.dma_start(out=xt[:].rearrange("p (tt d) -> p tt d", d=D),
                        in_=x.rearrange("(tt p) d -> p tt d", p=128)).then_inc(sL, 16)
            for tt in range(NT):
                e.wait_ge(sE, 4 * tt + 4)
                e.dma_start(
                    out=lnT.rearrange("(c p) t -> p c t", p=128)[:, :, tt * 128:(tt + 1) * 128],
                    in_=lt[:, (tt % 2) * D:(tt % 2 + 1) * D].rearrange("p (c t) -> p c t", t=128),
                ).then_inc(sO, 16)
            e.wait_ge(sO, 16 * NT)

        @block.scalar
        def _(s):
            s.wait_ge(sL, 32)
            for tt in range(NT):
                s.activation(scr[:], xt[:, tt * D:(tt + 1) * D], AF.Square,
                             accum_out=sq[:, tt:tt + 1]).then_inc(sA, 1)
            s.wait_ge(sV, 1)
            s.activation(veps, veps, AF.Sqrt).then_inc(sA, 1)
            s.wait_ge(sV, 2)
            for tt in range(NT):
                if tt >= 1:
                    s.wait_ge(sT, 4 * tt)
                s.activation(ln[:], xt[:, tt * D:(tt + 1) * D], AF.Identity,
                             bias=nmrs[:, tt:tt + 1],
                             scale=rstd[:, tt:tt + 1]).then_inc(sA, 1)

        @block.vector
        def _(v):
            v.wait_ge(sL, 32)
            for tt in range(NT):
                v.reduce_sum(sm[:, tt:tt + 1], xt[:, tt * D:(tt + 1) * D], axis=AX.X)
            v.wait_ge(sA, NT)
            v.nop()
            v.tensor_scalar(mean, sm[:], 1.0 / D, None, ALU.mult)
            v.tensor_scalar(veps, sq[:], 1.0 / D, LN_EPS, ALU.mult, ALU.add)
            v.nop()
            v.tensor_tensor(rstd, mean, mean, ALU.mult)
            v.nop()
            v.tensor_sub(veps, veps, rstd).then_inc(sV, 1)
            v.wait_ge(sA, NT + 1)
            v.nop()
            v.reciprocal(rstd, veps)
            v.nop()
            v.tensor_tensor(nmrs, mean, rstd, ALU.mult)
            v.nop()
            v.tensor_scalar(nmrs, nmrs, -1.0, None, ALU.mult).then_inc(sV, 1)
            for tt in range(NT):
                for grp in range(4):
                    gi = tt * 4 + grp
                    v.wait_ge(sT, gi + 1)
                    if tt >= 2:
                        v.wait_ge(sO, 16 * (tt - 1))
                    v.tensor_copy(lt[:, (tt % 2) * D + grp * 512:(tt % 2) * D + (grp + 1) * 512],
                                  pst[:, (gi % 2) * 1024:(gi % 2) * 1024 + 512]).then_inc(sE, 1)

        @block.tensor
        def _(t):
            t.wait_ge(sL, 32)
            for tt in range(NT):
                t.wait_ge(sA, NT + 2 + tt)
                for grp in range(4):
                    gi = tt * 4 + grp
                    if gi >= 2:
                        t.wait_ge(sE, gi - 1)
                    ins = None
                    for j in range(4):
                        fc = grp * 4 + j
                        ins = t.transpose(pst[:, (gi % 2) * 1024 + j * 128:(gi % 2) * 1024 + (j + 1) * 128],
                                          ln[:, fc * 128:(fc + 1) * 128], ident[:])
                    ins.then_inc(sT, 1)

    return nc




def build_att(layer, debug=False, p2=True):
    import os
    DBGV = os.environ.get("ATTDBG", "")
    nc = bass.Bass("TRN2", target_bir_lowering=False)
    lnT = nc.dram_tensor("lnT", [D, S], BF16, kind="ExternalInput").ap()
    w_sl = nc.dram_tensor("w_sl", [D, 1536], F32, kind="ExternalInput").ap()
    if layer == 0:
        bq = nc.dram_tensor("bq", [1536], F32, kind="ExternalInput").ap()
        att_c = nc.dram_tensor("att_c", [512, S], BF16, kind="ExternalOutput").ap()
    else:
        tau_d = nc.dram_tensor("tau", [1], F32, kind="ExternalInput").ap()
        nu_d = nc.dram_tensor("nu", [1], F32, kind="ExternalInput").ap()
        att_c = nc.dram_tensor("att_c", [S, 512], BF16, kind="ExternalOutput").ap()
    if debug:
        qf_d = nc.dram_tensor("qf_d", [128, 4 * S], BF16, kind="ExternalOutput").ap()
        kf_d = nc.dram_tensor("kf_d", [128, 4 * S], BF16, kind="ExternalOutput").ap()
        vt_d = nc.dram_tensor("vt_d", [128, 32 * 528], BF16, kind="ExternalOutput").ap()

    ident_c = nc.inline_tensor(np.eye(128, dtype=bfdt), name="c_ident").ap()
    mask_np = (np.arange(128)[:, None] <= np.arange(128)[None, :]).astype(bfdt)
    mask_c = nc.inline_tensor(mask_np, name="c_mask").ap()
    ones_f = nc.inline_tensor(np.ones((1, 128), dtype=np.float32), name="c_onesf").ap()
    ones_b = nc.inline_tensor(np.ones((1, 128), dtype=bfdt), name="c_onesb").ap()

    NTB = 8
    NQB = 8
    NH = 4

    def kts(qb):
        return 4 * qb + 4

    def hq(h, qb):
        return h * NQB + qb

    def exp_base(h, qb):
        return sum(kts(i % NQB) for i in range(hq(h, qb)))

    smm_ct, pv_ct, rb_ct, pvqt_ct = {}, {}, {}, {}
    ctr = 0
    for i in range(NH * NQB):
        nk_i = kts(i % NQB)
        for kt in range(nk_i):
            ctr += 1
            smm_ct[(i, kt)] = ctr
        if layer == 0:
            if i >= 1:
                ctr += 1
                rb_ct[i - 1] = ctr
            ctr += 1
            pv_ct[i] = ctr
        else:
            for qt in range(4):
                ctr += 1
                pvqt_ct[(i, qt)] = ctr
    if layer == 0:
        ctr += 1
        rb_ct[NH * NQB - 1] = ctr
    qkv_ct, tr_ct = {}, {}
    if layer == 1:
        c2 = 0
        c2 += 1; qkv_ct[0] = c2
        for gt in range(1, 32):
            c2 += 1; qkv_ct[gt] = c2
            c2 += 1; tr_ct[gt - 1] = c2
        c2 += 1; tr_ct[31] = c2

    P1_PE_TB = 12 if layer == 0 else 8
    SCTOT = 80 if layer == 0 else 96
    SWTOT = 32 if layer == 0 else 16

    import contextlib
    es = contextlib.ExitStack()
    with es:
        sb = lambda *a: es.enter_context(nc.sbuf_tensor(*a))
        ps = lambda *a: es.enter_context(nc.psum_tensor(*a))
        sem = lambda n: es.enter_context(nc.semaphore(n))
        QF = sb("QF", [128, NH * S], BF16)
        KF = sb("KF", [128, NH * S], BF16)
        VT = sb("VT", [128, 32 * 528], BF16)
        AR = sb("AR", [128, 41984], BF16)
        PT = AR[:, 0:16384]
        lnb = AR[:, 0:16384]
        ws = AR[:, 16384:40960]
        qtok = AR[:, 40960:41984]
        scrq = sb("scrq", [128, 1536], F32)
        ident = sb("ident", [128, 128], BF16)
        mask01 = sb("mask01", [128, 128], BF16)
        ones128 = sb("ones128", [1, 128], F32)
        bqs = sb("bqs", [128, 12], F32)
        bvrow = sb("bvrow", [1, 512], BF16)
        taunu = sb("taunu", [1, 2], F32)
        tnb = sb("tnb", [128, 4], F32)
        csh = sb("csh", [128, 2], F32)
        ssq = sb("ssq", [128, 36], F32)
        rB = sb("rB", [128, 512], F32)
        rr = sb("rr", [1, 512], F32)
        onesb = sb("onesb", [1, 128], BF16)
        attnb = sb("attnb", [128, 512], BF16)
        psA = ps("psA", [128, 4096], F32)
        sL = sem("sL"); sP0 = sem("sP0"); sPE = sem("sPE"); sA1 = sem("sA1")
        sC = sem("sC"); sB0 = sem("sB0"); sB1 = sem("sB1"); sW = sem("sW"); sD = sem("sD")
        sV1 = sem("sV1"); sT2 = sem("sT2"); sX = sem("sX"); sM = sem("sM")
        sV2 = sem("sV2"); sO = sem("sO")
        block = es.enter_context(nc.Block())
        psQK = psA[:, 0:1024]
        psV = psA[:, 1024:2048]
        psQ1 = psA[:, 0:512]
        psK1 = psA[:, 512:1024]
        psV1_ = psA[:, 1024:1536]
        psT1 = psA[:, 1536:2560].bitcast(BF16)
        psS = psA[:, 0:2048]
        psO = psA[:, 2048:2560]
        psR = psA[:, 2560:3072]
        psSum = psA[0:1, 3072:3584]
        psO3 = psA[:, 2048:4096]
        psB0 = psA[:, 3584:4096]

        @block.sync
        def _(e):
            e.dma_start(out=ident[:], in_=ident_c[:, :]).then_inc(sC, 16)
            e.dma_start(out=mask01[:], in_=mask_c[:, :]).then_inc(sC, 16)
            e.dma_start(out=ones128[:], in_=ones_f[:, :]).then_inc(sC, 16)
            e.dma_start(out=onesb[:], in_=ones_b[:, :]).then_inc(sC, 16)
            if layer == 0:
                e.dma_start(out=bqs[:, 0:12],
                            in_=bq.rearrange("(o p) -> p o", p=128),
                            allow_slow_non_contiguous=True).then_inc(sC, 16)
            else:
                e.dma_start(out=taunu[0:1, 0:1], in_=tau_d.unsqueeze(0)).then_inc(sC, 16)
                e.dma_start(out=taunu[0:1, 1:2], in_=nu_d.unsqueeze(0)).then_inc(sC, 16)
            for tb in range(NTB):
                if tb >= 2:
                    e.wait_ge(sPE, (tb - 1) * P1_PE_TB)
                e.dma_start(
                    out=lnb[:, (tb % 2) * NDC * 512:(tb % 2 + 1) * NDC * 512].rearrange("p (c t) -> p c t", t=512),
                    in_=lnT.rearrange("(c p) t -> p c t", p=128)[:, :, tb * 512:(tb + 1) * 512],
                ).then_inc(sB0 if tb % 2 == 0 else sB1, 16)
            nst = 0
            for h in range(NH):
                if not p2:
                    break
                for qb in range(NQB):
                    i = hq(h, qb)
                    if layer == 0:
                        e.wait_ge(sV2, 2 * i + 2)
                        e.dma_start(out=att_c[h * 128:(h + 1) * 128, qb * 512:(qb + 1) * 512],
                                    in_=attnb[:]).then_inc(sO, 16)
                        nst += 16
                    else:
                        for qt in range(4):
                            e.wait_ge(sV2, 4 * i + qt + 1)
                            e.dma_start(
                                out=att_c[qb * 512 + qt * 128:qb * 512 + (qt + 1) * 128,
                                          h * 128:(h + 1) * 128],
                                in_=attnb[:, qt * 128:(qt + 1) * 128]).then_inc(sO, 16)
                            nst += 16
            if debug:
                e.wait_ge(sPE, NTB * P1_PE_TB)
                if layer == 0:
                    e.wait_ge(sA1, 64)
                    e.wait_ge(sV1, 32)
                else:
                    e.wait_ge(sV1, 128)
                e.dma_start(out=qf_d[:, :], in_=QF[:]).then_inc(sO, 16)
                e.dma_start(out=kf_d[:, :], in_=KF[:]).then_inc(sO, 16)
                e.dma_start(out=vt_d[:, :], in_=VT[:]).then_inc(sO, 16)
                nst += 48
            e.wait_ge(sO, nst)

        @block.gpsimd
        def _(g):
            g.dma_start(out=ws[:].rearrange("p (c o) -> p c o", o=1536),
                        in_=w_sl.rearrange("(c p) o -> p c o", p=128)).then_inc(sW, 16)
            if layer == 0:
                g.dma_start(out=bvrow[0:1, :], in_=bq[1024:1536].unsqueeze(0)).then_inc(sW, 16)

        @block.tensor
        def _(t):
            t.wait_ge(sC, SCTOT)
            t.wait_ge(sW, SWTOT)
            if layer == 1:
                if "notau" in DBGV:
                    t.nop().then_inc(sP0, 1)
                else:
                    t.matmul(psB0[:, 0:2], ones128[0:1, :], taunu[0:1, :],
                             start=True, stop=True).then_inc(sP0, 1)
            for tb in range(NTB):
                t.wait_ge(sB0 if tb % 2 == 0 else sB1, 16 * (tb // 2 + 1))
                if layer == 0:
                    lnv = lnb[:, (tb % 2) * NDC * 512:].rearrange("p (c t) -> p c t", t=512)
                    for of in range(8):
                        if 8 * tb + of >= 2:
                            t.wait_ge(sA1, 8 * tb + of - 1)
                        pso = psQK[:, (of % 2) * 512:(of % 2) * 512 + 512]
                        for dc in range(NDC):
                            ins = t.matmul(pso, ws[:, dc * 1536 + of * 128:dc * 1536 + (of + 1) * 128],
                                           lnv[:, dc, :], start=(dc == 0), stop=(dc == NDC - 1))
                        ins.then_inc(sPE, 1)
                    for tt in range(4):
                        if 4 * tb + tt >= 2:
                            t.wait_ge(sV1, 4 * tb + tt - 1)
                        psv = psV[:, (tt % 2) * 512:(tt % 2) * 512 + 512]
                        for dc in range(NDC):
                            t.matmul(psv, lnb[:, (tb % 2) * NDC * 512 + dc * 512 + tt * 128:
                                               (tb % 2) * NDC * 512 + dc * 512 + (tt + 1) * 128],
                                     ws[:, dc * 1536 + 1024:dc * 1536 + 1536],
                                     start=(dc == 0), stop=False)
                        t.matmul(psv, onesb[0:1, :], bvrow[0:1, :],
                                 start=False, stop=True).then_inc(sPE, 1)
                else:
                    for tt in range(4):
                        gt = tb * 4 + tt
                        if gt == 0:
                            t.wait_ge(sP0, 3)
                        if gt >= 1:
                            t.wait_ge(sV1, 4 * (gt - 1) + 3)
                        for pj, pso in enumerate((psQ1, psK1, psV1_)):
                            for dc in range(NDC):
                                lhs = lnb[:, (tb % 2) * NDC * 512 + dc * 512 + tt * 128:
                                             (tb % 2) * NDC * 512 + dc * 512 + (tt + 1) * 128]
                                ins = t.matmul(pso, lhs,
                                               ws[:, dc * 1536 + pj * 512:dc * 1536 + (pj + 1) * 512],
                                               start=(dc == 0), stop=(dc == NDC - 1))
                        ins.then_inc(sPE, 1)
                        t.wait_ge(sV1, 4 * gt + 3)
                        if "notr" in DBGV:
                            t.nop().then_inc(sPE, 1)
                        else:
                            ins = None
                            for j in range(4):
                                ins = t.transpose(psT1[:, j * 128:(j + 1) * 128],
                                                  qtok[:, j * 128:(j + 1) * 128], ident[:])
                            for j in range(4):
                                ins = t.transpose(psT1[:, 512 + j * 128:512 + (j + 1) * 128],
                                                  qtok[:, 512 + j * 128:512 + (j + 1) * 128], ident[:])
                            ins.then_inc(sPE, 1)
            if not p2:
                return
            t.wait_ge(sA1 if layer == 0 else sV1, 64 if layer == 0 else 128)
            t.wait_ge(sV1 if layer == 0 else sV1, 32 if layer == 0 else 128)
            for h in range(NH):
                for qb in range(NQB):
                    i = hq(h, qb)
                    nk = kts(qb)
                    eb = exp_base(h, qb)
                    pb = pe_base(h, qb)
                    for kt in range(nk):
                        if kt >= 4:
                            t.wait_ge(sX, eb + kt - 3)
                        j = kt - 4 * qb
                        n0 = 128 * max(0, j)
                        t.matmul(psS[:, (kt % 4) * 512 + n0:(kt % 4) * 512 + 512],
                                 KF[:, h * S + kt * 128:h * S + (kt + 1) * 128],
                                 QF[:, h * S + qb * 512 + n0:h * S + (qb + 1) * 512],
                                 start=True, stop=True).then_inc(sT2, 1)
                    t.wait_ge(sX, eb + nk)
                    t.wait_ge(sM, 4 * i + 5)
                    if layer == 0:
                        t.wait_ge(sV2, 2 * i)
                        for kt in range(nk):
                            j = kt - 4 * qb
                            n0 = 128 * max(0, j)
                            vtb = VT[:, kt * 528 + h * 132:]
                            t.matmul(psO[:, n0:512], vtb[:, 0:128],
                                     PT[:, kt * 512 + n0:(kt + 1) * 512],
                                     start=(kt == 0), stop=(kt == nk - 1))
                            ins = t.matmul(psSum[:, n0:512], vtb[:, 128:129],
                                            PT[:, kt * 512 + n0:(kt + 1) * 512],
                                            start=(kt == 0), stop=(kt == nk - 1))
                        ins.then_inc(sT2, 1)
                        t.wait_ge(sV2, 2 * i + 1)
                        t.matmul(psR[:], ones128[0:1, :], rr[0:1, :],
                                 start=True, stop=True).then_inc(sT2, 1)
                    else:
                        t.wait_ge(sV2, 4 * i)
                        for qt in range(4):
                            qtg = 4 * qb + qt
                            for kt in range(qtg + 1):
                                vtb = VT[:, kt * 528 + h * 132:]
                                ins = t.matmul(psO3[:, qt * 512:qt * 512 + 129],
                                               PT[:, kt * 512 + qt * 128:kt * 512 + (qt + 1) * 128],
                                               vtb[:, 0:129],
                                               start=(kt == 0), stop=(kt == qtg))
                            ins.then_inc(sT2, 1)

        @block.scalar
        def _(s):
            if layer == 1:
                s.wait_ge(sP0, 1)
                if "notau" in DBGV:
                    s.nop().then_inc(sP0, 1)
                else:
                    s.activation(tnb[:, 3:4], psB0[:, 0:1], AF.Abs).then_inc(sP0, 1)
            if layer == 0:
                for tb in range(NTB):
                    for of in range(8):
                        s.wait_ge(sPE, tb * 12 + of + 1)
                        dst = (QF if of < 4 else KF)[:, (of % 4) * S + tb * 512:(of % 4) * S + (tb + 1) * 512]
                        s.activation(dst, psQK[:, (of % 2) * 512:(of % 2) * 512 + 512],
                                     AF.Identity, bias=bqs[:, of:of + 1], scale=1.0).then_inc(sA1, 1)
            else:
                for tb in range(NTB):
                    for tt in range(4):
                        gt = tb * 4 + tt
                        s.wait_ge(sPE, 2 * gt + 1)
                        if gt >= 1:
                            s.wait_ge(sV1, 4 * (gt - 1) + 1)
                        s.activation(scrq[:, 0:512], psQ1, AF.Square).then_inc(sA1, 1)
                        s.activation(scrq[:, 512:1024], psK1, AF.Square).then_inc(sA1, 1)
                        s.activation(scrq[:, 1024:1536], psV1_, AF.Square).then_inc(sA1, 1)
                        s.wait_ge(sV1, 4 * gt + 1)
                        s.activation(ssq[:, 12:24], ssq[:, 0:12], AF.Sqrt).then_inc(sA1, 1)
            if not p2:
                return
            s.wait_ge(sM, 1)
            if layer == 1:
                s.wait_ge(sP0, 3)
            for h in range(NH):
                for qb in range(NQB):
                    i = hq(h, qb)
                    nk = kts(qb)
                    eb = exp_base(h, qb)
                    pb = pe_base(h, qb)
                    for kt in range(nk):
                        s.wait_ge(sT2, pb + kt + 1)
                        j = kt - 4 * qb
                        n0 = 128 * max(0, j)
                        if layer == 0:
                            s.activation(PT[:, kt * 512 + n0:(kt + 1) * 512],
                                         psS[:, (kt % 4) * 512 + n0:(kt % 4) * 512 + 512],
                                         AF.Exp, bias=csh[:, 0:1],
                                         scale=csh[:, 1:2]).then_inc(sX, 1)
                        else:
                            s.activation(PT[:, kt * 512 + n0:(kt + 1) * 512],
                                         psS[:, (kt % 4) * 512 + n0:(kt % 4) * 512 + 512],
                                         AF.Exp, bias=tnb[:, 1:2],
                                         scale=tnb[:, 0:1]).then_inc(sX, 1)

        @block.vector
        def _(v):
            nd = [0]

            def dedge(ins):
                nd[0] += 1
                ins.then_inc(sD, 1)
                v.wait_ge(sD, nd[0])

            v.wait_ge(sC, SCTOT)
            v.wait_ge(sW, SWTOT)
            v.memset(VT[:], 1.0)
            v.memset(csh[:, 0:1], -C0_SHIFT)
            v.memset(csh[:, 1:2], 0.08838834764831843).then_inc(sM, 1)
            if layer == 1:
                v.wait_ge(sP0, 2)
                if "notau" in DBGV:
                    v.memset(tnb[:, 0:1], 1.0)
                    v.memset(tnb[:, 1:2], -1.0)
                    v.memset(tnb[:, 2:3], 1.0).then_inc(sP0, 1)
                else:
                    v.tensor_copy(tnb[:, 0:1], psB0[:, 0:1])
                    v.nop()
                    v.tensor_scalar(tnb[:, 1:2], tnb[:, 3:4], -1.0, None, ALU.mult)
                    v.tensor_copy(tnb[:, 2:3], psB0[:, 1:2]).then_inc(sP0, 1)
            if layer == 0:
                for tb in range(NTB):
                    for tt in range(4):
                        v.wait_ge(sM, 1)
                        v.wait_ge(sPE, tb * 12 + 8 + tt + 1)
                        gt = tb * 4 + tt
                        dst = VT[:, gt * 528:(gt + 1) * 528].rearrange("p (h d) -> p h d", d=132)[:, :, 0:128]
                        srcv = psV[:, (tt % 2) * 512:(tt % 2) * 512 + 512].rearrange("p (h d) -> p h d", d=128)
                        v.tensor_copy(dst, srcv).then_inc(sV1, 1)
            else:
                for tb in range(NTB):
                    for tt in range(4):
                        gt = tb * 4 + tt
                        v.wait_ge(sM, 1)
                        v.wait_ge(sA1, 4 * gt + 3)
                        for pj in range(3):
                            ins = v.tensor_reduce(ssq[:, pj * 4:(pj + 1) * 4].unsqueeze(2),
                                                  scrq[:, pj * 512:(pj + 1) * 512].rearrange("p (h d) -> p h d", d=128),
                                                  axis=AX.X, op=ALU.add)
                        ins.then_inc(sV1, 1)
                        v.wait_ge(sA1, 4 * gt + 4)
                        v.nop()
                        dedge(v.tensor_scalar(ssq[:, 24:36], ssq[:, 12:24], L2_EPS, None, ALU.add))
                        v.reciprocal(ssq[:, 12:24], ssq[:, 24:36]).then_inc(sV1, 1)
                        v.wait_ge(sV1, 4 * gt + 2)
                        rq = ssq[:, 12:16].unsqueeze(2).broadcast_to([128, 4, 128])
                        rk = ssq[:, 16:20].unsqueeze(2).broadcast_to([128, 4, 128])
                        rv = ssq[:, 20:24].unsqueeze(2).broadcast_to([128, 4, 128])
                        v.tensor_tensor(qtok[:, 0:512].rearrange("p (h d) -> p h d", d=128),
                                        psQ1.rearrange("p (h d) -> p h d", d=128), rq, ALU.mult)
                        v.tensor_tensor(qtok[:, 512:1024].rearrange("p (h d) -> p h d", d=128),
                                        psK1.rearrange("p (h d) -> p h d", d=128), rk, ALU.mult)
                        dstv = VT[:, gt * 528:(gt + 1) * 528].rearrange("p (h d) -> p h d", d=132)[:, :, 0:128]
                        v.tensor_tensor(dstv, psV1_.rearrange("p (h d) -> p h d", d=128), rv, ALU.mult).then_inc(sV1, 1)
                        v.wait_ge(sPE, 2 * gt + 2)
                        if "notr" in DBGV:
                            v.nop().then_inc(sV1, 1)
                        else:
                            v.tensor_copy(QF[:].rearrange("p (h t) -> p h t", t=S)[:, :, gt * 128:(gt + 1) * 128],
                                          psT1[:, 0:512].rearrange("p (h d) -> p h d", d=128))
                            v.tensor_copy(KF[:].rearrange("p (h t) -> p h t", t=S)[:, :, gt * 128:(gt + 1) * 128],
                                          psT1[:, 512:1024].rearrange("p (h d) -> p h d", d=128)).then_inc(sV1, 1)
            if not p2:
                return
            for h in range(NH):
                for qb in range(NQB):
                    i = hq(h, qb)
                    nk = kts(qb)
                    eb = exp_base(h, qb)
                    pb = pe_base(h, qb)
                    for jj in range(4):
                        kt = 4 * qb + jj
                        v.wait_ge(sX, eb + kt + 1)
                        dg = PT[:, kt * 512 + jj * 128:kt * 512 + (jj + 1) * 128]
                        v.tensor_tensor(dg, dg, mask01[:], ALU.mult).then_inc(sM, 1)
                    if layer == 0:
                        v.wait_ge(sT2, pb + nk + 1)
                        v.nop()
                        v.reciprocal(rr[0:1, :], psSum[:]).then_inc(sV2, 1)
                        v.wait_ge(sT2, pb + nk + 2)
                        if i >= 1:
                            v.wait_ge(sO, 16 * i)
                        dedge(v.tensor_copy(rB[:], psR[:]))
                        v.tensor_tensor(attnb[:], psO[:], rB[:], ALU.mult).then_inc(sV2, 1)
                    else:
                        for qt in range(4):
                            v.wait_ge(sT2, pb + nk + qt + 1)
                            v.wait_ge(sV2, 4 * i + qt)
                            if i >= 1:
                                v.wait_ge(sO, 64 * i)
                            dedge(v.reciprocal(tnb[:, 3:4], psO3[:, qt * 512 + 128:qt * 512 + 129]))
                            dedge(v.tensor_tensor(tnb[:, 3:4], tnb[:, 3:4], tnb[:, 2:3], ALU.mult))
                            v.tensor_scalar(attnb[:, qt * 128:(qt + 1) * 128],
                                            psO3[:, qt * 512:qt * 512 + 128],
                                            tnb[:, 3:4], None, ALU.mult).then_inc(sV2, 1)

    return nc




def build_post(layer):
    nc = bass.Bass("TRN2", target_bir_lowering=False)
    x_in = nc.dram_tensor("x", [TOK, D], F32, kind="ExternalInput").ap()
    if layer == 0:
        attnT = nc.dram_tensor("attnT", [D, TOK], BF16, kind="ExternalInput").ap()
        wd_d = nc.dram_tensor("wd", [D, D], F32, kind="ExternalInput").ap()
        bd_d = nc.dram_tensor("bd", [D], F32, kind="ExternalInput").ap()
    else:
        attn_tok = nc.dram_tensor("attn_tok", [TOK, D], BF16, kind="ExternalInput").ap()
        su_d = nc.dram_tensor("su_ug", [D], F32, kind="ExternalInput").ap()
        wsu_d = nc.dram_tensor("wsu", [HF], F32, kind="ExternalInput").ap()
        wsg_d = nc.dram_tensor("wsg", [HF], F32, kind="ExternalInput").ap()
        sdp_d = nc.dram_tensor("sdp", [HF], F32, kind="ExternalInput").ap()
        wdp_d = nc.dram_tensor("wdp", [D], F32, kind="ExternalInput").ap()
    w1_d = nc.dram_tensor("w1t", [D, HF], F32, kind="ExternalInput").ap()
    w3_d = nc.dram_tensor("w3t", [D, HF], F32, kind="ExternalInput").ap()
    w2_d = nc.dram_tensor("w2t", [HF, D], F32, kind="ExternalInput").ap()
    x2_out = nc.dram_tensor("x2", [TOK, D], F32, kind="ExternalOutput").ap()
    x1_dram = nc.dram_tensor("x1buf", [TOK, D], F32, kind="Internal").ap()
    bounce8 = nc.dram_tensor("bounce8", [HF, TOK], BF16, kind="Internal").ap()
    if layer == 1:
        bounce2 = nc.dram_tensor("bounce2", [D, TOK], BF16, kind="Internal").ap()

    ident_c = nc.inline_tensor(np.eye(128, dtype=bfdt), name="c_ident").ap()
    h128_c = nc.inline_tensor(H128_NP.astype(bfdt), name="c_h128").ap()
    bd64_c = nc.inline_tensor(BD64_NP.astype(bfdt), name="c_bd64").ap()
    bd16_c = nc.inline_tensor(BD16_NP.astype(bfdt), name="c_bd16").ap()

    FWN = float(1.0 / np.sqrt(8192.0))
    FWN2 = float(1.0 / np.sqrt(2048.0))
    NHC = 64
    NSB = 16
    VA_TT = 4 if layer == 0 else 1

    import contextlib
    es = contextlib.ExitStack()
    with es:
        sb = lambda *a: es.enter_context(nc.sbuf_tensor(*a))
        psal = lambda *a: es.enter_context(nc.psum_tensor(*a))
        sem = lambda n: es.enter_context(nc.semaphore(n))

        lnF = sb("lnF", [128, NDC * 1024], BF16)
        AR = sb("AR", [128, 65536], BF16)
        WX = sb("WX", [128, 16384], BF16)
        x1cb = sb("x1cb", [128, 2 * 1024], F32)
        ysb = sb("ysb", [128, 1024], BF16)
        x2b = sb("x2b", [128, 2 * 128], F32)
        consts = sb("consts", [128, 512], BF16)
        pcol = sb("pcol", [128, 32], F32)
        pcol2 = sb("pcol2", [128, 192], F32)
        smq = sb("smq", [128, 2 * NT], F32)
        stt = sb("stt", [128, 4 * NT], F32)
        psA = psal("psA", [128, 4096], F32)

        ident = consts[:, 0:128]
        h128k = consts[:, 128:256]
        bd64k = consts[:, 256:384]
        bd16k = consts[:, 384:512]

        atT = AR[:, 0:16384]
        dnsF = AR[:, 16384:32768]
        wdstr = AR[:, 32768:36864]
        at1 = AR[:, 32768:36864]
        a32 = AR[:, 16384:20480].bitcast(F32)
        xinA = AR[:, 36864:45056].bitcast(F32)
        x1tA = AR[:, 45056:53248].bitcast(F32)
        xtB = AR[:, 0:32768].bitcast(F32)
        lntB = AR[:, 32768:34816]
        scrB = AR[:, 53248:57344].bitcast(F32)
        scrB2 = AR[:, 57344:61440].bitcast(F32)
        wstr = AR[:, 0:32768]
        sgb = AR[:, 32768:33792]
        hck = AR[:, 33792:34816]
        y1b = AR[:, 34816:36864]
        c1b = AR[:, 36864:38912]
        h2 = AR[:, 0:65536]
        rcb = WX[:, 0:2048]
        w2str = WX[:, 0:16384]

        psF = psA[:, 0:2048]
        psTd = psA[:, 2048:3072].bitcast(BF16)
        psU = psA[:, 0:1024]
        psG = psA[:, 1024:2048]
        psH = psA[:, 2048:3072]
        psB2 = psA[:, 3072:4096]
        psY = psA[:, 0:1024]

        sC = sem("sC"); sAT = sem("sAT")
        sATa = sem("sATa"); sATb = sem("sATb")
        sXa = sem("sXa"); sXb = sem("sXb"); sXB = sem("sXB")
        sWDa = sem("sWDa"); sWDb = sem("sWDb")
        sW2a = sem("sW2a"); sW2b = sem("sW2b")
        sXCa = sem("sXCa"); sXCb = sem("sXCb")
        sPA = sem("sPA"); sEA = sem("sEA"); sVA = sem("sVA")
        sPB = sem("sPB"); sAB = sem("sAB"); sVB = sem("sVB")
        sP1 = sem("sP1"); sV1b = sem("sV1b")
        sP2 = sem("sP2"); sA3 = sem("sA3"); sV3 = sem("sV3")
        sP5 = sem("sP5"); sV5 = sem("sV5")
        sP6 = sem("sP6"); sA6 = sem("sA6"); sV6 = sem("sV6")
        sX1Oa = sem("sX1Oa"); sX1Ob = sem("sX1Ob"); sB8wa = sem("sB8wa"); sB8wb = sem("sB8wb")
        sB8ra = sem("sB8ra"); sB8rb = sem("sB8rb")
        sB2wa = sem("sB2wa"); sB2wb = sem("sB2wb"); sB2ra = sem("sB2ra"); sB2rb = sem("sB2rb")
        sOa = sem("sOa"); sOb = sem("sOb"); sD = sem("sD")
        block = es.enter_context(nc.Block())

        SCTOT = 80 if layer == 0 else 144
        W1BASE = 8 * 16 if layer == 0 else 0

        @block.sync
        def _(e):
            e.dma_start(out=consts[:, 0:128], in_=ident_c[:, :]).then_inc(sC, 16)
            e.dma_start(out=consts[:, 128:256], in_=h128_c[:, :]).then_inc(sC, 16)
            e.dma_start(out=consts[:, 256:384], in_=bd64_c[:, :]).then_inc(sC, 16)
            e.dma_start(out=consts[:, 384:512], in_=bd16_c[:, :]).then_inc(sC, 16)
            if layer == 0:
                e.dma_start(out=pcol[:, 0:16], in_=bd_d.rearrange("(c p) -> p c", p=128),
                            allow_slow_non_contiguous=True).then_inc(sC, 16)
                e.dma_start(out=atT[:].rearrange("p (c t) -> p c t", t=1024),
                            in_=attnT.rearrange("(c p) t -> p c t", p=128)).then_inc(sAT, 16)
            else:
                e.dma_start(out=pcol[:, 0:16], in_=su_d.rearrange("(c p) -> p c", p=128),
                            allow_slow_non_contiguous=True).then_inc(sC, 16)
                e.dma_start(out=pcol[:, 16:32], in_=wdp_d.rearrange("(c p) -> p c", p=128),
                            allow_slow_non_contiguous=True).then_inc(sC, 16)
                e.dma_start(out=pcol2[:, 0:64], in_=wsu_d.rearrange("(c p) -> p c", p=128),
                            allow_slow_non_contiguous=True).then_inc(sC, 16)
                e.dma_start(out=pcol2[:, 64:128], in_=wsg_d.rearrange("(c p) -> p c", p=128),
                            allow_slow_non_contiguous=True).then_inc(sC, 16)
                e.dma_start(out=pcol2[:, 128:192], in_=sdp_d.rearrange("(c p) -> p c", p=128),
                            allow_slow_non_contiguous=True).then_inc(sC, 16)
            def ld_x(tt):
                if tt >= 2:
                    e.wait_ge(sVA, VA_TT * (tt - 1))
                e.dma_start(out=xinA[:, (tt % 2) * D:(tt % 2 + 1) * D],
                            in_=x_in[tt * 128:(tt + 1) * 128, :]).then_inc(
                                sXa if tt % 2 == 0 else sXb, 16)
                if layer == 1:
                    e.dma_start(out=at1[:, (tt % 2) * 2048:(tt % 2 + 1) * 2048],
                                in_=attn_tok[tt * 128:(tt + 1) * 128, :]).then_inc(
                                    sATa if tt % 2 == 0 else sATb, 16)

            ld_x(0)
            ld_x(1)
            for tt in range(NT):
                if tt + 2 < NT:
                    ld_x(tt + 2)
                e.wait_ge(sVA, VA_TT * (tt + 1))
                e.dma_start(out=x1_dram[tt * 128:(tt + 1) * 128, :],
                            in_=x1tA[:, (tt % 2) * D:(tt % 2 + 1) * D]).then_inc(
                                sX1Oa if tt % 2 == 0 else sX1Ob, 16)
            e.wait_ge(sX1Oa, 16 * (NT // 2))
            e.wait_ge(sX1Ob, 16 * (NT // 2))
            e.dma_start(out=xtB[:].rearrange("p (tt d) -> p tt d", d=D),
                        in_=x1_dram.rearrange("(tt p) d -> p tt d", p=128)).then_inc(sXB, 16)
            if layer == 1:
                for fc in range(NDC):
                    e.wait_ge(sV1b, fc + 1)
                    e.dma_start(out=bounce2[fc * 128:(fc + 1) * 128, :],
                                in_=c1b[:, (fc % 2) * 1024:(fc % 2 + 1) * 1024]).then_inc(
                                    sB2wa if fc % 2 == 0 else sB2wb, 16)
                b2v = bounce2.rearrange("(p c) t -> p c t", c=16)
                for c in range(NDC):
                    e.wait_ge(sB2wa, 16 * (NDC // 2))
                    e.wait_ge(sB2wb, 16 * (NDC // 2))
                    if c >= 2:
                        e.wait_ge(sP1, 16 + c - 1)
                    e.dma_start(out=rcb[:, (c % 2) * 1024:(c % 2 + 1) * 1024],
                                in_=b2v[:, c, :]).then_inc(sB2ra if c % 2 == 0 else sB2rb, 16)
            for hc in range(NHC):
                e.wait_ge(sV3, 2 * hc + 2)
                e.dma_start(out=bounce8[hc * 128:(hc + 1) * 128, :],
                            in_=y1b[:, (hc % 2) * 1024:(hc % 2 + 1) * 1024]).then_inc(
                                sB8wa if hc % 2 == 0 else sB8wb, 16)
            b8v = bounce8.rearrange("(p c) t -> p c t", c=64)
            for c in range(NHC):
                e.wait_ge(sB8wa, 16 * (NHC // 2))
                e.wait_ge(sB8wb, 16 * (NHC // 2))
                if c >= 2:
                    e.wait_ge(sP5, c - 1)
                e.dma_start(out=rcb[:, (c % 2) * 1024:(c % 2 + 1) * 1024],
                            in_=b8v[:, c, :]).then_inc(sB8ra if c % 2 == 0 else sB8rb, 16)
            def ld_x1cb(doc):
                if doc >= 2:
                    e.wait_ge(sV6, 8 * (doc - 1))
                e.dma_start(out=x1cb[:, (doc % 2) * 1024:(doc % 2 + 1) * 1024].rearrange("p (tt c) -> p tt c", c=128),
                            in_=x1_dram.rearrange("(tt p) d -> p tt d", p=128)[:, :, doc * 128:(doc + 1) * 128],
                            ).then_inc(sXCa if doc % 2 == 0 else sXCb, 16)

            ld_x1cb(0)
            ld_x1cb(1)
            for doc in range(16):
                for tt in range(NT):
                    ti = doc * 8 + tt
                    e.wait_ge(sV6, ti + 1)
                    e.dma_start(out=x2_out[tt * 128:(tt + 1) * 128, doc * 128:(doc + 1) * 128],
                                in_=x2b[:, (ti % 2) * 128:(ti % 2 + 1) * 128]).then_inc(
                                    sOa if ti % 2 == 0 else sOb, 16)
                if doc + 2 < 16:
                    ld_x1cb(doc + 2)
            e.wait_ge(sOa, 16 * 64)
            e.wait_ge(sOb, 16 * 64)

        @block.gpsimd
        def _(g):
            if layer == 0:
                wdv = wd_d.rearrange("(c p) f -> p c f", p=128)
                for fc in range(NDC):
                    if fc >= 2:
                        g.wait_ge(sPA, fc - 1)
                    g.dma_start(out=wdstr[:, (fc % 2) * 2048:(fc % 2 + 1) * 2048].rearrange("p (c f) -> p c f", f=128),
                                in_=wdv[:, :, fc * 128:(fc + 1) * 128]).then_inc(
                                    sWDa if fc % 2 == 0 else sWDb, 16)
            if layer == 0:
                w1v = w1_d.rearrange("(c p) h -> p c h", p=128)
                w3v = w3_d.rearrange("(c p) h -> p c h", p=128)
            else:
                w1v = w1_d.rearrange("(p c) h -> p c h", c=16)
                w3v = w3_d.rearrange("(p c) h -> p c h", c=16)
            for sbk in range(NSB):
                if sbk >= 2:
                    g.wait_ge(sP2, 2 * 4 * (sbk - 1))
                else:
                    g.wait_ge(sAB, 17)
                    g.wait_ge(sVB, 34)
                base = (sbk % 2) * 16384
                g.dma_start(out=wstr[:, base:base + 8192].rearrange("p (c h) -> p c h", h=512),
                            in_=w1v[:, :, sbk * 512:(sbk + 1) * 512]).then_inc(
                                sWDa if sbk % 2 == 0 else sWDb, 16)
                g.dma_start(out=wstr[:, base + 8192:base + 16384].rearrange("p (c h) -> p c h", h=512),
                            in_=w3v[:, :, sbk * 512:(sbk + 1) * 512]).then_inc(
                                sWDa if sbk % 2 == 0 else sWDb, 16)
            w2v = w2_d.rearrange("(p c) dd -> p c dd", c=64)
            for doc in range(16):
                if doc >= 2:
                    g.wait_ge(sP6, 9 * (doc - 1) + 1)
                else:
                    g.wait_ge(sP5, 64)
                g.dma_start(out=w2str[:, (doc % 2) * 8192:(doc % 2 + 1) * 8192].rearrange("p (c dd) -> p c dd", dd=128),
                            in_=w2v[:, :, doc * 128:(doc + 1) * 128]).then_inc(
                                sW2a if doc % 2 == 0 else sW2b, 16)

        @block.tensor
        def _(t):
            t.wait_ge(sC, SCTOT)
            if layer == 0:
                t.wait_ge(sAT, 16)
                atv = atT.rearrange("p (c t) -> p c t", t=1024)
                for fc in range(NDC):
                    t.wait_ge(sWDa if fc % 2 == 0 else sWDb, 16 * (fc // 2 + 1))
                    if fc >= 2:
                        t.wait_ge(sEA, fc - 1)
                    half = (fc % 2) * 1024
                    ins = None
                    for tc in range(2):
                        for dc in range(NDC):
                            ins = t.matmul(psF[:, half + tc * 512:half + (tc + 1) * 512],
                                           wdstr[:, (fc % 2) * 2048 + dc * 128:(fc % 2) * 2048 + (dc + 1) * 128],
                                           atv[:, dc, tc * 512:(tc + 1) * 512],
                                           start=(dc == 0), stop=(dc == NDC - 1))
                    ins.then_inc(sPA, 1)
                t.wait_ge(sEA, 16)
                for tt in range(NT):
                    for grp in range(4):
                        gi = tt * 4 + grp
                        if gi >= 2:
                            t.wait_ge(sVA, 0 if gi < 2 else (gi - 1))
                        ins = None
                        for j in range(4):
                            fc = grp * 4 + j
                            ins = t.transpose(psTd[:, (gi % 2) * 1024 + j * 128:(gi % 2) * 1024 + (j + 1) * 128],
                                              dnsF[:, fc * 1024 + tt * 128:fc * 1024 + (tt + 1) * 128],
                                              ident)
                        ins.then_inc(sPA, 1)
            for tt in range(NT):
                t.wait_ge(sAB, 10 + tt)
                for grp in range(4):
                    gi = tt * 4 + grp
                    if gi >= 2:
                        t.wait_ge(sVB, 2 + gi - 1)
                    ins = None
                    for j in range(4):
                        fc = grp * 4 + j
                        ins = t.transpose(psTd[:, (gi % 2) * 1024 + j * 128:(gi % 2) * 1024 + (j + 1) * 128],
                                          lntB[:, fc * 128:(fc + 1) * 128],
                                          ident)
                    ins.then_inc(sPB, 1)
            if layer == 1:
                t.wait_ge(sVB, 34)
                for fc in range(NDC):
                    if fc >= 1:
                        t.wait_ge(sV1b, fc)
                    ins = None
                    for tc in range(2):
                        ins = t.matmul(psB2[:, tc * 512:(tc + 1) * 512], bd16k,
                                       lnF[:, fc * 1024 + tc * 512:fc * 1024 + (tc + 1) * 512],
                                       start=True, stop=True)
                    ins.then_inc(sP1, 1)
                for c in range(NDC):
                    t.wait_ge(sB2ra if c % 2 == 0 else sB2rb, 16 * (c // 2 + 1))
                    t.wait_ge(sV1b, 16 + c)
                    ins = None
                    for tc in range(2):
                        ins = t.matmul(psB2[:, tc * 512:(tc + 1) * 512], h128k,
                                       rcb[:, (c % 2) * 1024 + tc * 512:(c % 2) * 1024 + (tc + 1) * 512],
                                       start=True, stop=True)
                    ins.then_inc(sP1, 1)
            else:
                t.wait_ge(sVB, 34)
            if layer == 1:
                t.wait_ge(sV1b, 32)
            for sbk in range(NSB):
                t.wait_ge(sWDa if sbk % 2 == 0 else sWDb,
                          W1BASE + 32 * (sbk // 2 + 1))
                for hcl in range(4):
                    hc = sbk * 4 + hcl
                    wb = (sbk % 2) * 16384
                    if hc >= 1:
                        t.wait_ge(sV3, 2 * (hc - 1) + 1)
                        t.wait_ge(sA3, hc)
                    ins = None
                    for cc in range(NDC):
                        for tc in range(2):
                            t.matmul(psU[:, tc * 512:(tc + 1) * 512],
                                     wstr[:, wb + cc * 512 + hcl * 128:wb + cc * 512 + (hcl + 1) * 128],
                                     lnF[:, cc * 1024 + tc * 512:cc * 1024 + (tc + 1) * 512],
                                     start=(cc == 0), stop=(cc == NDC - 1))
                            ins = t.matmul(psG[:, tc * 512:(tc + 1) * 512],
                                           wstr[:, wb + 8192 + cc * 512 + hcl * 128:wb + 8192 + cc * 512 + (hcl + 1) * 128],
                                           lnF[:, cc * 1024 + tc * 512:cc * 1024 + (tc + 1) * 512],
                                           start=(cc == 0), stop=(cc == NDC - 1))
                    ins.then_inc(sP2, 1)
                    t.wait_ge(sV3, 2 * hc + 1)
                    if hc >= 1:
                        t.wait_ge(sV3, 2 * (hc - 1) + 2)
                    ins = None
                    for tc in range(2):
                        ins = t.matmul(psH[:, tc * 512:(tc + 1) * 512], bd64k,
                                       hck[:, tc * 512:(tc + 1) * 512],
                                       start=True, stop=True)
                    ins.then_inc(sP2, 1)
            for c in range(NHC):
                t.wait_ge(sB8ra if c % 2 == 0 else sB8rb, 16 * (c // 2 + 1))
                if c >= 1:
                    t.wait_ge(sV5, c)
                ins = None
                for tc in range(2):
                    ins = t.matmul(psB2[:, tc * 512:(tc + 1) * 512], h128k,
                                   rcb[:, (c % 2) * 1024 + tc * 512:(c % 2) * 1024 + (tc + 1) * 512],
                                   start=True, stop=True)
                ins.then_inc(sP5, 1)
            t.wait_ge(sV5, 64)
            for doc in range(16):
                t.wait_ge(sW2a if doc % 2 == 0 else sW2b, 16 * (doc // 2 + 1))
                if doc >= 1:
                    t.wait_ge(sA6, doc)
                ins = None
                for c in range(NHC):
                    for tc in range(2):
                        ins = t.matmul(psY[:, tc * 512:(tc + 1) * 512],
                                       w2str[:, (doc % 2) * 8192 + c * 128:(doc % 2) * 8192 + (c + 1) * 128],
                                       h2[:, c * 1024 + tc * 512:c * 1024 + (tc + 1) * 512],
                                       start=(c == 0), stop=(c == NHC - 1))
                ins.then_inc(sP6, 1)
                t.wait_ge(sA6, doc + 1)
                for tt in range(NT):
                    ti = doc * 8 + tt
                    if ti >= 2:
                        t.wait_ge(sV6, ti - 1)
                    t.transpose(psTd[:, (ti % 2) * 1024:(ti % 2) * 1024 + 128],
                                ysb[:, tt * 128:(tt + 1) * 128],
                                ident).then_inc(sP6, 1)

        @block.scalar
        def _(s):
            s.wait_ge(sC, SCTOT)
            if layer == 0:
                for fc in range(NDC):
                    s.wait_ge(sPA, fc + 1)
                    s.activation(dnsF[:, fc * 1024:(fc + 1) * 1024],
                                 psF[:, (fc % 2) * 1024:(fc % 2 + 1) * 1024],
                                 AF.Identity, bias=pcol[:, fc:fc + 1],
                                 scale=1.0).then_inc(sEA, 1)
            s.wait_ge(sXB, 16)
            for tt in range(NT):
                if tt >= 1:
                    s.wait_ge(sAB, tt)
                s.activation(scrB[:] if tt % 2 == 0 else scrB2[:],
                             xtB[:, tt * D:(tt + 1) * D], AF.Square,
                             accum_out=smq[:, NT + tt:NT + tt + 1]).then_inc(sAB, 1)
            s.wait_ge(sVB, 1)
            s.activation(stt[:, NT:2 * NT], stt[:, NT:2 * NT], AF.Sqrt).then_inc(sAB, 1)
            s.wait_ge(sVB, 2)
            for tt in range(NT):
                if tt >= 1:
                    s.wait_ge(sPB, 4 * tt)
                s.activation(lntB[:], xtB[:, tt * D:(tt + 1) * D], AF.Identity,
                             bias=stt[:, 3 * NT + tt:3 * NT + tt + 1],
                             scale=stt[:, 2 * NT + tt:2 * NT + tt + 1]).then_inc(sAB, 1)
            for hc in range(NHC):
                s.wait_ge(sP2, 2 * hc + 1)
                if hc >= 1:
                    s.wait_ge(sV3, 2 * (hc - 1) + 1)
                if layer == 0:
                    s.activation(sgb[:], psU[:], AF.Silu).then_inc(sA3, 1)
                else:
                    s.activation(sgb[:], psG[:], AF.Silu,
                                 scale=pcol2[:, 64 + hc:64 + hc + 1]).then_inc(sA3, 1)
            for doc in range(16):
                s.wait_ge(sP6, 9 * doc + 1)
                if layer == 0:
                    s.activation(ysb[:], psY[:], AF.Identity).then_inc(sA6, 1)
                else:
                    s.activation(ysb[:], psY[:], AF.Identity,
                                 scale=pcol[:, 16 + doc:16 + doc + 1]).then_inc(sA6, 1)

        @block.vector
        def _(v):
            nd = [0]

            def dedge(ins):
                nd[0] += 1
                ins.then_inc(sD, 1)
                v.wait_ge(sD, nd[0])

            v.wait_ge(sC, SCTOT)
            if layer == 1:
                dedge(v.tensor_tensor(pcol2[:, 0:64], pcol2[:, 0:64], pcol2[:, 128:192], ALU.mult))
            if layer == 0:
                for tt in range(NT):
                    v.wait_ge(sXa if tt % 2 == 0 else sXb, 16 * (tt // 2 + 1))
                    if tt >= 2:
                        v.wait_ge(sX1Oa if tt % 2 == 0 else sX1Ob, 16 * (tt // 2))
                    for grp in range(4):
                        gi = tt * 4 + grp
                        v.wait_ge(sPA, 16 + gi + 1)
                        v.tensor_tensor(x1tA[:, (tt % 2) * D + grp * 512:(tt % 2) * D + (grp + 1) * 512],
                                        psTd[:, (gi % 2) * 1024:(gi % 2) * 1024 + 512],
                                        xinA[:, (tt % 2) * D + grp * 512:(tt % 2) * D + (grp + 1) * 512],
                                        ALU.add).then_inc(sVA, 1)
            else:
                for tt in range(NT):
                    v.wait_ge(sXa if tt % 2 == 0 else sXb, 16 * (tt // 2 + 1))
                    v.wait_ge(sATa if tt % 2 == 0 else sATb, 16 * (tt // 2 + 1))
                    if tt >= 2:
                        v.wait_ge(sX1Oa if tt % 2 == 0 else sX1Ob, 16 * (tt // 2))
                    if tt >= 1:
                        v.wait_ge(sVA, tt)
                    dedge(v.tensor_copy(a32[:, 0:2048], at1[:, (tt % 2) * 2048:(tt % 2 + 1) * 2048]))
                    v.tensor_tensor(x1tA[:, (tt % 2) * D:(tt % 2 + 1) * D],
                                    a32[:, 0:2048],
                                    xinA[:, (tt % 2) * D:(tt % 2 + 1) * D],
                                    ALU.add).then_inc(sVA, 1)
            n = NT
            mean, veps, rstd, nmrs = (stt[:, 0:n], stt[:, n:2 * n],
                                      stt[:, 2 * n:3 * n], stt[:, 3 * n:4 * n])
            v.wait_ge(sXB, 16)
            for tt in range(NT):
                dedge(v.reduce_sum(smq[:, tt:tt + 1], xtB[:, tt * D:(tt + 1) * D], axis=AX.X))
            v.wait_ge(sAB, NT)
            dedge(v.tensor_scalar(mean, smq[:, 0:NT], 1.0 / D, None, ALU.mult))
            dedge(v.tensor_scalar(veps, smq[:, NT:2 * NT], 1.0 / D, LN_EPS, ALU.mult, ALU.add))
            dedge(v.tensor_tensor(rstd, mean, mean, ALU.mult))
            v.tensor_sub(veps, veps, rstd).then_inc(sVB, 1)
            v.wait_ge(sAB, NT + 1)
            dedge(v.reciprocal(rstd, veps))
            dedge(v.tensor_tensor(nmrs, mean, rstd, ALU.mult))
            v.tensor_scalar(nmrs, nmrs, -1.0, None, ALU.mult).then_inc(sVB, 1)
            for tt in range(NT):
                for grp in range(4):
                    gi = tt * 4 + grp
                    v.wait_ge(sPB, gi + 1)
                    dst3 = lnF[:].rearrange("p (fc t) -> p fc t", t=1024)[
                        :, grp * 4:(grp + 1) * 4, tt * 128:(tt + 1) * 128]
                    src3 = psTd[:, (gi % 2) * 1024:(gi % 2) * 1024 + 512].rearrange(
                        "p (j t) -> p j t", t=128)
                    if layer == 0:
                        v.tensor_copy(dst3, src3).then_inc(sVB, 1)
                    else:
                        sucol = pcol[:, grp * 4:(grp + 1) * 4].unsqueeze(2).broadcast_to([128, 4, 128])
                        v.tensor_tensor(dst3, src3, sucol, ALU.mult).then_inc(sVB, 1)
            if layer == 1:
                for fc in range(NDC):
                    v.wait_ge(sP1, fc + 1)
                    if fc >= 2:
                        v.wait_ge(sB2wa if fc % 2 == 0 else sB2wb, 16 * (fc // 2))
                    v.tensor_copy(c1b[:, (fc % 2) * 1024:(fc % 2 + 1) * 1024],
                                  psB2[:]).then_inc(sV1b, 1)
                for c in range(NDC):
                    v.wait_ge(sP1, 16 + c + 1)
                    v.tensor_scalar(lnF[:, c * 1024:(c + 1) * 1024], psB2[:],
                                    FWN2, None, ALU.mult).then_inc(sV1b, 1)
            for hc in range(NHC):
                v.wait_ge(sA3, hc + 1)
                if hc >= 1:
                    v.wait_ge(sP2, 2 * (hc - 1) + 2)
                if layer == 0:
                    v.tensor_tensor(hck[:], psG[:], sgb[:], ALU.mult).then_inc(sV3, 1)
                else:
                    v.scalar_tensor_tensor(hck[:], psU[:], pcol2[:, hc:hc + 1], sgb[:],
                                           ALU.mult, ALU.mult).then_inc(sV3, 1)
                v.wait_ge(sP2, 2 * hc + 2)
                if hc >= 2:
                    v.wait_ge(sB8wa if hc % 2 == 0 else sB8wb, 16 * (hc // 2))
                v.tensor_copy(y1b[:, (hc % 2) * 1024:(hc % 2 + 1) * 1024],
                              psH[:]).then_inc(sV3, 1)
            for c in range(NHC):
                v.wait_ge(sP5, c + 1)
                v.tensor_scalar(h2[:, c * 1024:(c + 1) * 1024], psB2[:],
                                FWN, None, ALU.mult).then_inc(sV5, 1)
            for doc in range(16):
                v.wait_ge(sXCa if doc % 2 == 0 else sXCb, 16 * (doc // 2 + 1))
                for tt in range(NT):
                    ti = doc * 8 + tt
                    v.wait_ge(sP6, 9 * doc + 1 + tt + 1)
                    if ti >= 2:
                        v.wait_ge(sOa if ti % 2 == 0 else sOb, 16 * (ti // 2))
                    v.tensor_tensor(x2b[:, (ti % 2) * 128:(ti % 2 + 1) * 128],
                                    psTd[:, (ti % 2) * 1024:(ti % 2) * 1024 + 128],
                                    x1cb[:, (doc % 2) * 1024 + tt * 128:(doc % 2) * 1024 + (tt + 1) * 128],
                                    ALU.add).then_inc(sV6, 1)

    return nc



class _NS:
    def __init__(self, **kw):
        self.__dict__.update(kw)


def emit_pre_f(C):
    sL, sL2, sV, sA, sT, sE, sO = (C.sems[k] for k in "sL sL2 sV sA sT sE sO".split())
    B, io = C.B, C.io
    n = NT

    def f_sync(e):
        xv = io.x.rearrange("(tt p) d -> p tt d", p=128)
        xo = B.xt[:].rearrange("p (tt d) -> p tt d", d=D)
        e.dma_start(out=xo[:, 0:4, :], in_=xv[:, 0:4, :]).then_inc(sL, 16)
        e.dma_start(out=xo[:, 4:8, :], in_=xv[:, 4:8, :]).then_inc(sL2, 16)
        for tt in range(NT):
            e.wait_ge(sE, 4 * tt + 4)
            e.dma_start(
                out=io.lnT[tt // 2].rearrange("(c p) t -> p c t", p=128)[:, :, (tt % 2) * 128:(tt % 2 + 1) * 128],
                in_=B.lt[:, (tt % 2) * D:(tt % 2 + 1) * D].rearrange("p (c t) -> p c t", t=128),
            ).then_inc(sO, 16)
        e.wait_ge(sO, 16 * NT)

    def f_scalar(s):
        s.wait_ge(sL, 16)
        for tt in range(NT):
            if tt == 4:
                s.wait_ge(sL2, 16)
            if tt >= 1:
                s.wait_ge(sA, tt)
            s.activation(B.scr[:], B.xt[:, tt * D:(tt + 1) * D], AF.Square,
                         accum_out=B.sq[:, tt:tt + 1]).then_inc(sA, 1)
        s.wait_ge(sV, 1)
        s.activation(B.st[:, n:2 * n], B.st[:, n:2 * n], AF.Sqrt).then_inc(sA, 1)
        s.wait_ge(sV, 2)
        for tt in range(NT):
            if tt >= 2:
                s.wait_ge(sT, 4 * (tt - 1))
            s.activation(B.ln[:, (tt % 2) * D:(tt % 2 + 1) * D],
                         B.xt[:, tt * D:(tt + 1) * D], AF.Identity,
                         bias=B.st[:, 3 * n + tt:3 * n + tt + 1],
                         scale=B.st[:, 2 * n + tt:2 * n + tt + 1]).then_inc(sA, 1)

    def f_vector(v):
        nd = [0]
        sD = C.sems["sD"]

        def dedge(ins):
            nd[0] += 1
            ins.then_inc(sD, 1)
            v.wait_ge(sD, nd[0])

        mean, veps, rstd, nmrs = (B.st[:, 0:n], B.st[:, n:2 * n],
                                  B.st[:, 2 * n:3 * n], B.st[:, 3 * n:4 * n])
        v.wait_ge(sL, 16)
        for tt in range(NT):
            if tt == 4:
                v.wait_ge(sL2, 16)
            dedge(v.reduce_sum(B.sm[:, tt:tt + 1], B.xt[:, tt * D:(tt + 1) * D], axis=AX.X))
        v.wait_ge(sA, NT)
        dedge(v.tensor_scalar(mean, B.sm[:], 1.0 / D, None, ALU.mult))
        dedge(v.tensor_scalar(veps, B.sq[:], 1.0 / D, LN_EPS, ALU.mult, ALU.add))
        dedge(v.tensor_tensor(rstd, mean, mean, ALU.mult))
        v.tensor_sub(veps, veps, rstd).then_inc(sV, 1)
        v.wait_ge(sA, NT + 1)
        dedge(v.reciprocal(rstd, veps))
        dedge(v.tensor_tensor(nmrs, mean, rstd, ALU.mult))
        v.tensor_scalar(nmrs, nmrs, -1.0, None, ALU.mult).then_inc(sV, 1)
        for tt in range(NT):
            for grp in range(4):
                gi = tt * 4 + grp
                v.wait_ge(sT, gi + 1)
                if tt >= 2:
                    v.wait_ge(sO, 16 * tt)
                v.tensor_copy(B.lt[:, (tt % 2) * D + grp * 512:(tt % 2) * D + (grp + 1) * 512],
                              B.psT[:, (gi % 2) * 1024:(gi % 2) * 1024 + 512]).then_inc(sE, 1)

    def f_tensor(t):
        t.wait_ge(sL, 16)
        t.wait_ge(sL2, 16)
        for tt in range(NT):
            t.wait_ge(sA, NT + 2 + tt)
            for grp in range(4):
                gi = tt * 4 + grp
                if gi >= 2:
                    t.wait_ge(sE, gi - 1)
                ins = None
                for j in range(4):
                    fc = grp * 4 + j
                    ins = t.transpose(B.psT[:, (gi % 2) * 1024 + j * 128:(gi % 2) * 1024 + (j + 1) * 128],
                                      B.ln[:, (tt % 2) * D + fc * 128:(tt % 2) * D + (fc + 1) * 128],
                                      C.cst.ident)
                ins.then_inc(sT, 1)

    def f_gpsimd(g):
        pf = getattr(C, "prefetch", None)
        if pf is not None:
            wsb, wsl, wsem = pf
            wv = wsl.rearrange("(c p) o -> p c o", p=128)
            for q in range(4):
                g.dma_start(out=wsb[:, q * 6144:(q + 1) * 6144].rearrange("p (c o) -> p c o", o=1536),
                            in_=wv[:, q * 4:(q + 1) * 4, :]).then_inc(wsem, 16)

    C.bodies = {"sync": f_sync, "scalar": f_scalar, "vector": f_vector,
                "tensor": f_tensor, "gpsimd": f_gpsimd}


def emit_att_f(C, layer):
    sm = C.sems
    sL, sW = sm["sL"], sm["sW"]
    sP0, sPE, sA1, sV1 = sm["sP0"], sm["sPE"], sm["sA1"], sm["sV1"]
    sT2, sX, sM, sV2, sO = sm["sT2"], sm["sX"], sm["sM"], sm["sV2"], sm["sO"]
    sB0, sB1 = sm["sB0"], sm["sB1"]
    B, io, cst = C.B, C.io, C.cst
    NTB, NQB, NH = 8, 8, 4

    def kts(qb):
        return 4 * qb + 4

    def hq(h, qb):
        return h * NQB + qb if layer == 0 else qb * NH + h

    def qb_of(i):
        return i % NQB if layer == 0 else i // NH

    def exp_base(h, qb):
        return sum(kts(qb_of(i)) for i in range(hq(h, qb)))

    smm_ct, pv_ct, rb_ct, pvqt_ct = {}, {}, {}, {}
    ctr = 0
    for i in range(NH * NQB):
        nk_i = kts(qb_of(i))
        for kt in range(nk_i):
            ctr += 1
            smm_ct[(i, kt)] = ctr
        if layer == 0:
            if i >= 1:
                ctr += 1
                rb_ct[i - 1] = ctr
            ctr += 1
            pv_ct[i] = ctr
        else:
            for qt in range(4):
                ctr += 1
                pvqt_ct[(i, qt)] = ctr
    if layer == 0:
        ctr += 1
        rb_ct[NH * NQB - 1] = ctr
    qkv_ct, tr_ct = {}, {}
    if layer == 1:
        c2 = 0
        c2 += 1; qkv_ct[0] = c2
        for gt in range(1, 32):
            c2 += 1; qkv_ct[gt] = c2
            c2 += 1; tr_ct[gt - 1] = c2
        c2 += 1; tr_ct[31] = c2

    P1_PE_TB = 12 if layer == 0 else 8
    SCTOT = 16 if layer == 0 else 32
    SWTOT = 80 if layer == 0 else 64


    def f_sync(e):
        if layer == 0:
            e.dma_start(out=B.bqs[:, 0:12],
                        in_=io.bq.rearrange("(o p) -> p o", p=128),
                        allow_slow_non_contiguous=True).then_inc(sL, 16)
        else:
            e.dma_start(out=B.taunu[0:1, 0:1], in_=io.tau.unsqueeze(0)).then_inc(sL, 16)
            e.dma_start(out=B.taunu[0:1, 1:2], in_=io.nu.unsqueeze(0)).then_inc(sL, 16)
        for tb in range(NTB):
            if tb >= 2:
                if layer == 0:
                    e.wait_ge(sPE, (tb - 1) * P1_PE_TB)
                else:
                    e.wait_ge(sPE, qkv_ct[4 * (tb - 1)])
            for j in range(2):
                q = 2 * (tb % 2) + j
                e.wait_ge(C.ccsem, q + 1)
                e.dma_start(
                    out=B.lnb[:, (tb % 2) * NDC * 512:(tb % 2 + 1) * NDC * 512].rearrange("p (c t) -> p c t", t=512)[:, :, j * 256:(j + 1) * 256],
                    in_=io.lnTg[q].rearrange("(r cp) t -> r cp t", r=4)[tb // 2].rearrange("(c p) t -> p c t", p=128),
                ).then_inc(sB0 if tb % 2 == 0 else sB1, 16)
        nst = 0
        for a in range(NH if layer == 0 else NQB):
            for b2 in range(NQB if layer == 0 else NH):
                h, qb = (a, b2) if layer == 0 else (b2, a)
                i = hq(h, qb)
                if layer == 0:
                    e.wait_ge(sV2, 2 * i + 2)
                    e.dma_start(out=io.att_c[h * 128:(h + 1) * 128, qb * 512:(qb + 1) * 512],
                                in_=B.attnb[:]).then_inc(sO, 16)
                    nst += 16
                else:
                    for qt in range(4):
                        e.wait_ge(sV2, 4 * i + qt + 1)
                        e.dma_start(
                            out=io.att_c[qb // 2][(qb % 2) * 512 + qt * 128:(qb % 2) * 512 + (qt + 1) * 128,
                                                  h * 128:(h + 1) * 128],
                            in_=B.attnb[:, qt * 128:(qt + 1) * 128]).then_inc(sO, 16)
                        nst += 16
        e.wait_ge(sO, nst)

    def f_gpsimd(g):
        if layer == 0:
            g.dma_start(out=B.bvrow[0:1, :], in_=io.bq[1024:1536].unsqueeze(0)).then_inc(sW, 16)

    def f_tensor(t):
        t.wait_ge(sL, SCTOT)
        t.wait_ge(sW, SWTOT)
        if layer == 1:
            t.matmul(B.psB0[:, 0:2], C.cst.ones128[0:1, :], B.taunu[0:1, :],
                     start=True, stop=True).then_inc(sP0, 1)
        for tb in range(NTB if layer == 0 else 0):
            t.wait_ge(sB0 if tb % 2 == 0 else sB1, 32 * (tb // 2 + 1))
            if layer == 0:
                lnv = B.lnb[:, (tb % 2) * NDC * 512:].rearrange("p (c t) -> p c t", t=512)
                for of in range(8):
                    if 8 * tb + of >= 2:
                        t.wait_ge(sA1, 8 * tb + of - 1)
                    pso = B.psQK[:, (of % 2) * 512:(of % 2) * 512 + 512]
                    ins = None
                    for dc in range(NDC):
                        ins = t.matmul(pso, B.ws[:, dc * 1536 + of * 128:dc * 1536 + (of + 1) * 128],
                                       lnv[:, dc, :], start=(dc == 0), stop=(dc == NDC - 1))
                    ins.then_inc(sPE, 1)
                for tt in range(4):
                    if 4 * tb + tt >= 2:
                        t.wait_ge(sV1, 4 * tb + tt - 1)
                    psv = B.psV[:, (tt % 2) * 512:(tt % 2) * 512 + 512]
                    for dc in range(NDC):
                        t.matmul(psv, B.lnb[:, (tb % 2) * NDC * 512 + dc * 512 + tt * 128:
                                            (tb % 2) * NDC * 512 + dc * 512 + (tt + 1) * 128],
                                 B.ws[:, dc * 1536 + 1024:dc * 1536 + 1536],
                                 start=(dc == 0), stop=False)
                    t.matmul(psv, C.cst.onesb[0:1, :], B.bvrow[0:1, :],
                             start=False, stop=True).then_inc(sPE, 1)
            else:
                pass
        if layer == 1:
            def qkv1(gt):
                tb, tt = gt // 4, gt % 4
                if tt == 0:
                    t.wait_ge(sB0 if tb % 2 == 0 else sB1, 32 * (tb // 2 + 1))
                p = gt % 2
                if gt == 0:
                    t.wait_ge(sP0, 3)
                if gt >= 2:
                    t.wait_ge(sV1, 4 * (gt - 2) + 3)
                ins = None
                for pj, pso in enumerate((B.psQ1, B.psK1, B.psV1_)):
                    for dc in range(NDC):
                        lhs = B.lnb[:, (tb % 2) * NDC * 512 + dc * 512 + tt * 128:
                                    (tb % 2) * NDC * 512 + dc * 512 + (tt + 1) * 128]
                        ins = t.matmul(pso[:, p * 512:(p + 1) * 512], lhs,
                                       B.ws[:, dc * 1536 + pj * 512:dc * 1536 + (pj + 1) * 512],
                                       start=(dc == 0), stop=(dc == NDC - 1))
                ins.then_inc(sPE, 1)

            def tr1(gt):
                t.wait_ge(sV1, 4 * gt + 3)
                ins = None
                for j in range(4):
                    ins = t.transpose(B.psT1[:, j * 128:(j + 1) * 128],
                                      B.qtok[:, j * 128:(j + 1) * 128], C.cst.ident)
                for j in range(4):
                    ins = t.transpose(B.psT1[:, 512 + j * 128:512 + (j + 1) * 128],
                                      B.qtok[:, 512 + j * 128:512 + (j + 1) * 128], C.cst.ident)
                ins.then_inc(sPE, 1)

            qkv1(0)
            for gt in range(1, 32):
                qkv1(gt)
                tr1(gt - 1)
            tr1(31)
        t.wait_ge(sA1 if layer == 0 else sV1, 64 if layer == 0 else 128)
        t.wait_ge(sV1, 32 if layer == 0 else 128)
        def rbcast(i):
            t.wait_ge(sV2, 2 * i + 1)
            t.matmul(B.psR[:], C.cst.ones128[0:1, :], B.rr[0:1, :],
                     start=True, stop=True).then_inc(sT2, 1)

        for a in range(NH if layer == 0 else NQB):
            for b2 in range(NQB if layer == 0 else NH):
                h, qb = (a, b2) if layer == 0 else (b2, a)
                i = hq(h, qb)
                nk = kts(qb)
                eb = exp_base(h, qb)
                for kt in range(nk):
                    if kt >= 4:
                        t.wait_ge(sX, eb + kt - 3)
                    j = kt - 4 * qb
                    n0 = 128 * max(0, j)
                    t.matmul(B.psS[:, (kt % 4) * 512 + n0:(kt % 4) * 512 + 512],
                             B.KF[:, h * S + kt * 128:h * S + (kt + 1) * 128],
                             B.QF[:, h * S + qb * 512 + n0:h * S + (qb + 1) * 512],
                             start=True, stop=True).then_inc(sT2, 1)
                if layer == 0 and i >= 1:
                    rbcast(i - 1)
                if layer == 0:
                    t.wait_ge(sV2, 2 * i)
                    ins = None
                    for kt in range(nk):
                        j = kt - 4 * qb
                        n0 = 128 * max(0, j)
                        t.wait_ge(sX, eb + kt + 1)
                        if j >= 0:
                            t.wait_ge(sM, 4 * i + j + 2)
                        vtb = B.VT[:, kt * 528 + h * 132:]
                        t.matmul(B.psO[:, n0:512], vtb[:, 0:128],
                                 B.PT[:, kt * 512 + n0:(kt + 1) * 512],
                                 start=(kt == 0), stop=(kt == nk - 1))
                        ins = t.matmul(B.psSum[:, n0:512], vtb[:, 128:129],
                                       B.PT[:, kt * 512 + n0:(kt + 1) * 512],
                                       start=(kt == 0), stop=(kt == nk - 1))
                    ins.then_inc(sT2, 1)
                    if i == NH * NQB - 1:
                        rbcast(i)
                else:
                    t.wait_ge(sV2, 4 * i)
                    for qt in range(4):
                        qtg = 4 * qb + qt
                        ins = None
                        for kt in range(qtg + 1):
                            t.wait_ge(sX, eb + kt + 1)
                            j = kt - 4 * qb
                            if j >= 0:
                                t.wait_ge(sM, 4 * i + j + 2)
                            vtb = B.VT[:, kt * 528 + h * 132:]
                            ins = t.matmul(B.psO3[:, qt * 512:qt * 512 + 129],
                                           B.PT[:, kt * 512 + qt * 128:kt * 512 + (qt + 1) * 128],
                                           vtb[:, 0:129],
                                           start=(kt == 0), stop=(kt == qtg))
                        ins.then_inc(sT2, 1)

    def f_scalar(s):
        if layer == 1:
            s.wait_ge(sP0, 1)
            s.activation(B.tnb[:, 3:4], B.psB0[:, 0:1], AF.Abs).then_inc(sP0, 1)
        if layer == 0:
            for tb in range(8):
                for of in range(8):
                    s.wait_ge(sPE, tb * 12 + of + 1)
                    dst = (B.QF if of < 4 else B.KF)[:, (of % 4) * S + tb * 512:(of % 4) * S + (tb + 1) * 512]
                    s.activation(dst, B.psQK[:, (of % 2) * 512:(of % 2) * 512 + 512],
                                 AF.Identity, bias=B.bqs[:, of:of + 1], scale=1.0).then_inc(sA1, 1)
        else:
            for gt in range(32):
                p = gt % 2
                s.wait_ge(sPE, qkv_ct[gt])
                if gt >= 1:
                    s.wait_ge(sV1, 4 * (gt - 1) + 1)
                s.activation(B.scrq[:, 0:512], B.psQ1[:, p * 512:(p + 1) * 512], AF.Square).then_inc(sA1, 1)
                s.activation(B.scrq[:, 512:1024], B.psK1[:, p * 512:(p + 1) * 512], AF.Square).then_inc(sA1, 1)
                s.activation(B.scrq[:, 1024:1536], B.psV1_[:, p * 512:(p + 1) * 512], AF.Square).then_inc(sA1, 1)
                s.wait_ge(sV1, 4 * gt + 1)
                s.activation(B.ssq[:, 12:24], B.ssq[:, 0:12], AF.Sqrt).then_inc(sA1, 1)
        s.wait_ge(sM, 1)
        if layer == 1:
            s.wait_ge(sP0, 3)
        for a in range(NH if layer == 0 else NQB):
            for b2 in range(NQB if layer == 0 else NH):
                h, qb = (a, b2) if layer == 0 else (b2, a)
                i = hq(h, qb)
                nk = kts(qb)
                eb = exp_base(h, qb)
                for kt in range(nk):
                    s.wait_ge(sT2, smm_ct[(i, kt)])
                    j = kt - 4 * qb
                    n0 = 128 * max(0, j)
                    if layer == 0:
                        s.activation(B.PT[:, kt * 512 + n0:(kt + 1) * 512],
                                     B.psS[:, (kt % 4) * 512 + n0:(kt % 4) * 512 + 512],
                                     AF.Exp, bias=B.csh[:, 0:1],
                                     scale=B.csh[:, 1:2]).then_inc(sX, 1)
                    else:
                        s.activation(B.PT[:, kt * 512 + n0:(kt + 1) * 512],
                                     B.psS[:, (kt % 4) * 512 + n0:(kt % 4) * 512 + 512],
                                     AF.Exp, bias=B.tnb[:, 1:2],
                                     scale=B.tnb[:, 0:1]).then_inc(sX, 1)

    def f_vector(v):
        nd = [0]
        sD = sm["sD"]

        def dedge(ins):
            nd[0] += 1
            ins.then_inc(sD, 1)
            v.wait_ge(sD, nd[0])

        v.wait_ge(sL, SCTOT)
        v.wait_ge(sW, SWTOT)
        v.memset(B.VT[:], 1.0)
        v.memset(B.csh[:, 0:1], -C0_SHIFT)
        v.memset(B.csh[:, 1:2], 0.08838834764831843).then_inc(sM, 1)
        if layer == 1:
            v.wait_ge(sP0, 2)
            v.tensor_copy(B.tnb[:, 0:1], B.psB0[:, 0:1])
            v.nop()
            v.tensor_scalar(B.tnb[:, 1:2], B.tnb[:, 3:4], -1.0, None, ALU.mult)
            v.tensor_copy(B.tnb[:, 2:3], B.psB0[:, 1:2]).then_inc(sP0, 1)
        if layer == 0:
            for tb in range(8):
                for tt in range(4):
                    v.wait_ge(sM, 1)
                    v.wait_ge(sPE, tb * 12 + 8 + tt + 1)
                    gt = tb * 4 + tt
                    dst = B.VT[:, gt * 528:(gt + 1) * 528].rearrange("p (h d) -> p h d", d=132)[:, :, 0:128]
                    srcv = B.psV[:, (tt % 2) * 512:(tt % 2) * 512 + 512].rearrange("p (h d) -> p h d", d=128)
                    v.tensor_copy(dst, srcv).then_inc(sV1, 1)
        else:
            for gt in range(32):
                p = gt % 2
                v.wait_ge(sM, 1)
                v.wait_ge(sA1, 4 * gt + 3)
                ins = None
                for pj in range(3):
                    ins = v.tensor_reduce(B.ssq[:, pj * 4:(pj + 1) * 4].unsqueeze(2),
                                          B.scrq[:, pj * 512:(pj + 1) * 512].rearrange("p (h d) -> p h d", d=128),
                                          axis=AX.X, op=ALU.add)
                ins.then_inc(sV1, 1)
                v.wait_ge(sA1, 4 * gt + 4)
                v.nop()
                dedge(v.tensor_scalar(B.ssq[:, 24:36], B.ssq[:, 12:24], L2_EPS, None, ALU.add))
                v.reciprocal(B.ssq[:, 12:24], B.ssq[:, 24:36]).then_inc(sV1, 1)
                v.wait_ge(sV1, 4 * gt + 2)
                if gt >= 1:
                    v.wait_ge(sPE, tr_ct[gt - 1])
                rq = B.ssq[:, 12:16].unsqueeze(2).broadcast_to([128, 4, 128])
                rk = B.ssq[:, 16:20].unsqueeze(2).broadcast_to([128, 4, 128])
                rv = B.ssq[:, 20:24].unsqueeze(2).broadcast_to([128, 4, 128])
                v.tensor_tensor(B.qtok[:, 0:512].rearrange("p (h d) -> p h d", d=128),
                                B.psQ1[:, p * 512:(p + 1) * 512].rearrange("p (h d) -> p h d", d=128), rq, ALU.mult)
                v.tensor_tensor(B.qtok[:, 512:1024].rearrange("p (h d) -> p h d", d=128),
                                B.psK1[:, p * 512:(p + 1) * 512].rearrange("p (h d) -> p h d", d=128), rk, ALU.mult)
                dstv = B.VT[:, gt * 528:(gt + 1) * 528].rearrange("p (h d) -> p h d", d=132)[:, :, 0:128]
                v.tensor_tensor(dstv, B.psV1_[:, p * 512:(p + 1) * 512].rearrange("p (h d) -> p h d", d=128), rv, ALU.mult).then_inc(sV1, 1)
                v.wait_ge(sPE, tr_ct[gt])
                v.tensor_copy(B.QF[:].rearrange("p (h t) -> p h t", t=S)[:, :, gt * 128:(gt + 1) * 128],
                              B.psT1[:, 0:512].rearrange("p (h d) -> p h d", d=128))
                v.tensor_copy(B.KF[:].rearrange("p (h t) -> p h t", t=S)[:, :, gt * 128:(gt + 1) * 128],
                              B.psT1[:, 512:1024].rearrange("p (h d) -> p h d", d=128)).then_inc(sV1, 1)
        for a in range(NH if layer == 0 else NQB):
            for b2 in range(NQB if layer == 0 else NH):
                h, qb = (a, b2) if layer == 0 else (b2, a)
                i = hq(h, qb)
                nk = kts(qb)
                eb = exp_base(h, qb)
                for jj in range(4):
                    kt = 4 * qb + jj
                    v.wait_ge(sX, eb + kt + 1)
                    dg = B.PT[:, kt * 512 + jj * 128:kt * 512 + (jj + 1) * 128]
                    v.tensor_tensor(dg, dg, C.cst.mask01, ALU.mult).then_inc(sM, 1)
                if layer == 0:
                    v.wait_ge(sT2, pv_ct[i])
                    v.reciprocal(B.rr[0:1, :], B.psSum[:]).then_inc(sV2, 1)
                    v.wait_ge(sT2, rb_ct[i])
                    if i >= 1:
                        v.wait_ge(sO, 16 * i)
                    dedge(v.tensor_copy(B.rB[:], B.psR[:]))
                    v.tensor_tensor(B.attnb[:], B.psO[:], B.rB[:], ALU.mult).then_inc(sV2, 1)
                else:
                    for qt in range(4):
                        v.wait_ge(sT2, pvqt_ct[(i, qt)])
                        v.wait_ge(sV2, 4 * i + qt)
                        if i >= 1:
                            v.wait_ge(sO, 64 * i)
                        dedge(v.reciprocal(B.tnb[:, 3:4], B.psO3[:, qt * 512 + 128:qt * 512 + 129]))
                        dedge(v.tensor_tensor(B.tnb[:, 3:4], B.tnb[:, 3:4], B.tnb[:, 2:3], ALU.mult))
                        v.tensor_scalar(B.attnb[:, qt * 128:(qt + 1) * 128],
                                        B.psO3[:, qt * 512:qt * 512 + 128],
                                        B.tnb[:, 3:4], None, ALU.mult).then_inc(sV2, 1)

    C.bodies = {"sync": f_sync, "scalar": f_scalar, "vector": f_vector,
                "tensor": f_tensor, "gpsimd": f_gpsimd}


def emit_post_f(C, layer):
    sm = C.sems
    B, io, cst = C.B, C.io, C.cst
    sC2, sAT = sm["sC2"], sm["sAT"]
    sATa, sATb, sATc, sATd = sm["sATa"], sm["sATb"], sm["sATc"], sm["sATd"]
    sXc, sXd, sX1Oc, sX1Od = sm["sXc"], sm["sXd"], sm["sX1Oc"], sm["sX1Od"]
    SX4 = None
    sXa, sXb, sXB, sXBb = sm["sXa"], sm["sXb"], sm["sXB"], sm["sXBb"]
    SX4 = [sXa, sXb, sXc, sXd]
    SAT4 = [sATa, sATb, sATc, sATd]
    SXO4 = [sm["sX1Oa"], sm["sX1Ob"], sX1Oc, sX1Od]
    sWDa, sWDb = sm["sWDa"], sm["sWDb"]
    sW2a, sW2b = sm["sW2a"], sm["sW2b"]
    sXCa, sXCb = sm["sXCa"], sm["sXCb"]
    sPA, sEA, sVA = sm["sPA"], sm["sEA"], sm["sVA"]
    sPB, sAB, sVB = sm["sPB"], sm["sAB"], sm["sVB"]
    sP1, sV1b = sm["sP1"], sm["sV1b"]
    sP2, sP2a, sA3, sV3 = sm["sP2"], sm["sP2a"], sm["sA3"], sm["sV3"]
    sP5, sV5 = sm["sP5"], sm["sV5"]
    sP6, sP6a, sA6, sV6 = sm["sP6"], sm["sP6a"], sm["sA6"], sm["sV6"]
    sX1Oa, sX1Ob = sm["sX1Oa"], sm["sX1Ob"]
    sB8wa, sB8wb = sm["sB8wa"], sm["sB8wb"]
    sB8ra, sB8rb, sB8rc = sm["sB8ra"], sm["sB8rb"], sm["sB8rc"]
    sB2wa, sB2wb = sm["sB2wa"], sm["sB2wb"]
    sB2ra, sB2rb, sB2rc = sm["sB2ra"], sm["sB2rb"], sm["sB2rc"]
    sOa, sOb = sm["sOa"], sm["sOb"]

    NHC = 64
    NSB = 16
    VA_TT = 4 if layer == 0 else 1
    FWN = float(1.0 / np.sqrt(8192.0))
    FWN2 = float(1.0 / np.sqrt(2048.0))
    SCTOT = 16 if layer == 0 else 80
    W1BASE = 8 * 16 if layer == 0 else 0
    n = NT

    def f_sync(e):
        def ld_xx(tt):
            if layer == 0:
                if tt >= 2:
                    e.wait_ge(sVA, VA_TT * (tt - 1))
                e.dma_start(out=B.xinA[:, (tt % 2) * D:(tt % 2 + 1) * D],
                            in_=io.x[tt * 128:(tt + 1) * 128, :]).then_inc(
                                sXa if tt % 2 == 0 else sXb, 16)
            else:
                if tt >= 4:
                    e.wait_ge(sVA, tt - 3)
                e.dma_start(out=B.xinA[:, (tt % 4) * D:(tt % 4 + 1) * D],
                            in_=io.x[tt * 128:(tt + 1) * 128, :]).then_inc(SX4[tt % 4], 16)

        def ld_at(tt):
            attg4 = io.attg.rearrange("q (r t) f -> q t r f", r=4)
            e.dma_start(
                out=B.at1[:, (tt % 4) * 2048:(tt % 4 + 1) * 2048],
                in_=attg4[bass.ds(C.locq, 1), tt * 128:(tt + 1) * 128, :, :],
            ).then_inc(SAT4[tt % 4], 16)

        if layer == 0:
            e.dma_start(out=B.pcol[:, 0:16], in_=io.bd.rearrange("(c p) -> p c", p=128),
                        allow_slow_non_contiguous=True).then_inc(sC2, 16)
            ld_xx(0)
            ld_xx(1)
            e.wait_ge(C.ccsem, C.ccval)
            e.dma_start(out=B.atT[:].rearrange("p (c t) -> p c t", t=1024),
                        in_=io.attg.rearrange("h (r p) t -> p (h r) t", p=128)[:, :, bass.ds(C.loc, 1024)],
                        ).then_inc(sAT, 16)
        else:
            e.dma_start(out=B.pcol[:, 0:16], in_=io.su_ug.rearrange("(c p) -> p c", p=128),
                        allow_slow_non_contiguous=True).then_inc(sC2, 16)
            e.dma_start(out=B.pcol[:, 16:32], in_=io.wdp.rearrange("(c p) -> p c", p=128),
                        allow_slow_non_contiguous=True).then_inc(sC2, 16)
            e.dma_start(out=B.pcol2[:, 0:64], in_=io.wsu.rearrange("(c p) -> p c", p=128),
                        allow_slow_non_contiguous=True).then_inc(sC2, 16)
            e.dma_start(out=B.pcol2[:, 64:128], in_=io.wsg.rearrange("(c p) -> p c", p=128),
                        allow_slow_non_contiguous=True).then_inc(sC2, 16)
            e.dma_start(out=B.pcol2[:, 128:192], in_=io.sdp.rearrange("(c p) -> p c", p=128),
                        allow_slow_non_contiguous=True).then_inc(sC2, 16)
            for q in range(4):
                ld_xx(q)
            e.wait_ge(C.ccsem, C.ccval)
            for q in range(4):
                ld_at(q)

        NPF = 2 if layer == 0 else 4
        for tt in range(NT):
            if tt + NPF < NT:
                ld_xx(tt + NPF)
                if layer == 1:
                    ld_at(tt + NPF)
            e.wait_ge(sVA, VA_TT * (tt + 1))
            if layer == 0:
                e.dma_start(out=io.x1buf[tt * 128:(tt + 1) * 128, :],
                            in_=B.x1tA[:, (tt % 2) * D:(tt % 2 + 1) * D]).then_inc(
                                sX1Oa if tt % 2 == 0 else sX1Ob, 16)
            else:
                e.dma_start(out=io.x1buf[tt * 128:(tt + 1) * 128, :],
                            in_=B.x1tA[:, (tt % 4) * D:(tt % 4 + 1) * D]).then_inc(SXO4[tt % 4], 16)
        if layer == 0:
            e.wait_ge(sX1Oa, 16 * (NT // 2))
            e.wait_ge(sX1Ob, 16 * (NT // 2))
        else:
            for q in range(4):
                e.wait_ge(SXO4[q], 32)
        for tt in range(NT if layer == 0 else 4):
            if tt >= 2:
                e.wait_ge(sAB, 8 + tt)
            e.dma_start(out=B.xtB[:, (tt % 2) * D:(tt % 2 + 1) * D],
                        in_=io.x1buf[tt * 128:(tt + 1) * 128, :]).then_inc(
                            sXB if tt % 2 == 0 else sXBb, 16)
        if layer == 1:
            for fc in range(NDC):
                e.wait_ge(sV1b, fc + 1)
                e.dma_start(out=io.bounce2[fc * 128:(fc + 1) * 128, :],
                            in_=B.c1b[:, (fc % 2) * 1024:(fc % 2 + 1) * 1024]).then_inc(
                                sB2wa if fc % 2 == 0 else sB2wb, 16)
            b2v = io.bounce2.rearrange("(p c) t -> p c t", c=16)
            e.wait_ge(sB2wa, 16 * (NDC // 2))
            e.wait_ge(sB2wb, 16 * (NDC // 2))
            for g in range(NDC // 2):
                if g >= 3:
                    e.wait_ge(sP1, 16 + 2 * (g - 2))
                e.dma_start(out=B.rcb[:, (g % 3) * 2048:(g % 3) * 2048 + 2048].rearrange("p (c t) -> p c t", t=1024),
                            in_=b2v[:, 2 * g:2 * g + 2, :]).then_inc(
                                [sB2ra, sB2rb, sB2rc][g % 3], 16)
        for hc in range(NHC):
            e.wait_ge(sV3, 2 * hc + 2)
            e.dma_start(out=io.bounce8[hc * 128:(hc + 1) * 128, :],
                        in_=B.y1b[:, (hc % 2) * 1024:(hc % 2 + 1) * 1024]).then_inc(
                            sB8wa if hc % 2 == 0 else sB8wb, 16)
        b8v = io.bounce8.rearrange("(p c) t -> p c t", c=64)
        e.wait_ge(sB8wa, 16 * (NHC // 2))
        e.wait_ge(sB8wb, 16 * (NHC // 2))
        for g in range(NHC // 2):
            if g >= 3:
                e.wait_ge(sP5, 2 * (g - 2))
            e.dma_start(out=B.rcb[:, (g % 3) * 2048:(g % 3) * 2048 + 2048].rearrange("p (c t) -> p c t", t=1024),
                        in_=b8v[:, 2 * g:2 * g + 2, :]).then_inc(
                            [sB8ra, sB8rb, sB8rc][g % 3], 16)

        def ld_x1cb(doc):
            if doc >= 2:
                e.wait_ge(sV6, 8 * (doc - 1))
            e.dma_start(out=B.x1cb[:, (doc % 2) * 1024:(doc % 2 + 1) * 1024].rearrange("p (tt c) -> p tt c", c=128),
                        in_=io.x1buf.rearrange("(tt p) d -> p tt d", p=128)[:, :, doc * 128:(doc + 1) * 128],
                        ).then_inc(sXCa if doc % 2 == 0 else sXCb, 16)

        ld_x1cb(0)
        ld_x1cb(1)
        for doc in range(16):
            for tt in range(NT):
                ti = doc * 8 + tt
                e.wait_ge(sV6, ti + 1)
                e.dma_start(out=io.xout[tt * 128:(tt + 1) * 128, doc * 128:(doc + 1) * 128],
                            in_=B.x2b[:, (ti % 2) * 128:(ti % 2 + 1) * 128]).then_inc(
                                sOa if ti % 2 == 0 else sOb, 16)
            if doc + 2 < 16:
                ld_x1cb(doc + 2)
        e.wait_ge(sOa, 16 * 64)
        e.wait_ge(sOb, 16 * 64)

    def f_gpsimd(g):
        if layer == 0:
            wdv = io.wd.rearrange("(c p) f -> p c f", p=128)
            for fc in range(NDC):
                if fc >= 2:
                    g.wait_ge(sPA, fc - 1)
                g.dma_start(out=B.wdstr[:, (fc % 2) * 2048:(fc % 2 + 1) * 2048].rearrange("p (c f) -> p c f", f=128),
                            in_=wdv[:, :, fc * 128:(fc + 1) * 128]).then_inc(
                                sWDa if fc % 2 == 0 else sWDb, 16)
        if layer == 0:
            w1v = io.w1t.rearrange("(c p) h -> p c h", p=128)
            w3v = io.w3t.rearrange("(c p) h -> p c h", p=128)
        else:
            w1v = io.w1t.rearrange("(p c) h -> p c h", c=16)
            w3v = io.w3t.rearrange("(p c) h -> p c h", c=16)
        for sbk in range(NSB):
            if sbk >= 2:
                g.wait_ge(sP2, 4 * (sbk - 1))
            else:
                g.wait_ge(sAB, 17)
                g.wait_ge(sVB, 34)
            base = (sbk % 2) * 16384
            g.dma_start(out=B.wstr[:, base:base + 8192].rearrange("p (c h) -> p c h", h=512),
                        in_=w1v[:, :, sbk * 512:(sbk + 1) * 512]).then_inc(
                            sWDa if sbk % 2 == 0 else sWDb, 16)
            g.dma_start(out=B.wstr[:, base + 8192:base + 16384].rearrange("p (c h) -> p c h", h=512),
                        in_=w3v[:, :, sbk * 512:(sbk + 1) * 512]).then_inc(
                            sWDa if sbk % 2 == 0 else sWDb, 16)
        w2v = io.w2t.rearrange("(p c) dd -> p c dd", c=64)
        for doc in range(16):
            if doc >= 2:
                g.wait_ge(sP6, doc - 1)
            else:
                g.wait_ge(sP2, 64)
            g.dma_start(out=B.w2str[:, (doc % 2) * 8192:(doc % 2 + 1) * 8192].rearrange("p (c dd) -> p c dd", dd=128),
                        in_=w2v[:, :, doc * 128:(doc + 1) * 128]).then_inc(
                            sW2a if doc % 2 == 0 else sW2b, 16)

    def f_tensor(t):
        t.wait_ge(sC2, SCTOT)
        if layer == 0:
            t.wait_ge(sAT, 16)
            atv = B.atT.rearrange("p (c t) -> p c t", t=1024)
            for fc in range(NDC):
                t.wait_ge(sWDa if fc % 2 == 0 else sWDb, 16 * (fc // 2 + 1))
                if fc >= 2:
                    t.wait_ge(sEA, fc - 1)
                half = (fc % 2) * 1024
                ins = None
                for tc in range(2):
                    for dc in range(NDC):
                        ins = t.matmul(B.psF[:, half + tc * 512:half + (tc + 1) * 512],
                                       B.wdstr[:, (fc % 2) * 2048 + dc * 128:(fc % 2) * 2048 + (dc + 1) * 128],
                                       atv[:, dc, tc * 512:(tc + 1) * 512],
                                       start=(dc == 0), stop=(dc == NDC - 1))
                ins.then_inc(sPA, 1)
            t.wait_ge(sEA, 16)
            for tt in range(NT):
                for grp in range(4):
                    gi = tt * 4 + grp
                    if gi >= 2:
                        t.wait_ge(sVA, gi - 1)
                    ins = None
                    for j in range(4):
                        fc = grp * 4 + j
                        ins = t.transpose(B.psTd[:, (gi % 2) * 1024 + j * 128:(gi % 2) * 1024 + (j + 1) * 128],
                                          B.dnsF[:, fc * 1024 + tt * 128:fc * 1024 + (tt + 1) * 128],
                                          cst.ident)
                    ins.then_inc(sPA, 1)
        for tt in range(NT):
            t.wait_ge(sAB, 10 + tt)
            for grp in range(4):
                gi = tt * 4 + grp
                if gi >= 2:
                    t.wait_ge(sVB, 2 + gi - 1)
                ins = None
                for j in range(4):
                    fc = grp * 4 + j
                    ins = t.transpose(B.psTd[:, (gi % 2) * 1024 + j * 128:(gi % 2) * 1024 + (j + 1) * 128],
                                      B.lntB[:, (tt % 2) * D + fc * 128:(tt % 2) * D + (fc + 1) * 128],
                                      cst.ident)
                ins.then_inc(sPB, 1)
        if layer == 1:
            t.wait_ge(sVB, 34)
            for fc in range(NDC):
                if fc >= 1:
                    t.wait_ge(sV1b, fc)
                ins = None
                for tc in range(2):
                    ins = t.matmul(B.psB2[:, tc * 512:(tc + 1) * 512], cst.bd16,
                                   B.lnF[:, fc * 1024 + tc * 512:fc * 1024 + (tc + 1) * 512],
                                   start=True, stop=True)
                ins.then_inc(sP1, 1)
            for c in range(NDC):
                g = c // 2
                rb = (g % 3) * 2048 + (c % 2) * 1024
                t.wait_ge([sB2ra, sB2rb, sB2rc][g % 3], 16 * (g // 3 + 1))
                t.wait_ge(sV1b, 16 + c)
                ins = None
                for tc in range(2):
                    ins = t.matmul(B.psB2[:, tc * 512:(tc + 1) * 512], cst.h128,
                                   B.rcb[:, rb + tc * 512:rb + (tc + 1) * 512],
                                   start=True, stop=True)
                ins.then_inc(sP1, 1)
        else:
            t.wait_ge(sVB, 34)
        if layer == 1:
            t.wait_ge(sV1b, 32)

        def upgate(hc):
            sbk, hcl = hc // 4, hc % 4
            if hcl == 0:
                t.wait_ge(sWDa if sbk % 2 == 0 else sWDb, W1BASE + 32 * (sbk // 2 + 1))
            p = hc % 2
            if hc >= 2:
                t.wait_ge(sV3, 2 * (hc - 2) + 2)
                t.wait_ge(sA3, hc - 1)
            wb = (sbk % 2) * 16384
            ins = None
            for cc in range(NDC):
                for tc in range(2):
                    t.matmul(B.psU[:, p * 1024 + tc * 512:p * 1024 + (tc + 1) * 512],
                             B.wstr[:, wb + cc * 512 + hcl * 128:wb + cc * 512 + (hcl + 1) * 128],
                             B.lnF[:, cc * 1024 + tc * 512:cc * 1024 + (tc + 1) * 512],
                             start=(cc == 0), stop=(cc == NDC - 1))
                    ins = t.matmul(B.psG[:, p * 1024 + tc * 512:p * 1024 + (tc + 1) * 512],
                                   B.wstr[:, wb + 8192 + cc * 512 + hcl * 128:wb + 8192 + cc * 512 + (hcl + 1) * 128],
                                   B.lnF[:, cc * 1024 + tc * 512:cc * 1024 + (tc + 1) * 512],
                                   start=(cc == 0), stop=(cc == NDC - 1))
            ins.then_inc(sP2, 1)

        def stageA(hc):
            p = hc % 2
            t.wait_ge(sV3, 2 * hc + 1)
            t.wait_ge(sA3, hc + 1)
            ins = None
            for tc in range(2):
                ins = t.matmul(B.psU[:, p * 1024 + tc * 512:p * 1024 + (tc + 1) * 512],
                               cst.bd64,
                               B.hck[:, p * 1024 + tc * 512:p * 1024 + (tc + 1) * 512],
                               start=True, stop=True)
            ins.then_inc(sP2a, 1)

        upgate(0)
        for hc in range(1, NHC):
            upgate(hc)
            stageA(hc - 1)
        stageA(NHC - 1)
        for c in range(NHC):
            g = c // 2
            rb = (g % 3) * 2048 + (c % 2) * 1024
            ps5 = B.psB2 if c % 2 == 0 else B.psB2b
            t.wait_ge([sB8ra, sB8rb, sB8rc][g % 3], 16 * (g // 3 + 1))
            if c >= 2:
                t.wait_ge(sV5, c - 1)
            ins = None
            for tc in range(2):
                ins = t.matmul(ps5[:, tc * 512:(tc + 1) * 512], cst.h128,
                               B.rcb[:, rb + tc * 512:rb + (tc + 1) * 512],
                               start=True, stop=True)
            ins.then_inc(sP5, 1)
        t.wait_ge(sV5, 64)

        def psy(doc):
            p = doc % 2
            t.wait_ge(sW2a if p == 0 else sW2b, 16 * (doc // 2 + 1))
            if doc >= 2:
                t.wait_ge(sA6, doc - 1)
            ins = None
            for c in range(NHC):
                for tc in range(2):
                    ins = t.matmul(B.psY[:, p * 1024 + tc * 512:p * 1024 + (tc + 1) * 512],
                                   B.w2str[:, p * 8192 + c * 128:p * 8192 + (c + 1) * 128],
                                   B.h2[:, c * 1024 + tc * 512:c * 1024 + (tc + 1) * 512],
                                   start=(c == 0), stop=(c == NHC - 1))
            ins.then_inc(sP6, 1)

        def trans6(doc):
            t.wait_ge(sA6, doc + 1)
            for tt in range(NT):
                ti = doc * 8 + tt
                if ti >= 2:
                    t.wait_ge(sV6, ti - 1)
                t.transpose(B.psTd[:, (ti % 2) * 1024:(ti % 2) * 1024 + 128],
                            B.ysb[:, (doc % 2) * 1024 + tt * 128:(doc % 2) * 1024 + (tt + 1) * 128],
                            cst.ident).then_inc(sP6a, 1)

        psy(0)
        for doc in range(1, 16):
            psy(doc)
            trans6(doc - 1)
        trans6(15)

    def f_scalar(s):
        s.wait_ge(sC2, SCTOT)
        if layer == 0:
            for fc in range(NDC):
                s.wait_ge(sPA, fc + 1)
                s.activation(B.dnsF[:, fc * 1024:(fc + 1) * 1024],
                             B.psF[:, (fc % 2) * 1024:(fc % 2 + 1) * 1024],
                             AF.Identity, bias=B.pcol[:, fc:fc + 1],
                             scale=1.0).then_inc(sEA, 1)
        NSL = 2 if layer == 0 else 4
        for tt in range(NT):
            s.wait_ge(sVA, VA_TT * (tt + 1))
            if tt >= 1:
                s.wait_ge(sAB, tt)
            s.activation(B.scrB[:],
                         B.x1tA[:, (tt % NSL) * D:(tt % NSL + 1) * D], AF.Square,
                         accum_out=B.smq[:, NT + tt:NT + tt + 1]).then_inc(sAB, 1)
        s.wait_ge(sVB, 1)
        s.activation(B.stt[:, NT:2 * NT], B.stt[:, NT:2 * NT], AF.Sqrt).then_inc(sAB, 1)
        s.wait_ge(sVB, 2)
        for tt in range(NT):
            if layer == 0 or tt < 4:
                s.wait_ge(sXB if tt % 2 == 0 else sXBb, 16 * (tt // 2 + 1))
                asrc = B.xtB[:, (tt % 2) * D:(tt % 2 + 1) * D]
            else:
                asrc = B.x1tA[:, (tt % 4) * D:(tt % 4 + 1) * D]
            if tt >= 2:
                s.wait_ge(sPB, 4 * (tt - 1))
            s.activation(B.lntB[:, (tt % 2) * D:(tt % 2 + 1) * D],
                         asrc, AF.Identity,
                         bias=B.stt[:, 3 * NT + tt:3 * NT + tt + 1],
                         scale=B.stt[:, 2 * NT + tt:2 * NT + tt + 1]).then_inc(sAB, 1)
        for hc in range(NHC):
            p = hc % 2
            s.wait_ge(sP2, hc + 1)
            if hc >= 2:
                s.wait_ge(sV3, 2 * (hc - 2) + 1)
            if layer == 0:
                s.activation(B.sgb[:, p * 1024:(p + 1) * 1024],
                             B.psU[:, p * 1024:(p + 1) * 1024], AF.Silu).then_inc(sA3, 1)
            else:
                s.activation(B.sgb[:, p * 1024:(p + 1) * 1024],
                             B.psG[:, p * 1024:(p + 1) * 1024], AF.Silu,
                             scale=B.pcol2[:, 64 + hc:64 + hc + 1]).then_inc(sA3, 1)
        for doc in range(16):
            p = doc % 2
            s.wait_ge(sP6, doc + 1)
            if doc >= 2:
                s.wait_ge(sP6a, 8 * (doc - 1))
            if layer == 0:
                s.activation(B.ysb[:, p * 1024:(p + 1) * 1024],
                             B.psY[:, p * 1024:(p + 1) * 1024], AF.Identity).then_inc(sA6, 1)
            else:
                s.activation(B.ysb[:, p * 1024:(p + 1) * 1024],
                             B.psY[:, p * 1024:(p + 1) * 1024], AF.Identity,
                             scale=B.pcol[:, 16 + doc:16 + doc + 1]).then_inc(sA6, 1)

    def f_vector(v):
        nd = [0]
        sD = sm["sD"]

        def dedge(ins):
            nd[0] += 1
            ins.then_inc(sD, 1)
            v.wait_ge(sD, nd[0])

        v.wait_ge(sC2, SCTOT)
        if layer == 1:
            dedge(v.tensor_tensor(B.pcol2[:, 0:64], B.pcol2[:, 0:64], B.pcol2[:, 128:192], ALU.mult))
        if layer == 0:
            for tt in range(NT):
                v.wait_ge(sXa if tt % 2 == 0 else sXb, 16 * (tt // 2 + 1))
                if tt >= 2:
                    v.wait_ge(sX1Oa if tt % 2 == 0 else sX1Ob, 16 * (tt // 2))
                    v.wait_ge(sAB, tt - 1)
                for grp in range(4):
                    gi = tt * 4 + grp
                    v.wait_ge(sPA, 16 + gi + 1)
                    v.tensor_tensor(B.x1tA[:, (tt % 2) * D + grp * 512:(tt % 2) * D + (grp + 1) * 512],
                                    B.psTd[:, (gi % 2) * 1024:(gi % 2) * 1024 + 512],
                                    B.xinA[:, (tt % 2) * D + grp * 512:(tt % 2) * D + (grp + 1) * 512],
                                    ALU.add).then_inc(sVA, 1)
                v.wait_ge(sVA, 4 * (tt + 1))
                dedge(v.reduce_sum(B.smq[:, tt:tt + 1],
                                   B.x1tA[:, (tt % 2) * D:(tt % 2 + 1) * D], axis=AX.X))
        else:
            for tt in range(NT):
                v.wait_ge(SX4[tt % 4], 16 * (tt // 4 + 1))
                v.wait_ge(SAT4[tt % 4], 16 * (tt // 4 + 1))
                if tt >= 4:
                    v.wait_ge(SXO4[tt % 4], 16 * (tt // 4))
                    v.wait_ge(sAB, tt - 3)
                if tt >= 1:
                    v.wait_ge(sVA, tt)
                dedge(v.tensor_copy(B.a32[:, 0:2048],
                                     B.at1[:, (tt % 4) * 2048:(tt % 4 + 1) * 2048]))
                v.tensor_tensor(B.x1tA[:, (tt % 4) * D:(tt % 4 + 1) * D],
                                B.a32[:, 0:2048],
                                B.xinA[:, (tt % 4) * D:(tt % 4 + 1) * D],
                                ALU.add).then_inc(sVA, 1)
                v.wait_ge(sVA, tt + 1)
                dedge(v.reduce_sum(B.smq[:, tt:tt + 1],
                                   B.x1tA[:, (tt % 4) * D:(tt % 4 + 1) * D], axis=AX.X))
        mean, veps, rstd, nmrs = (B.stt[:, 0:n], B.stt[:, n:2 * n],
                                  B.stt[:, 2 * n:3 * n], B.stt[:, 3 * n:4 * n])

        v.wait_ge(sAB, NT)
        dedge(v.tensor_scalar(mean, B.smq[:, 0:NT], 1.0 / D, None, ALU.mult))
        dedge(v.tensor_scalar(veps, B.smq[:, NT:2 * NT], 1.0 / D, LN_EPS, ALU.mult, ALU.add))
        dedge(v.tensor_tensor(rstd, mean, mean, ALU.mult))
        v.tensor_sub(veps, veps, rstd).then_inc(sVB, 1)
        v.wait_ge(sAB, NT + 1)
        dedge(v.reciprocal(rstd, veps))
        dedge(v.tensor_tensor(nmrs, mean, rstd, ALU.mult))
        v.tensor_scalar(nmrs, nmrs, -1.0, None, ALU.mult).then_inc(sVB, 1)
        for tt in range(NT):
            for grp in range(4):
                gi = tt * 4 + grp
                v.wait_ge(sPB, gi + 1)
                dst3 = B.lnF[:].rearrange("p (fc t) -> p fc t", t=1024)[
                    :, grp * 4:(grp + 1) * 4, tt * 128:(tt + 1) * 128]
                src3 = B.psTd[:, (gi % 2) * 1024:(gi % 2) * 1024 + 512].rearrange(
                    "p (j t) -> p j t", t=128)
                if layer == 0:
                    v.tensor_copy(dst3, src3).then_inc(sVB, 1)
                else:
                    sucol = B.pcol[:, grp * 4:(grp + 1) * 4].unsqueeze(2).broadcast_to([128, 4, 128])
                    v.tensor_tensor(dst3, src3, sucol, ALU.mult).then_inc(sVB, 1)
        if layer == 1:
            for fc in range(NDC):
                v.wait_ge(sP1, fc + 1)
                if fc >= 2:
                    v.wait_ge(sB2wa if fc % 2 == 0 else sB2wb, 16 * (fc // 2))
                v.tensor_copy(B.c1b[:, (fc % 2) * 1024:(fc % 2 + 1) * 1024],
                              B.psB2[:]).then_inc(sV1b, 1)
            for c in range(NDC):
                v.wait_ge(sP1, 16 + c + 1)
                v.tensor_scalar(B.lnF[:, c * 1024:(c + 1) * 1024], B.psB2[:],
                                FWN2, None, ALU.mult).then_inc(sV1b, 1)
        for hc in range(NHC):
            p = hc % 2
            v.wait_ge(sA3, hc + 1)
            if hc >= 2:
                v.wait_ge(sP2a, hc - 1)
            if layer == 0:
                v.tensor_tensor(B.hck[:, p * 1024:(p + 1) * 1024],
                                B.psG[:, p * 1024:(p + 1) * 1024],
                                B.sgb[:, p * 1024:(p + 1) * 1024], ALU.mult).then_inc(sV3, 1)
            else:
                v.scalar_tensor_tensor(B.hck[:, p * 1024:(p + 1) * 1024],
                                       B.psU[:, p * 1024:(p + 1) * 1024],
                                       B.pcol2[:, hc:hc + 1],
                                       B.sgb[:, p * 1024:(p + 1) * 1024],
                                       ALU.mult, ALU.mult).then_inc(sV3, 1)
            v.wait_ge(sP2a, hc + 1)
            if hc >= 2:
                v.wait_ge(sB8wa if hc % 2 == 0 else sB8wb, 16 * (hc // 2))
            v.tensor_copy(B.y1b[:, p * 1024:(p + 1) * 1024],
                          B.psU[:, p * 1024:(p + 1) * 1024]).then_inc(sV3, 1)
        for c in range(NHC):
            v.wait_ge(sP5, c + 1)
            v.tensor_scalar(B.h2[:, c * 1024:(c + 1) * 1024],
                            (B.psB2 if c % 2 == 0 else B.psB2b)[:],
                            FWN, None, ALU.mult).then_inc(sV5, 1)
        for doc in range(16):
            v.wait_ge(sXCa if doc % 2 == 0 else sXCb, 16 * (doc // 2 + 1))
            for tt in range(NT):
                ti = doc * 8 + tt
                v.wait_ge(sP6a, ti + 1)
                if ti >= 2:
                    v.wait_ge(sOa if ti % 2 == 0 else sOb, 16 * (ti // 2))
                v.tensor_tensor(B.x2b[:, (ti % 2) * 128:(ti % 2 + 1) * 128],
                                B.psTd[:, (ti % 2) * 1024:(ti % 2) * 1024 + 128],
                                B.x1cb[:, (doc % 2) * 1024 + tt * 128:(doc % 2) * 1024 + (tt + 1) * 128],
                                ALU.add).then_inc(sV6, 1)

    C.bodies = {"sync": f_sync, "scalar": f_scalar, "vector": f_vector,
                "tensor": f_tensor, "gpsimd": f_gpsimd}


def build_fused():
    nc = bass.Bass("TRN2", target_bir_lowering=False)
    I32 = mybir.dt.int32
    dt_ = nc.dram_tensor
    io_x = dt_("x", [TOK, D], F32, kind="ExternalInput").ap()
    io_wqkv = dt_("wqkv_sl", [D, 1536], F32, kind="ExternalInput").ap()
    io_bqkv = dt_("bqkv_sl", [1536], F32, kind="ExternalInput").ap()
    io_wd = dt_("wd", [D, D], F32, kind="ExternalInput").ap()
    io_bd = dt_("bd", [D], F32, kind="ExternalInput").ap()
    io_w1t0 = dt_("w1t0", [D, HF], F32, kind="ExternalInput").ap()
    io_w3t0 = dt_("w3t0", [D, HF], F32, kind="ExternalInput").ap()
    io_w2t0 = dt_("w2t0", [HF, D], F32, kind="ExternalInput").ap()
    io_wsl1 = dt_("wsl1", [D, 1536], F32, kind="ExternalInput").ap()
    io_tau = dt_("tau", [1], F32, kind="ExternalInput").ap()
    io_nu = dt_("nu", [1], F32, kind="ExternalInput").ap()
    io_su = dt_("su_ug", [D], F32, kind="ExternalInput").ap()
    io_wsu = dt_("wsu", [HF], F32, kind="ExternalInput").ap()
    io_wsg = dt_("wsg", [HF], F32, kind="ExternalInput").ap()
    io_sdp = dt_("sdp", [HF], F32, kind="ExternalInput").ap()
    io_wdp = dt_("wdp", [D], F32, kind="ExternalInput").ap()
    io_w1t1 = dt_("w1t1", [D, HF], F32, kind="ExternalInput").ap()
    io_w3t1 = dt_("w3t1", [D, HF], F32, kind="ExternalInput").ap()
    io_w2t1 = dt_("w2t1", [HF, D], F32, kind="ExternalInput").ap()
    io_loc = dt_("myloc", [2], I32, kind="ExternalInput").ap()
    io_out = dt_("out", [TOK, D], F32, kind="ExternalOutput").ap()

    cc_ln0_in = dt_("cc_ln0_in", [4, D, 256], BF16, kind="Internal").ap()
    cc_ln0_out = dt_("cc_ln0_out", [4, 4 * D, 256], BF16, kind="Internal").ap()
    cc_at0_in = dt_("cc_at0_in", [512, S], BF16, kind="Internal").ap()
    cc_at0_out = dt_("cc_at0_out", [4, 512, S], BF16, kind="Internal").ap()
    cc_ln1_in = dt_("cc_ln1_in", [4, D, 256], BF16, kind="Internal").ap()
    cc_ln1_out = dt_("cc_ln1_out", [4, 4 * D, 256], BF16, kind="Internal").ap()
    cc_at1_in = dt_("cc_at1_in", [4, TOK, 512], BF16, kind="Internal").ap()
    cc_at1_out = dt_("cc_at1_out", [4, 4 * TOK, 512], BF16, kind="Internal").ap()
    x1mid = dt_("x1mid", [TOK, D], F32, kind="Internal").ap()
    x1buf0 = dt_("x1buf0", [TOK, D], F32, kind="Internal").ap()
    x1buf1 = dt_("x1buf1", [TOK, D], F32, kind="Internal").ap()
    bounce8a = dt_("bounce8a", [HF, TOK], BF16, kind="Internal").ap()
    bounce8b = dt_("bounce8b", [HF, TOK], BF16, kind="Internal").ap()
    bounce2 = dt_("bounce2", [D, TOK], BF16, kind="Internal").ap()

    ident_c = nc.inline_tensor(np.eye(128, dtype=bfdt), name="c_ident").ap()
    mask_np = (np.arange(128)[:, None] <= np.arange(128)[None, :]).astype(bfdt)
    mask_c = nc.inline_tensor(mask_np, name="c_mask").ap()
    h128_c = nc.inline_tensor(H128_NP.astype(bfdt), name="c_h128").ap()
    bd64_c = nc.inline_tensor(BD64_NP.astype(bfdt), name="c_bd64").ap()
    bd16_c = nc.inline_tensor(BD16_NP.astype(bfdt), name="c_bd16").ap()
    ones_f = nc.inline_tensor(np.ones((1, 128), dtype=np.float32), name="c_onesf").ap()
    ones_b = nc.inline_tensor(np.ones((1, 128), dtype=bfdt), name="c_onesb").ap()

    import contextlib
    es = contextlib.ExitStack()
    with es:
        sb = lambda *a: es.enter_context(nc.sbuf_tensor(*a))
        sem = lambda nm: es.enter_context(nc.semaphore(nm))

        ARN = sb("ARN", [128, 91648], BF16)
        cst = _NS(
            ident=sb("c_ident_s", [128, 128], BF16)[:],
            mask01=sb("c_mask_s", [128, 128], BF16)[:],
            h128=sb("c_h128_s", [128, 128], BF16)[:],
            bd64=sb("c_bd64_s", [128, 128], BF16)[:],
            bd16=sb("c_bd16_s", [128, 128], BF16)[:],
            ones128=sb("c_onesf_s", [1, 128], F32)[:],
            onesb=sb("c_onesb_s", [1, 128], BF16)[:],
        )
        locs_t = sb("locs", [1, 2], I32)
        att_small = dict(
            bqs=sb("a_bqs", [128, 12], F32)[:],
            bvrow=sb("a_bvrow", [1, 512], BF16)[:],
            taunu=sb("a_taunu", [1, 2], F32)[:],
            tnb=sb("a_tnb", [128, 4], F32)[:],
            ssq=sb("a_ssq", [128, 36], F32)[:],
            csh=sb("a_csh", [128, 2], F32)[:],
            rr=sb("a_rr", [1, 512], F32)[:],
            rB=sb("a_rB", [128, 512], F32)[:],
            attnb=sb("a_attnb", [128, 512], BF16)[:],
            scrq=sb("a_scrq", [128, 1536], F32)[:],
        )
        post_small = dict(
            pcol=sb("p_pcol", [128, 32], F32)[:],
            pcol2=sb("p_pcol2", [128, 192], F32)[:],
            smq=sb("p_smq", [128, 2 * NT], F32)[:],
            stt=sb("p_stt", [128, 4 * NT], F32)[:],
            x1cb=sb("p_x1cb", [128, 2048], F32)[:],
            ysb=sb("p_ysb", [128, 2048], BF16)[:],
            x2b=sb("p_x2b", [128, 256], F32)[:],
        )
        psA = es.enter_context(nc.psum_tensor("psA", [128, 4096], F32))
        reg = es.enter_context(nc.sync.register("locreg"))
        reg2 = es.enter_context(nc.sync.register("locqreg"))

        sCst = sem("sCst")
        sBAR = sem("sBAR")
        sCC = [sem(f"sCC{i}") for i in range(4)]
        PRE_KEYS = ["sL", "sL2", "sV", "sA", "sT", "sE", "sO", "sD"]
        ATT_KEYS = ["sL", "sW", "sP0", "sPE", "sA1", "sV1", "sT2", "sX",
                    "sM", "sV2", "sO", "sB0", "sB1", "sD"]
        POST_KEYS = ["sC2", "sAT", "sATa", "sATb", "sATc", "sATd",
                     "sXa", "sXb", "sXc", "sXd", "sX1Oc", "sX1Od", "sXB",
                     "sWDa", "sWDb", "sW2a", "sW2b", "sXCa", "sXCb",
                     "sPA", "sEA", "sVA", "sPB", "sAB", "sVB", "sP1", "sV1b",
                     "sP2", "sP2a", "sA3", "sV3", "sP5", "sV5", "sP6", "sP6a", "sA6", "sV6",
                     "sX1Oa", "sX1Ob", "sB8wa", "sB8wb", "sB8ra", "sB8rb", "sB8rc",
                     "sB2wa", "sB2wb", "sB2ra", "sB2rb", "sB2rc", "sOa", "sOb",
                     "sXBb", "sD"]
        PRE_SEMS = {k: sem("pre_" + k) for k in PRE_KEYS}
        ATT_SEMS = {k: sem("att_" + k) for k in ATT_KEYS}
        POST_SEMS = {k: sem("post_" + k) for k in POST_KEYS}

        def sem_range(d):
            nums = sorted(h.num for h in d.values())
            assert nums == list(range(nums[0], nums[-1] + 1)), nums
            return range(nums[0], nums[-1] + 1)

        PRE_RANGE = sem_range(PRE_SEMS)
        ATT_RANGE = sem_range(ATT_SEMS)
        POST_RANGE = sem_range(POST_SEMS)

        A = ARN

        pre_sm = sb("pre_sm", [128, NT], F32)[:]
        pre_sq = sb("pre_sq", [128, NT], F32)[:]
        pre_st = sb("pre_st", [128, 4 * NT], F32)[:]

        def mk_pre(io_xa, io_lnTa, _n=[0]):
            C = _NS()
            C.cst = cst
            C.sems = PRE_SEMS
            C.B = _NS(
                xt=A[:, 0:32768].bitcast(F32),
                scr=A[:, 32768:36864].bitcast(F32),
                ln=A[:, 36864:40960],
                lt=A[:, 40960:49152],
                sm=pre_sm, sq=pre_sq, st=pre_st,
                psT=psA[:, 2048:3072].bitcast(BF16),
            )
            C.io = _NS(x=io_xa, lnT=io_lnTa)
            emit_pre_f(C)
            return C

        def mk_att(layer, io_lnTg, io_attc, ccsem):
            C = _NS()
            C.cst = cst
            C.sems = ATT_SEMS
            ab = 49664
            C.B = _NS(
                QF=A[:, 0:16384], KF=A[:, 16384:32768], VT=A[:, 32768:49664],
                PT=A[:, ab:ab + 16384], lnb=A[:, ab:ab + 16384],
                ws=A[:, ab + 16384:ab + 40960], qtok=A[:, ab + 40960:ab + 41984],
                psQK=psA[:, 0:1024], psV=psA[:, 1024:2048],
                psQ1=psA[:, 0:1024], psK1=psA[:, 1024:2048], psV1_=psA[:, 2048:3072],
                psT1=psA[:, 3072:4096].bitcast(BF16),
                psS=psA[:, 0:2048], psO=psA[:, 2048:2560], psR=psA[:, 2560:3072],
                psSum=psA[0:1, 3072:3584], psO3=psA[:, 2048:4096],
                psB0=psA[:, 3584:4096],
                **att_small,
            )
            if layer == 0:
                C.io = _NS(lnTg=io_lnTg, att_c=io_attc, w_sl=io_wqkv, bq=io_bqkv)
            else:
                C.io = _NS(lnTg=io_lnTg, att_c=io_attc, w_sl=io_wsl1, tau=io_tau, nu=io_nu)
            C.ccsem = ccsem
            C.ccval = 1
            emit_att_f(C, layer)
            return C

        def mk_post(layer, io_xa, io_attg, io_xouta, io_x1bufa, io_b8, ccsem, loc_ref):
            C = _NS()
            C.cst = cst
            C.sems = POST_SEMS
            if layer == 0:
                lay = dict(
                    at1=A[:, 32768:36864],
                    a32=A[:, 16384:20480].bitcast(F32),
                    xinA=A[:, 36864:45056].bitcast(F32),
                    x1tA=A[:, 45056:53248].bitcast(F32),
                    lntB=A[:, 32768:36864],
                )
            else:
                lay = dict(
                    at1=A[:, 0:8192],
                    a32=A[:, 24576:28672].bitcast(F32),
                    xinA=A[:, 8192:24576].bitcast(F32),
                    x1tA=A[:, 28672:45056].bitcast(F32),
                    lntB=A[:, 45056:49152],
                )
            C.B = _NS(
                atT=A[:, 0:16384], dnsF=A[:, 16384:32768],
                wdstr=A[:, 32768:36864],
                xtB=A[:, 53248:61440].bitcast(F32),
                scrB=A[:, 61440:65536].bitcast(F32),
                **lay,
                wstr=A[:, 0:32768],
                sgb=A[:, 32768:34816], hck=A[:, 34816:38912],
                y1b=A[:, 38912:40960], c1b=A[:, 36864 + 4096:40960 + 0] if False else A[:, 61440:63488],
                h2=A[:, 0:65536],
                lnF=A[:, 65536:81920],
                rcb=A[:, 83968:90112], w2str=A[:, 67584:83968],
                psF=psA[:, 0:2048], psTd=psA[:, 2048:3072].bitcast(BF16),
                psU=psA[:, 0:2048], psG=psA[:, 2048:4096],
                psB2=psA[:, 3072:4096], psB2b=psA[:, 0:1024], psY=psA[:, 0:2048],
                **post_small,
            )
            if layer == 0:
                C.io = _NS(x=io_xa, attg=io_attg, xout=io_xouta, x1buf=io_x1bufa,
                           bounce8=io_b8, wd=io_wd, bd=io_bd,
                           w1t=io_w1t0, w3t=io_w3t0, w2t=io_w2t0)
            else:
                C.io = _NS(x=io_xa, attg=io_attg, xout=io_xouta, x1buf=io_x1bufa,
                           bounce8=io_b8, bounce2=bounce2,
                           su_ug=io_su, wsu=io_wsu, wsg=io_wsg, sdp=io_sdp,
                           wdp=io_wdp, w1t=io_w1t1, w3t=io_w3t1, w2t=io_w2t1)
            C.ccsem = ccsem
            C.ccval = 4
            C.loc = loc_ref
            emit_post_f(C, layer)
            return C

        loc_holder = _NS(v=None)

        class _LazyLoc:
            def __add__(self, other):
                return loc_holder.v + other

            def __radd__(self, other):
                return other + loc_holder.v

        phases = []
        pre0 = mk_pre(io_x, cc_ln0_in)
        att0 = mk_att(0, cc_ln0_out, cc_at0_in, sCC[0])
        post0 = mk_post(0, io_x, cc_at0_out, x1mid, x1buf0, bounce8a, sCC[1], None)
        pre1 = mk_pre(x1mid, cc_ln1_in)
        pre1.prefetch = (att0.B.ws, io_wsl1, ATT_SEMS["sW"])
        att1 = mk_att(1, cc_ln1_out, cc_at1_in, sCC[2])
        post1 = mk_post(1, x1mid, cc_at1_out, io_out, x1buf1, bounce8b, sCC[3], None)
        phases = [pre0, att0, post0, pre1, att1, post1]

        rg4 = [[0, 1, 2, 3], [4, 5, 6, 7]]

        def ln_pieces(ph, cin, cout, ccs):
            return [(ph, "sO", 16 * 2 * (q + 1), cin[q], cout[q], ccs)
                    for q in range(4)]

        def at0_pieces(ph, cin, cout, ccs):
            return [(ph, "sO", 16 * 8 * (h + 1), cin[h * 128:(h + 1) * 128, :],
                     cout[h], ccs) for h in range(4)]

        def at1_pieces(ph, cin, cout, ccs):
            return [(ph, "sO", 16 * 16 * (2 * q + 2), cin[q], cout[q], ccs)
                    for q in range(4)]

        ag_spec = {
            0: ln_pieces(pre0, cc_ln0_in, cc_ln0_out, sCC[0]),
            1: at0_pieces(att0, cc_at0_in, cc_at0_out, sCC[1]),
            3: ln_pieces(pre1, cc_ln1_in, cc_ln1_out, sCC[2]),
            4: at1_pieces(att1, cc_at1_in, cc_at1_out, sCC[3]),
        }

        block = es.enter_context(nc.Block())
        NPH = len(phases)

        def run_bodies(eng_name):
            def body(eng):
                if eng_name == "sync":
                    eng.dma_start(out=cst.ident, in_=ident_c[:, :]).then_inc(sCst, 16)
                    eng.dma_start(out=cst.mask01, in_=mask_c[:, :]).then_inc(sCst, 16)
                    eng.dma_start(out=cst.h128, in_=h128_c[:, :]).then_inc(sCst, 16)
                    eng.dma_start(out=cst.bd64, in_=bd64_c[:, :]).then_inc(sCst, 16)
                    eng.dma_start(out=cst.bd16, in_=bd16_c[:, :]).then_inc(sCst, 16)
                    eng.dma_start(out=cst.ones128, in_=ones_f[:, :]).then_inc(sCst, 16)
                    eng.dma_start(out=cst.onesb, in_=ones_b[:, :]).then_inc(sCst, 16)
                    eng.dma_start(out=locs_t[0:1, :], in_=io_loc.unsqueeze(0)).then_inc(sCst, 16)
                    eng.wait_ge(sCst, 128)
                    eng.reg_load(reg, locs_t[0:1, 0:1])
                    loc_holder.v = eng.snap(reg)
                    post0.loc = loc_holder.v
                    post1.loc = loc_holder.v
                    eng.reg_load(reg2, locs_t[0:1, 1:2])
                    post0.locq = post1.locq = eng.snap(reg2)
                else:
                    if eng_name == "gpsimd":
                        wv0 = io_wqkv.rearrange("(c p) o -> p c o", p=128)
                        for q in range(4):
                            eng.dma_start(
                                out=att0.B.ws[:, q * 6144:(q + 1) * 6144].rearrange("p (c o) -> p c o", o=1536),
                                in_=wv0[:, q * 4:(q + 1) * 4, :]).then_inc(ATT_SEMS["sW"], 16)
                    eng.wait_ge(sCst, 128)
                clear_spec = {0: PRE_RANGE, 1: ATT_RANGE, 2: POST_RANGE,
                              3: PRE_RANGE, 4: ATT_RANGE}
                for k, C in enumerate(phases):
                    C.bodies[eng_name](eng)
                    if eng_name == "gpsimd" and k in ag_spec:
                        for (ph, semname, cnt, cin, cout, ccs) in ag_spec[k]:
                            eng.wait_ge(ph.sems[semname], cnt)
                            eng.collective_compute(
                                "AllGather", ALU.bypass, replica_groups=rg4,
                                ins=[cin.opt()], outs=[cout.opt()],
                            ).then_inc(ccs)
                    eng.sem_inc(sBAR, 1)
                    eng.wait_ge(sBAR, 6 * k + 5)
                    if eng_name == "gpsimd":
                        if k in clear_spec:
                            eng.dma_reset(clear_spec[k])
                            eng.sem_clear(clear_spec[k])
                        eng.sem_inc(sBAR, 1)
                    eng.wait_ge(sBAR, 6 * k + 6)
            return body

        block.sync(run_bodies("sync"))
        block.gpsimd(run_bodies("gpsimd"))
        block.tensor(run_bodies("tensor"))
        block.scalar(run_bodies("scalar"))
        block.vector(run_bodies("vector"))

    return nc


def kernel_fused(inputs):
    cN = np.ascontiguousarray
    x = cN(inputs["L0_attn_hidden_states"].reshape(2 * S, D)).astype(np.float32)
    Wqkv = inputs["L0_attn_W_qkv"]
    bqkv = inputs["L0_attn_b_qkv"]
    wq, wk, wv = inputs["L1_attn_wq"], inputs["L1_attn_wk"], inputs["L1_attn_wv"]
    wsc = inputs["L1_ffn_Wscale_ug"]
    maps = []
    shared = {
        "wd": cN(inputs["L0_attn_W_dense"].reshape(4, 4, 128, D).transpose(1, 0, 2, 3).reshape(D, D)),
        "bd": cN(inputs["L0_attn_b_dense"]),
        "w1t0": cN(inputs["L0_ffn_W1"].T),
        "w3t0": cN(inputs["L0_ffn_W3"].T),
        "w2t0": cN(inputs["L0_ffn_W2"].T),
        "tau": cN(inputs["L1_attn_tau"]),
        "nu": cN(inputs["L1_attn_nu"]),
        "su_ug": cN(inputs["L1_ffn_SU_ug"]),
        "wsu": cN(wsc[:HF]), "wsg": cN(wsc[HF:]),
        "sdp": cN(inputs["L1_ffn_SU_dp"]),
        "wdp": cN(inputs["L1_ffn_Wscale_dp"]),
        "w1t1": cN(inputs["L1_ffn_W_up"].T),
        "w3t1": cN(inputs["L1_ffn_W_gate"].T),
        "w2t1": cN(inputs["L1_ffn_W_down"].T),
    }
    for c in range(NCORES):
        g = c % 4
        cols = np.concatenate([
            Wqkv[:, 512 * g:512 * (g + 1)],
            Wqkv[:, 2048 + 512 * g:2048 + 512 * (g + 1)],
            Wqkv[:, 4096 + 512 * g:4096 + 512 * (g + 1)]], axis=1)
        bsl = np.concatenate([
            bqkv[512 * g:512 * (g + 1)],
            bqkv[2048 + 512 * g:2048 + 512 * (g + 1)],
            bqkv[4096 + 512 * g:4096 + 512 * (g + 1)]])
        wsl1 = np.concatenate([
            cN(wq[512 * g:512 * (g + 1), :].T),
            cN(wk[512 * g:512 * (g + 1), :].T),
            cN(wv[512 * g:512 * (g + 1), :].T)], axis=1)
        m = dict(shared)
        m["x"] = cN(x[c * TOK:(c + 1) * TOK])
        m["wqkv_sl"] = cN(cols)
        m["bqkv_sl"] = cN(bsl)
        m["wsl1"] = cN(wsl1)
        m["myloc"] = np.array([g * TOK, g], dtype=np.int32)
        maps.append(m)
    nc = _get_nc("fused", build_fused)
    res = run_bass_kernel_spmd(nc, maps, core_ids=list(range(NCORES)))
    out = np.concatenate([res.results[c]["out"] for c in range(NCORES)], axis=0)
    return out.reshape(2, S, D).astype(np.float32)


_NC_CACHE = {}


def _get_nc(name, builder):
    if name not in _NC_CACHE:
        _NC_CACHE[name] = builder()
    return _NC_CACHE[name]


def _run(nc, in_maps):
    return run_bass_kernel_spmd(nc, in_maps, core_ids=list(range(NCORES)))


def kernel(**inputs):
    return kernel_fused(inputs)


def kernel_unfused(**inputs):
    cN = np.ascontiguousarray
    x = cN(inputs["L0_attn_hidden_states"].reshape(2 * S, D)).astype(np.float32)
    xs = [cN(x[c * TOK:(c + 1) * TOK]) for c in range(NCORES)]

    Wqkv = inputs["L0_attn_W_qkv"]
    bqkv = inputs["L0_attn_b_qkv"]
    Wd = cN(inputs["L0_attn_W_dense"])
    bd = cN(inputs["L0_attn_b_dense"])
    W1T = cN(inputs["L0_ffn_W1"].T)
    W2T = cN(inputs["L0_ffn_W2"].T)
    W3T = cN(inputs["L0_ffn_W3"].T)
    wq, wk, wv = inputs["L1_attn_wq"], inputs["L1_attn_wk"], inputs["L1_attn_wv"]
    tau = cN(inputs["L1_attn_tau"])
    nu = cN(inputs["L1_attn_nu"])
    su_ug = cN(inputs["L1_ffn_SU_ug"])
    sdp = cN(inputs["L1_ffn_SU_dp"])
    wsc = inputs["L1_ffn_Wscale_ug"]
    wsu = cN(wsc[:HF]); wsg = cN(wsc[HF:])
    wdp = cN(inputs["L1_ffn_Wscale_dp"])
    WupT = cN(inputs["L1_ffn_W_up"].T)
    WgateT = cN(inputs["L1_ffn_W_gate"].T)
    WdownT = cN(inputs["L1_ffn_W_down"].T)

    nc_pre = _get_nc("pre", build_pre)
    nc_att0 = _get_nc("att0", lambda: build_att(0))
    nc_att1 = _get_nc("att1", lambda: build_att(1))
    nc_post0 = _get_nc("post0", lambda: build_post(0))
    nc_post1 = _get_nc("post1", lambda: build_post(1))

    def gather_lnT(res):
        out = []
        for b in range(2):
            out.append(np.concatenate(
                [res.results[b * 4 + j]["lnT"] for j in range(4)], axis=1))
        return out

    r = _run(nc_pre, [{"x": xs[c]} for c in range(NCORES)])
    lnT_b = gather_lnT(r)

    att0_maps = []
    for c in range(NCORES):
        g = c % 4
        cols = np.concatenate([
            Wqkv[:, 512 * g:512 * (g + 1)],
            Wqkv[:, 2048 + 512 * g:2048 + 512 * (g + 1)],
            Wqkv[:, 4096 + 512 * g:4096 + 512 * (g + 1)]], axis=1)
        bsl = np.concatenate([
            bqkv[512 * g:512 * (g + 1)],
            bqkv[2048 + 512 * g:2048 + 512 * (g + 1)],
            bqkv[4096 + 512 * g:4096 + 512 * (g + 1)]])
        att0_maps.append({"lnT": cN(lnT_b[c // 4]), "w_sl": cN(cols), "bq": cN(bsl)})
    r = _run(nc_att0, att0_maps)
    attF_b = [np.concatenate([r.results[b * 4 + j]["att_c"] for j in range(4)], axis=0)
              for b in range(2)]

    post0_maps = []
    for c in range(NCORES):
        b, loc = c // 4, (c % 4) * TOK
        post0_maps.append({
            "x": xs[c],
            "attnT": cN(attF_b[b][:, loc:loc + TOK]),
            "wd": Wd, "bd": bd, "w1t": W1T, "w3t": W3T, "w2t": W2T})
    r = _run(nc_post0, post0_maps)
    x1s = [r.results[c]["x2"] for c in range(NCORES)]

    r = _run(nc_pre, [{"x": cN(x1s[c])} for c in range(NCORES)])
    lnT_b = gather_lnT(r)

    att1_maps = []
    for c in range(NCORES):
        g = c % 4
        wsl = np.concatenate([
            cN(wq[512 * g:512 * (g + 1), :].T),
            cN(wk[512 * g:512 * (g + 1), :].T),
            cN(wv[512 * g:512 * (g + 1), :].T)], axis=1)
        att1_maps.append({"lnT": cN(lnT_b[c // 4]), "w_sl": cN(wsl),
                          "tau": tau, "nu": nu})
    r = _run(nc_att1, att1_maps)
    attT_b = [np.concatenate([r.results[b * 4 + j]["att_c"] for j in range(4)], axis=1)
              for b in range(2)]

    post1_maps = []
    for c in range(NCORES):
        b, loc = c // 4, (c % 4) * TOK
        post1_maps.append({
            "x": cN(x1s[c]),
            "attn_tok": cN(attT_b[b][loc:loc + TOK, :]),
            "su_ug": su_ug, "wsu": wsu, "wsg": wsg, "sdp": sdp, "wdp": wdp,
            "w1t": WupT, "w3t": WgateT, "w2t": WdownT})
    r = _run(nc_post1, post1_maps)
    out = np.concatenate([r.results[c]["x2"] for c in range(NCORES)], axis=0)
    return out.reshape(2, S, D).astype(np.float32)
